# version 02 (final)
# baseline (speedup 1.0000x reference)
import numpy as np
import concourse.bass as bass
import concourse.mybir as mybir

F32 = mybir.dt.float32
BF16 = mybir.dt.bfloat16
I32 = mybir.dt.int32
AF = mybir.ActivationFunctionType
ALU = mybir.AluOpType
AX = mybir.AxisListType

T = 4096
D = 1024
NT = T // 128
ND = D // 128
RG8 = [list(range(8))]


class Ctx:

    def __init__(self, nc, block, stack):
        self.nc = nc
        self.block = block
        self.stack = stack
        self.sems = {}
        self.cnt = {}
        self.uid = 0
        self.cvals = {}
        self.cpool_sb = None
        self.cpool_dram = None

    def setup_cpool(self, max_consts=64):
        self.cpool_dram = self.nc.dram_tensor("cpool", [128, max_consts], F32,
                                              kind="ExternalInput").ap()
        self.cpool_sb = self.sb("cpool_sb", [128, max_consts], F32)
        block = self.block

        @block.sync
        def _(sync):
            self.inc(sync.dma_start(out=self.cpool_sb[:, :], in_=self.cpool_dram[:, :]),
                     "cpool", 16)

    def c(self, val):
        key = float(val)
        if key not in self.cvals:
            self.cvals[key] = len(self.cvals)
            assert len(self.cvals) <= self.cpool_sb.shape[-1]
        i = self.cvals[key]
        return self.cpool_sb[:, i:i + 1]

    def cpool_np(self, max_consts=64):
        a = np.zeros((128, max_consts), dtype=np.float32)
        for v, i in self.cvals.items():
            a[:, i] = v
        return a

    def wcp(self, eng):
        eng.wait_ge(self.sem("cpool"), 16)

    def wait(self, eng, name, target=None):
        s = self.sem(name)
        eng.wait_ge(s, self.cnt[name] if target is None else target)

    def sem(self, name):
        if name not in self.sems:
            self.sems[name] = self.stack.enter_context(self.nc.semaphore(name))
            self.cnt[name] = 0
        return self.sems[name]

    def inc(self, instr, name, n=1):
        instr.then_inc(self.sem(name), n)
        self.cnt[name] += n
        return self.cnt[name]

    def val(self, name):
        self.sem(name)
        return self.cnt[name]

    def fresh(self, prefix="s"):
        self.uid += 1
        return f"{prefix}{self.uid}"

    def sb(self, name, shape, dtype):
        return self.stack.enter_context(self.nc.sbuf_tensor(name, shape, dtype))

    def shared(self, key, shape, dtype):
        if not hasattr(self, "_shared"):
            self._shared = {}
        if key not in self._shared:
            self._shared[key] = self.sb(key, shape, dtype)
        t = self._shared[key]
        assert list(t.shape) == list(shape) and t.dtype == dtype
        return t

    def ps(self, shape, dtype=F32, name=None):
        if name is None:
            self.uid += 1
            name = "ps%d" % self.uid
        return self.stack.enter_context(self.nc.psum_tensor(name, shape, dtype))


def ln_stats_emit(ctx, v, x_ap, stats, mv, sem, width=None):
    width = width if width is not None else x_ap.shape[-1]
    nchunk = (width + 511) // 512
    s0 = ctx.val(sem)
    for c in range(nchunk):
        w = min(512, width - c * 512)
        ctx.inc(v.bn_stats(stats[:, 6 * c:6 * c + 6], x_ap[:, c * 512:c * 512 + w]), sem, 1)
    v.wait_ge(ctx.sems[sem], s0 + nchunk)
    ctx.inc(v.bn_aggr(mv[:, 0:2], stats[:, 0:6 * nchunk]), sem, 1)
    return ctx.val(sem)


def rstd_emit(sc, v, mv, eps, mubuf, rbuf):
    sc.activation(out=rbuf[:, 0:1], in_=mv[:, 1:2], func=AF.Ln, bias=float(eps))
    sc.activation(out=rbuf[:, 0:1], in_=rbuf[:, 0:1], func=AF.Exp, scale=-0.5)
    v.tensor_tensor(out=mubuf[:, 0:1], in0=mv[:, 0:1], in1=rbuf[:, 0:1], op=ALU.mult)
    v.tensor_scalar(out=mubuf[:, 0:1], in0=mubuf[:, 0:1], scalar1=-1.0, scalar2=None, op0=ALU.mult)


def np_bf16(x):
    import ml_dtypes
    return np.ascontiguousarray(x).astype(ml_dtypes.bfloat16)


class DmaStream:

    def __init__(self, ctx, name, depth=4):
        self.ctx, self.name, self.depth = ctx, name, depth
        self.n = 0
        self.bases = [ctx.val(f"{name}{s}") for s in range(depth)]

    def issue(self, eng, out, in_, transpose=False, **kw):
        s = self.n % self.depth
        if transpose:
            ins = eng.dma_start_transpose(out, in_)
        else:
            ins = eng.dma_start(out=out, in_=in_, **kw)
        self.ctx.inc(ins, f"{self.name}{s}", 16)
        self.n += 1
        return ins

    def wait(self, eng, i):
        s = i % self.depth
        eng.wait_ge(self.ctx.sem(f"{self.name}{s}"),
                    self.bases[s] + (i // self.depth + 1) * 16)

    def wait_all(self, eng):
        for s in range(self.depth):
            c = self.ctx.val(f"{self.name}{s}")
            if c > self.bases[s]:
                eng.wait_ge(self.ctx.sem(f"{self.name}{s}"), c)


"""Sub-block emitters for mk_sandwich_d4 (v1).

Per-core shapes: x [4096, 1024] f32 in DRAM. Each sub-block computes
x_out = x_in + LN(f(LN(x_in))).

Sync discipline:
- DMA via DmaStream (per-slot sems; one in-flight per slot, paced by consumer).
- Compute sems: per-tile ANALYTIC targets (base + (i+1)*K), asserted.
- Queues: sync = loads; gpsimd = stores/collectives/memsets; scalar = dma
  transposes. then_inc on real instructions; reduce-class ops get edges.
"""

RD = 2


def an(base, i, k):
    return base + (i + 1) * k



def pre_ln_to_fm(ctx, name, x_in, xn_bounce, xn_fm, inner_eps=None, xn_tok=None):
    block = ctx.block
    vs1, svs, ssc, sc2 = name + "vs1", name + "svs", name + "ssc", name + "sc2"

    xbuf = ctx.shared("sh_xb", [128, RD * D], F32)
    xnb = ctx.sb(name + "xn", [128, RD * D], BF16)
    stats = ctx.sb(name + "stt", [128, 12], F32)
    mv = ctx.sb(name + "mv", [128, 2 * NT], F32)
    acol = ctx.sb(name + "ac", [128, NT], F32)
    ccol = ctx.sb(name + "cc", [128, NT], F32)
    tmp = ctx.sb(name + "tmp", [128, NT], F32)

    ld = DmaStream(ctx, "pre_ld")
    b_vs1 = ctx.val(vs1)

    @block.vector
    def _(v):
        for i in range(NT):
            ld.wait(v, i)
            if i >= 1:
                v.wait_ge(ctx.sems[vs1], an(b_vs1, i - 1, 3))
            sl = xbuf[:, (i % RD) * D:(i % RD + 1) * D]
            ctx.inc(v.bn_stats(stats[:, 0:6], sl[:, 0:512]), vs1, 1)
            ctx.inc(v.bn_stats(stats[:, 6:12], sl[:, 512:1024]), vs1, 1)
            v.wait_ge(ctx.sems[vs1], b_vs1 + 3 * i + 2)
            ctx.inc(v.bn_aggr(mv[:, 2 * i:2 * i + 2], stats[:, 0:12]), vs1, 1)
    assert ctx.val(vs1) == b_vs1 + 3 * NT

    @block.sync
    def _(sync):
        for i in range(NT):
            if i >= RD:
                sync.wait_ge(ctx.sems[vs1], an(b_vs1, i - RD, 3))
            ld.issue(sync, xbuf[:, (i % RD) * D:(i % RD + 1) * D],
                     x_in[i * 128:(i + 1) * 128, :])

    @block.scalar
    def _(sc):
        sc.wait_ge(ctx.sems[vs1], b_vs1 + 3 * NT)
        ctx.wcp(sc)
        ctx.inc(sc.activation(out=tmp[:, :], in_=mv[:, 1:2 * NT:2], func=AF.Ln, bias=ctx.c(1e-5)), ssc, 1)
        sc.wait_ge(ctx.sems[ssc], ctx.val(ssc))
        ctx.inc(sc.activation(out=acol[:, :], in_=tmp[:, :], func=AF.Exp, scale=-0.5), ssc, 1)

    if inner_eps is not None:
        @block.vector
        def _(v):
            v.wait_ge(ctx.sems[ssc], ctx.val(ssc))
            ctx.inc(v.tensor_tensor(out=tmp[:, :], in0=acol[:, :], in1=acol[:, :], op=ALU.mult), svs, 1)
            v.wait_ge(ctx.sems[svs], ctx.val(svs))
            ctx.inc(v.tensor_tensor(out=tmp[:, :], in0=tmp[:, :], in1=mv[:, 1:2 * NT:2],
                                    op=ALU.mult), svs, 1)

        @block.scalar
        def _(sc):
            sc.wait_ge(ctx.sems[svs], ctx.val(svs))
            ctx.inc(sc.activation(out=tmp[:, :], in_=tmp[:, :], func=AF.Ln, bias=ctx.c(inner_eps)), ssc, 1)
            sc.wait_ge(ctx.sems[ssc], ctx.val(ssc))
            ctx.inc(sc.activation(out=tmp[:, :], in_=tmp[:, :], func=AF.Exp, scale=-0.5), ssc, 1)

        @block.vector
        def _(v):
            v.wait_ge(ctx.sems[ssc], ctx.val(ssc))
            ctx.inc(v.tensor_tensor(out=acol[:, :], in0=acol[:, :], in1=tmp[:, :], op=ALU.mult), svs, 1)

    @block.vector
    def _(v):
        v.wait_ge(ctx.sems[ssc], ctx.val(ssc))
        ctx.wait(v, svs)
        ctx.inc(v.tensor_tensor(out=ccol[:, :], in0=mv[:, 0:2 * NT:2], in1=acol[:, :], op=ALU.mult), svs, 1)
        v.wait_ge(ctx.sems[svs], ctx.val(svs))
        ctx.inc(v.tensor_scalar(out=ccol[:, :], in0=ccol[:, :], scalar1=-1.0, scalar2=None,
                                op0=ALU.mult), svs, 1)
    coeff_t = ctx.val(svs)

    ld2 = DmaStream(ctx, "pre_ld")
    st2 = DmaStream(ctx, "pre_st")
    b_sc2 = ctx.val(sc2)

    @block.scalar
    def _(sc):
        sc.wait_ge(ctx.sems[svs], coeff_t)
        for i in range(NT):
            ld2.wait(sc, i)
            if i >= RD:
                st2.wait(sc, i - RD)
            sl = xnb[:, (i % RD) * D:(i % RD + 1) * D]
            ctx.inc(sc.activation(out=sl, in_=xbuf[:, (i % RD) * D:(i % RD + 1) * D],
                                  func=AF.Identity, scale=acol[:, i:i + 1],
                                  bias=ccol[:, i:i + 1]), sc2, 1)
            if xn_tok is not None:
                sc.copy(out=xn_tok[:, i * D:(i + 1) * D], in_=sl)
    assert ctx.val(sc2) == b_sc2 + NT

    @block.sync
    def _(sync):
        sync.wait_ge(ctx.sems[vs1], b_vs1 + 3 * NT)
        for i in range(NT):
            if i >= RD:
                sync.wait_ge(ctx.sems[sc2], an(b_sc2, i - RD, 1))
            ld2.issue(sync, xbuf[:, (i % RD) * D:(i % RD + 1) * D],
                      x_in[i * 128:(i + 1) * 128, :])

    @block.gpsimd
    def _(g):
        for i in range(NT):
            g.wait_ge(ctx.sems[sc2], an(b_sc2, i, 1))
            st2.issue(g, xn_bounce[i * 128:(i + 1) * 128, :],
                      xnb[:, (i % RD) * D:(i % RD + 1) * D])

    fmk = name + "fmk"
    if xn_fm is None:
        @block.scalar
        def _(sc):
            st2.wait_all(sc)
            ctx.inc(sc.copy(out=stats[:, 0:1], in_=stats[:, 0:1]), fmk, 1)
        return fmk, ctx.val(fmk)
    tp = DmaStream(ctx, "pre_tp", depth=8)

    @block.scalar
    def _(sc):
        st2.wait_all(sc)
        for j in range(ND):
            tp.issue(sc, xn_fm[:, j * T:(j + 1) * T],
                     xn_bounce[:, j * 128:(j + 1) * 128], transpose=True)
        for j in range(ND):
            tp.wait(sc, j)
        ctx.inc(sc.copy(out=stats[:, 0:1], in_=stats[:, 0:1]), fmk, 1)
    return fmk, ctx.val(fmk)



def post_ln_residual(ctx, name, x_in, x_out, y_half_ap, y_target, y_sem,
                     scale_col=None, pre_bases=None):
    block = ctx.block
    vs, scs = name + "vs", name + "sc"
    b_vs, b_sc = ctx.val(vs), ctx.val(scs)
    if pre_bases is not None:
        assert (b_vs, b_sc) == pre_bases, (b_vs, b_sc, pre_bases)
    PR = 1 if scale_col is not None else 0
    KV = 9 if scale_col is not None else 7

    def var_ready(i):
        return b_vs + PR + i * KV + (4 if scale_col is not None else 3)

    def apply_done(i):
        return b_vs + PR + (i + 1) * KV

    ld = DmaStream(ctx, "pst_ld")
    st = DmaStream(ctx, "pst_st")
    xbuf = ctx.sb(name + "xb", [128, RD * D], F32)
    obuf = ctx.sb(name + "ob", [128, RD * D], F32)
    stats = ctx.sb(name + "stt", [128, 12], F32)
    mv = ctx.sb(name + "mv", [128, 2], F32)
    rcol = ctx.sb(name + "rc", [128, 4], F32)

    @block.vector
    def _(v):
        if scale_col is not None:
            ctx.inc(v.tensor_tensor(out=rcol[:, 2:3], in0=scale_col, in1=scale_col,
                                    op=ALU.mult), vs, 1)
            v.wait_ge(ctx.sems[vs], ctx.val(vs))
        for i in range(NT):
            v.wait_ge(ctx.sems[y_sem], y_target(i))
            if i >= 1:
                v.wait_ge(ctx.sems[vs], b_vs + PR + (i - 1) * KV + 3)
            if i == 0 and getattr(ctx, "dbg_y", None) is not None:
                v.tensor_copy(out=ctx.dbg_y[:, 0:512], in_=y_half_ap(0, 0))
                ctx.inc(v.tensor_copy(out=ctx.dbg_y[:, 512:1024], in_=y_half_ap(0, 1)), "dbgy", 1)
            ctx.inc(v.bn_stats(stats[:, 0:6], y_half_ap(i, 0)), vs, 1)
            ctx.inc(v.bn_stats(stats[:, 6:12], y_half_ap(i, 1)), vs, 1)
            v.wait_ge(ctx.sems[vs], ctx.val(vs))
            ctx.inc(v.bn_aggr(mv[:, 0:2], stats[:, 0:12]), vs, 1)
            if scale_col is not None:
                v.wait_ge(ctx.sems[vs], ctx.val(vs))
                ctx.inc(v.tensor_tensor(out=mv[:, 1:2], in0=mv[:, 1:2], in1=rcol[:, 2:3],
                                        op=ALU.mult), vs, 1)
            assert ctx.val(vs) == var_ready(i)
            v.wait_ge(ctx.sems[scs], an(b_sc, i, 2))
            ld.wait(v, i)
            if i >= RD:
                st.wait(v, i - RD)
            ob = obuf[:, (i % RD) * D:(i % RD + 1) * D]
            xb = xbuf[:, (i % RD) * D:(i % RD + 1) * D]
            if scale_col is not None:
                ctx.inc(v.tensor_tensor(out=rcol[:, 0:1], in0=rcol[:, 0:1], in1=scale_col,
                                        op=ALU.mult), vs, 1)
                v.wait_ge(ctx.sems[vs], ctx.val(vs))
            ctx.inc(v.tensor_tensor(out=rcol[:, 1:2], in0=mv[:, 0:1], in1=rcol[:, 0:1],
                                    op=ALU.mult), vs, 1)
            v.wait_ge(ctx.sems[vs], ctx.val(vs))
            for h in range(2):
                ctx.inc(v.scalar_tensor_tensor(out=ob[:, h * 512:(h + 1) * 512],
                                               in0=y_half_ap(i, h), scalar=rcol[:, 0:1],
                                               in1=xb[:, h * 512:(h + 1) * 512],
                                               op0=ALU.mult, op1=ALU.add), vs, 1)
            v.wait_ge(ctx.sems[vs], ctx.val(vs))
            ctx.inc(v.tensor_scalar(out=ob, in0=ob, scalar1=rcol[:, 1:2], scalar2=None,
                                    op0=ALU.subtract), vs, 1)
            assert ctx.val(vs) == apply_done(i)

    @block.scalar
    def _(sc):
        ctx.wcp(sc)
        for i in range(NT):
            sc.wait_ge(ctx.sems[vs], var_ready(i))
            ctx.inc(sc.activation(out=rcol[:, 3:4], in_=mv[:, 1:2], func=AF.Ln,
                                  bias=ctx.c(1e-5)), scs, 1)
            sc.wait_ge(ctx.sems[scs], b_sc + 2 * i + 1)
            ctx.inc(sc.activation(out=rcol[:, 0:1], in_=rcol[:, 3:4], func=AF.Exp, scale=-0.5),
                    scs, 1)

    @block.sync
    def _(sync):
        for i in range(NT):
            if i >= RD:
                sync.wait_ge(ctx.sems[vs], apply_done(i - RD))
            ld.issue(sync, xbuf[:, (i % RD) * D:(i % RD + 1) * D],
                     x_in[i * 128:(i + 1) * 128, :])

    @block.gpsimd
    def _(g):
        for i in range(NT):
            g.wait_ge(ctx.sems[vs], apply_done(i))
            st.issue(g, x_out[i * 128:(i + 1) * 128, :],
                     obuf[:, (i % RD) * D:(i % RD + 1) * D])

    fin = name + "fin"

    @block.gpsimd
    def _(g):
        st.wait_all(g)
        ctx.inc(g.memset(stats[0:1, 0:1], 0.0), fin, 1)
    return fin, ctx.val(fin)


def plan_post_ln(ctx, name, scale=False):
    vs, scs = name + "vs", name + "sc"
    bases = (ctx.val(vs), ctx.val(scs))
    PR = 1 if scale else 0
    KV = 9 if scale else 7
    b_vs = bases[0]

    def apply_done(i):
        return b_vs + PR + (i + 1) * KV
    return bases, apply_done, vs



def emit_l0_ffn(ctx, x_in, x_out, w_T, ln_gamma, ln_beta, xn_bounce, dbg=None, dbg_bf=None):
    nc, block = ctx.nc, ctx.block
    name = "l0f"
    QB = 128.0

    xn_fm = ctx.sb(name + "xnfm", [128, ND * T], BF16)
    wbuf = ctx.sb(name + "w", [128, ND * D], F32)
    wq = ctx.sb(name + "wq", [128, ND * D], BF16)
    tmpf = ctx.sb(name + "tmpf", [128, T], F32)
    gb = ctx.sb(name + "gb", [128, 2 * ND], F32)
    red = ctx.sb(name + "red", [128, 24], F32)
    smalls = ctx.sb(name + "sml", [128, 8], F32)
    one_r = ctx.sb(name + "oner", [1, 128], F32)
    one_rb = ctx.sb(name + "onerb", [1, 128], BF16)
    scal = ctx.sb(name + "scal", [1, 16], F32)
    wi = ctx.sb(name + "wi", [128, D], I32)
    ps_small = ctx.ps([128, 16], F32)
    gmax_i = nc.dram_tensor(name + "_gmax_i", [1, 1], F32, kind="Internal").ap()
    gmax_o = nc.dram_tensor(name + "_gmax_o", [1, 1], F32, kind="Internal", addr_space="Shared").ap()
    red_dram = nc.dram_tensor(name + "_red_d", [128, 1], F32, kind="Internal").ap()

    vsm, gps, ccs = name + "vsm", name + "gp", name + "cc"
    scB, scq, vsq, mmq = name + "scB", name + "scq", name + "vsq", name + "mmq"

    fm_ready = pre_ln_to_fm(ctx, name + "P", x_in, xn_bounce, xn_fm, inner_eps=1e-3)

    wld = DmaStream(ctx, "wld")

    @block.sync
    def _(sync):
        wld.issue(sync, gb[:, 0:ND], ln_gamma[:, :])
        wld.issue(sync, gb[:, ND:2 * ND], ln_beta[:, :])
        wld.issue(sync, wbuf[:, :], w_T.rearrange("(j p) o -> p j o", p=128))

    @block.gpsimd
    def _(g):
        g.memset(one_r[:, :], 1.0)
        ctx.inc(g.memset(smalls[:, 0:1], 1.0), gps, 1)

    b_scB = ctx.val(scB)

    @block.scalar
    def _(sc):
        sc.wait_ge(ctx.sems[fm_ready[0]], fm_ready[1])
        wld.wait(sc, 0)
        wld.wait(sc, 1)
        for j in range(ND):
            ctx.inc(sc.activation(out=xn_fm[:, j * T:(j + 1) * T], in_=xn_fm[:, j * T:(j + 1) * T],
                                  func=AF.Identity, scale=gb[:, j:j + 1],
                                  bias=gb[:, ND + j:ND + j + 1]), scB, 1)

    b_vsm = ctx.val(vsm)

    @block.vector
    def _(v):
        for j in range(ND):
            v.wait_ge(ctx.sems[scB], an(b_scB, j, 1))
            ctx.inc(v.reduce_max(red[:, j:j + 1], xn_fm[:, j * T:(j + 1) * T],
                                 axis=AX.X, apply_absolute_value=True), vsm, 1)
        v.wait_ge(ctx.sems[vsm], b_vsm + ND)
        ctx.inc(v.reduce_max(red[:, 8:9], red[:, 0:ND], axis=AX.X), vsm, 1)
    absmax_t = ctx.val(vsm)

    msc = DmaStream(ctx, "msc")

    @block.sync
    def _(sync):
        sync.wait_ge(ctx.sems[vsm], absmax_t)
        msc.issue(sync, red_dram[:, :], red[:, 8:9])
        msc.wait(sync, 0)
        msc.issue(sync, tmpf[0:1, 0:128], red_dram.rearrange("p one -> one p"))

    @block.vector
    def _(v):
        msc.wait(v, 1)
        ctx.inc(v.reduce_max(scal[0:1, 0:1], tmpf[0:1, 0:128], axis=AX.X), vsq, 1)

    @block.sync
    def _(sync):
        sync.wait_ge(ctx.sems[vsq], ctx.val(vsq))
        msc.issue(sync, gmax_i[:, :], scal[0:1, 0:1])

    @block.gpsimd
    def _(g):
        msc.wait(g, 2)
        ctx.inc(g.collective_compute("AllReduce", ALU.max, RG8, [gmax_i.opt()], [gmax_o.opt()]),
                ccs, 1)

    @block.sync
    def _(sync):
        sync.wait_ge(ctx.sems[ccs], ctx.val(ccs))
        msc.issue(sync, scal[0:1, 0:1], gmax_o[:, :])
    gmax_idx = 3

    b_scq = ctx.val(scq)

    @block.scalar
    def _(sc):
        wld.wait(sc, 2)
        for j in range(ND):
            ctx.inc(sc.activation(out=wq[:, j * D:(j + 1) * D], in_=wbuf[:, j * D:(j + 1) * D],
                                  func=AF.Abs, accum_out=red[:, 10 + j:11 + j]), scq, 1)

    @block.vector
    def _(v):
        v.wait_ge(ctx.sems[scq], b_scq + ND)
        ctx.inc(v.reduce_sum(red[:, 18:19], red[:, 10:18], axis=AX.X), vsq, 1)

    @block.tensor
    def _(t):
        t.wait_ge(ctx.sems[vsq], ctx.val(vsq))
        t.wait_ge(ctx.sems[gps], 1)
        ctx.inc(t.matmul(ps_small[0:1, 0:1], smalls[:, 0:1], red[:, 18:19], start=True, stop=True),
                mmq, 1)

    @block.scalar
    def _(sc):
        sc.wait_ge(ctx.sems[mmq], ctx.val(mmq))
        ctx.inc(sc.activation(out=scal[0:1, 1:2], in_=ps_small[0:1, 0:1], func=AF.Identity,
                              scale=1.0 / (1024.0 * 1024.0)), scq, 1)
    bh_t = ctx.val(scq)

    @block.vector
    def _(v):
        v.wait_ge(ctx.sems[scq], bh_t)
        msc.wait(v, gmax_idx)
        ctx.inc(v.tensor_scalar(out=scal[0:1, 6:7], in0=scal[0:1, 1:2], scalar1=1e-5,
                                scalar2=None, op0=ALU.add), vsq, 1)
        v.wait_ge(ctx.sems[vsq], ctx.val(vsq))
        ctx.inc(v.reciprocal(scal[0:1, 2:3], scal[0:1, 6:7]), vsq, 1)
        ctx.inc(v.reciprocal(scal[0:1, 3:4], scal[0:1, 0:1]), vsq, 1)
        v.wait_ge(ctx.sems[vsq], ctx.val(vsq))
        ctx.inc(v.tensor_tensor(out=scal[0:1, 4:5], in0=scal[0:1, 1:2], in1=scal[0:1, 0:1],
                                op=ALU.mult), vsq, 1)
        v.wait_ge(ctx.sems[vsq], ctx.val(vsq))
        ctx.inc(v.tensor_scalar(out=scal[0:1, 4:5], in0=scal[0:1, 4:5], scalar1=1.0 / QB,
                                scalar2=None, op0=ALU.mult), vsq, 1)

    @block.tensor
    def _(t):
        t.wait_ge(ctx.sems[vsq], ctx.val(vsq))
        ctx.inc(t.matmul(ps_small[:, 1:4], one_r[0:1, :], scal[0:1, 2:5], start=True, stop=True),
                mmq, 1)

    @block.scalar
    def _(sc):
        sc.wait_ge(ctx.sems[mmq], ctx.val(mmq))
        sc.copy(out=smalls[:, 2:3], in_=ps_small[:, 1:2])
        sc.activation(out=smalls[:, 3:4], in_=ps_small[:, 2:3], func=AF.Identity, scale=QB)
        ctx.inc(sc.copy(out=smalls[:, 4:5], in_=ps_small[:, 3:4]), scq, 1)
    bcast_t = ctx.val(scq)

    qa = ctx.sb(name + "qa", [128, D], F32)
    qs = ctx.sb(name + "qs", [128, D], F32)
    qt = ctx.sb(name + "qt", [128, D], F32)
    qf = ctx.sb(name + "qf", [128, D], F32)

    @block.vector
    def _(v):
        v.wait_ge(ctx.sems[scq], bcast_t)
        for j in range(ND):
            wj = wbuf[:, j * D:(j + 1) * D]
            ctx.inc(v.tensor_scalar(out=wj, in0=wj, scalar1=smalls[:, 2:3], scalar2=None,
                                    op0=ALU.mult), vsq, 1)
            v.wait_ge(ctx.sems[vsq], ctx.val(vsq))
            ctx.inc(v.tensor_scalar(out=qs[:, :], in0=wj, scalar1=0.0, scalar2=None,
                                    op0=ALU.is_ge), vsq, 1)
            v.wait_ge(ctx.sems[vsq], ctx.val(vsq))
            ctx.inc(v.tensor_scalar(out=qs[:, :], in0=qs[:, :], scalar1=2.0, scalar2=-1.0,
                                    op0=ALU.mult, op1=ALU.add), vsq, 1)
            v.wait_ge(ctx.sems[vsq], ctx.val(vsq))
            ctx.inc(v.tensor_tensor(out=qt[:, :], in0=wj, in1=qs[:, :], op=ALU.mult), vsq, 1)
            v.wait_ge(ctx.sems[vsq], ctx.val(vsq))
            ctx.inc(v.tensor_scalar(out=qt[:, :], in0=qt[:, :], scalar1=0.5, scalar2=None,
                                    op0=ALU.add), vsq, 1)
            v.wait_ge(ctx.sems[vsq], ctx.val(vsq))
            ctx.inc(v.tensor_copy(out=wi[:, :], in_=qt[:, :]), vsq, 1)
            v.wait_ge(ctx.sems[vsq], ctx.val(vsq))
            ctx.inc(v.tensor_copy(out=qf[:, :], in_=wi[:, :]), vsq, 1)
            v.wait_ge(ctx.sems[vsq], ctx.val(vsq))
            ctx.inc(v.tensor_tensor(out=qa[:, :], in0=qf[:, :], in1=qt[:, :], op=ALU.is_gt),
                    vsq, 1)
            v.wait_ge(ctx.sems[vsq], ctx.val(vsq))
            ctx.inc(v.tensor_tensor(out=qf[:, :], in0=qf[:, :], in1=qa[:, :], op=ALU.subtract),
                    vsq, 1)
            v.wait_ge(ctx.sems[vsq], ctx.val(vsq))
            ctx.inc(v.tensor_tensor(out=qf[:, :], in0=qf[:, :], in1=qs[:, :], op=ALU.mult),
                    vsq, 1)
            v.wait_ge(ctx.sems[vsq], ctx.val(vsq))
            ctx.inc(v.tensor_scalar(out=wq[:, j * D:(j + 1) * D], in0=qf[:, :], scalar1=1.0,
                                    scalar2=-1.0, op0=ALU.min, op1=ALU.max), vsq, 1)
    wq_t = ctx.val(vsq)

    vcl = name + "vcl"
    b_scx, b_vcl = ctx.val(scq), ctx.val(vcl)

    @block.scalar
    def _(sc):
        sc.wait_ge(ctx.sems[vsm], absmax_t)
        sc.wait_ge(ctx.sems[scq], bcast_t)
        for j in range(ND):
            if j >= 1:
                sc.wait_ge(ctx.sems[vcl], an(b_vcl, j - 1, 1))
            ctx.inc(sc.activation(out=tmpf[:, 0:T],
                                  in_=xn_fm[:, j * T:(j + 1) * T], func=AF.Identity,
                                  scale=smalls[:, 3:4]), scq, 1)

    @block.vector
    def _(v):
        for j in range(ND):
            v.wait_ge(ctx.sems[scq], an(b_scx, j, 1))
            ctx.inc(v.tensor_scalar(out=xn_fm[:, j * T:(j + 1) * T],
                                    in0=tmpf[:, 0:T],
                                    scalar1=QB - 1e-5, scalar2=-(QB - 1e-5),
                                    op0=ALU.min, op1=ALU.max), vcl, 1)
    xq_t = ctx.val(vcl)

    mmy = name + "mmy"
    ps_y = [ctx.ps([128, 512], F32) for _ in range(4)]
    bases, apply_done, post_vs = plan_post_ln(ctx, "l0fR", scale=True)
    b_mmy = ctx.val(mmy)

    @block.tensor
    def _(t):
        t.wait_ge(ctx.sems[vcl], xq_t)
        t.wait_ge(ctx.sems[vsq], wq_t)
        for i in range(NT):
            if i >= 2:
                t.wait_ge(ctx.sems[post_vs], apply_done(i - 2))
            for e in range(2):
                pb = ps_y[2 * (i % 2) + e]
                for j in range(ND):
                    ins = t.matmul(pb[:, :], xn_fm[:, j * T + i * 128:j * T + (i + 1) * 128],
                                   wq[:, j * D + e * 512:j * D + (e + 1) * 512],
                                   start=(j == 0), stop=(j == ND - 1))
                ctx.inc(ins, mmy, 1)

    def y_half(i, h):
        return ps_y[2 * (i % 2) + h][:, :]

    def y_target(i):
        return an(b_mmy, i, 2)

    if dbg is not None:
        dbs = DmaStream(ctx, "dbs")

        @block.gpsimd
        def _(g):
            g.wait_ge(ctx.sems[vcl], xq_t)
            g.wait_ge(ctx.sems[vsq], wq_t)
            dbs.issue(g, dbg[0:1, 0:5], scal[0:1, 0:5])
            dbs.issue(g, dbg[:, 16:19], smalls[:, 2:5])
            dbs.issue(g, dbg[:, 24:33], red[:, 0:9])
            dbs.issue(g, dbg_bf[:, 64:192], xn_fm[:, 0:128])
            for jj in range(ND):
                dbs.issue(g, dbg_bf[:, 320 + 8 * jj:328 + 8 * jj], xn_fm[:, jj * T:jj * T + 8])
            dbs.issue(g, dbg_bf[:, 192:320], wq[:, 0:128])
            dbs.issue(g, ctx.dbg_w1, wbuf[:, 0:128])
            dbs.issue(g, ctx.dbg_w2, wi[:, 0:128])

    return post_ln_residual(ctx, "l0fR", x_in, x_out, y_half, y_target, mmy,
                            scale_col=smalls[:, 4:5], pre_bases=bases)



def emit_l3_attn(ctx, x_in, x_out, bw_T, dw_T, cw_T, a_col, bb_col, cb_row, db_row,
                 xn_bounce):
    nc, block = ctx.nc, ctx.block
    name = "l3a"
    NST = 256

    xn_fm = ctx.sb(name + "xnfm", [128, ND * T], BF16)
    bw_sb = ctx.sb(name + "bw", [128, ND * NST], BF16)
    dw_sb = ctx.sb(name + "dw", [128, ND * D], BF16)
    cw_f32 = ctx.sb(name + "cw", [128, 2 * D], F32)
    bu_il = ctx.sb(name + "bu", [128, 2 * T], F32)
    st_il = ctx.sb(name + "st", [128, 2 * T], F32)
    cols = ctx.sb(name + "cols", [128, 8], F32)
    rows = ctx.sb(name + "rows", [1, 3 * D], F32)
    one_r = ctx.sb(name + "oner", [1, 128], F32)
    one_rb = ctx.sb(name + "onerb", [1, 128], BF16)

    ld = DmaStream(ctx, "wld")
    cld = DmaStream(ctx, "cld")
    scs, vss, mmb, gps = name + "sc", name + "vs", name + "mmb", name + "gp"

    @block.sync
    def _(sync):
        ld.issue(sync, cols[:, 0:2], a_col[:, :])
        ld.issue(sync, cols[:, 2:4], bb_col[:, :])
        ld.issue(sync, rows[0:1, 0:D], cb_row[:, :])
        ld.issue(sync, rows[0:1, D:2 * D], db_row[:, :])
    i_a, i_bb, i_cb, i_db = 0, 1, 2, 3

    @block.gpsimd
    def _(g):
        cld.issue(g, bw_sb.rearrange("p (j n) -> p j n", j=ND),
                  bw_T.rearrange("(j p) n -> p j n", p=128))
        cld.issue(g, dw_sb.rearrange("p (j o) -> p j o", j=ND),
                  dw_T.rearrange("(j p) o -> p j o", p=128))
        cld.issue(g, cw_f32.rearrange("p (j o) -> p j o", j=2),
                  cw_T.rearrange("(j p) o -> p j o", p=128))
        ctx.inc(g.memset(one_r[:, :], 1.0), gps, 1)
    i_bw, i_dw, i_cw = 0, 1, 2

    @block.scalar
    def _(sc):
        ld.wait(sc, i_a)
        ctx.inc(sc.activation(out=cols[:, 4:6], in_=cols[:, 0:2], func=AF.Exp), scs, 1)

    @block.vector
    def _(v):
        ld.wait(v, i_cb)
        ld.wait(v, i_db)
        ctx.inc(v.tensor_tensor(out=rows[0:1, 2 * D:3 * D], in0=rows[0:1, 0:D],
                                in1=rows[0:1, D:2 * D], op=ALU.add), vss, 1)
    brow_t = ctx.val(vss)

    fm_ready = pre_ln_to_fm(ctx, name + "P", x_in, xn_bounce, xn_fm)

    mmu = name + "mmu"
    ps_bu = [ctx.ps([128, 512], F32) for _ in range(4)]
    b_mmu = ctx.val(mmu)
    b_sev = ctx.val(scs)

    @block.tensor
    def _(t):
        t.wait_ge(ctx.sems[fm_ready[0]], fm_ready[1])
        cld.wait(t, i_bw)
        k = 0
        for m in range(2):
            for c in range(8):
                pb = ps_bu[k % 4]
                if k >= 4:
                    t.wait_ge(ctx.sems[scs], b_sev + (k - 4 + 1))
                for j in range(ND):
                    ins = t.matmul(pb[:, :], bw_sb[:, j * NST + m * 128:j * NST + (m + 1) * 128],
                                   xn_fm[:, j * T + c * 512:j * T + (c + 1) * 512],
                                   start=(j == 0), stop=(j == ND - 1))
                ctx.inc(ins, mmu, 1)
                k += 1

    @block.scalar
    def _(sc):
        ld.wait(sc, i_bb)
        k = 0
        for m in range(2):
            for c in range(8):
                sc.wait_ge(ctx.sems[mmu], an(b_mmu, k, 1))
                pb = ps_bu[k % 4]
                dst = bu_il[:, 2 * c * 512 + m: 2 * (c + 1) * 512: 2]
                ctx.inc(sc.activation(out=dst, in_=pb[:, :], func=AF.Identity,
                                      bias=cols[:, 2 + m:3 + m]), scs, 1)
                k += 1
    bu_done = ctx.val(scs)

    b_scan = ctx.val(scs)

    @block.scalar
    def _(sc):
        sc.wait_ge(ctx.sems[scs], bu_done)
        for m in range(2):
            ctx.inc(sc.activation(out=st_il[:, m:m + 1], in_=bu_il[:, m:m + 1],
                                  func=AF.Tanh), scs, 1)
            for tt in range(1, T):
                sc.wait_ge(ctx.sems[scs], ctx.val(scs))
                ctx.inc(sc.activation(out=st_il[:, 2 * tt + m:2 * tt + m + 1],
                                      in_=st_il[:, 2 * (tt - 1) + m:2 * (tt - 1) + m + 1],
                                      func=AF.Tanh, scale=cols[:, 4 + m:5 + m],
                                      bias=bu_il[:, 2 * tt + m:2 * tt + m + 1]), scs, 1)
    scan_done = ctx.val(scs)


    mmy = name + "mmy"
    ps_y = [ctx.ps([128, 512], F32) for _ in range(4)]
    bases, apply_done, post_vs = plan_post_ln(ctx, name + "R", scale=False)
    b_mmy = ctx.val(mmy)

    @block.tensor
    def _(t):
        t.wait_ge(ctx.sems[scs], scan_done)
        cld.wait(t, i_dw)
        cld.wait(t, i_cw)
        t.wait_ge(ctx.sems[vss], brow_t)
        t.wait_ge(ctx.sems[gps], 1)
        for i in range(NT):
            if i >= 2:
                t.wait_ge(ctx.sems[post_vs], apply_done(i - 2))
            for e in range(2):
                pb = ps_y[2 * (i % 2) + e]
                for j in range(ND):
                    t.matmul(pb[:, :], xn_fm[:, j * T + i * 128:j * T + (i + 1) * 128],
                             dw_sb[:, j * D + e * 512:j * D + (e + 1) * 512],
                             start=(j == 0), stop=False)
                for m in range(2):
                    t.matmul(pb[:, :], st_il[:, 2 * i * 128 + m:2 * (i + 1) * 128:2],
                             cw_f32[:, m * D + e * 512:m * D + (e + 1) * 512],
                             start=False, stop=False)
                ctx.inc(t.matmul(pb[:, :], one_r[0:1, :],
                                 rows[0:1, 2 * D + e * 512:2 * D + (e + 1) * 512],
                                 start=False, stop=True), mmy, 1)

    def y_half(i, h):
        return ps_y[2 * (i % 2) + h][:, :]

    def y_target(i):
        return an(b_mmy, i, 2)

    return post_ln_residual(ctx, name + "R", x_in, x_out, y_half, y_target, mmy,
                            pre_bases=bases)



class PostLN:

    def __init__(self, ctx, name, x_in, x_out, y_half_ap, y_target, y_sem,
                 scale_col=None, obuf=None):
        self.ctx, self.name = ctx, name
        self.x_in, self.x_out = x_in, x_out
        self.y_half_ap, self.y_target, self.y_sem = y_half_ap, y_target, y_sem
        self.scale_col = scale_col
        self.vs, self.scs = name + "vs", name + "sc"
        self.b_vs, self.b_sc = ctx.val(self.vs), ctx.val(self.scs)
        self.PR = 1 if scale_col is not None else 0
        self.KV = 9 if scale_col is not None else 7
        self.ld = DmaStream(ctx, name + "_ld")
        self.st = DmaStream(ctx, name + "_st")
        self.xbuf = ctx.shared("sh_xb", [128, RD * D], F32)
        self.obuf = obuf if obuf is not None else ctx.sb(name + "ob", [128, RD * D], F32)
        self.stats = ctx.sb(name + "stt", [128, 12], F32)
        self.mv = ctx.sb(name + "mv", [128, 2], F32)
        self.rcol = ctx.sb(name + "rc", [128, 4], F32)
        self.prologue_done = False

    def var_ready(self, i):
        return self.b_vs + self.PR + i * self.KV + (4 if self.scale_col is not None else 3)

    def apply_done(self, i):
        return self.b_vs + self.PR + (i + 1) * self.KV

    def vec_tile(self, v, i):
        ctx = self.ctx
        vs, scs = self.vs, self.scs
        if not self.prologue_done:
            if self.scale_col is not None:
                ctx.inc(v.tensor_tensor(out=self.rcol[:, 2:3], in0=self.scale_col,
                                        in1=self.scale_col, op=ALU.mult), vs, 1)
                v.wait_ge(ctx.sems[vs], ctx.val(vs))
            self.prologue_done = True
        v.wait_ge(ctx.sems[self.y_sem], self.y_target(i))
        if i >= 1:
            v.wait_ge(ctx.sems[vs], self.b_vs + self.PR + (i - 1) * self.KV + 3)
        ctx.inc(v.bn_stats(self.stats[:, 0:6], self.y_half_ap(i, 0)), vs, 1)
        ctx.inc(v.bn_stats(self.stats[:, 6:12], self.y_half_ap(i, 1)), vs, 1)
        v.wait_ge(ctx.sems[vs], ctx.val(vs))
        ctx.inc(v.bn_aggr(self.mv[:, 0:2], self.stats[:, 0:12]), vs, 1)
        if self.scale_col is not None:
            v.wait_ge(ctx.sems[vs], ctx.val(vs))
            ctx.inc(v.tensor_tensor(out=self.mv[:, 1:2], in0=self.mv[:, 1:2],
                                    in1=self.rcol[:, 2:3], op=ALU.mult), vs, 1)
        assert ctx.val(vs) == self.var_ready(i)
        v.wait_ge(ctx.sems[scs], an(self.b_sc, i, 2))
        self.ld.wait(v, i)
        if i >= RD:
            self.st.wait(v, i - RD)
        ob = self.obuf[:, (i % RD) * D:(i % RD + 1) * D]
        xb = self.xbuf[:, (i % RD) * D:(i % RD + 1) * D]
        if self.scale_col is not None:
            ctx.inc(v.tensor_tensor(out=self.rcol[:, 0:1], in0=self.rcol[:, 0:1],
                                    in1=self.scale_col, op=ALU.mult), vs, 1)
            v.wait_ge(ctx.sems[vs], ctx.val(vs))
        ctx.inc(v.tensor_tensor(out=self.rcol[:, 1:2], in0=self.mv[:, 0:1],
                                in1=self.rcol[:, 0:1], op=ALU.mult), vs, 1)
        v.wait_ge(ctx.sems[vs], ctx.val(vs))
        for h in range(2):
            ctx.inc(v.scalar_tensor_tensor(out=ob[:, h * 512:(h + 1) * 512],
                                           in0=self.y_half_ap(i, h), scalar=self.rcol[:, 0:1],
                                           in1=xb[:, h * 512:(h + 1) * 512],
                                           op0=ALU.mult, op1=ALU.add), vs, 1)
        v.wait_ge(ctx.sems[vs], ctx.val(vs))
        ctx.inc(v.tensor_scalar(out=ob, in0=ob, scalar1=self.rcol[:, 1:2], scalar2=None,
                                op0=ALU.subtract), vs, 1)
        assert ctx.val(vs) == self.apply_done(i)

    def sc_tile(self, sc, i):
        ctx = self.ctx
        if i == 0:
            ctx.wcp(sc)
        sc.wait_ge(ctx.sems[self.vs], self.var_ready(i))
        ctx.inc(sc.activation(out=self.rcol[:, 3:4], in_=self.mv[:, 1:2], func=AF.Ln,
                              bias=ctx.c(1e-5)), self.scs, 1)
        sc.wait_ge(ctx.sems[self.scs], self.b_sc + 2 * i + 1)
        ctx.inc(sc.activation(out=self.rcol[:, 0:1], in_=self.rcol[:, 3:4], func=AF.Exp,
                              scale=-0.5), self.scs, 1)

    def emit_sync_loads(self, first_wait=None):
        ctx, block = self.ctx, self.ctx.block
        @block.sync
        def _(sync):
            if first_wait is not None:
                sync.wait_ge(ctx.sems[first_wait[0]], first_wait[1])
            for i in range(NT):
                self.ld_tile(sync, i)

    def ld_tile(self, sync, i):
        ctx = self.ctx
        if i >= RD:
            sync.wait_ge(ctx.sems[self.vs], self.apply_done(i - RD))
        self.ld.issue(sync, self.xbuf[:, (i % RD) * D:(i % RD + 1) * D],
                      self.x_in[i * 128:(i + 1) * 128, :])

    def emit_gpsimd_stores(self):
        ctx, block = self.ctx, self.ctx.block
        @block.gpsimd
        def _(g):
            for i in range(NT):
                self.st_tile(g, i)

    def st_tile(self, g, i):
        ctx = self.ctx
        g.wait_ge(ctx.sems[self.vs], self.apply_done(i))
        self.st.issue(g, self.x_out[i * 128:(i + 1) * 128, :],
                      self.obuf[:, (i % RD) * D:(i % RD + 1) * D])

    def finish(self):
        ctx, block = self.ctx, self.ctx.block
        fin = self.name + "fin"
        @block.gpsimd
        def _(g):
            self.st.wait_all(g)
            ctx.inc(g.memset(self.stats[0:1, 0:1], 0.0), fin, 1)
        return fin, ctx.val(fin)



def emit_l3_ffn(ctx, x_in, x_out, w1_T, w2_T, h1_T, h2_T, w3_T,
                s1_col, s2_col, hs1_col, hs2_col, s3_row, xn_bounce):
    nc, block = ctx.nc, ctx.block
    name = "l3f"
    TB = 1024
    NP = T // TB

    xn_p = ctx.sb(name + "xnp", [128, ND * TB], BF16)
    P1 = ctx.sb(name + "p1", [128, 32 * TB], BF16)
    P2 = ctx.sb(name + "p2", [128, 32 * TB], BF16)
    slab = ctx.sb(name + "slab", [128, 32 * 512], BF16)
    tmp1 = slab[:, 8 * 512:16 * 512]
    scol = ctx.sb(name + "scol", [128, 4 * 32], F32)
    s3b = ctx.sb(name + "s3b", [128, D], BF16)
    s3r = ctx.sb(name + "s3r", [1, D], F32)
    one_r = ctx.sb(name + "oner", [1, 128], F32)
    one_rb = ctx.sb(name + "onerb", [1, 128], BF16)

    def ybuf_ap(i, h):
        base = (i % 8) * 2 * D + h * D
        return P2[:, base:base + D].bitcast(F32)

    ld = DmaStream(ctx, "wld8", depth=8)
    sl = DmaStream(ctx, "l3sl", depth=1)
    tpx = DmaStream(ctx, "pre_tp", depth=8)
    mm, scv, vsv, yv, gps = name + "mm", name + "sc", name + "vs", name + "yv", name + "gp"
    mmb = name + "mmb"

    bounce_done = pre_ln_to_fm(ctx, name + "P", x_in, xn_bounce, None)

    @block.sync
    def _(sync):
        ld.issue(sync, scol[:, 0:32], s1_col[:, :])
        ld.issue(sync, scol[:, 32:64], s2_col[:, :])
        ld.issue(sync, scol[:, 64:96], hs1_col[:, :])
        ld.issue(sync, scol[:, 96:128], hs2_col[:, :])
        ld.issue(sync, s3r[0:1, :], s3_row[:, :])
    i_s1, i_s2, i_h1, i_h2, i_s3 = 0, 1, 2, 3, 4

    @block.gpsimd
    def _(g):
        ctx.inc(g.memset(one_r[:, :], 1.0), gps, 1)

    ps_pool = [ctx.ps([128, 512], F32) for _ in range(8)]

    @block.tensor
    def _(t):
        ld.wait(t, i_s3)
        t.wait_ge(ctx.sems[gps], 1)
        ctx.inc(t.matmul(ps_pool[0][:, :], one_r[0:1, :], s3r[0:1, 0:512],
                         start=True, stop=True), mmb, 1)
        ctx.inc(t.matmul(ps_pool[1][:, :], one_r[0:1, :], s3r[0:1, 512:1024],
                         start=True, stop=True), mmb, 1)

    @block.scalar
    def _(sc):
        sc.wait_ge(ctx.sems[mmb], ctx.val(mmb))
        sc.copy(out=s3b[:, 0:512], in_=ps_pool[0][:, :])
        ctx.inc(sc.copy(out=s3b[:, 512:1024], in_=ps_pool[1][:, :]), scv, 1)
    s3b_t = ctx.val(scv)

    pl = PostLN(ctx, name + "R", x_in, x_out, ybuf_ap, None, yv)
    b_mm = ctx.val(mm)
    b_scv = ctx.val(scv)
    b_vsv = ctx.val(vsv)
    b_yv = ctx.val(yv)

    def y_target(i):
        p, tl = divmod(i, 8)
        return b_yv + p * 16 + 8 + (tl + 1)
    pl.y_target = y_target

    GA, GB, GC = 128, 64, 16
    GP = GA + 2 * GB + GC

    def mm_t(p, idx):
        return b_mm + p * GP + idx + 1

    def scv_t(p, idx):
        return b_scv + p * 192 + idx + 1

    def sl_i(p, idx):
        return p * 34 + idx

    slab_consumed = {}

    @block.tensor
    def _(t):
        t.wait_ge(ctx.sems[scv], s3b_t)
        for p in range(NP):
            for j in range(ND):
                tpx.wait(t, p * ND + j)
            for mb in range(8):
                for w12 in range(2):
                    if w12 == 1:
                        t.wait_ge(ctx.sems[scv], scv_t(p, mb * 8 + 7))
                    elif mb >= 1:
                        t.wait_ge(ctx.sems[vsv], b_vsv + p * 64 + (mb - 1) * 8 + 8)
                    elif p >= 1:
                        t.wait_ge(ctx.sems[yv], b_yv + p * 16)
                    sl.wait(t, sl_i(p, mb * 2 + w12))
                    for grp in range(8):
                        m_l, tc = divmod(grp, 2)
                        pb = ps_pool[grp]
                        for k in range(ND):
                            ins = t.matmul(
                                pb[:, :],
                                slab[:, k * 512 + m_l * 128:k * 512 + (m_l + 1) * 128],
                                xn_p[:, k * TB + tc * 512:k * TB + (tc + 1) * 512],
                                start=(k == 0), stop=(k == ND - 1))
                        ctx.inc(ins, mm, 1)
                    slab_consumed[sl_i(p, mb * 2 + w12)] = ctx.val(mm)
            for stg in range(2):
                src_p = P1 if stg == 0 else P2
                for mb in range(8):
                    if stg == 0 and mb == 0:
                        t.wait_ge(ctx.sems[vsv], b_vsv + p * 64 + 64)
                    elif stg == 1 and mb == 0:
                        t.wait_ge(ctx.sems[scv], scv_t(p, 64 + 63))
                    else:
                        t.wait_ge(ctx.sems[scv], scv_t(p, 64 + stg * 64 + (mb - 1) * 8 + 7))
                    li = sl_i(p, 16 + stg * 8 + mb)
                    sl.wait(t, li)
                    for grp in range(8):
                        m_l, tc = divmod(grp, 2)
                        pb = ps_pool[grp]
                        for k in range(32):
                            ins = t.matmul(
                                pb[:, :],
                                slab[:, k * 512 + m_l * 128:k * 512 + (m_l + 1) * 128],
                                src_p[:, k * TB + tc * 512:k * TB + (tc + 1) * 512],
                                start=(k == 0), stop=(k == 31))
                        ctx.inc(ins, mm, 1)
                    slab_consumed[li] = ctx.val(mm)
            for e in range(2):
                li = sl_i(p, 32 + e)
                sl.wait(t, li)
                for tl in range(8):
                    pb = ps_pool[tl]
                    if e == 0:
                        if tl == 0:
                            t.wait_ge(ctx.sems[scv], scv_t(p, 191))
                    else:
                        t.wait_ge(ctx.sems[yv], b_yv + p * 16 + tl + 1)
                    for k in range(32):
                        ins = t.matmul(
                            pb[:, :],
                            P1[:, k * TB + tl * 128:k * TB + (tl + 1) * 128],
                            slab[:, k * 512:(k + 1) * 512],
                            start=(k == 0), stop=(k == 31))
                    ctx.inc(ins, mm, 1)
                slab_consumed[li] = ctx.val(mm)
    assert ctx.val(mm) == b_mm + NP * GP

    @block.scalar
    def _(sc):
        sc.wait_ge(ctx.sems[bounce_done[0]], bounce_done[1])
        ld.wait(sc, i_s1)
        ld.wait(sc, i_s2)
        ld.wait(sc, i_h1)
        ld.wait(sc, i_h2)
        for p in range(NP):
            if p >= 1:
                sc.wait_ge(ctx.sems[mm], mm_t(p - 1, GA - 1))
            for j in range(ND):
                tpx.issue(sc, xn_p[:, j * TB:(j + 1) * TB],
                          xn_bounce[p * TB:(p + 1) * TB, j * 128:(j + 1) * 128],
                          transpose=True)
            for mb in range(8):
                for grp in range(8):
                    sc.wait_ge(ctx.sems[mm], mm_t(p, mb * 16 + grp))
                    m_l, tc = divmod(grp, 2)
                    m_glob = mb * 4 + m_l
                    ctx.inc(sc.activation(
                        out=tmp1[:, grp * 512:(grp + 1) * 512],
                        in_=ps_pool[grp][:, :], func=AF.Silu,
                        scale=scol[:, m_glob:m_glob + 1]), scv, 1)
            for stg in range(2):
                dstP = P2 if stg == 0 else P1
                off = 64 if stg == 0 else 96
                for mb in range(8):
                    for grp in range(8):
                        if stg == 0 and mb == 0 and grp == 0 and p >= 1:
                            sc.wait_ge(ctx.sems[pl.vs], pl.apply_done(p * 8 - 1))
                        sc.wait_ge(ctx.sems[mm], mm_t(p, 128 + stg * 64 + mb * 8 + grp))
                        m_l, tc = divmod(grp, 2)
                        m_glob = mb * 4 + m_l
                        ctx.inc(sc.activation(
                            out=dstP[:, m_glob * TB + tc * 512:m_glob * TB + (tc + 1) * 512],
                            in_=ps_pool[grp][:, :], func=AF.Gelu,
                            scale=scol[:, off + m_glob:off + m_glob + 1]), scv, 1)
            for tl in range(8):
                pl.sc_tile(sc, p * 8 + tl)

    @block.vector
    def _(v):
        for p in range(NP):
            for mb in range(8):
                for grp in range(8):
                    v.wait_ge(ctx.sems[mm], mm_t(p, mb * 16 + 8 + grp))
                    v.wait_ge(ctx.sems[scv], scv_t(p, mb * 8 + grp))
                    if p >= 1 and mb == 0 and grp == 0:
                        v.wait_ge(ctx.sems[mm], mm_t(p - 1, GP - 1))
                    m_l, tc = divmod(grp, 2)
                    m_glob = mb * 4 + m_l
                    ctx.inc(v.scalar_tensor_tensor(
                        out=P1[:, m_glob * TB + tc * 512:m_glob * TB + (tc + 1) * 512],
                        in0=ps_pool[grp][:, :], scalar=scol[:, 32 + m_glob:33 + m_glob],
                        in1=tmp1[:, grp * 512:(grp + 1) * 512],
                        op0=ALU.mult, op1=ALU.mult), vsv, 1)
            for e in range(2):
                for tl in range(8):
                    i_glob = p * 8 + tl
                    v.wait_ge(ctx.sems[mm], mm_t(p, 256 + e * 8 + tl))
                    if i_glob >= 8 and e == 0:
                        v.wait_ge(ctx.sems[pl.vs], pl.apply_done(i_glob - 8))
                    if p == 0 and e == 0 and tl == 0:
                        v.wait_ge(ctx.sems[scv], s3b_t)
                    ctx.inc(v.scalar_tensor_tensor(
                        out=ybuf_ap(i_glob, e),
                        in0=ps_pool[tl][:, :], scalar=1.0,
                        in1=s3b[:, e * 512:(e + 1) * 512],
                        op0=ALU.bypass, op1=ALU.mult), yv, 1)
            for tl in range(8):
                pl.vec_tile(v, p * 8 + tl)

    @block.gpsimd
    def _(g):
        n = 0
        for p in range(NP):
            defs = []
            for mb in range(8):
                for w12 in range(2):
                    w = w1_T if w12 == 0 else w2_T
                    defs.append((w, ND, mb))
            for stg in range(2):
                for mb in range(8):
                    defs.append((h1_T if stg == 0 else h2_T, 32, mb))
            for e in range(2):
                defs.append((w3_T, 32, e))
            for li, (w, nk, mb) in enumerate(defs):
                if li == 16:
                    g.wait_ge(ctx.sems[vsv], b_vsv + p * 64 + 64)
                if n >= 1:
                    g.wait_ge(ctx.sems[mm], slab_consumed[n - 1])
                sl.issue(g, slab.rearrange("q (k c) -> q k c", c=512)[:, 0:nk, :],
                         w.rearrange("(k q) o -> q k o", q=128)[:, :, mb * 512:(mb + 1) * 512])
                n += 1
            for tl in range(8):
                pl.st_tile(g, p * 8 + tl)

    pl.emit_sync_loads(first_wait=bounce_done)
    return pl.finish()



def emit_l2_ffn(ctx, x_in, x_out, w1_d, w2_d, wr_col, b1_col, b2_row, xn_bounce, stop=None):
    nc, block = ctx.nc, ctx.block
    name = "l2f"
    TB = 1024
    NP = T // TB
    NITER = 80
    CGELU = 0.044715
    SGELU = 0.7978845608028654

    xn_p = ctx.sb(name + "xnp", [128, ND * TB], BF16)
    P1 = ctx.sb(name + "p1", [128, 32 * TB], BF16)
    slab = ctx.sb(name + "slab", [128, 32 * 512], BF16)
    zbuf = ctx.sb(name + "zb", [128, 4 * 512], F32)
    tbuf = ctx.sb(name + "tb", [128, 4 * 512], F32)
    ybuf = ctx.sb(name + "yb", [128, 8 * D], F32)
    xtok = ctx.sb(name + "xtok", [128, 8 * D], BF16)
    wrc = ctx.sb(name + "wrc", [128, 8], BF16)
    b1c = ctx.sb(name + "b1c", [128, 32], F32)
    b2r = ctx.sb(name + "b2r", [1, D], F32)
    s32 = ctx.sb(name + "s32", [32, 128], F32)
    stil = ctx.sb(name + "stil", [128, 32], F32)
    ge = ctx.sb(name + "ge", [128, 32], F32)
    gate = ctx.sb(name + "gate", [128, 32], F32)
    bis = ctx.sb(name + "bis", [128, 8], F32)
    srow = ctx.sb(name + "srow", [1, 512], F32)
    ones_m = ctx.sb(name + "onesm", [128, 128], F32)
    idm = ctx.sb(name + "idm", [128, 128], F32)
    one_r = ctx.sb(name + "oner", [1, 128], F32)
    one_rb = ctx.sb(name + "onerb", [1, 128], BF16)
    sc_d = nc.dram_tensor(name + "_scores", [32, 128], F32, kind="Internal").ap()

    ld = DmaStream(ctx, "wld8", depth=8)
    sl = DmaStream(ctx, "l3sl", depth=1)
    tpx = DmaStream(ctx, "pre_tp", depth=8)
    msc = DmaStream(ctx, "msc")
    sst = DmaStream(ctx, "l2ss", depth=8)
    mm, scv, vsv, yv, gps = name + "mm", name + "sc", name + "vs", name + "yv", name + "gp"

    bounce_done = pre_ln_to_fm(ctx, name + "P", x_in, xn_bounce, None)

    @block.sync
    def _(sync):
        ld.issue(sync, b1c[:, :], b1_col[:, :])
        ld.issue(sync, b2r[0:1, :], b2_row[:, :])
    i_b1, i_b2 = 0, 1
    wrl = DmaStream(ctx, "l2wr")

    @block.gpsimd
    def _(g):
        wrl.issue(g, wrc[:, :], wr_col[:, :])
    i_wr = 0

    @block.gpsimd
    def _(g):
        ctx.inc(g.memset(ones_m[:, :], 1.0), gps, 1)
        g.memset(one_r[:, :], 1.0)
        g.wait_ge(ctx.sems[gps], ctx.val(gps))
        ctx.inc(g.affine_select(idm[:, :], ones_m[:, :], pattern=[[1, 128]],
                                compare_op=ALU.is_equal, fill=0.0, base=0,
                                channel_multiplier=-1), gps, 1)

    b_mm0 = ctx.val(mm)

    ps_s = ctx.ps([1, 512], F32)
    ps_t = ctx.ps([128, 32], F32)
    ps_c = ctx.ps([128, 1], F32)
    b_mm, b_scv, b_vsv = ctx.val(mm), ctx.val(scv), ctx.val(vsv)

    @block.tensor
    def _(t):
        wrl.wait(t, i_wr)
        for p in range(NP):
            for j in range(ND):
                tpx.wait(t, p * ND + j)
            for c in range(2):
                if p + c > 0:
                    t.wait_ge(ctx.sems[scv], b_scv + p * 2 + c)
                for j in range(ND):
                    ins = t.matmul(ps_s[0:1, :], wrc[:, j:j + 1],
                                   xn_p[:, j * TB + c * 512:j * TB + (c + 1) * 512],
                                   start=(j == 0), stop=(j == ND - 1))
                ctx.inc(ins, mm, 1)
    sc_mm_t = ctx.val(mm)

    @block.scalar
    def _(sc):
        sc.wait_ge(ctx.sems[bounce_done[0]], bounce_done[1])
        for p in range(NP):
            if p >= 1:
                sc.wait_ge(ctx.sems[mm], b_mm + 2 * p)
            for j in range(ND):
                tpx.issue(sc, xn_p[:, j * TB:(j + 1) * TB],
                          xn_bounce[p * TB:(p + 1) * TB, j * 128:(j + 1) * 128],
                          transpose=True)
            for c in range(2):
                n = p * 2 + c
                sc.wait_ge(ctx.sems[mm], an(b_mm, n, 1))
                if n >= 1:
                    sst.wait(sc, n - 1)
                ctx.inc(sc.copy(out=srow[0:1, :], in_=ps_s[0:1, :]), scv, 1)
    b_scst = ctx.val(scv)

    @block.gpsimd
    def _(g):
        for p in range(NP):
            for c in range(2):
                g.wait_ge(ctx.sems[scv], an(b_scv, p * 2 + c, 1))
                sst.issue(g, sc_d[(p * 2 + c) * 4:(p * 2 + c) * 4 + 4, :], srow[0:1, :])

    @block.sync
    def _(sync):
        for k in range(8):
            sst.wait(sync, k)
        msc.issue(sync, s32[:, :], sc_d[:, :])
    i_s32 = 0

    mmt = name + "mmt"

    @block.tensor
    def _(t):
        msc.wait(t, i_s32)
        t.wait_ge(ctx.sems[gps], 2)
        ctx.inc(t.transpose(ps_t[:, :], s32[:, :], idm[0:32, 0:32]), mmt, 1)

    @block.scalar
    def _(sc):
        sc.wait_ge(ctx.sems[mmt], ctx.val(mmt))
        ctx.inc(sc.copy(out=stil[:, :], in_=ps_t[:, :]), scv, 1)
    stil_t = ctx.val(scv)

    b2_mm = ctx.val(mm)
    b2_scv = ctx.val(scv)
    b2_vsv = ctx.val(vsv)
    KB = 12

    @block.vector
    def _(v):
        v.wait_ge(ctx.sems[scv], stil_t)
        v.memset(bis[:, 0:1], -16.0)
        ctx.inc(v.memset(bis[:, 1:2], 16.0), vsv, 1)
        for it in range(NITER):
            v.wait_ge(ctx.sems[vsv], ctx.val(vsv))
            if it >= 1:
                v.wait_ge(ctx.sems[scv], b2_scv + it)
            ctx.inc(v.tensor_tensor(out=bis[:, 2:3], in0=bis[:, 0:1], in1=bis[:, 1:2],
                                    op=ALU.add), vsv, 1)
            v.wait_ge(ctx.sems[vsv], ctx.val(vsv))
            ctx.inc(v.tensor_scalar(out=bis[:, 2:3], in0=bis[:, 2:3], scalar1=0.5,
                                    scalar2=None, op0=ALU.mult), vsv, 1)
            v.wait_ge(ctx.sems[vsv], ctx.val(vsv))
            ctx.inc(v.tensor_scalar(out=ge[:, :], in0=stil[:, :], scalar1=bis[:, 2:3],
                                    scalar2=None, op0=ALU.is_ge), vsv, 1)
            v.wait_ge(ctx.sems[vsv], ctx.val(vsv))
            ctx.inc(v.reduce_sum(bis[:, 4:5], ge[:, :], axis=AX.X), vsv, 1)
            v.wait_ge(ctx.sems[scv], b2_scv + it + 1)
            ctx.inc(v.tensor_scalar(out=bis[:, 3:4], in0=bis[:, 5:6], scalar1=2048.0,
                                    scalar2=None, op0=ALU.is_ge), vsv, 1)
            v.wait_ge(ctx.sems[vsv], ctx.val(vsv))
            ctx.inc(v.tensor_tensor(out=bis[:, 6:7], in0=bis[:, 2:3], in1=bis[:, 0:1],
                                    op=ALU.subtract), vsv, 1)
            v.wait_ge(ctx.sems[vsv], ctx.val(vsv))
            ctx.inc(v.tensor_tensor(out=bis[:, 6:7], in0=bis[:, 6:7], in1=bis[:, 3:4],
                                    op=ALU.mult), vsv, 1)
            v.wait_ge(ctx.sems[vsv], ctx.val(vsv))
            ctx.inc(v.tensor_tensor(out=bis[:, 0:1], in0=bis[:, 0:1], in1=bis[:, 6:7],
                                    op=ALU.add), vsv, 1)
            ctx.inc(v.tensor_scalar(out=bis[:, 7:8], in0=bis[:, 3:4], scalar1=-1.0,
                                    scalar2=1.0, op0=ALU.mult, op1=ALU.add), vsv, 1)
            v.wait_ge(ctx.sems[vsv], ctx.val(vsv))
            ctx.inc(v.tensor_tensor(out=bis[:, 6:7], in0=bis[:, 2:3], in1=bis[:, 1:2],
                                    op=ALU.subtract), vsv, 1)
            v.wait_ge(ctx.sems[vsv], ctx.val(vsv))
            ctx.inc(v.tensor_tensor(out=bis[:, 6:7], in0=bis[:, 6:7], in1=bis[:, 7:8],
                                    op=ALU.mult), vsv, 1)
            v.wait_ge(ctx.sems[vsv], ctx.val(vsv))
            ctx.inc(v.tensor_tensor(out=bis[:, 1:2], in0=bis[:, 1:2], in1=bis[:, 6:7],
                                    op=ALU.add), vsv, 1)
        v.wait_ge(ctx.sems[vsv], ctx.val(vsv))
        ctx.inc(v.tensor_scalar(out=ge[:, :], in0=stil[:, :], scalar1=bis[:, 0:1],
                                scalar2=None, op0=ALU.is_ge), vsv, 1)
        v.wait_ge(ctx.sems[scv], b2_scv + NITER + 1)
        v.wait_ge(ctx.sems[vsv], ctx.val(vsv))
        ctx.inc(v.tensor_tensor(out=gate[:, :], in0=gate[:, :], in1=ge[:, :],
                                op=ALU.mult), vsv, 1)
    gate_t = ctx.val(vsv)
    assert gate_t == b2_vsv + 1 + NITER * KB + 2

    @block.tensor
    def _(t):
        for it in range(NITER):
            t.wait_ge(ctx.sems[vsv], b2_vsv + 1 + it * KB + 4)
            if it >= 1:
                t.wait_ge(ctx.sems[scv], b2_scv + it)
            ctx.inc(t.matmul(ps_c[:, :], ones_m[:, :], bis[:, 4:5], start=True, stop=True),
                    mm, 1)
    b3_mm = ctx.val(mm) - NITER

    @block.scalar
    def _(sc):
        for it in range(NITER):
            sc.wait_ge(ctx.sems[mm], b3_mm + it + 1)
            ctx.inc(sc.copy(out=bis[:, 5:6], in_=ps_c[:, :]), scv, 1)
        sc.wait_ge(ctx.sems[scv], stil_t)
        ctx.inc(sc.activation(out=gate[:, :], in_=stil[:, :], func=AF.Sigmoid), scv, 1)
    assert ctx.val(scv) == b2_scv + NITER + 1

    if stop == "p2":
        dmp = DmaStream(ctx, "l2dmp")

        @block.gpsimd
        def _(g):
            g.wait_ge(ctx.sems[vsv], gate_t)
            dmp.issue(g, x_out[0:128, 0:32], gate[:, :])
            dmp.issue(g, x_out[128:256, 0:32], stil[:, :])
            dmp.issue(g, x_out[256:384, 0:8], bis[:, :])

        fin2 = name + "f2"

        @block.gpsimd
        def _(g):
            dmp.wait(g, 0)
            dmp.wait(g, 1)
            dmp.wait(g, 2)
            ctx.inc(g.memset(srow[0:1, 0:1], 0.0), fin2, 1)
        return fin2, ctx.val(fin2)

    ps_pool = [ctx.ps([128, 512], F32) for _ in range(4)]
    pl = PostLN(ctx, name + "R", x_in, x_out,
                lambda i, h: ybuf[:, (i % 8) * D + h * 512:(i % 8) * D + (h + 1) * 512],
                None, yv)
    b4_mm = ctx.val(mm)
    b4_scv = ctx.val(scv)
    b4_vsv = ctx.val(vsv)
    b_yv = ctx.val(yv)

    def y_target(i):
        p, tl = divmod(i, 8)
        return b_yv + p * 16 + 8 + (tl + 1)
    pl.y_target = y_target

    GA, GC = 64, 16
    GP = (GA + GC) if stop != "A" else GA

    def mm4_t(p, idx):
        return b4_mm + p * GP + idx + 1

    SLP = 10 if stop != "A" else 8

    def sl_i(p, idx):
        return p * SLP + idx

    slab_consumed = {}

    def scvA_t(p, mb, grp, k):
        return b4_scv + p * 192 + (mb * 8 + grp) * 3 + k + 1

    def vsvA_t(p, mb, grp, k):
        return b4_vsv + p * 256 + (mb * 8 + grp) * 4 + k + 1

    @block.tensor
    def _(t):
        ld.wait(t, i_b1)
        ld.wait(t, i_b2)
        t.wait_ge(ctx.sems[vsv], gate_t)
        for p in range(NP):
            for j in range(ND):
                tpx.wait(t, 32 + p * ND + j)
            for mb in range(8):
                sl.wait(t, sl_i(p, mb))
                for grp in range(8):
                    m_l, tc = divmod(grp, 2)
                    pb = ps_pool[grp % 4]
                    if mb * 8 + grp >= 4:
                        pi = mb * 8 + grp - 4
                        t.wait_ge(ctx.sems[scv], scvA_t(p, pi // 8, pi % 8, 0))
                    elif p >= 1:
                        if stop == "A":
                            t.wait_ge(ctx.sems[scv], scvA_t(p - 1, 7, 4 + grp, 0))
                        else:
                            t.wait_ge(ctx.sems[yv], b_yv + p * 16)
                    for k in range(ND):
                        ins = t.matmul(
                            pb[:, :],
                            slab[:, k * 512 + m_l * 128:k * 512 + (m_l + 1) * 128],
                            xn_p[:, k * TB + tc * 512:k * TB + (tc + 1) * 512],
                            start=(k == 0), stop=(k == ND - 1))
                    ctx.inc(ins, mm, 1)
                slab_consumed[sl_i(p, mb)] = ctx.val(mm)
            if stop == "A":
                continue
            for e in range(2):
                li = sl_i(p, 8 + e)
                sl.wait(t, li)
                for tl in range(8):
                    pb = ps_pool[tl % 4]
                    gi = e * 8 + tl
                    if gi >= 4:
                        t.wait_ge(ctx.sems[yv], b_yv + p * 16 + gi - 3)
                    else:
                        pi = 60 + gi
                        t.wait_ge(ctx.sems[scv], scvA_t(p, pi // 8, pi % 8, 0))
                    t.wait_ge(ctx.sems[vsv], b4_vsv + p * 256 + 256)
                    for k in range(32):
                        t.matmul(pb[:, :],
                                 P1[:, k * TB + tl * 128:k * TB + (tl + 1) * 128],
                                 slab[:, k * 512:(k + 1) * 512],
                                 start=(k == 0), stop=False)
                    ctx.inc(t.matmul(pb[:, :], one_r[0:1, :],
                                     b2r[0:1, e * 512:(e + 1) * 512],
                                     start=False, stop=True), mm, 1)
                slab_consumed[li] = ctx.val(mm)
    if stop != "A":
        assert ctx.val(mm) == b4_mm + NP * GP

    @block.scalar
    def _(sc):
        ctx.wcp(sc)
        ld.wait(sc, i_b1)
        for p in range(NP):
            if p == 0:
                sc.wait_ge(ctx.sems[mm], sc_mm_t)
            else:
                sc.wait_ge(ctx.sems[mm], mm4_t(p - 1, GA - 1))
            for j in range(ND):
                tpx.issue(sc, xn_p[:, j * TB:(j + 1) * TB],
                          xn_bounce[p * TB:(p + 1) * TB, j * 128:(j + 1) * 128],
                          transpose=True)
            for mb in range(8):
                for grp in range(8):
                    ti = mb * 8 + grp
                    sc.wait_ge(ctx.sems[mm], mm4_t(p, ti))
                    zs = zbuf[:, (ti % 4) * 512:(ti % 4 + 1) * 512]
                    ts_ = tbuf[:, (ti % 4) * 512:(ti % 4 + 1) * 512]
                    m_l, tc = divmod(grp, 2)
                    m_glob = mb * 4 + m_l
                    if ti >= 4:
                        pi = ti - 4
                        sc.wait_ge(ctx.sems[vsv], vsvA_t(p, pi // 8, pi % 8, 3))
                    ctx.inc(sc.activation(out=zs, in_=ps_pool[grp % 4][:, :], func=AF.Identity,
                                          bias=b1c[:, m_glob:m_glob + 1]), scv, 1)
                    sc.wait_ge(ctx.sems[scv], ctx.val(scv))
                    ctx.inc(sc.activation(out=ts_, in_=zs, func=AF.Square), scv, 1)
                    sc.wait_ge(ctx.sems[vsv], vsvA_t(p, mb, grp, 1))
                    ctx.inc(sc.activation(out=ts_, in_=ts_, func=AF.Tanh, scale=SGELU), scv, 1)
            if stop not in ("A", "B"):
                for tl in range(8):
                    pl.sc_tile(sc, p * 8 + tl)

    xls = DmaStream(ctx, "l2xl", depth=8)

    @block.vector
    def _(v):
        for p in range(NP):
            for mb in range(8):
                for grp in range(8):
                    ti = mb * 8 + grp
                    zs = zbuf[:, (ti % 4) * 512:(ti % 4 + 1) * 512]
                    ts_ = tbuf[:, (ti % 4) * 512:(ti % 4 + 1) * 512]
                    m_l, tc = divmod(grp, 2)
                    m_glob = mb * 4 + m_l
                    v.wait_ge(ctx.sems[scv], scvA_t(p, mb, grp, 1))
                    ctx.inc(v.tensor_scalar(out=ts_, in0=ts_, scalar1=CGELU, scalar2=1.0,
                                            op0=ALU.mult, op1=ALU.add), vsv, 1)
                    v.wait_ge(ctx.sems[vsv], ctx.val(vsv))
                    ctx.inc(v.tensor_tensor(out=ts_, in0=ts_, in1=zs, op=ALU.mult), vsv, 1)
                    v.wait_ge(ctx.sems[scv], scvA_t(p, mb, grp, 2))
                    ctx.inc(v.tensor_scalar(out=ts_, in0=ts_, scalar1=0.5, scalar2=0.5,
                                            op0=ALU.mult, op1=ALU.add), vsv, 1)
                    v.wait_ge(ctx.sems[vsv], ctx.val(vsv))
                    ctx.inc(v.tensor_tensor(
                        out=P1[:, m_glob * TB + tc * 512:m_glob * TB + (tc + 1) * 512],
                        in0=ts_, in1=zs, op=ALU.mult), vsv, 1)
            if stop == "A":
                continue
            for e in range(2):
                for tl in range(8):
                    i_glob = p * 8 + tl
                    v.wait_ge(ctx.sems[mm], mm4_t(p, GA + e * 8 + tl))
                    xls.wait(v, i_glob)
                    if i_glob >= 8 and e == 0 and stop != "B":
                        v.wait_ge(ctx.sems[pl.vs], pl.apply_done(i_glob - 8))
                    ctx.inc(v.scalar_tensor_tensor(
                        out=ybuf[:, (i_glob % 8) * D + e * 512:(i_glob % 8) * D + (e + 1) * 512],
                        in0=ps_pool[tl % 4][:, :], scalar=gate[:, (i_glob % 32):(i_glob % 32) + 1],
                        in1=xtok[:, (i_glob % 8) * D + e * 512:(i_glob % 8) * D + (e + 1) * 512],
                        op0=ALU.mult, op1=ALU.add), yv, 1)
            if stop not in ("A", "B"):
                for tl in range(8):
                    pl.vec_tile(v, p * 8 + tl)

    @block.sync
    def _(sync):
        if stop == "A":
            pass
        else:
            sync.wait_ge(ctx.sems[bounce_done[0]], bounce_done[1])
            for p in range(NP):
                for tl in range(8):
                    i = p * 8 + tl
                    if i >= 8:
                        sync.wait_ge(ctx.sems[yv], b_yv + ((i - 8) // 8) * 16 + 8 + ((i - 8) % 8) + 1)
                    xls.issue(sync, xtok[:, (i % 8) * D:(i % 8 + 1) * D],
                              xn_bounce[i * 128:(i + 1) * 128, :])
                if stop != "B":
                    for tl in range(8):
                        pl.ld_tile(sync, p * 8 + tl)

    @block.gpsimd
    def _(g):
        last_key = None
        for p in range(NP):
            for mb in range(8):
                if last_key is not None:
                    g.wait_ge(ctx.sems[mm], slab_consumed[last_key])
                sl.issue(g, slab.rearrange("q (k c) -> q k c", c=512)[:, 0:ND, :],
                         w1_d.rearrange("(k q) o -> q k o", q=128)[:, :, mb * 512:(mb + 1) * 512])
                last_key = sl_i(p, mb)
            if stop == "A":
                continue
            for e in range(2):
                g.wait_ge(ctx.sems[mm], slab_consumed[last_key])
                sl.issue(g, slab.rearrange("q (k c) -> q k c", c=512)[:, :, :],
                         w2_d.rearrange("(k q) o -> q k o", q=128)[:, :, e * 512:(e + 1) * 512])
                last_key = sl_i(p, 8 + e)
            if stop not in ("A", "B"):
                for tl in range(8):
                    pl.st_tile(g, p * 8 + tl)

    if stop == "B":
        dmpB = DmaStream(ctx, "l2dmB")
        finB = name + "fB"

        @block.gpsimd
        def _(g):
            g.wait_ge(ctx.sems[yv], b_yv + NP * 16)
            dmpB.issue(g, x_out[0:128, :], ybuf[:, 0:1024])
            dmpB.wait(g, 0)
            ctx.inc(g.memset(srow[0:1, 0:1], 0.0), finB, 1)
        return finB, ctx.val(finB)

    if stop == "A":
        dmp2 = DmaStream(ctx, "l2dm2")
        finA = name + "fA"

        @block.gpsimd
        def _(g):
            g.wait_ge(ctx.sems[vsv], b4_vsv + NP * 256)
            dmp2.issue(g, x_out[0:128, :], P1.bitcast(F32)[:, 0:1024])
            dmp2.wait(g, 0)
            ctx.inc(g.memset(srow[0:1, 0:1], 0.0), finA, 1)
        return finA, ctx.val(finA)

    return pl.finish()



def emit_l1_ffn(ctx, x_in, x_out, wr_T_d, ws0_T, ws1_T, cen_T, ebias_row, br_d, bs_row,
                xn_bounce):
    nc, block = ctx.nc, ctx.block
    name = "l1f"
    W = 512
    NW = T // W

    xn_fm = ctx.sb(name + "xnfm", [128, ND * T], BF16)
    wsum = ctx.sb(name + "wsum", [128, ND * D], BF16)
    slab = ctx.sb(name + "slab", [128, ND * D], BF16)
    gu = ctx.sb(name + "gu", [128, 2 * ND * W], BF16)
    gb16 = ctx.sb(name + "gb", [128, 16 * W], BF16)
    ybuf = ctx.sb(name + "yb", [128, 4 * D], F32)
    xtok = ctx.sb(name + "xtok", [128, 4 * D], BF16)
    cenc = ctx.sb(name + "cen", [128, ND * 16], BF16)
    sfm = ctx.sb(name + "sfm", [16, 2 * 512], F32)
    stok = ctx.sb(name + "stok", [128, 32 * 16], F32)
    g_tok = ctx.sb(name + "gtok", [128, 32 * 16], F32)
    g_fm = ctx.sb(name + "gfm", [16, T], BF16)
    brs = ctx.sb(name + "brs", [16, D], BF16)
    bsr = ctx.sb(name + "bsr", [1, 2 * D], F32)
    brow = ctx.sb(name + "brow", [1, D], F32)
    ebias = ctx.sb(name + "eb", [1, 16], F32)
    ebias_b = ctx.sb(name + "ebb", [128, 16], F32)
    one_r = ctx.sb(name + "oner", [1, 128], F32)
    one_rb = ctx.sb(name + "onerb", [1, 128], BF16)
    idm = ctx.sb(name + "idm", [128, 128], F32)
    ones_m = ctx.sb(name + "onesm", [128, 128], F32)
    red = ctx.sb(name + "red", [128, 8], F32)
    wtmp = None
    wr_bf = nc.dram_tensor(name + "_wrbf", [16 * D, D], BF16, kind="Internal").ap()
    g_dram = nc.dram_tensor(name + "_gd", [16, T], BF16, kind="Internal").ap()
    grow = ctx.sb(name + "grow", [1, 2 * 512], BF16)

    wtmp = gu[:, 0:4 * D].bitcast(F32)
    ld = DmaStream(ctx, "wld8", depth=8)
    sl = DmaStream(ctx, "l3sl", depth=1)
    pc = DmaStream(ctx, "l1pc", depth=1)
    wsl = DmaStream(ctx, "l1ws", depth=2)
    xls = DmaStream(ctx, "l2xl", depth=8)
    mm, scv, vsv, yv, gps = name + "mm", name + "sc", name + "vs", name + "yv", name + "gp"

    fm_ready = pre_ln_to_fm(ctx, name + "P", x_in, xn_bounce, xn_fm)

    @block.sync
    def _(sync):
        ld.issue(sync, bsr[0:1, :], bs_row[:, :])
        ld.issue(sync, ebias[0:1, :], ebias_row[:, :])
    i_bs, i_eb = 0, 1
    brl = DmaStream(ctx, "l1br", depth=1)

    @block.gpsimd
    def _(g):
        brl.issue(g, brs[:, :], br_d[:, :])
    i_br = 0

    cenl = DmaStream(ctx, "l1ce", depth=1)

    @block.gpsimd
    def _(g):
        g.memset(one_r[:, :], 1.0)
        g.memset(one_rb[:, :], 1.0)
        ctx.inc(g.memset(ones_m[:, :], 1.0), gps, 1)
        g.wait_ge(ctx.sems[gps], ctx.val(gps))
        ctx.inc(g.affine_select(idm[:, :], ones_m[:, :], pattern=[[1, 128]],
                                compare_op=ALU.is_equal, fill=0.0, base=0,
                                channel_multiplier=-1), gps, 1)
        cenl.issue(g, cenc.rearrange("q (j e) -> q j e", e=16),
                   cen_T.rearrange("(j q) e -> q j e", q=128))
        for e in range(16):
            if e >= 1:
                pc.wait(g, 2 * e - 1)
            pc.issue(g, slab.rearrange("q (j o) -> q j o", o=D),
                     wr_T_d[e].rearrange("(j q) o -> q j o", q=128))
            pc.wait(g, 2 * e)
            pc.issue(g, wr_bf.rearrange("(e j q) o -> e q j o", e=16, q=128)[e],
                     slab.rearrange("q (j o) -> q j o", o=D))
    precast_done = 32

    b_vsw = ctx.val(vsv)

    @block.sync
    def _(sync):
        for j in range(ND):
            if j >= 1:
                sync.wait_ge(ctx.sems[vsv], b_vsw + 2 * j)
            wsl.issue(sync, wtmp[:, 0:D], ws0_T[j * 128:(j + 1) * 128, :])
            wsl.issue(sync, wtmp[:, D:2 * D], ws1_T[j * 128:(j + 1) * 128, :])

    @block.vector
    def _(v):
        for j in range(ND):
            wsl.wait(v, 2 * j)
            wsl.wait(v, 2 * j + 1)
            ctx.inc(v.tensor_tensor(out=wtmp[:, 0:D], in0=wtmp[:, 0:D], in1=wtmp[:, D:2 * D],
                                    op=ALU.add), vsv, 1)
            v.wait_ge(ctx.sems[vsv], ctx.val(vsv))
            ctx.inc(v.tensor_scalar(out=wsum[:, j * D:(j + 1) * D], in0=wtmp[:, 0:D],
                                    scalar1=0.5, scalar2=None, op0=ALU.mult), vsv, 1)
    wsum_t = ctx.val(vsv)

    b_vsb = ctx.val(vsv)

    @block.vector
    def _(v):
        ld.wait(v, i_bs)
        ctx.inc(v.tensor_tensor(out=brow[0:1, :], in0=bsr[0:1, 0:D], in1=bsr[0:1, D:2 * D],
                                op=ALU.add), vsv, 1)
        v.wait_ge(ctx.sems[vsv], ctx.val(vsv))
        ctx.inc(v.tensor_scalar(out=brow[0:1, :], in0=brow[0:1, :], scalar1=0.5,
                                scalar2=None, op0=ALU.mult), vsv, 1)
    brow_t = ctx.val(vsv)

    ps_pool = [ctx.ps([128, 512], F32) for _ in range(8)]
    b_mm = ctx.val(mm)
    mmb = name + "mmb"
    sfmE, stokE = name + "sfE", name + "stE"
    b_sfmE, b_stokE, b_mmb = ctx.val(sfmE), ctx.val(stokE), ctx.val(mmb)

    @block.tensor
    def _(t):
        t.wait_ge(ctx.sems[fm_ready[0]], fm_ready[1])
        cenl.wait(t, 0)
        ld.wait(t, i_eb)
        t.wait_ge(ctx.sems[gps], 2)
        ctx.inc(t.matmul(ps_pool[2][:, 0:16], one_r[0:1, :], ebias[0:1, :],
                         start=True, stop=True), mmb, 1)
        for c in range(8):
            if c >= 2:
                t.wait_ge(ctx.sems[sfmE], an(b_sfmE, c - 2, 1))
            for j in range(ND):
                ins = t.matmul(ps_pool[c % 2][0:16, :], cenc[:, j * 16:(j + 1) * 16],
                               xn_fm[:, j * T + c * 512:j * T + (c + 1) * 512],
                               start=(j == 0), stop=(j == ND - 1))
            ctx.inc(ins, mm, 1)
            if c >= 1:
                cc = c - 1
                t.wait_ge(ctx.sems[sfmE], an(b_sfmE, cc, 1))
                for j in range(4):
                    i = cc * 4 + j
                    if i >= 4:
                        t.wait_ge(ctx.sems[stokE], an(b_stokE, i - 4, 1))
                    ctx.inc(t.transpose(ps_pool[3 + i % 4][0:128, 0:16],
                                        sfm[:, (cc % 2) * 512 + j * 128:(cc % 2) * 512 + (j + 1) * 128],
                                        idm[0:16, 0:16]), mmb, 1)
        cc = 7
        t.wait_ge(ctx.sems[sfmE], an(b_sfmE, cc, 1))
        for j in range(4):
            i = cc * 4 + j
            t.wait_ge(ctx.sems[stokE], an(b_stokE, i - 4, 1))
            ctx.inc(t.transpose(ps_pool[3 + i % 4][0:128, 0:16],
                                sfm[:, (cc % 2) * 512 + j * 128:(cc % 2) * 512 + (j + 1) * 128],
                                idm[0:16, 0:16]), mmb, 1)
    sc_mm = ctx.val(mm)

    @block.scalar
    def _(sc):
        sc.wait_ge(ctx.sems[mmb], b_mmb + 1)
        ctx.inc(sc.copy(out=ebias_b[:, :], in_=ps_pool[2][:, 0:16]), scv, 1)
        for c in range(8):
            sc.wait_ge(ctx.sems[mm], an(b_mm, c, 1))
            if c >= 2:
                sc.wait_ge(ctx.sems[mmb], b_mmb + 1 + 4 * (c - 1))
            ctx.inc(sc.copy(out=sfm[:, (c % 2) * 512:(c % 2 + 1) * 512],
                            in_=ps_pool[c % 2][0:16, :]), sfmE, 1)
            if c >= 1:
                for j in range(4):
                    i = (c - 1) * 4 + j
                    sc.wait_ge(ctx.sems[mmb], b_mmb + 1 + i + 1)
                    ctx.inc(sc.copy(out=stok[:, i * 16:(i + 1) * 16],
                                    in_=ps_pool[3 + i % 4][0:128, 0:16]), stokE, 1)
        for j in range(4):
            i = 28 + j
            sc.wait_ge(ctx.sems[mmb], b_mmb + 1 + i + 1)
            ctx.inc(sc.copy(out=stok[:, i * 16:(i + 1) * 16],
                            in_=ps_pool[3 + i % 4][0:128, 0:16]), stokE, 1)
    stok_t = ctx.val(stokE)

    b_vsg = ctx.val(vsv)
    b_scg = ctx.val(scv)
    KG = 11

    @block.vector
    def _(v):
        v.wait_ge(ctx.sems[stokE], stok_t)
        for i in range(NT):
            sl_ = stok[:, i * 16:(i + 1) * 16]
            gl = g_tok[:, i * 16:(i + 1) * 16]
            ctx.inc(v.tensor_tensor(out=sl_, in0=sl_, in1=ebias_b[:, :], op=ALU.add), vsv, 1)
            v.wait_ge(ctx.sems[vsv], ctx.val(vsv))
            ctx.inc(v.reduce_max(red[:, 0:1], sl_, axis=AX.X, negate=True), vsv, 1)
            v.wait_ge(ctx.sems[vsv], ctx.val(vsv))
            v.wait_ge(ctx.sems[scv], an(b_scg, i, 1))
            ctx.inc(v.reduce_sum(red[:, 1:2], gl, axis=AX.X), vsv, 1)
            v.wait_ge(ctx.sems[vsv], ctx.val(vsv))
            ctx.inc(v.reciprocal(red[:, 2:3], red[:, 1:2]), vsv, 1)
            v.wait_ge(ctx.sems[vsv], ctx.val(vsv))
            ctx.inc(v.tensor_scalar(out=gl, in0=gl, scalar1=red[:, 2:3], scalar2=None,
                                    op0=ALU.mult), vsv, 1)
            v.wait_ge(ctx.sems[vsv], ctx.val(vsv))
            ctx.inc(v.reduce_max(red[:, 3:4], gl, axis=AX.X), vsv, 1)
            v.wait_ge(ctx.sems[vsv], ctx.val(vsv))
            ctx.inc(v.tensor_scalar(out=sl_, in0=gl, scalar1=red[:, 3:4], scalar2=None,
                                    op0=ALU.is_lt), vsv, 1)
            v.wait_ge(ctx.sems[vsv], ctx.val(vsv))
            ctx.inc(v.tensor_tensor(out=sl_, in0=gl, in1=sl_, op=ALU.mult), vsv, 1)
            v.wait_ge(ctx.sems[vsv], ctx.val(vsv))
            ctx.inc(v.reduce_max(red[:, 4:5], sl_, axis=AX.X), vsv, 1)
            v.wait_ge(ctx.sems[vsv], ctx.val(vsv))
            ctx.inc(v.tensor_scalar(out=sl_, in0=gl, scalar1=red[:, 4:5], scalar2=None,
                                    op0=ALU.is_ge), vsv, 1)
            v.wait_ge(ctx.sems[vsv], ctx.val(vsv))
            ctx.inc(v.tensor_tensor(out=gl, in0=gl, in1=sl_, op=ALU.mult), vsv, 1)
    gate_t = ctx.val(vsv)
    assert gate_t == b_vsg + NT * KG

    @block.scalar
    def _(sc):
        for i in range(NT):
            sc.wait_ge(ctx.sems[vsv], b_vsg + i * KG + 2)
            ctx.inc(sc.activation(out=g_tok[:, i * 16:(i + 1) * 16],
                                  in_=stok[:, i * 16:(i + 1) * 16], func=AF.Exp,
                                  bias=red[:, 0:1]), scv, 1)

    mmg = name + "mmg"
    b_scf = ctx.val(scv)

    @block.tensor
    def _(t):
        t.wait_ge(ctx.sems[vsv], gate_t)
        for i in range(NT):
            if i >= 4:
                t.wait_ge(ctx.sems[scv], an(b_scf, i - 4, 1))
            ctx.inc(t.transpose(ps_pool[3 + i % 4][0:16, 0:128],
                                g_tok[:, i * 16:(i + 1) * 16], idm[:, :]), mmg, 1)

    @block.scalar
    def _(sc):
        for i in range(NT):
            sc.wait_ge(ctx.sems[mmg], i + 1)
            ctx.inc(sc.copy(out=g_fm[:, i * 128:(i + 1) * 128], in_=ps_pool[3 + i % 4][0:16, 0:128]),
                    scv, 1)
    gfm_t = ctx.val(scv)

    gst = DmaStream(ctx, "l1gs", depth=1)

    @block.gpsimd
    def _(g):
        g.wait_ge(ctx.sems[scv], gfm_t)
        gst.issue(g, g_dram[:, :], g_fm[:, :])

    grl = DmaStream(ctx, "l1gr", depth=2)
    mmw = name + "mmw"
    b_mmw = ctx.val(mmw)

    pl = PostLN(ctx, name + "R", x_in, x_out,
                lambda i, h: ybuf[:, (i % 4) * D + h * 512:(i % 4) * D + (h + 1) * 512],
                None, yv)
    b2_mm = ctx.val(mm)
    b2_scv = ctx.val(scv)
    b2_vsv = ctx.val(vsv)
    b_yv = ctx.val(yv)

    def y_target(i):
        w, tt = divmod(i, 4)
        return b_yv + w * 8 + 2 * tt + 2
    pl.y_target = y_target

    slab_done = {}

    def grp_mm_t(w):
        return b2_mm + (w + 1) * 17

    @block.tensor
    def _(t):
        t.wait_ge(ctx.sems[scv], gfm_t)
        t.wait_ge(ctx.sems[vsv], wsum_t)
        t.wait_ge(ctx.sems[vsv], brow_t)
        brl.wait(t, i_br)
        for w in range(NW):
            for e in range(16):
                grl.wait(t, w * 16 + e)
                if w * 16 + e >= 1:
                    t.wait_ge(ctx.sems[scv], b2_scv + w * 16 + e)
                if w >= 1:
                    t.wait_ge(ctx.sems[vsv], b2_vsv + ((w - 1) * 16 + e + 1) * ND)
                if w >= 1 and e == 0:
                    t.wait_ge(ctx.sems[yv], b_yv + w * 8)
                n = w * 16 + e
                ctx.inc(t.matmul(ps_pool[7][:, :], one_rb[0:1, :],
                                 grow[0:1, (n % 2) * 512:(n % 2 + 1) * 512],
                                 start=True, stop=True), mmw, 1)
            t.wait_ge(ctx.sems[scv], b2_scv + (w + 1) * 16)
            for e in range(16):
                sl.wait(t, w * 16 + e)
                t.wait_ge(ctx.sems[vsv], b2_vsv + (w * 16 + e + 1) * ND)
                for tt in range(4):
                    for o in range(2):
                        pb = ps_pool[tt * 2 + o]
                        if e == 0 and w >= 1:
                            t.wait_ge(ctx.sems[yv], b_yv + w * 8)
                        for k in range(ND):
                            ins = t.matmul(
                                pb[:, :],
                                gu[:, (e % 2) * ND * W + k * W + tt * 128:
                                      (e % 2) * ND * W + k * W + (tt + 1) * 128],
                                slab[:, k * D + o * 512:k * D + (o + 1) * 512],
                                start=(e == 0 and k == 0), stop=False)
                ctx.inc(ins, mm, 1)
                slab_done[w * 16 + e] = ctx.val(mm)
            for tt in range(4):
                for o in range(2):
                    pb = ps_pool[tt * 2 + o]
                    for k in range(ND):
                        t.matmul(pb[:, :],
                                 xn_fm[:, k * T + w * 512 + tt * 128:k * T + w * 512 + (tt + 1) * 128],
                                 wsum[:, k * D + o * 512:k * D + (o + 1) * 512],
                                 start=False, stop=False)
                    t.matmul(pb[:, :], g_fm[:, w * 512 + tt * 128:w * 512 + (tt + 1) * 128],
                             brs[:, o * 512:(o + 1) * 512], start=False, stop=False)
                    ins = t.matmul(pb[:, :], one_r[0:1, :], brow[0:1, o * 512:(o + 1) * 512],
                                   start=False, stop=True)
            ctx.inc(ins, mm, 1)
    grp_mm = ctx.val(mm)

    @block.scalar
    def _(sc):
        for w in range(NW):
            for e in range(16):
                sc.wait_ge(ctx.sems[mmw], an(b_mmw, w * 16 + e, 1))
                ctx.inc(sc.copy(out=gb16[:, e * W:(e + 1) * W], in_=ps_pool[7][:, :]), scv, 1)
            for tt in range(4):
                pl.sc_tile(sc, w * 4 + tt)

    @block.vector
    def _(v):
        for w in range(NW):
            for e in range(16):
                v.wait_ge(ctx.sems[scv], b2_scv + w * 16 + e + 1)
                if w * 16 + e >= 2:
                    v.wait_ge(ctx.sems[mm], slab_done[w * 16 + e - 2])
                for k in range(ND):
                    ctx.inc(v.tensor_tensor(
                        out=gu[:, (e % 2) * ND * W + k * W:(e % 2) * ND * W + (k + 1) * W],
                        in0=xn_fm[:, k * T + w * 512:k * T + (w + 1) * 512],
                        in1=gb16[:, e * W:(e + 1) * W], op=ALU.mult), vsv, 1)
            for tt in range(4):
                for o in range(2):
                    i_glob = w * 4 + tt
                    v.wait_ge(ctx.sems[mm], grp_mm_t(w))
                    xls.wait(v, i_glob)
                    if i_glob >= 4 and o == 0:
                        v.wait_ge(ctx.sems[pl.vs], pl.apply_done(i_glob - 4))
                    ctx.inc(v.scalar_tensor_tensor(
                        out=ybuf[:, (i_glob % 4) * D + o * 512:(i_glob % 4) * D + (o + 1) * 512],
                        in0=ps_pool[tt * 2 + o][:, :], scalar=1.0,
                        in1=xtok[:, (i_glob % 4) * D + o * 512:(i_glob % 4) * D + (o + 1) * 512],
                        op0=ALU.bypass, op1=ALU.add), yv, 1)
            for tt in range(4):
                pl.vec_tile(v, w * 4 + tt)

    @block.sync
    def _(sync):
        sync.wait_ge(ctx.sems[fm_ready[0]], fm_ready[1])
        gst.wait(sync, 0)
        for w in range(NW):
            for e in range(16):
                n = w * 16 + e
                if n >= 2:
                    sync.wait_ge(ctx.sems[mmw], an(b_mmw, n - 2, 1))
                grl.issue(sync, grow[0:1, (n % 2) * 512:(n % 2 + 1) * 512],
                          g_dram[e:e + 1, w * W:(w + 1) * W])
            for tt in range(4):
                i = w * 4 + tt
                if i >= 4:
                    sync.wait_ge(ctx.sems[yv], y_target(i - 4))
                xls.issue(sync, xtok[:, (i % 4) * D:(i % 4 + 1) * D],
                          xn_bounce[i * 128:(i + 1) * 128, :])
            for tt in range(4):
                pl.ld_tile(sync, w * 4 + tt)

    @block.gpsimd
    def _(g):
        pc.wait(g, precast_done - 1)
        for w in range(NW):
            for e in range(16):
                n = w * 16 + e
                if n >= 1:
                    g.wait_ge(ctx.sems[mm], slab_done[n - 1])
                sl.issue(g, slab.rearrange("q (j o) -> q j o", o=D),
                         wr_bf.rearrange("(e j q) o -> e q j o", e=16, q=128)[e])
            for tt in range(4):
                pl.st_tile(g, w * 4 + tt)

    return pl.finish()



def emit_l2_attn(ctx, x_in, x_out, wq_T, wk_T, wv_T, wg_T, wo_T, gn_cols,
                 cos_d, sin_d, maskT_d, crossdec_d, kdec_d, xn_bounce):
    nc, block = ctx.nc, ctx.block
    name = "l2a"
    H = 4
    NCH = 32
    GC = [float((1.0 - 2.0 ** (-5.0 - h)) ** 128) for h in range(H)]

    xn_fm = ctx.sb(name + "xnfm", [128, ND * T], BF16)
    wbuf = ctx.sb(name + "wbuf", [128, 16 * 1024], BF16)
    cosb = ctx.sb(name + "cos", [128, 2 * 512], F32)
    sinb = ctx.sb(name + "sin", [128, 2 * 512], F32)
    rsc = ctx.sb(name + "rsc", [128, 4 * 512], F32)
    stg = ctx.sb(name + "stg", [128, 2048], BF16)
    mask = ctx.sb(name + "mask", [128, H * 128], F32)
    cdec = ctx.sb(name + "cdec", [128, H * 128], F32)
    kdecc = ctx.sb(name + "kdec", [128, H], F32)
    gnc = ctx.sb(name + "gnc", [128, H], F32)
    S = ctx.sb(name + "S", [128, H * 1024], F32)
    S_bf = ctx.sb(name + "Sbf", [128, H * 1024], BF16)
    qf = ctx.sb(name + "qf", [128, 2 * 1024], BF16)
    qt = ctx.sb(name + "qt", [128, 1024], BF16)
    kf = ctx.sb(name + "kf", [128, 2 * 1024], BF16)
    ktok = ctx.sb(name + "ktok", [128, 2 * 1024], BF16)
    ktd = ctx.sb(name + "ktd", [128, 1024], BF16)
    sqb = ctx.sb(name + "sqb", [128, 2 * 512], BF16)
    atm = ctx.sb(name + "atm", [128, 2 * 128], BF16)
    rc2 = ctx.sb(name + "rc2", [128, H], F32)
    rcols = ctx.sb(name + "rcols", [128, H], F32)
    ybuf = ctx.sb(name + "yb", [128, 2 * 1024], F32)
    onec_bf = ctx.sb(name + "onebf", [128, 1], BF16)
    og = xn_fm[:, 0:4096]
    o32 = xn_fm[:, 4096:8192].bitcast(F32)
    vtok = xn_fm[:, 12288:16384]
    sg = xn_fm[:, 16384:20480]

    qd = nc.dram_tensor(name + "_qd", [D, T], BF16, kind="Internal").ap()
    kd = nc.dram_tensor(name + "_kd", [D, T], BF16, kind="Internal").ap()
    vd = nc.dram_tensor(name + "_vd", [2048, T], BF16, kind="Internal").ap()
    gd = nc.dram_tensor(name + "_gd", [2048, T], BF16, kind="Internal").ap()

    ld = DmaStream(ctx, "wld8", depth=8)
    wl = DmaStream(ctx, "l2awl", depth=2)
    pst = DmaStream(ctx, "l2aps", depth=4)
    rl = DmaStream(ctx, "l2arl", depth=4)
    cl = DmaStream(ctx, "l2acl", depth=8)
    ct = DmaStream(ctx, "l2act", depth=4)
    mm, scv, vsv, gpv, gps = (name + "mm", name + "sc", name + "vs",
                              name + "gp", name + "g0")

    fm_ready = pre_ln_to_fm(ctx, name + "P", x_in, xn_bounce, xn_fm)

    @block.sync
    def _(sync):
        ld.issue(sync, mask[:, :], maskT_d[:, :])
        ld.issue(sync, cdec[:, :], crossdec_d[:, :])
        ld.issue(sync, kdecc[:, :], kdec_d[:, :])
        ld.issue(sync, gnc[:, :], gn_cols[:, :])
    i_mask, i_cdec, i_kdec, i_gn = 0, 1, 2, 3

    @block.gpsimd
    def _(g):
        ctx.inc(g.memset(onec_bf[:, :], 1.0), gps, 1)
    gps_ones = ctx.val(gps)

    ps_pool = [ctx.ps([128, 512], F32) for _ in range(7)]
    b_mm = ctx.val(mm)
    b_rope = ctx.val(vsv)
    b_scv0 = ctx.val(scv)
    b_gpv = ctx.val(gpv)
    b2_mm = b_mm + 384
    b2_vsv = b_rope + 384 + 16
    b2_scv = b_scv0 + 256
    b2_gpv = b_gpv + 2

    def mm_r(n, off):
        return b2_mm + 40 * n + off

    def vs_r(n, off):
        return b2_vsv + 34 * n + off

    def sc_r(n, off):
        return b2_scv + 13 * n + off

    def gp_r(n, off):
        return b2_gpv + 4 * n + off

    @block.tensor
    def _(t):
        t.wait_ge(ctx.sems[fm_ready[0]], fm_ready[1])
        for p in range(2):
            wl.wait(t, p)
            for tc in range(8):
                for h in range(H):
                    idx = p * 32 + tc * 4 + h
                    if idx >= 1:
                        t.wait_ge(ctx.sems[vsv], b_rope + 6 * idx)
                    for half in range(2):
                        co = (2 * h + half) * 128
                        for j in range(ND):
                            ins = t.matmul(
                                ps_pool[half][:, :],
                                wbuf[:, j * 2048 + co:j * 2048 + co + 128],
                                xn_fm[:, j * T + tc * 512:j * T + (tc + 1) * 512],
                                start=(j == 0), stop=(j == ND - 1))
                        ctx.inc(ins, mm, 1)
        t.wait_ge(ctx.sems[vsv], b_rope + 384)
        for p2 in range(2):
            wl.wait(t, 2 + p2)
            for j16 in range(16):
                for tc in range(8):
                    gi = p2 * 128 + j16 * 8 + tc
                    if gi >= 2:
                        t.wait_ge(ctx.sems[scv], b_scv0 + gi - 1)
                    for j in range(ND):
                        ins = t.matmul(
                            ps_pool[gi % 2][:, :],
                            wbuf[:, j * 2048 + j16 * 128:j * 2048 + (j16 + 1) * 128],
                            xn_fm[:, j * T + tc * 512:j * T + (tc + 1) * 512],
                            start=(j == 0), stop=(j == ND - 1))
                    ctx.inc(ins, mm, 1)
    assert ctx.val(mm) == b2_mm

    @block.sync
    def _(sync):
        for p in range(2):
            for tc in range(8):
                rli = p * 8 + tc
                if rli >= 2:
                    sync.wait_ge(ctx.sems[vsv], b_rope + 24 * (rli - 1))
                rl.issue(sync, cosb[:, (rli % 2) * 512:(rli % 2 + 1) * 512],
                         cos_d[:, tc * 512:(tc + 1) * 512])
                rl.issue(sync, sinb[:, (rli % 2) * 512:(rli % 2 + 1) * 512],
                         sin_d[:, tc * 512:(tc + 1) * 512])

    @block.vector
    def _(v):
        for p in range(2):
            for tc in range(8):
                rli = p * 8 + tc
                for h in range(H):
                    idx = p * 32 + tc * 4 + h
                    v.wait_ge(ctx.sems[mm], b_mm + 2 * idx + 2)
                    if h == 0:
                        rl.wait(v, 2 * rli)
                        rl.wait(v, 2 * rli + 1)
                    if idx >= 2:
                        pst.wait(v, 2 * (idx - 2))
                        pst.wait(v, 2 * (idx - 2) + 1)
                    cs = cosb[:, (rli % 2) * 512:(rli % 2 + 1) * 512]
                    sn = sinb[:, (rli % 2) * 512:(rli % 2 + 1) * 512]
                    s0 = stg[:, (idx % 2) * 1024:(idx % 2) * 1024 + 512]
                    s1 = stg[:, (idx % 2) * 1024 + 512:(idx % 2 + 1) * 1024]
                    t1, t2, t3, t4 = (rsc[:, c * 512:(c + 1) * 512] for c in range(4))
                    ctx.inc(v.tensor_tensor(out=t1, in0=ps_pool[0][:, :], in1=cs,
                                            op=ALU.mult), vsv, 1)
                    ctx.inc(v.tensor_tensor(out=t2, in0=ps_pool[1][:, :], in1=sn,
                                            op=ALU.mult), vsv, 1)
                    v.wait_ge(ctx.sems[vsv], ctx.val(vsv))
                    ctx.inc(v.tensor_tensor(out=s0, in0=t1, in1=t2, op=ALU.subtract), vsv, 1)
                    ctx.inc(v.tensor_tensor(out=t3, in0=ps_pool[1][:, :], in1=cs,
                                            op=ALU.mult), vsv, 1)
                    ctx.inc(v.tensor_tensor(out=t4, in0=ps_pool[0][:, :], in1=sn,
                                            op=ALU.mult), vsv, 1)
                    v.wait_ge(ctx.sems[vsv], ctx.val(vsv))
                    ctx.inc(v.tensor_tensor(out=s1, in0=t3, in1=t4, op=ALU.add), vsv, 1)
    assert ctx.val(vsv) == b_rope + 384

    @block.scalar
    def _(sc):
        for s in range(4):
            pst.wait(sc, 124 + s)
        for gi in range(256):
            sc.wait_ge(ctx.sems[mm], b_mm + 128 + gi + 1)
            if gi >= 4:
                pst.wait(sc, 128 + gi - 4)
            ctx.inc(sc.copy(out=stg[:, (gi % 4) * 512:(gi % 4 + 1) * 512],
                            in_=ps_pool[gi % 2][:, :]), scv, 1)
    assert ctx.val(scv) == b2_scv

    @block.gpsimd
    def _(g):
        wl.issue(g, wbuf.rearrange("q (j o) -> q j o", o=2048)[:, :, 0:1024],
                 wq_T.rearrange("(j q) o -> q j o", q=128))
        for p in range(2):
            dst = qd if p == 0 else kd
            for tc in range(8):
                for h in range(H):
                    idx = p * 32 + tc * 4 + h
                    g.wait_ge(ctx.sems[vsv], b_rope + 6 * (idx + 1))
                    s0 = stg[:, (idx % 2) * 1024:(idx % 2) * 1024 + 512]
                    s1 = stg[:, (idx % 2) * 1024 + 512:(idx % 2 + 1) * 1024]
                    pst.issue(g, dst[(2 * h) * 128:(2 * h + 1) * 128,
                                     tc * 512:(tc + 1) * 512], s0)
                    pst.issue(g, dst[(2 * h + 1) * 128:(2 * h + 2) * 128,
                                     tc * 512:(tc + 1) * 512], s1)
            if p == 0:
                g.wait_ge(ctx.sems[mm], b_mm + 64)
                wl.issue(g, wbuf.rearrange("q (j o) -> q j o", o=2048)[:, :, 0:1024],
                         wk_T.rearrange("(j q) o -> q j o", q=128))
        g.wait_ge(ctx.sems[mm], b_mm + 128)
        wl.issue(g, wbuf.rearrange("q (j o) -> q j o", o=2048),
                 wv_T.rearrange("(j q) o -> q j o", q=128))
        for p2 in range(2):
            dst = vd if p2 == 0 else gd
            for j16 in range(16):
                for tc in range(8):
                    gi = p2 * 128 + j16 * 8 + tc
                    g.wait_ge(ctx.sems[scv], b_scv0 + gi + 1)
                    pst.issue(g, dst[j16 * 128:(j16 + 1) * 128,
                                     tc * 512:(tc + 1) * 512],
                              stg[:, (gi % 4) * 512:(gi % 4 + 1) * 512])
            if p2 == 0:
                g.wait_ge(ctx.sems[mm], b_mm + 256)
                wl.issue(g, wbuf.rearrange("q (j o) -> q j o", o=2048),
                         wg_T.rearrange("(j q) o -> q j o", q=128))
        g.wait_ge(ctx.sems[mm], b_mm + 384)
        wl.issue(g, wbuf.rearrange("q (j o) -> q j o", o=1024),
                 wo_T.rearrange("(j q) o -> q j o", q=128))
    assert pst.n == 384

    def y_half_ap(i, hh):
        base = (i % 2) * 1024 + hh * 512
        return ybuf[:, base:base + 512]

    pl = PostLN(ctx, name + "R", x_in, x_out, y_half_ap, None, vsv)

    def y_target(i):
        return vs_r(i, 34)
    pl.y_target = y_target

    @block.tensor
    def _(t):
        t.wait_ge(ctx.sems[scv], b2_scv)
        wl.wait(t, 4)
        psF = [ps_pool[0], ps_pool[1], ps_pool[4], ps_pool[5]]
        for n in range(NCH):
            sl = n % 2
            for h in range(H):
                if h >= 1:
                    t.wait_ge(ctx.sems[vsv], vs_r(n, 13 + 4 * (h - 1)))
                elif n >= 1:
                    t.wait_ge(ctx.sems[vsv], vs_r(n - 1, 25))
                if h == 0:
                    cl.wait(t, 3 * n)
                    cl.wait(t, 3 * n + 1)
                    ct.wait(t, 2 * n + 1)
                psA = ps_pool[2][:, 0:128]
                for kk in range(2):
                    j = 2 * h + kk
                    ins = t.matmul(psA,
                                   kf[:, sl * 1024 + j * 128:sl * 1024 + (j + 1) * 128],
                                   qf[:, sl * 1024 + j * 128:sl * 1024 + (j + 1) * 128],
                                   start=(kk == 0), stop=(kk == 1))
                ctx.inc(ins, mm, 1)
                t.wait_ge(ctx.sems[vsv], vs_r(n, 13 + 4 * h))
                t.wait_ge(ctx.sems[gpv], gp_r(n, h + 1))
                if h >= 1:
                    t.wait_ge(ctx.sems[scv], sc_r(n, 2 * h + 1))
                elif n >= 1:
                    t.wait_ge(ctx.sems[scv], sc_r(n - 1, 9))
                for m in range(4):
                    reg = ps_pool[3][:, m * 128:(m + 1) * 128]
                    t.matmul(reg,
                             vtok[:, sl * 2048 + h * 512 + m * 128:
                                  sl * 2048 + h * 512 + (m + 1) * 128],
                             atm[:, (h % 2) * 128:(h % 2 + 1) * 128],
                             start=True, stop=False)
                    t.matmul(reg,
                             S_bf[:, h * 1024 + m * 128:h * 1024 + (m + 1) * 128],
                             qt[:, (2 * h) * 128:(2 * h + 1) * 128],
                             start=False, stop=False)
                    ins = t.matmul(reg,
                                   S_bf[:, h * 1024 + 512 + m * 128:
                                        h * 1024 + 512 + (m + 1) * 128],
                                   qt[:, (2 * h + 1) * 128:(2 * h + 2) * 128],
                                   start=False, stop=True)
                    ctx.inc(ins, mm, 1)
                if h == 0:
                    t.wait_ge(ctx.sems[vsv], vs_r(n, 12))
                    if n >= 1:
                        t.wait_ge(ctx.sems[vsv], vs_r(n - 1, 34))
                else:
                    t.wait_ge(ctx.sems[vsv], vs_r(n, 12 + 4 * h))
                for kk in range(2):
                    ins = t.matmul(ps_pool[4 + kk][:, :],
                                   ktd[:, h * 256 + kk * 128:h * 256 + (kk + 1) * 128],
                                   vtok[:, sl * 2048 + h * 512:sl * 2048 + (h + 1) * 512],
                                   start=True, stop=True)
                    ctx.inc(ins, mm, 1)
                t.wait_ge(ctx.sems[scv], sc_r(n, 2 * h + 3))
                if h == 0:
                    t.wait_ge(ctx.sems[gps], gps_ones)
                    if n >= 1:
                        t.wait_ge(ctx.sems[scv], sc_r(n - 1, 10))
                for m in range(4):
                    ins = t.matmul(ps_pool[6][:, h:h + 1],
                                   sqb[:, (h % 2) * 512 + m * 128:
                                       (h % 2) * 512 + (m + 1) * 128],
                                   onec_bf[:, :],
                                   start=(m == 0), stop=(m == 3))
                ctx.inc(ins, mm, 1)
            t.wait_ge(ctx.sems[vsv], vs_r(n, 28))
            if n >= 1:
                t.wait_ge(ctx.sems[scv], sc_r(n - 1, 13))
            for e in range(2):
                if e == 1:
                    t.wait_ge(ctx.sems[scv], sc_r(n, 12))
                    t.wait_ge(ctx.sems[vsv], vs_r(n, 31))
                for h in range(H):
                    for m in range(4):
                        j16 = 4 * h + m
                        ins = t.matmul(psF[h][:, :],
                                       og[:, sl * 2048 + j16 * 128:
                                          sl * 2048 + (j16 + 1) * 128],
                                       wbuf[:, j16 * 1024 + e * 512:
                                            j16 * 1024 + e * 512 + 512],
                                       start=(m == 0), stop=(m == 3))
                    ctx.inc(ins, mm, 1)
            assert ctx.val(mm) == mm_r(n, 40)

    @block.vector
    def _(v):
        wl.wait(v, 4)
        ld.wait(v, i_gn)
        for j in range(16):
            ctx.inc(v.tensor_scalar(out=wbuf[:, j * 1024:(j + 1) * 1024],
                                    in0=wbuf[:, j * 1024:(j + 1) * 1024],
                                    scalar1=gnc[:, j % 4:j % 4 + 1], scalar2=None,
                                    op0=ALU.mult), vsv, 1)
        assert ctx.val(vsv) == b2_vsv
        psF = [ps_pool[0], ps_pool[1], ps_pool[4], ps_pool[5]]
        for n in range(NCH):
            sl = n % 2
            cl.wait(v, 3 * n)
            if n == 0:
                ld.wait(v, i_cdec)
            else:
                v.wait_ge(ctx.sems[mm], mm_r(n - 1, 29))
            for j in range(ND):
                h = j // 2
                ctx.inc(v.tensor_tensor(out=qt[:, j * 128:(j + 1) * 128],
                                        in0=qf[:, sl * 1024 + j * 128:
                                             sl * 1024 + (j + 1) * 128],
                                        in1=cdec[:, h * 128:(h + 1) * 128],
                                        op=ALU.mult), vsv, 1)
            ct.wait(v, 2 * n)
            if n == 0:
                ld.wait(v, i_kdec)
            else:
                v.wait_ge(ctx.sems[mm], mm_r(n - 1, 31))
            for h in range(H):
                ctx.inc(v.tensor_scalar(out=ktd[:, h * 256:(h + 1) * 256],
                                        in0=ktok[:, sl * 1024 + h * 256:
                                             sl * 1024 + (h + 1) * 256],
                                        scalar1=kdecc[:, h:h + 1], scalar2=None,
                                        op0=ALU.mult), vsv, 1)
            for h in range(H):
                v.wait_ge(ctx.sems[mm], mm_r(n, 8 * h + 1))
                if n == 0 and h == 0:
                    ld.wait(v, i_mask)
                if h >= 2:
                    v.wait_ge(ctx.sems[mm], mm_r(n, 8 * (h - 2) + 5))
                elif n >= 1:
                    v.wait_ge(ctx.sems[mm], mm_r(n - 1, 8 * (h + 2) + 5))
                ctx.inc(v.tensor_tensor(out=atm[:, (h % 2) * 128:(h % 2 + 1) * 128],
                                        in0=ps_pool[2][:, 0:128],
                                        in1=mask[:, h * 128:(h + 1) * 128],
                                        op=ALU.mult), vsv, 1)
                v.wait_ge(ctx.sems[scv], sc_r(n, 2 * h + 2))
                if n >= 2:
                    v.wait_ge(ctx.sems[mm], mm_r(n - 2, 40))
                ctx.inc(v.tensor_tensor(out=og[:, sl * 2048 + h * 512:
                                             sl * 2048 + (h + 1) * 512],
                                        in0=o32[:, h * 512:(h + 1) * 512],
                                        in1=sg[:, sl * 2048 + h * 512:
                                             sl * 2048 + (h + 1) * 512],
                                        op=ALU.mult), vsv, 1)
                v.wait_ge(ctx.sems[mm], mm_r(n, 8 * h + 7))
                v.wait_ge(ctx.sems[gpv], gp_r(n, h + 1))
                for kk in range(2):
                    ctx.inc(v.scalar_tensor_tensor(
                        out=S[:, h * 1024 + kk * 512:h * 1024 + (kk + 1) * 512],
                        in0=S[:, h * 1024 + kk * 512:h * 1024 + (kk + 1) * 512],
                        scalar=GC[h], in1=ps_pool[4 + kk][:, :],
                        op0=ALU.mult, op1=ALU.add), vsv, 1)
            for e in range(2):
                v.wait_ge(ctx.sems[scv], sc_r(n, 12 + e))
                for h in range(1, H):
                    v.wait_ge(ctx.sems[mm], mm_r(n, 32 + 4 * e + h + 1))
                    if h >= 2:
                        v.wait_ge(ctx.sems[vsv], ctx.val(vsv))
                    ctx.inc(v.scalar_tensor_tensor(
                        out=ybuf[:, sl * 1024 + e * 512:sl * 1024 + (e + 1) * 512],
                        in0=psF[h][:, :], scalar=rcols[:, h:h + 1],
                        in1=ybuf[:, sl * 1024 + e * 512:sl * 1024 + (e + 1) * 512],
                        op0=ALU.mult, op1=ALU.add), vsv, 1)
            assert ctx.val(vsv) == vs_r(n, 34)
            pl.vec_tile(v, n)

    @block.scalar
    def _(sc):
        for s in range(4):
            pst.wait(sc, 380 + s)
        for nn in range(2):
            ct.issue(sc, ktok[:, nn * 1024:(nn + 1) * 1024],
                     kd[:, nn * 128:(nn + 1) * 128], transpose=True)
            ct.issue(sc, vtok[:, nn * 2048:(nn + 1) * 2048],
                     vd[:, nn * 128:(nn + 1) * 128], transpose=True)
        for n in range(NCH):
            sl = n % 2
            cl.wait(sc, 3 * n + 2)
            if n >= 2:
                sc.wait_ge(ctx.sems[vsv], vs_r(n - 2, 26))
            ctx.inc(sc.activation(out=sg[:, sl * 2048:(sl + 1) * 2048],
                                  in_=sg[:, sl * 2048:(sl + 1) * 2048],
                                  func=AF.Silu), scv, 1)
            for h in range(H):
                sc.wait_ge(ctx.sems[mm], mm_r(n, 8 * h + 5))
                if n >= 1:
                    sc.wait_ge(ctx.sems[vsv], vs_r(n - 1, 14 + 4 * h))
                ctx.inc(sc.copy(out=o32[:, h * 512:(h + 1) * 512],
                                in_=ps_pool[3][:, :]), scv, 1)
                if h >= 2:
                    sc.wait_ge(ctx.sems[mm], mm_r(n, 8 * (h - 2) + 8))
                elif n >= 1:
                    sc.wait_ge(ctx.sems[mm], mm_r(n - 1, 8 * (h + 2) + 8))
                ctx.inc(sc.activation(out=sqb[:, (h % 2) * 512:(h % 2 + 1) * 512],
                                      in_=ps_pool[3][:, :], func=AF.Square), scv, 1)
            sc.wait_ge(ctx.sems[mm], mm_r(n, 32))
            if n >= 1:
                sc.wait_ge(ctx.sems[vsv], vs_r(n - 1, 34))
            ctx.inc(sc.activation(out=rc2[:, :], in_=ps_pool[6][:, 0:4], func=AF.Ln,
                                  scale=1.0 / 512.0, bias=ctx.c(1e-5)), scv, 1)
            sc.wait_ge(ctx.sems[scv], ctx.val(scv))
            ctx.inc(sc.activation(out=rcols[:, :], in_=rc2[:, :], func=AF.Exp,
                                  scale=-0.5), scv, 1)
            sc.wait_ge(ctx.sems[scv], ctx.val(scv))
            for e in range(2):
                sc.wait_ge(ctx.sems[mm], mm_r(n, 32 + 4 * e + 1))
                if n >= 2:
                    sc.wait_ge(ctx.sems[pl.vs], pl.apply_done(n - 2))
                ctx.inc(sc.activation(out=ybuf[:, sl * 1024 + e * 512:
                                           sl * 1024 + (e + 1) * 512],
                                      in_=ps_pool[0][:, :], func=AF.Identity,
                                      scale=rcols[:, 0:1]), scv, 1)
            assert ctx.val(scv) == sc_r(n, 13)
            if n + 2 < NCH:
                sc.wait_ge(ctx.sems[vsv], vs_r(n, 12))
                sc.wait_ge(ctx.sems[mm], mm_r(n, 31))
                s2 = (n + 2) % 2
                ct.issue(sc, ktok[:, s2 * 1024:(s2 + 1) * 1024],
                         kd[:, (n + 2) * 128:(n + 3) * 128], transpose=True)
                ct.issue(sc, vtok[:, s2 * 2048:(s2 + 1) * 2048],
                         vd[:, (n + 2) * 128:(n + 3) * 128], transpose=True)
            pl.sc_tile(sc, n)

    @block.gpsimd
    def _(g):
        ctx.inc(g.memset(S[:, :], 0.0), gpv, 1)
        ctx.inc(g.memset(S_bf[:, :], 0.0), gpv, 1)
        g.wait_ge(ctx.sems[gpv], b2_gpv)
        for n in range(NCH):
            for h in range(H):
                if n >= 1:
                    g.wait_ge(ctx.sems[vsv], vs_r(n - 1, 16 + 4 * h))
                    g.wait_ge(ctx.sems[mm], mm_r(n - 1, 8 * h + 5))
                ctx.inc(g.tensor_copy(out=S_bf[:, h * 1024:(h + 1) * 1024],
                                      in_=S[:, h * 1024:(h + 1) * 1024]), gpv, 1)
            pl.st_tile(g, n)

    @block.sync
    def _(sync):
        for s in range(4):
            pst.wait(sync, 380 + s)
        for n in range(NCH):
            if n >= 2:
                sync.wait_ge(ctx.sems[vsv], vs_r(n - 2, 26))
                sync.wait_ge(ctx.sems[mm], mm_r(n - 2, 25))
            sl = n % 2
            cl.issue(sync, qf.rearrange("p (s j c) -> p s j c", s=2, c=128)[:, sl],
                     qd.rearrange("(j p) t -> p j t", p=128)[:, :, n * 128:(n + 1) * 128])
            cl.issue(sync, kf.rearrange("p (s j c) -> p s j c", s=2, c=128)[:, sl],
                     kd.rearrange("(j p) t -> p j t", p=128)[:, :, n * 128:(n + 1) * 128])
            cl.issue(sync, sg.rearrange("p (s j c) -> p s j c", s=2, c=128)[:, sl],
                     gd.rearrange("(j p) t -> p j t", p=128)[:, :, n * 128:(n + 1) * 128])
            pl.ld_tile(sync, n)

    return pl.finish()



def emit_l0_attn(ctx, x_in, x_out, ddw_T, rw_T, kw_T, vw_T, gw_T, ww_T, ow_T,
                 mix_cols, gnw_row, gnb_cols, maskL2_d, lx_d, xn_bounce, stop=None, n2=64, dbg2=(), n3=NT):
    nc, block = ctx.nc, ctx.block
    name = "l0a"
    GN_EPS = 0.00064

    wpool = ctx.sb(name + "wp", [128, 6 * 8192], BF16)
    wof = {"dd": 0, "w": 8192, "k": 2 * 8192, "r": 3 * 8192, "v": 4 * 8192,
           "g": 5 * 8192}
    xsb_s = ctx.sb(name + "xs", [128, 4096], BF16)
    xnb_s = ctx.sb(name + "xn", [128, 4096], BF16)
    lerpb = ctx.sb(name + "lp", [128, 4096], BF16)
    dlt = ctx.sb(name + "dl", [128, 4096], BF16)
    ddl = ctx.sb(name + "ddl", [128, 4096], BF16)
    ddb = ctx.sb(name + "ddb", [128, 2 * 512], BF16)
    tnh = ctx.sb(name + "tnh", [128, 2 * 512], BF16)
    Eb = ctx.sb(name + "Eb", [128, 4096], BF16)
    kab = ctx.sb(name + "kab", [128, 2 * 512], BF16)
    wtmp = ctx.sb(name + "wtm", [128, 2 * 512], F32)
    wtok = ctx.sb(name + "wtk", [64, 1024], F32)
    sumsb = ctx.sb(name + "sms", [128, 2 * 16], F32)
    lngb = ctx.sb(name + "lng", [128, 2 * 8], F32)
    fmmb = ctx.sb(name + "fmm", [128, 8], F32)
    carry = ctx.sb(name + "cry", [128, 8], F32)
    gstg = ctx.sb(name + "gst", [128, 2 * 8], F32)
    stg = ctx.sb(name + "stg", [128, 4 * 512], BF16)
    katg = ctx.sb(name + "ktg", [64, 2 * 1024], BF16)
    mixc = ctx.sb(name + "mx", [128, 40], F32)
    omix = ctx.sb(name + "om", [128, 40], F32)
    maskL = ctx.sb(name + "mkl", [64, 128], F32)
    lx = ctx.sb(name + "lxc", [64, 66], F32)
    idm = ctx.sb(name + "idm", [128, 128], F32)
    idmb = ctx.sb(name + "idb", [128, 128], BF16)
    onesb = ctx.sb(name + "on1", [128, 128], F32)
    onesbb = ctx.sb(name + "on2", [128, 128], BF16)
    gnwr = ctx.sb(name + "gnw", [1, 1024], F32)
    gnbc = ctx.sb(name + "gnb", [128, 8], F32)
    ind2 = ctx.sb(name + "ind", [128, 2], BF16)
    Stv = ctx.sb(name + "S2v", [64, 1024], BF16)
    rbB = ctx.sb(name + "rbB", [64, 2 * 512], BF16)
    vbB = ctx.sb(name + "vbB", [64, 2 * 512], BF16)
    kat2 = ctx.sb(name + "kt2", [64, 2 * 1024], BF16)
    vtk2 = ctx.sb(name + "vt2", [64, 2 * 1024], BF16)
    mmb2 = ctx.sb(name + "Mm2", [64, 2 * 128], BF16)
    gcb = ctx.sb(name + "gcb", [128, 2 * 64], F32)

    stv = ctx.sb(name + "stv", [128, 48], F32)
    rc3 = ctx.sb(name + "rc3", [128, 16], F32)
    scols = ctx.sb(name + "sco", [128, 32], F32)

    ow_buf = wpool[:, 0:8192]
    S = wpool[:, 8192:9216].bitcast(F32)
    o32 = wpool[:, 9216:10240].bitcast(F32)
    t_i = wpool[:, 10240:11264].bitcast(F32)
    odb = wpool[:, 11264:13312]
    gtb = wpool[:, 13312:15360]
    sqb = wpool[:, 15360:16384]
    Qs = wpool[:, 16384:18432]
    srows = wpool[0:1, 18432:26624].bitcast(F32)
    e1b = wpool[:, 30720:30976].bitcast(F32)
    e2b = wpool[:, 30976:31232].bitcast(F32)
    s32 = wpool[0:32, 31232:31744].bitcast(F32)
    Stmp = wpool[:, 31744:31872].bitcast(F32)
    rb2 = wpool[:, 31872:32896]
    vb2 = wpool[:, 32896:33920]
    Ab2 = wpool[:, 33920:34944]
    ostg = wpool[:, 34944:35968]
    bev = wpool[:, 35968:36992].bitcast(F32)

    xsd = nc.dram_tensor(name + "_xs", [T, D], BF16, kind="Internal").ap()
    rd = nc.dram_tensor(name + "_rd", [D, T], BF16, kind="Internal").ap()
    vd = nc.dram_tensor(name + "_vd", [D, T], BF16, kind="Internal").ap()
    gtd = nc.dram_tensor(name + "_gd", [D, T], BF16, kind="Internal").ap()
    Ad = nc.dram_tensor(name + "_Ad", [D, T], BF16, kind="Internal").ap()
    katd = nc.dram_tensor(name + "_kA", [T, D], BF16, kind="Internal").ap()
    gCd = nc.dram_tensor(name + "_gC", [D, 64], F32, kind="Internal").ap()
    od = nc.dram_tensor(name + "_od", [D, T], BF16, kind="Internal").ap()
    statd = nc.dram_tensor(name + "_st", [2, 32, 128], F32, kind="Internal").ap()

    ld = DmaStream(ctx, "wld8", depth=8)
    wl = DmaStream(ctx, name + "wl", depth=8)
    tpx = DmaStream(ctx, name + "tp", depth=16)
    xstr = DmaStream(ctx, name + "xd", depth=2)
    pstA = DmaStream(ctx, name + "pA", depth=4)
    pstK = DmaStream(ctx, name + "pK", depth=2)
    pstG = DmaStream(ctx, name + "pG", depth=2)
    pstO = DmaStream(ctx, name + "pO", depth=2)
    pstS = DmaStream(ctx, name + "pS", depth=2)
    cl = DmaStream(ctx, name + "cl", depth=16)
    cg = DmaStream(ctx, name + "cg", depth=2)
    ct = DmaStream(ctx, name + "ct", depth=4)
    mm, scv, vsv, gpv, gps = (name + "mm", name + "sc", name + "vs",
                              name + "gp", name + "g0")

    bd = pre_ln_to_fm(ctx, name + "P", x_in, xn_bounce, None)

    @block.sync
    def _(sync):
        ld.issue(sync, mixc[:, :], mix_cols[:, :])
        ld.issue(sync, gnwr[:, :], gnw_row[:, :])
        ld.issue(sync, gnbc[:, :], gnb_cols[:, :])
        ld.issue(sync, maskL[:, :], maskL2_d[:, :])
        ld.issue(sync, lx[:, :], lx_d[:, :])
    i_mix, i_gnw, i_gnb, i_mskl, i_lx = 0, 1, 2, 3, 4

    @block.gpsimd
    def _(g):
        ctx.inc(g.memset(onesb[:, :], 1.0), gps, 1)
        ctx.inc(g.memset(onesbb[:, :], 1.0), gps, 1)
        g.wait_ge(ctx.sems[gps], ctx.val(gps))
        ctx.inc(g.affine_select(idm[:, :], onesb[:, :], pattern=[[1, 128]],
                                compare_op=ALU.is_equal, fill=0.0, base=0,
                                channel_multiplier=-1), gps, 1)
        ctx.inc(g.affine_select(idmb[:, :], onesbb[:, :], pattern=[[1, 128]],
                                compare_op=ALU.is_equal, fill=0.0, base=0,
                                channel_multiplier=-1), gps, 1)
        ctx.inc(g.memset(ind2[:, :], 0.0), gps, 1)
        g.wait_ge(ctx.sems[gps], ctx.val(gps))
        ctx.inc(g.memset(ind2[0:64, 0:1], 1.0), gps, 1)
        ctx.inc(g.memset(ind2[64:128, 1:2], 1.0), gps, 1)
        ctx.inc(g.memset(carry[:, :], 0.0), gps, 1)
        ctx.inc(g.memset(stg[0:1, 0:1024], 0.0), gps, 1)
        g.wait_ge(ctx.sems[gps], ctx.val(gps))
        xstr.issue(g, xsd[0:1, :], stg[0:1, 0:1024])
        g.wait_ge(ctx.sems[bd[0]], bd[1])
        xstr.issue(g, xsd[1:T, :], xn_bounce[0:T - 1, :])
        for nm, src in [("dd", ddw_T), ("w", ww_T), ("k", kw_T), ("r", rw_T),
                        ("v", vw_T), ("g", gw_T)]:
            wl.issue(g, wpool.rearrange("q (j o) -> q j o", o=1024)[:, wof[nm] // 1024:wof[nm] // 1024 + 8, :],
                     src.rearrange("(j q) o -> q j o", q=128))
    i_wdd, i_ww, i_wk, i_wr, i_wv, i_wg = 0, 1, 2, 3, 4, 5

    PA = ctx.ps([128, 512], F32)
    PB = ctx.ps([128, 512], F32)
    PC = ctx.ps([128, 512], F32)
    PD = ctx.ps([128, 512], F32)
    PE = ctx.ps([128, 1024], F32)
    PF = ctx.ps([128, 512], F32)
    PG9 = ctx.ps([128, 192], F32)
    PG = PG9[:, 0:64]
    PH = PG9[0:32, 64:192]

    M0 = ctx.val(mm)
    S0 = ctx.val(scv)

    def mm1(tc, off):
        return M0 + 80 * tc + off

    def sc1(tc, off):
        return S0 + 112 * tc + off

    WPR = [9, 10, 12, 16, 20, 24, 28, 32]
    WTR = [11, 15, 19, 23, 27, 31, 35, 38]
    WCU = [13, 17, 21, 25, 29, 33, 36, 39]
    WSU = [14, 18, 22, 26, 30, 34, 37, 40]
    WCH1 = [17, 20, 23, 31, 39, 47, 55, 63]
    EVB = [25, 33, 41, 49, 57, 65, 70, 75]

    def wpr(jo):
        return WPR[jo]

    def wtr(j):
        return WTR[j]

    def wcu(j):
        return WCU[j]

    def wsu(j):
        return WSU[j]

    def kpr(jo):
        return 41 + jo if jo <= 2 else 2 * jo + 39

    def ktr(j):
        return 2 * j + 44 if j <= 5 else 49 + j


    @block.vector
    def _(v):
        ld.wait(v, i_mix)
        ctx.inc(v.tensor_scalar(out=omix[:, :], in0=mixc[:, :], scalar1=-1.0,
                                scalar2=None, op0=ALU.mult), vsv, 1)
        v.wait_ge(ctx.sems[vsv], ctx.val(vsv))
        ctx.inc(v.tensor_scalar(out=omix[:, :], in0=omix[:, :], scalar1=1.0,
                                scalar2=None, op0=ALU.add), vsv, 1)
    V0 = ctx.val(vsv)

    def vs1(tc, off):
        return V0 + 136 * tc + off

    def emit_lerp(v, tc, pidx, dst):
        for j in range(ND):
            ctx.inc(v.scalar_tensor_tensor(
                out=dst[:, j * 512:(j + 1) * 512], in0=dlt[:, j * 512:(j + 1) * 512],
                scalar=mixc[:, pidx * 8 + j:pidx * 8 + j + 1],
                in1=xsb_s[:, j * 512:(j + 1) * 512],
                op0=ALU.mult, op1=ALU.add), vsv, 1)
            v.wait_ge(ctx.sems[vsv], ctx.val(vsv))
            ctx.inc(v.scalar_tensor_tensor(
                out=dst[:, j * 512:(j + 1) * 512], in0=ddl[:, j * 512:(j + 1) * 512],
                scalar=omix[:, pidx * 8 + j:pidx * 8 + j + 1],
                in1=dst[:, j * 512:(j + 1) * 512],
                op0=ALU.mult, op1=ALU.add), vsv, 1)


    @block.tensor
    def _(t):
        for tc in range(8):
            for jo in range(ND):
                if jo == 0:
                    for j in range(ND):
                        tpx.wait(t, tc * 16 + 8 + j)
                    wl.wait(t, i_wdd)
                if jo >= 2:
                    t.wait_ge(ctx.sems[scv], sc1(tc, 2 + 2 * (jo - 2)))
                elif tc >= 1:
                    t.wait_ge(ctx.sems[scv], sc1(tc - 1, 2 + 2 * (6 + jo)))
                pb = [PA, PB][jo % 2]
                for j in range(ND):
                    ins = t.matmul(pb[:, :],
                                   wpool[:, wof["dd"] + j * 1024 + jo * 128:
                                         wof["dd"] + j * 1024 + (jo + 1) * 128],
                                   xsb_s[:, j * 512:(j + 1) * 512],
                                   start=(j == 0), stop=(j == ND - 1))
                ctx.inc(ins, mm, 1)
            def emit_wproj(jo):
                if jo == 0:
                    t.wait_ge(ctx.sems[vsv], vs1(tc, 32))
                    wl.wait(t, i_ww)
                if jo >= 2:
                    t.wait_ge(ctx.sems[scv], sc1(tc, WCH1[jo - 2]))
                elif tc >= 1:
                    t.wait_ge(ctx.sems[scv], sc1(tc - 1, 105 + 6 + jo))
                pb = [PC, PD][jo % 2]
                for j in range(ND):
                    ins = t.matmul(pb[:, :],
                                   wpool[:, wof["w"] + j * 1024 + jo * 128:
                                         wof["w"] + j * 1024 + (jo + 1) * 128],
                                   lerpb[:, j * 512:(j + 1) * 512],
                                   start=(j == 0), stop=(j == ND - 1))
                ctx.inc(ins, mm, 1)

            def emit_wtrans(jo):
                t.wait_ge(ctx.sems[scv], sc1(tc, WCH1[jo] + 2))
                if jo >= 1:
                    t.wait_ge(ctx.sems[scv], sc1(tc, EVB[jo - 1] + 1))
                elif tc >= 1:
                    t.wait_ge(ctx.sems[scv], sc1(tc - 1, 81 + 7))
                wslab = wtmp[:, (jo % 2) * 512:(jo % 2 + 1) * 512]
                for c8 in range(8):
                    ins = t.transpose(PE[0:64, c8 * 128:(c8 + 1) * 128],
                                      wslab[:, c8 * 64:(c8 + 1) * 64], idm[:, :])
                ctx.inc(ins, mm, 1)

            def emit_wcum(jo):
                t.wait_ge(ctx.sems[scv], sc1(tc, EVB[jo] + 1))
                if jo >= 1:
                    t.wait_ge(ctx.sems[scv], sc1(tc, EVB[jo - 1] + 4))
                ld.wait(t, i_lx)
                for c8 in range(8):
                    ins = t.matmul(PF[:, c8 * 64:(c8 + 1) * 64],
                                   wtok[:, c8 * 128:(c8 + 1) * 128], lx[:, 0:64],
                                   start=True, stop=True)
                ctx.inc(ins, mm, 1)

            def emit_wsums(jo):
                if jo >= 1:
                    t.wait_ge(ctx.sems[scv], sc1(tc, EVB[jo - 1] + 2))
                for c8 in range(8):
                    ins = t.matmul(PG[:, c8 * 2:(c8 + 1) * 2],
                                   wtok[:, c8 * 128:(c8 + 1) * 128], lx[:, 64:66],
                                   start=True, stop=True)
                ctx.inc(ins, mm, 1)

            wseq = [("wp", 0), ("wp", 1), ("tr", 0), ("wp", 2), ("cu", 0),
                    ("su", 0), ("tr", 1), ("wp", 3), ("cu", 1), ("su", 1),
                    ("tr", 2), ("wp", 4), ("cu", 2), ("su", 2), ("tr", 3),
                    ("wp", 5), ("cu", 3), ("su", 3), ("tr", 4), ("wp", 6),
                    ("cu", 4), ("su", 4), ("tr", 5), ("wp", 7), ("cu", 5),
                    ("su", 5), ("tr", 6), ("cu", 6), ("su", 6), ("tr", 7),
                    ("cu", 7), ("su", 7)]
            for op, jo in wseq:
                {"wp": emit_wproj, "tr": emit_wtrans, "cu": emit_wcum,
                 "su": emit_wsums}[op](jo)
                assert ctx.val(mm) == mm1(tc, {"wp": wpr, "tr": wtr, "cu": wcu,
                                               "su": wsu}[op](jo))
            def emit_kproj(jo):
                if jo == 0:
                    t.wait_ge(ctx.sems[vsv], vs1(tc, 80))
                    wl.wait(t, i_wk)
                if jo >= 2:
                    t.wait_ge(ctx.sems[vsv], vs1(tc, 81 + jo - 2))
                else:
                    t.wait_ge(ctx.sems[scv], sc1(tc, WCH1[6 + jo]))
                pb = [PC, PD][jo % 2]
                for j in range(ND):
                    ins = t.matmul(pb[:, :],
                                   wpool[:, wof["k"] + j * 1024 + jo * 128:
                                         wof["k"] + j * 1024 + (jo + 1) * 128],
                                   xnb_s[:, j * 512:(j + 1) * 512],
                                   start=(j == 0), stop=(j == ND - 1))
                ctx.inc(ins, mm, 1)

            def emit_ktrans(jo):
                t.wait_ge(ctx.sems[vsv], vs1(tc, 81 + jo))
                if jo >= 1:
                    t.wait_ge(ctx.sems[scv], sc1(tc, 81 + jo - 1))
                else:
                    t.wait_ge(ctx.sems[scv], sc1(tc, EVB[7] + 1))
                ka = kab[:, (jo % 2) * 512:(jo % 2 + 1) * 512]
                PEb = PE.bitcast(BF16)
                for c8 in range(8):
                    ins = t.transpose(PEb[0:64, c8 * 128:(c8 + 1) * 128],
                                      ka[:, c8 * 64:(c8 + 1) * 64], idmb[:, :])
                ctx.inc(ins, mm, 1)

            emit_kproj(0)
            emit_kproj(1)
            emit_kproj(2)
            emit_ktrans(0)
            for jo in range(3, ND):
                emit_kproj(jo)
                emit_ktrans(jo - 3 + 1)
            emit_ktrans(6)
            emit_ktrans(7)
            for pi, nm in enumerate(["r", "v", "g"]):
                rhs = [lerpb, xnb_s, lerpb][pi]
                for jo in range(ND):
                    if jo == 0:
                        t.wait_ge(ctx.sems[vsv], vs1(tc, 104 + 16 * pi))
                        wl.wait(t, [i_wr, i_wv, i_wg][pi])
                    if jo >= 2:
                        t.wait_ge(ctx.sems[scv], sc1(tc, 89 + 8 * pi + jo - 2))
                    elif pi == 0:
                        t.wait_ge(ctx.sems[vsv], vs1(tc, 81 + 6 + jo))
                    else:
                        t.wait_ge(ctx.sems[scv], sc1(tc, 89 + 8 * (pi - 1) + 6 + jo))
                    pb = [PC, PD][jo % 2]
                    for j in range(ND):
                        ins = t.matmul(pb[:, :],
                                       wpool[:, wof[nm] + j * 1024 + jo * 128:
                                             wof[nm] + j * 1024 + (jo + 1) * 128],
                                       rhs[:, j * 512:(j + 1) * 512],
                                       start=(j == 0), stop=(j == ND - 1))
                    ctx.inc(ins, mm, 1)
            assert ctx.val(mm) == mm1(tc, 80)

    @block.vector
    def _(v):
        for tc in range(8):
            for j in range(ND):
                tpx.wait(v, tc * 16 + j)
                tpx.wait(v, tc * 16 + 8 + j)
                ctx.inc(v.tensor_tensor(out=dlt[:, j * 512:(j + 1) * 512],
                                        in0=xnb_s[:, j * 512:(j + 1) * 512],
                                        in1=xsb_s[:, j * 512:(j + 1) * 512],
                                        op=ALU.subtract), vsv, 1)
            v.wait_ge(ctx.sems[vsv], ctx.val(vsv))
            for j in range(ND):
                v.wait_ge(ctx.sems[scv], sc1(tc, 2 + 2 * j))
                ctx.inc(v.tensor_tensor(out=ddl[:, j * 512:(j + 1) * 512],
                                        in0=ddb[:, (j % 2) * 512:(j % 2 + 1) * 512],
                                        in1=dlt[:, j * 512:(j + 1) * 512],
                                        op=ALU.mult), vsv, 1)
            assert ctx.val(vsv) == vs1(tc, 16)
            v.wait_ge(ctx.sems[vsv], ctx.val(vsv))
            if tc >= 1:
                v.wait_ge(ctx.sems[mm], mm1(tc - 1, 80))
            emit_lerp(v, tc, 4, lerpb)
            assert ctx.val(vsv) == vs1(tc, 32)
            for jo in range(ND):
                v.wait_ge(ctx.sems[scv], sc1(tc, EVB[jo] + 2))
                if jo >= 2:
                    v.wait_ge(ctx.sems[scv], sc1(tc, EVB[jo - 2] + 5))
                elif tc >= 1:
                    v.wait_ge(ctx.sems[scv], sc1(tc - 1, EVB[6 + jo] + 5))
                sl_ = sumsb[:, (jo % 2) * 16:(jo % 2 + 1) * 16]
                lg = lngb[:, (jo % 2) * 8:(jo % 2 + 1) * 8]
                ctx.inc(v.tensor_tensor(out=fmmb[:, :], in0=sl_[:, 1:16:2],
                                        in1=sl_[:, 0:16:2], op=ALU.subtract), vsv, 1)
                v.wait_ge(ctx.sems[vsv], ctx.val(vsv))
                ctx.inc(v.tensor_tensor(out=lg[:, 0:1], in0=sl_[:, 0:1],
                                        in1=carry[:, jo:jo + 1], op=ALU.add), vsv, 1)
                ctx.inc(v.tensor_tensor(out=lg[:, 1:8], in0=sl_[:, 2:16:2],
                                        in1=fmmb[:, 0:7], op=ALU.add), vsv, 1)
                v.wait_ge(ctx.sems[vsv], ctx.val(vsv))
                ctx.inc(v.tensor_copy(out=carry[:, jo:jo + 1], in_=fmmb[:, 7:8]), vsv, 1)
            assert ctx.val(vsv) == vs1(tc, 64)
            emit_lerp(v, tc, 1, xnb_s)
            for jo in range(ND):
                v.wait_ge(ctx.sems[mm], mm1(tc, kpr(jo)))
                v.wait_ge(ctx.sems[scv], sc1(tc, EVB[jo] + 4))
                if jo >= 2:
                    v.wait_ge(ctx.sems[mm], mm1(tc, ktr(jo - 2)))
                pb = [PC, PD][jo % 2]
                ctx.inc(v.tensor_tensor(out=kab[:, (jo % 2) * 512:(jo % 2 + 1) * 512],
                                        in0=pb[:, :], in1=Eb[:, jo * 512:(jo + 1) * 512],
                                        op=ALU.mult), vsv, 1)
            assert ctx.val(vsv) == vs1(tc, 88)
            v.wait_ge(ctx.sems[mm], mm1(tc, 16))
            emit_lerp(v, tc, 0, lerpb)
            v.wait_ge(ctx.sems[mm], mm1(tc, 48))
            emit_lerp(v, tc, 2, xnb_s)
            v.wait_ge(ctx.sems[mm], mm1(tc, 64))
            emit_lerp(v, tc, 3, lerpb)
            assert ctx.val(vsv) == vs1(tc, 136)

    @block.scalar
    def _(sc):
        ctx.wcp(sc)
        for tc in range(8):
            for jo in range(ND):
                sc.wait_ge(ctx.sems[mm], mm1(tc, 1 + jo))
                if jo >= 2:
                    sc.wait_ge(ctx.sems[vsv], vs1(tc, 9 + jo - 2))
                elif tc >= 1:
                    sc.wait_ge(ctx.sems[vsv], vs1(tc - 1, 9 + 6 + jo))
                pb = [PA, PB][jo % 2]
                th = tnh[:, (jo % 2) * 512:(jo % 2 + 1) * 512]
                ctx.inc(sc.activation(out=th, in_=pb[:, :], func=AF.Tanh), scv, 1)
                sc.wait_ge(ctx.sems[scv], ctx.val(scv))
                ctx.inc(sc.activation(out=ddb[:, (jo % 2) * 512:(jo % 2 + 1) * 512],
                                      in_=th, func=AF.Sigmoid), scv, 1)
            def emit_wch(jo):
                sc.wait_ge(ctx.sems[mm], mm1(tc, wpr(jo)))
                if jo >= 2:
                    sc.wait_ge(ctx.sems[mm], mm1(tc, wtr(jo - 2)))
                pb = [PC, PD][jo % 2]
                ws = wtmp[:, (jo % 2) * 512:(jo % 2 + 1) * 512]
                ctx.inc(sc.activation(out=ws, in_=pb[:, :], func=AF.Exp, scale=-1.0), scv, 1)
                sc.wait_ge(ctx.sems[scv], ctx.val(scv))
                ctx.inc(sc.activation(out=ws, in_=ws, func=AF.Ln, bias=ctx.c(1.0)), scv, 1)
                sc.wait_ge(ctx.sems[scv], ctx.val(scv))
                ctx.inc(sc.activation(out=ws, in_=ws, func=AF.Identity, scale=-1.0,
                                      bias=ctx.c(-0.5)), scv, 1)

            def emit_evb(jo):
                sc.wait_ge(ctx.sems[mm], mm1(tc, wtr(jo)))
                if jo >= 1:
                    sc.wait_ge(ctx.sems[mm], mm1(tc, wcu(jo - 1)))
                ctx.inc(sc.copy(out=wtok[:, :], in_=PE[0:64, :]), scv, 1)
                sc.wait_ge(ctx.sems[mm], mm1(tc, wsu(jo)))
                if jo >= 2:
                    sc.wait_ge(ctx.sems[vsv], vs1(tc, 36 + 4 * (jo - 2)))
                elif tc >= 1 and jo < 2:
                    sc.wait_ge(ctx.sems[vsv], vs1(tc - 1, 36 + 4 * (6 + jo)))
                ctx.inc(sc.copy(out=sumsb[:, (jo % 2) * 16:(jo % 2 + 1) * 16],
                                in_=PG[:, 0:16]), scv, 1)
                sc.wait_ge(ctx.sems[mm], mm1(tc, wcu(jo)))
                sidx = tc * 32 + jo
                if sidx >= 4:
                    pstA.wait(sc, sidx - 4)
                if sidx <= 1:
                    xstr.wait(sc, 0)
                ctx.inc(sc.activation(out=stg[:, (sidx % 4) * 512:(sidx % 4 + 1) * 512],
                                      in_=PF[:, :], func=AF.Exp), scv, 1)
                if tc >= 1:
                    sc.wait_ge(ctx.sems[vsv], vs1(tc - 1, 81 + jo))
                ctx.inc(sc.activation(out=Eb[:, jo * 512:(jo + 1) * 512],
                                      in_=PF[:, :], func=AF.Exp, scale=-1.0), scv, 1)
                sc.wait_ge(ctx.sems[vsv], vs1(tc, 36 + 4 * jo))
                gidx = tc * 8 + jo
                if gidx >= 2:
                    pstG.wait(sc, gidx - 2)
                ctx.inc(sc.activation(out=gstg[:, (gidx % 2) * 8:(gidx % 2 + 1) * 8],
                                      in_=lngb[:, (jo % 2) * 8:(jo % 2 + 1) * 8],
                                      func=AF.Exp), scv, 1)

            sseq = [("wch", 0), ("wch", 1), ("wch", 2), ("evb", 0), ("wch", 3),
                    ("evb", 1), ("wch", 4), ("evb", 2), ("wch", 5), ("evb", 3),
                    ("wch", 6), ("evb", 4), ("wch", 7), ("evb", 5), ("evb", 6),
                    ("evb", 7)]
            for op, jo in sseq:
                {"wch": emit_wch, "evb": emit_evb}[op](jo)
                assert ctx.val(scv) == sc1(tc, (WCH1[jo] + 2) if op == "wch"
                                           else (EVB[jo] + 5))
            assert ctx.val(scv) == sc1(tc, 80)
            for jo in range(ND):
                sc.wait_ge(ctx.sems[mm], mm1(tc, ktr(jo)))
                kidx = tc * 8 + jo
                if kidx >= 2:
                    pstK.wait(sc, kidx - 2)
                ctx.inc(sc.copy(out=katg[:, (kidx % 2) * 1024:(kidx % 2 + 1) * 1024],
                                in_=PE.bitcast(BF16)[0:64, 0:1024]), scv, 1)
            for pi in range(3):
                for jo in range(ND):
                    sc.wait_ge(ctx.sems[mm], mm1(tc, 57 + 8 * pi + jo))
                    sidx = tc * 32 + 8 + pi * 8 + jo
                    if sidx >= 4:
                        pstA.wait(sc, sidx - 4)
                    pb = [PC, PD][jo % 2]
                    dst = stg[:, (sidx % 4) * 512:(sidx % 4 + 1) * 512]
                    if pi == 2:
                        ctx.inc(sc.activation(out=dst, in_=pb[:, :], func=AF.Silu), scv, 1)
                    else:
                        ctx.inc(sc.copy(out=dst, in_=pb[:, :]), scv, 1)
            assert ctx.val(scv) == sc1(tc, 112)

    @block.gpsimd
    def _(g):
        for tc in range(8):
            for jo in range(ND):
                g.wait_ge(ctx.sems[scv], sc1(tc, EVB[jo] + 3))
                sidx = tc * 32 + jo
                pstA.issue(g, Ad[jo * 128:(jo + 1) * 128, tc * 512:(tc + 1) * 512],
                           stg[:, (sidx % 4) * 512:(sidx % 4 + 1) * 512])
                g.wait_ge(ctx.sems[scv], sc1(tc, EVB[jo] + 5))
                gidx = tc * 8 + jo
                pstG.issue(g, gCd.rearrange("(j p) c -> p j c", p=128)[:, jo, tc * 8:(tc + 1) * 8],
                           gstg[:, (gidx % 2) * 8:(gidx % 2 + 1) * 8])
            for jo in range(ND):
                g.wait_ge(ctx.sems[scv], sc1(tc, 81 + jo))
                kidx = tc * 8 + jo
                pstK.issue(g, katd[tc * 512:(tc + 1) * 512, jo * 128:(jo + 1) * 128]
                           .rearrange("(c s) k -> s c k", s=64),
                           katg[:, (kidx % 2) * 1024:(kidx % 2 + 1) * 1024]
                           .rearrange("s (c k) -> s c k", k=128))
            for pi, dst_d in [(0, rd), (1, vd), (2, gtd)]:
                for jo in range(ND):
                    g.wait_ge(ctx.sems[scv], sc1(tc, 89 + 8 * pi + jo))
                    sidx = tc * 32 + 8 + pi * 8 + jo
                    pstA.issue(g, dst_d[jo * 128:(jo + 1) * 128, tc * 512:(tc + 1) * 512],
                               stg[:, (sidx % 4) * 512:(sidx % 4 + 1) * 512])
        g.wait_ge(ctx.sems[mm], M0 + 640)
        wl.issue(g, wpool.rearrange("q (j o) -> q j o", o=1024)[:, 0:8, :],
                 ow_T.rearrange("(j q) o -> q j o", q=128))
    i_wow = 6

    @block.sync
    def _(sync):
        sync.wait_ge(ctx.sems[bd[0]], bd[1])
        for tc in range(8):
            if tc >= 1:
                sync.wait_ge(ctx.sems[mm], mm1(tc - 1, 72))
            for j in range(ND):
                tpx.issue(sync, xnb_s[:, j * 512:(j + 1) * 512],
                          xn_bounce[tc * 512:(tc + 1) * 512, j * 128:(j + 1) * 128],
                          transpose=True)
            if tc == 0:
                xstr.wait(sync, 0)
                xstr.wait(sync, 1)
            else:
                sync.wait_ge(ctx.sems[vsv], vs1(tc - 1, 136))
            for j in range(ND):
                tpx.issue(sync, xsb_s[:, j * 512:(j + 1) * 512],
                          xsd[tc * 512:(tc + 1) * 512, j * 128:(j + 1) * 128],
                          transpose=True)

    if stop == "p1":
        @block.gpsimd
        def _(g):
            pstA.wait_all(g)
            pstK.wait_all(g)
            pstG.wait_all(g)
            wl.wait(g, i_wow)
            ctx.inc(g.memset(stv[0:1, 0:1], 0.0), name + "fin", 1)
        return name + "fin", ctx.val(name + "fin")

    M2 = M0 + 640
    S2 = S0 + 896
    V2 = V0 + 1088

    def mm2(c, off):
        return M2 + 40 * c + off

    def vs2(c, off):
        return V2 + 35 * c + off

    def sc2(c, off):
        return S2 + c + off

    @block.tensor
    def _(t):
        t.wait_ge(ctx.sems[scv], S2)
        t.wait_ge(ctx.sems[vsv], V2)
        for c in range(n2):
            sl = c % 2
            for j in range(ND):
                if c == 0:
                    if j == 0:
                        t.wait_ge(ctx.sem(gpv), 1)
                else:
                    t.wait_ge(ctx.sems[vsv], vs2(c - 1, 10 + 2 * j))
                    if j == 0:
                        t.wait_ge(ctx.sems[scv], sc2(c - 1, 1))
                if "tr" in dbg2:
                    ins = t.transpose(PE[0:64, j * 128:(j + 1) * 128],
                                      wtmp[:, 0:64], idm[:, :])
                else:
                    ins = t.transpose(PE[0:64, j * 128:(j + 1) * 128],
                                      S[:, j * 64:(j + 1) * 64], idm[:, :])
                ctx.inc(ins, mm, 1)
            for j in range(ND):
                if j == 0:
                    cl.wait(t, 6 * c)
                    cl.wait(t, 6 * c + 1)
                    cl.wait(t, 6 * c + 4)
                    cl.wait(t, 6 * c + 5)
                if j >= 2:
                    t.wait_ge(ctx.sems[vsv], vs2(c, 1 + j - 2))
                elif c >= 1:
                    t.wait_ge(ctx.sems[vsv], vs2(c - 1, 1 + 6 + j))
                for h2 in range(2):
                    b = h2 * 64
                    if "M" in dbg2:
                        ins = t.matmul(PF[0:64, (j % 2) * 128 + b:(j % 2) * 128 + b + 64],
                                       kat2[0:64, 0:64], kat2[0:64, 0:64],
                                       start=True, stop=True)
                    elif "M1" in dbg2:
                        ins = t.matmul(PF[0:64, (j % 2) * 128 + b:(j % 2) * 128 + b + 64],
                                       vb2[0:64, sl * 512 + j * 64:sl * 512 + (j + 1) * 64],
                                       rb2[0:64, sl * 512 + j * 64:sl * 512 + (j + 1) * 64],
                                       start=True, stop=True)
                    elif "M2" in dbg2:
                        ins = t.matmul(PF[0:64, (j % 2) * 128:(j % 2) * 128 + 64],
                                       vb2[b:b + 64, sl * 512 + j * 64:sl * 512 + (j + 1) * 64],
                                       rb2[b:b + 64, sl * 512 + j * 64:sl * 512 + (j + 1) * 64],
                                       start=(b == 0), stop=(b == 64))
                    else:
                        vop = (vb2 if b == 0 else vbB)[0:64, sl * 512 + j * 64:sl * 512 + (j + 1) * 64]
                        rop = (rb2 if b == 0 else rbB)[0:64, sl * 512 + j * 64:sl * 512 + (j + 1) * 64]
                        ins = t.matmul(PF[0:64, (j % 2) * 128 + b:(j % 2) * 128 + b + 64],
                                       vop, rop, start=True, stop=True)
                ctx.inc(ins, mm, 1)
                t.wait_ge(ctx.sems[vsv], vs2(c, 1 + j))
                if j == 0:
                    cl.wait(t, 6 * c + 3)
                    if c >= 1:
                        t.wait_ge(ctx.sems[vsv], vs2(c - 1, 25))
                for h2 in range(2):
                    b = h2 * 64
                    if "O" in dbg2:
                        ins = t.matmul(PA[b:b + 64, j * 64:(j + 1) * 64],
                                       kat2[0:64, 0:64], kat2[0:64, 0:64],
                                       start=True, stop=True)
                    else:
                        ins = t.matmul(PA[b:b + 64, j * 64:(j + 1) * 64],
                                       kat2[0:64, sl * 1024 + j * 128 + b:sl * 1024 + j * 128 + b + 64],
                                       mmb2[0:64, (j % 2) * 128 + b:(j % 2) * 128 + b + 64],
                                       start=True, stop=True)
                ctx.inc(ins, mm, 1)
                if j == 0:
                    cl.wait(t, 6 * c + 4)
                    t.wait_ge(ctx.sems[scv], sc2(c, 1))
                    if c >= 1:
                        t.wait_ge(ctx.sems[vsv], vs2(c - 1, 33))
                for h2 in range(2):
                    b = h2 * 64
                    rsl = (rb2[0:64, sl * 512 + j * 64:sl * 512 + (j + 1) * 64]
                           if h2 == 0 else
                           rbB[0:64, sl * 512 + j * 64:sl * 512 + (j + 1) * 64])
                    if "I" in dbg2:
                        ins = t.matmul(PB[b:b + 64, j * 64:(j + 1) * 64],
                                       kat2[0:64, 0:64], kat2[0:64, 0:64],
                                       start=True, stop=True)
                    else:
                        ins = t.matmul(PB[b:b + 64, j * 64:(j + 1) * 64],
                                       Stv[0:64, j * 128 + b:j * 128 + b + 64],
                                       rsl, start=True, stop=True)
                ctx.inc(ins, mm, 1)
                if j == 0:
                    ct.wait(t, c)
                for h2 in range(2):
                    b = h2 * 64
                    if "P" in dbg2:
                        ins = t.matmul(PC[b:b + 64, j * 64:(j + 1) * 64],
                                       kat2[0:64, 0:64], kat2[0:64, 0:64],
                                       start=True, stop=True)
                    else:
                        ins = t.matmul(PC[b:b + 64, j * 64:(j + 1) * 64],
                                       kat2[0:64, sl * 1024 + j * 128 + b:sl * 1024 + j * 128 + b + 64],
                                       vtk2[0:64, sl * 1024 + j * 128 + b:sl * 1024 + j * 128 + b + 64],
                                       start=True, stop=True)
                ctx.inc(ins, mm, 1)
            assert ctx.val(mm) == mm2(c, 40)

    @block.vector
    def _(v):
        for c in range(n2):
            sl = c % 2
            for j in range(ND):
                v.wait_ge(ctx.sems[mm], mm2(c, 9 + 4 * j))
                if c == 0 and j == 0:
                    ld.wait(v, i_mskl)
                if j >= 2:
                    v.wait_ge(ctx.sems[mm], mm2(c, 10 + 4 * (j - 2)))
                elif c >= 1:
                    v.wait_ge(ctx.sems[mm], mm2(c - 1, 10 + 4 * (6 + j)))
                if "msk" in dbg2:
                    ctx.inc(v.tensor_tensor(out=mmb2[:, (j % 2) * 128:(j % 2 + 1) * 128],
                                            in0=maskL[:, :], in1=maskL[:, :],
                                            op=ALU.mult), vsv, 1)
                else:
                    ctx.inc(v.tensor_tensor(out=mmb2[:, (j % 2) * 128:(j % 2 + 1) * 128],
                                            in0=PF[0:64, (j % 2) * 128:(j % 2 + 1) * 128],
                                            in1=maskL[:, :], op=ALU.mult), vsv, 1)
            for j in range(ND):
                v.wait_ge(ctx.sems[mm], mm2(c, 12 + 4 * j))
                v.wait_ge(ctx.sems[mm], mm2(c, 1 + j))
                if j == 0:
                    cg.wait(v, c // 8)
                v.wait_ge(ctx.sems[vsv], ctx.val(vsv))
                ctx.inc(v.tensor_scalar(
                    out=Stmp[:, :], in0=S[:, j * 64:(j + 1) * 64],
                    scalar1=gcb[:, (c // 8 % 2) * 64 + j * 8 + c % 8:(c // 8 % 2) * 64 + j * 8 + c % 8 + 1],
                    scalar2=None, op0=ALU.mult), vsv, 1)
                v.wait_ge(ctx.sems[vsv], ctx.val(vsv))
                ctx.inc(v.tensor_tensor(
                    out=S[:, j * 64:(j + 1) * 64], in0=Stmp[:, :],
                    in1=PC[:, j * 64:(j + 1) * 64], op=ALU.add), vsv, 1)
            assert ctx.val(vsv) == vs2(c, 24)
            v.wait_ge(ctx.sems[mm], mm2(c, 38))
            cl.wait(v, 6 * c + 2)
            if "oev" in dbg2:
                ctx.inc(v.tensor_scalar(out=o32[:, :], in0=o32[:, :], scalar1=0.0,
                                        scalar2=None, op0=ALU.mult), vsv, 1)
            else:
                ctx.inc(v.tensor_tensor(out=o32[:, :], in0=PA[:, :],
                                        in1=Ab2[:, sl * 512:(sl + 1) * 512],
                                        op=ALU.mult), vsv, 1)
            v.wait_ge(ctx.sems[mm], mm2(c, 39))
            for j in range(ND):
                gc_ = (c // 8 % 2) * 64 + j * 8 + c % 8
                if "oev" in dbg2:
                    ctx.inc(v.tensor_scalar(out=t_i[:, j * 64:(j + 1) * 64],
                                            in0=t_i[:, j * 64:(j + 1) * 64],
                                            scalar1=0.0, scalar2=None,
                                            op0=ALU.mult), vsv, 1)
                else:
                    ctx.inc(v.tensor_scalar(out=t_i[:, j * 64:(j + 1) * 64],
                                            in0=PB[:, j * 64:(j + 1) * 64],
                                            scalar1=gcb[:, gc_:gc_ + 1], scalar2=None,
                                            op0=ALU.mult), vsv, 1)
            v.wait_ge(ctx.sems[vsv], ctx.val(vsv))
            ctx.inc(v.tensor_tensor(out=t_i[:, :], in0=t_i[:, :],
                                    in1=Ab2[:, sl * 512:(sl + 1) * 512],
                                    op=ALU.mult), vsv, 1)
            v.wait_ge(ctx.sems[vsv], ctx.val(vsv))
            if c >= 2:
                pstO.wait(v, c - 2)
            ctx.inc(v.tensor_tensor(out=ostg[:, sl * 512:(sl + 1) * 512],
                                    in0=t_i[:, :], in1=o32[:, :],
                                    op=ALU.add), vsv, 1)
            assert ctx.val(vsv) == vs2(c, 35)

    @block.scalar
    def _(sc):
        for s in range(4):
            pstA.wait(sc, 252 + s)
        pstK.wait(sc, 62)
        pstK.wait(sc, 63)
        pstG.wait(sc, 62)
        pstG.wait(sc, 63)
        with nc.allow_non_contiguous_dma(reason="64-col transpose"):
            for nn in range(2):
                ct.issue(sc, vtk2[:, nn * 1024:(nn + 1) * 1024],
                         vd[:, nn * 64:(nn + 1) * 64], transpose=True)
        for c in range(n2):
            sc.wait_ge(ctx.sems[mm], mm2(c, 8))
            if c >= 1:
                sc.wait_ge(ctx.sems[mm], mm2(c - 1, 39))
            if "sev" in dbg2:
                ctx.inc(sc.copy(out=Stv[:, :], in_=katg[:, 0:1024]), scv, 1)
            else:
                ctx.inc(sc.copy(out=Stv[:, :], in_=PE[0:64, 0:1024]), scv, 1)
            if c + 2 < n2:
                sc.wait_ge(ctx.sems[mm], mm2(c, 40))
                s2_ = (c + 2) % 2
                with nc.allow_non_contiguous_dma(reason="64-col transpose"):
                    ct.issue(sc, vtk2[:, s2_ * 1024:(s2_ + 1) * 1024],
                             vd[:, (c + 2) * 64:(c + 3) * 64], transpose=True)

    @block.gpsimd
    def _(g):
        g.wait_ge(ctx.sems[mm], M0 + 640)
        ctx.inc(g.memset(S[:, :], 0.0), gpv, 1)
        for c in range(n2):
            g.wait_ge(ctx.sems[vsv], vs2(c, 35))
            pstO.issue(g, od.rearrange("(j p) t -> p j t", p=128)[:, :, c * 64:(c + 1) * 64],
                       ostg[:, (c % 2) * 512:(c % 2 + 1) * 512]
                       .rearrange("p (j f) -> p j f", f=64))

    @block.sync
    def _(sync):
        for s in range(4):
            pstA.wait(sync, 252 + s)
        pstK.wait(sync, 62)
        pstK.wait(sync, 63)
        pstG.wait(sync, 62)
        pstG.wait(sync, 63)
        for c in range(n2):
            if c >= 2:
                sync.wait_ge(ctx.sems[mm], mm2(c - 2, 40))
                sync.wait_ge(ctx.sems[vsv], vs2(c - 2, 35))
            sl = c % 2
            cl.issue(sync, rb2.rearrange("p (s j f) -> p s j f", s=2, f=64)[:, sl],
                     rd.rearrange("(j p) t -> p j t", p=128)[:, :, c * 64:(c + 1) * 64])
            cl.issue(sync, vb2.rearrange("p (s j f) -> p s j f", s=2, f=64)[:, sl],
                     vd.rearrange("(j p) t -> p j t", p=128)[:, :, c * 64:(c + 1) * 64])
            cl.issue(sync, Ab2.rearrange("p (s j f) -> p s j f", s=2, f=64)[:, sl],
                     Ad.rearrange("(j p) t -> p j t", p=128)[:, :, c * 64:(c + 1) * 64])
            cl.issue(sync, kat2[:, sl * 1024:(sl + 1) * 1024], katd[c * 64:(c + 1) * 64, :])
            cl.issue(sync, rbB.rearrange("p (s j f) -> p s j f", s=2, f=64)[:, sl],
                     rd.rearrange("(j two p) t -> p two j t", two=2, p=64)[:, 1, :, c * 64:(c + 1) * 64])
            cl.issue(sync, vbB.rearrange("p (s j f) -> p s j f", s=2, f=64)[:, sl],
                     vd.rearrange("(j two p) t -> p two j t", two=2, p=64)[:, 1, :, c * 64:(c + 1) * 64])
            if c % 8 == 0:
                pn = c // 8
                if pn >= 2:
                    sync.wait_ge(ctx.sems[vsv], vs2(8 * (pn - 2) + 7, 35))
                cg.issue(sync, gcb.rearrange("p (s j f) -> p s j f", s=2, f=8)[:, pn % 2],
                         gCd.rearrange("(j p) f -> p j f", p=128)[:, :, pn * 8:(pn + 1) * 8])

    if stop == "p2":
        @block.gpsimd
        def _(g):
            pstO.wait_all(g)
            ctx.inc(g.memset(stv[0:1, 0:1], 0.0), name + "fin", 1)
        return name + "fin", ctx.val(name + "fin")

    M3 = M2 + 2560
    S3 = S2 + 64
    V3 = V2 + 2240

    def mm3(tt, off):
        return M3 + 35 * tt + off

    def vs3(tt, off):
        return V3 + 45 * tt + off

    def sc3(tt, off):
        return S3 + 3 * tt + off

    def y_half_ap(i, e):
        return [[PA, PB], [PC, PD]][e][i % 2][:, 0:512]

    pl = PostLN(ctx, name + "R", x_in, x_out, y_half_ap, None, mm,
                obuf=wpool[:, 26624:30720].bitcast(F32))

    def y_target(i):
        return mm3(i, 35)
    pl.y_target = y_target

    @block.tensor
    def _(t):
        t.wait_ge(ctx.sems[vsv], V3)
        t.wait_ge(ctx.sems[scv], S3)
        wl.wait(t, i_wow)
        for tt in range(n3):
            sl = tt % 2
            for j in range(ND):
                t.wait_ge(ctx.sems[vsv], vs3(tt, 1 + j))
                if j == 0:
                    cl.wait(t, 384 + 3 * tt)
                    t.wait_ge(ctx.sems[gps], 7)
                    if tt >= 1:
                        t.wait_ge(ctx.sems[vsv], vs3(tt - 1, 10))
                if "S3" in dbg2:
                    ins = t.matmul(PG[:, j * 2:(j + 1) * 2], idmb[:, 0:128],
                                   ind2[:, :], start=True, stop=True)
                    ctx.inc(ins, mm, 1)
                    ins = t.matmul(PG[:, 16 + j * 2:16 + (j + 1) * 2], idmb[:, 0:128],
                                   ind2[:, :], start=True, stop=True)
                    ctx.inc(ins, mm, 1)
                else:
                    ins = t.matmul(PG[:, j * 2:(j + 1) * 2],
                                   odb[:, sl * 1024 + j * 128:sl * 1024 + (j + 1) * 128],
                                   ind2[:, :], start=True, stop=True)
                    ctx.inc(ins, mm, 1)
                    ins = t.matmul(PG[:, 16 + j * 2:16 + (j + 1) * 2],
                                   sqb[:, j * 128:(j + 1) * 128],
                                   ind2[:, :], start=True, stop=True)
                    ctx.inc(ins, mm, 1)
            t.wait_ge(ctx.sems[scv], sc3(tt, 2))
            t.wait_ge(ctx.sems[vsv], vs3(tt, 13))
            if tt >= 1:
                t.wait_ge(ctx.sems[scv], sc3(tt - 1, 3))
            if "T3" in dbg2:
                ctx.inc(t.transpose(PH[:, :], idm[:, 0:32], idm[:, :]), mm, 1)
            else:
                ctx.inc(t.transpose(PH[:, :], scols[:, :], idm[:, :]), mm, 1)
            for j in range(ND):
                if j == 0:
                    cl.wait(t, 384 + 3 * tt + 2)
                    ld.wait(t, i_gnw)
                    if tt >= 1:
                        t.wait_ge(ctx.sems[vsv], vs3(tt - 1, 45))
                else:
                    t.wait_ge(ctx.sems[vsv], vs3(tt, 14 + 4 * (j - 1)))
                for h2 in range(2):
                    h = 2 * j + h2
                    if "B3" in dbg2:
                        ins = t.matmul(PF[h2 * 64:h2 * 64 + 64, 0:128],
                                       gnwr[0:1, 0:64], gnwr[0:1, 0:128],
                                       start=True, stop=True)
                    else:
                        ins = t.matmul(PF[h2 * 64:h2 * 64 + 64, 0:128],
                                       gnwr[0:1, j * 128 + h2 * 64:j * 128 + h2 * 64 + 64],
                                       srows[0:1, h * 128:(h + 1) * 128],
                                       start=True, stop=True)
                ctx.inc(ins, mm, 1)
                for h2 in range(2):
                    h = 2 * j + h2
                    if "B3" in dbg2:
                        ins = t.matmul(PF[h2 * 64:h2 * 64 + 64, 128:256],
                                       gnwr[0:1, 0:64], gnwr[0:1, 0:128],
                                       start=True, stop=True)
                    else:
                        ins = t.matmul(PF[h2 * 64:h2 * 64 + 64, 128:256],
                                       gnwr[0:1, j * 128 + h2 * 64:j * 128 + h2 * 64 + 64],
                                       srows[0:1, 2048 + h * 128:2048 + (h + 1) * 128],
                                       start=True, stop=True)
                ctx.inc(ins, mm, 1)
            t.wait_ge(ctx.sems[vsv], vs3(tt, 45))
            if tt >= 2:
                t.wait_ge(ctx.sems[pl.vs], pl.apply_done(tt - 2))
            for e in range(2):
                pY = y_half_ap(tt, e)
                for j in range(ND):
                    if "F3" in dbg2:
                        ins = t.matmul(pY, idmb[:, :],
                                       ow_buf[:, 0:512],
                                       start=(j == 0), stop=(j == ND - 1))
                    else:
                        ins = t.matmul(pY,
                                       Qs[:, sl * 1024 + j * 128:sl * 1024 + (j + 1) * 128],
                                       ow_buf[:, j * 1024 + e * 512:j * 1024 + e * 512 + 512],
                                       start=(j == 0), stop=(j == ND - 1))
                ctx.inc(ins, mm, 1)
            assert ctx.val(mm) == mm3(tt, 35)

    @block.vector
    def _(v):
        for tt in range(n3):
            sl = tt % 2
            for j in range(ND):
                if j == 0:
                    cl.wait(v, 384 + 3 * tt)
                    if tt >= 1:
                        v.wait_ge(ctx.sems[mm], mm3(tt - 1, 16))
                ctx.inc(v.tensor_tensor(out=sqb[:, j * 128:(j + 1) * 128],
                                        in0=odb[:, sl * 1024 + j * 128:sl * 1024 + (j + 1) * 128],
                                        in1=odb[:, sl * 1024 + j * 128:sl * 1024 + (j + 1) * 128],
                                        op=ALU.mult), vsv, 1)
            v.wait_ge(ctx.sems[mm], mm3(tt, 16))
            ctx.inc(v.tensor_scalar(out=stv[:, 0:16], in0=PG[:, 0:16],
                                    scalar1=1.0 / 64.0, scalar2=None, op0=ALU.mult), vsv, 1)
            ctx.inc(v.tensor_scalar(out=stv[:, 16:32], in0=PG[:, 16:32],
                                    scalar1=1.0 / 64.0, scalar2=None, op0=ALU.mult), vsv, 1)
            v.wait_ge(ctx.sems[vsv], ctx.val(vsv))
            ctx.inc(v.tensor_tensor(out=stv[:, 32:48], in0=stv[:, 0:16],
                                    in1=stv[:, 0:16], op=ALU.mult), vsv, 1)
            v.wait_ge(ctx.sems[vsv], ctx.val(vsv))
            ctx.inc(v.tensor_tensor(out=stv[:, 32:48], in0=stv[:, 16:32],
                                    in1=stv[:, 32:48], op=ALU.subtract), vsv, 1)
            v.wait_ge(ctx.sems[scv], sc3(tt, 2))
            if "MR" in dbg2:
                ctx.inc(v.tensor_tensor(out=scols[:, 16:32], in0=stv[:, 0:16],
                                        in1=scols[:, 0:16], op=ALU.mult), vsv, 1)
            else:
                ctx.inc(v.scalar_tensor_tensor(out=scols[:, 16:32], in0=stv[:, 0:16],
                                               scalar=-1.0, in1=scols[:, 0:16],
                                               op0=ALU.mult, op1=ALU.mult), vsv, 1)
            assert ctx.val(vsv) == vs3(tt, 13)
            for j in range(ND):
                v.wait_ge(ctx.sems[mm], mm3(tt, 19 + 2 * j))
                v.wait_ge(ctx.sems[vsv], ctx.val(vsv))
                ctx.inc(v.tensor_scalar(out=bev[:, (j % 2) * 256:(j % 2 + 1) * 256],
                                        in0=PF[:, 0:256], scalar1=1.0, scalar2=None,
                                        op0=ALU.mult), vsv, 1)
                v.wait_ge(ctx.sems[vsv], ctx.val(vsv))
                ctx.inc(v.tensor_tensor(out=e1b[:, :],
                                        in0=bev[:, (j % 2) * 256:(j % 2) * 256 + 128],
                                        in1=odb[:, sl * 1024 + j * 128:sl * 1024 + (j + 1) * 128],
                                        op=ALU.mult), vsv, 1)
                v.wait_ge(ctx.sems[vsv], ctx.val(vsv))
                ctx.inc(v.tensor_tensor(out=e2b[:, :],
                                        in0=bev[:, (j % 2) * 256 + 128:(j % 2 + 1) * 256],
                                        in1=e1b[:, :], op=ALU.add), vsv, 1)
                v.wait_ge(ctx.sems[vsv], ctx.val(vsv))
                if j == 0:
                    cl.wait(v, 384 + 3 * tt + 1)
                    ld.wait(v, i_gnb)
                    if tt >= 2:
                        v.wait_ge(ctx.sems[mm], mm3(tt - 2, 35))
                ctx.inc(v.scalar_tensor_tensor(
                    out=Qs[:, sl * 1024 + j * 128:sl * 1024 + (j + 1) * 128],
                    in0=e2b[:, :], scalar=gnbc[:, j:j + 1],
                    in1=gtb[:, sl * 1024 + j * 128:sl * 1024 + (j + 1) * 128],
                    op0=ALU.add, op1=ALU.mult), vsv, 1)
            assert ctx.val(vsv) == vs3(tt, 45)
            pl.vec_tile(v, tt)

    @block.scalar
    def _(sc):
        for tt in range(n3):
            sc.wait_ge(ctx.sems[vsv], vs3(tt, 12))
            ctx.inc(sc.activation(out=rc3[:, :], in_=stv[:, 32:48], func=AF.Ln,
                                  bias=ctx.c(GN_EPS)), scv, 1)
            sc.wait_ge(ctx.sems[scv], ctx.val(scv))
            ctx.inc(sc.activation(out=scols[:, 0:16], in_=rc3[:, :], func=AF.Exp,
                                  scale=-0.5), scv, 1)
            sc.wait_ge(ctx.sems[mm], mm3(tt, 17))
            if tt >= 2:
                pstS.wait(sc, tt - 2)
            if "C3" in dbg2:
                ctx.inc(sc.copy(out=s32[:, (tt % 2) * 128:(tt % 2 + 1) * 128],
                                in_=katg[0:32, 0:128]), scv, 1)
            else:
                ctx.inc(sc.copy(out=s32[:, (tt % 2) * 128:(tt % 2 + 1) * 128],
                                in_=PH[:, :]), scv, 1)
            assert ctx.val(scv) == sc3(tt, 3)
            pl.sc_tile(sc, tt)

    @block.gpsimd
    def _(g):
        for tt in range(n3):
            g.wait_ge(ctx.sems[scv], sc3(tt, 3))
            if "ST3" in dbg2:
                pstS.issue(g, statd[tt % 2], katg[0:32, 0:128])
            else:
                pstS.issue(g, statd[tt % 2], s32[:, (tt % 2) * 128:(tt % 2 + 1) * 128])
            pl.st_tile(g, tt)

    @block.sync
    def _(sync):
        pstO.wait(sync, 62)
        pstO.wait(sync, 63)
        for tt in range(n3):
            if tt >= 2:
                sync.wait_ge(ctx.sems[mm], mm3(tt - 2, 16))
                sync.wait_ge(ctx.sems[vsv], vs3(tt - 2, 45))
            sl = tt % 2
            cl.issue(sync, odb.rearrange("p (s j f) -> p s j f", s=2, f=128)[:, sl],
                     od.rearrange("(j p) t -> p j t", p=128)[:, :, tt * 128:(tt + 1) * 128])
            cl.issue(sync, gtb.rearrange("p (s j f) -> p s j f", s=2, f=128)[:, sl],
                     gtd.rearrange("(j p) t -> p j t", p=128)[:, :, tt * 128:(tt + 1) * 128])
            pstS.wait(sync, tt)
            if tt >= 1:
                sync.wait_ge(ctx.sems[mm], mm3(tt - 1, 33))
            if "sr" in dbg2:
                cl.issue(sync, srows[0:1, :],
                         gCd.rearrange("a b -> (a b)").rearrange("(one f) -> one f", one=1)[0:1, 0:4096])
            else:
                cl.issue(sync, srows[0:1, :],
                         statd.rearrange("s r c -> s (r c)")[tt % 2:tt % 2 + 1, :])
            pl.ld_tile(sync, tt)

    return pl.finish()



def emit_l1_attn(ctx, x_in, x_out, ipw_T, ipb_cols, opw_h, opb_row, maskneg_d,
                 xn_bounce):
    nc, block = ctx.nc, ctx.block
    name = "l1a"
    NU = 32768
    NUT = 256

    xnfs = ctx.sb(name + "xnf", [128, 2 * 4096], BF16)
    stage = ctx.sb(name + "stg", [128, 3 * 4096], BF16)
    wip = ctx.sb(name + "wip", [128, 8 * 384], BF16)
    opw = ctx.sb(name + "opw", [64, 2 * 1024], BF16)
    qh = ctx.sb(name + "qh", [64, 2 * 2 * 2048], BF16)
    kh = ctx.sb(name + "kh", [64, 2 * 2 * 2048], BF16)
    vtk = ctx.sb(name + "vtk", [128, 4 * 64], BF16)
    Lpb = ctx.sb(name + "Lp", [128, 2 * 128], F32)
    Ebf = ctx.sb(name + "Eb", [128, 2 * 128], BF16)
    Pbf = ctx.sb(name + "Pb", [128, 2 * 128], BF16)
    Ptb = ctx.sb(name + "Pt", [128, 2 * 128], BF16)
    osl = ctx.sb(name + "osl", [64, 2 * 2 * 128], BF16)
    ystg = ctx.sb(name + "yst", [128, 2 * 1024], BF16)
    ybuf = ctx.sb(name + "yb", [128, 2 * 1024], BF16)
    sml = ctx.sb(name + "sml", [128, 8], F32)
    bic = ctx.sb(name + "bic", [128, 3], F32)
    qbc = ctx.sb(name + "qbc", [128, 1], F32)
    opbs = ctx.sb(name + "opb", [1, 1024], F32)
    mskn = ctx.sb(name + "msk", [128, 128], F32)
    one_r = ctx.sb(name + "onr", [1, 128], F32)
    idmb2 = ctx.sb(name + "idb", [128, 128], BF16)
    onesb2 = ctx.sb(name + "on2", [128, 128], BF16)

    qkvd = nc.dram_tensor(name + "_qkv", [384, NU], BF16, kind="Internal").ap()
    outd = nc.dram_tensor(name + "_out", [8, T, D], BF16, kind="Internal").ap()
    xnag = nc.dram_tensor(name + "_xag", [8, T, D], BF16, kind="Internal",
                          addr_space="Shared").ap()
    rsout = nc.dram_tensor(name + "_rs", [T, D], BF16, kind="Internal").ap()

    ld = DmaStream(ctx, "wld8", depth=8)
    wl = DmaStream(ctx, name + "wl", depth=2)
    tpw = DmaStream(ctx, name + "tp", depth=16)
    qst = DmaStream(ctx, name + "qs", depth=3)
    qld = DmaStream(ctx, name + "ql", depth=8)
    vld = DmaStream(ctx, name + "vl", depth=4)
    yt = DmaStream(ctx, name + "ys", depth=2)
    ygl = DmaStream(ctx, name + "yg", depth=2)
    mm, scv, vsv, gps, cc, yv = (name + "mm", name + "sc", name + "vs",
                                 name + "g0", name + "cc", name + "yv")

    bd = pre_ln_to_fm(ctx, name + "P", x_in, xn_bounce, None)

    @block.sync
    def _(sync):
        ld.issue(sync, bic[:, :], ipb_cols[:, :])
        ld.issue(sync, mskn[:, :], maskneg_d[:, :])
        ld.issue(sync, opbs[:, :], opb_row[:, :])
    i_bic, i_msk, i_opb = 0, 1, 2

    @block.gpsimd
    def _(g):
        ctx.inc(g.memset(one_r[:, :], 1.0), gps, 1)
        ctx.inc(g.memset(onesb2[:, :], 1.0), gps, 1)
        g.wait_ge(ctx.sems[gps], ctx.val(gps))
        ctx.inc(g.affine_select(idmb2[:, :], onesb2[:, :], pattern=[[1, 128]],
                                compare_op=ALU.is_equal, fill=0.0, base=0,
                                channel_multiplier=-1), gps, 1)
        wl.issue(g, wip.rearrange("q (j o) -> q j o", o=384),
                 ipw_T.rearrange("(j q) o -> q j o", q=128))
        wl.issue(g, opw.rearrange("q (h o) -> q h o", o=1024),
                 opw_h.rearrange("q (h o) -> q h o", o=1024))
        g.wait_ge(ctx.sems[bd[0]], bd[1])
        ctx.inc(g.collective_compute(
            "AllGather", ALU.bypass, ins=[xn_bounce[:, :]],
            outs=[xnag[:, :, :]], replica_groups=RG8), cc, 1)
    gps_t = ctx.val(gps)

    @block.vector
    def _(v):
        ld.wait(v, i_bic)
        ctx.inc(v.tensor_scalar(out=qbc[:, :], in0=bic[:, 0:1], scalar1=0.125,
                                scalar2=None, op0=ALU.mult), vsv, 1)
        ld.wait(v, i_opb)
        ctx.inc(v.tensor_scalar(out=opbs[:, :], in0=opbs[:, :], scalar1=0.125,
                                scalar2=None, op0=ALU.mult), vsv, 1)
    V0 = ctx.val(vsv)

    PA = ctx.ps([128, 512], F32)
    PB = ctx.ps([128, 512], F32)
    PC = ctx.ps([128, 512], F32)
    PL = ctx.ps([128, 2 * 128], F32)
    PT = ctx.ps([128, 2 * 128], F32)
    PO = ctx.ps([64, 2 * 128], F32)
    PY = ctx.ps([128, 2 * 512], F32)

    M0 = ctx.val(mm)
    S0 = ctx.val(scv)
    MB = M0 + 192
    SB = S0 + 192

    def mmA(g24):
        return M0 + g24 + 1

    def scA(g24):
        return S0 + g24 + 1

    def mmB(i, hl, k):
        return MB + 8 * i + 3 * hl + k

    def mmE(i, e):
        return MB + 8 * i + 6 + e + 1

    def vsB(i, hl, k):
        return V0 + 10 * i + 5 * hl + k

    def scB(i, hl, k):
        return SB + 8 * i + 3 * hl + k

    def scE(i, e):
        return SB + 8 * i + 6 + e + 1

    @block.tensor
    def _(t):
        wl.wait(t, 0)
        for tc in range(8):
            for s in range(8):
                k_use = tc * 8 + s
                for j in range(ND):
                    tpw.wait(t, k_use * 8 + j)
                for ot in range(3):
                    g24 = tc * 24 + s * 3 + ot
                    if g24 >= 3:
                        t.wait_ge(ctx.sems[scv], scA(g24 - 3))
                    pb = [PA, PB, PC][g24 % 3]
                    for j in range(ND):
                        ins = t.matmul(pb[:, :],
                                       wip[:, j * 384 + ot * 128:j * 384 + (ot + 1) * 128],
                                       xnfs[:, (s % 2) * 4096 + j * 512:(s % 2) * 4096 + (j + 1) * 512],
                                       start=(j == 0), stop=(j == ND - 1))
                    ctx.inc(ins, mm, 1)
    assert ctx.val(mm) == MB

    @block.scalar
    def _(sc):
        ctx.wcp(sc)
        sc.wait_ge(ctx.sems[vsv], V0)
        for tc in range(8):
            for s in range(8):
                for ot in range(3):
                    g24 = tc * 24 + s * 3 + ot
                    sc.wait_ge(ctx.sems[mm], mmA(g24))
                    if s == 0 and tc >= 1:
                        qst.wait(sc, (tc - 1) * 3 + ot)
                    pb = [PA, PB, PC][g24 % 3]
                    dst = stage.rearrange("p (o ut sx tl) -> p o ut sx tl",
                                          o=3, sx=8, tl=16)[:, ot, :, s, :]
                    psv = pb.rearrange("p (a b) -> p a b", b=16)
                    if ot == 0:
                        ctx.inc(sc.activation(out=dst, in_=psv, func=AF.Identity,
                                              scale=0.125, bias=qbc[:, 0:1]), scv, 1)
                    else:
                        ctx.inc(sc.activation(out=dst, in_=psv, func=AF.Identity,
                                              bias=bic[:, ot:ot + 1]), scv, 1)
    assert ctx.val(scv) == SB

    @block.sync
    def _(sync):
        sync.wait_ge(ctx.sems[cc], 1)
        for tc in range(8):
            for s in range(8):
                k_use = tc * 8 + s
                if k_use >= 2:
                    sync.wait_ge(ctx.sems[mm], mmA((k_use - 2) * 3 + 2))
                for j in range(ND):
                    tpw.issue(sync, xnfs[:, (s % 2) * 4096 + j * 512:(s % 2) * 4096 + (j + 1) * 512],
                              xnag[s, tc * 512:(tc + 1) * 512, j * 128:(j + 1) * 128],
                              transpose=True)

    @block.gpsimd
    def _(g):
        for tc in range(8):
            for ot in range(3):
                g.wait_ge(ctx.sems[scv], scA(tc * 24 + 21 + ot))
                qst.issue(g, qkvd[ot * 128:(ot + 1) * 128, tc * 4096:(tc + 1) * 4096],
                          stage[:, ot * 4096:(ot + 1) * 4096])

    @block.tensor
    def _(t):
        wl.wait(t, 1)
        for i in range(NUT):
            uc = i // 16
            for hl in range(2):
                if i % 16 == 0:
                    qld.wait(t, uc * 4 + hl * 2)
                    qld.wait(t, uc * 4 + hl * 2 + 1)
                if i >= 1:
                    t.wait_ge(ctx.sems[vsv], vsB(i - 1, hl, 1))
                sl2 = (uc % 2)
                co = hl * 2 * 2048 + sl2 * 2048 + (i % 16) * 128
                ins = t.matmul(PL[:, (hl % 2) * 128:(hl % 2 + 1) * 128],
                               qh[0:64, co:co + 128], kh[0:64, co:co + 128],
                               start=True, stop=True)
                ctx.inc(ins, mm, 1)
                t.wait_ge(ctx.sems[vsv], vsB(i, hl, 5))
                if i >= 1:
                    t.wait_ge(ctx.sems[scv], scB(i - 1, hl, 2))
                ins = t.transpose(PT.bitcast(BF16)[:, hl * 128:(hl + 1) * 128],
                                  Pbf[:, hl * 128:(hl + 1) * 128], idmb2[:, :])
                ctx.inc(ins, mm, 1)
                t.wait_ge(ctx.sems[scv], scB(i, hl, 2))
                vld.wait(t, 2 * i + hl)
                if i >= 1:
                    t.wait_ge(ctx.sems[scv], scB(i - 1, hl, 3))
                ins = t.matmul(PO[:, hl * 128:(hl + 1) * 128],
                               vtk[:, ((2 * i + hl) % 4) * 64:((2 * i + hl) % 4 + 1) * 64],
                               Ptb[:, hl * 128:(hl + 1) * 128],
                               start=True, stop=True)
                ctx.inc(ins, mm, 1)
            t.wait_ge(ctx.sems[scv], scB(i, 1, 3))
            t.wait_ge(ctx.sems[gps], gps_t)
            for e in range(2):
                if i >= 1:
                    t.wait_ge(ctx.sems[scv], scE(i - 1, e))
                for hl in range(2):
                    t.matmul(PY[:, e * 512:(e + 1) * 512],
                             osl[0:64, (i % 2) * 256 + hl * 128:(i % 2) * 256 + (hl + 1) * 128],
                             opw[0:64, hl * 1024 + e * 512:hl * 1024 + e * 512 + 512],
                             start=(hl == 0), stop=False)
                ins = t.matmul(PY[:, e * 512:(e + 1) * 512],
                               one_r[0:1, :], opbs[0:1, e * 512:(e + 1) * 512],
                               start=False, stop=True)
                ctx.inc(ins, mm, 1)
            assert ctx.val(mm) == mmE(i, 1)

    @block.vector
    def _(v):
        for i in range(NUT):
            for hl in range(2):
                it = 2 * i + hl
                v.wait_ge(ctx.sems[mm], mmB(i, hl, 1))
                if i == 0 and hl == 0:
                    ld.wait(v, i_msk)
                if i >= 1:
                    v.wait_ge(ctx.sems[scv], scB(i - 1, hl, 1))
                lp = Lpb[:, hl * 128:(hl + 1) * 128]
                ctx.inc(v.tensor_tensor(out=lp, in0=PL[:, (hl % 2) * 128:(hl % 2 + 1) * 128],
                                        in1=mskn[:, :], op=ALU.add), vsv, 1)
                v.wait_ge(ctx.sems[vsv], ctx.val(vsv))
                ctx.inc(v.reduce_max(sml[:, hl:hl + 1], lp, axis=AX.X,
                                     negate=True), vsv, 1)
                v.wait_ge(ctx.sems[vsv], ctx.val(vsv))
                v.wait_ge(ctx.sems[scv], scB(i, hl, 1))
                ctx.inc(v.reduce_sum(sml[:, 2 + hl:3 + hl],
                                     Ebf[:, hl * 128:(hl + 1) * 128], axis=AX.X), vsv, 1)
                v.wait_ge(ctx.sems[vsv], ctx.val(vsv))
                ctx.inc(v.reciprocal(sml[:, 4 + hl:5 + hl],
                                     sml[:, 2 + hl:3 + hl]), vsv, 1)
                v.wait_ge(ctx.sems[vsv], ctx.val(vsv))
                if i >= 1:
                    v.wait_ge(ctx.sems[mm], mmB(i - 1, hl, 2))
                ctx.inc(v.tensor_scalar(out=Pbf[:, hl * 128:(hl + 1) * 128],
                                        in0=Ebf[:, hl * 128:(hl + 1) * 128],
                                        scalar1=sml[:, 4 + hl:5 + hl], scalar2=None,
                                        op0=ALU.mult), vsv, 1)
                assert ctx.val(vsv) == vsB(i, hl, 5)

    @block.scalar
    def _(sc):
        qst.wait(sc, 2)
        for it0 in range(4):
            i0, hl0 = divmod(it0, 2)
            with nc.allow_non_contiguous_dma(reason="64x128 v transpose"):
                vld.issue(sc, vtk[:, (it0 % 4) * 64:(it0 % 4 + 1) * 64],
                          qkvd[256 + hl0 * 64:256 + (hl0 + 1) * 64,
                               i0 * 128:(i0 + 1) * 128], transpose=True)
        for i in range(NUT):
            for hl in range(2):
                sc.wait_ge(ctx.sems[vsv], vsB(i, hl, 2))
                if i >= 1:
                    sc.wait_ge(ctx.sems[vsv], vsB(i - 1, hl, 5))
                ctx.inc(sc.activation(out=Ebf[:, hl * 128:(hl + 1) * 128],
                                      in_=Lpb[:, hl * 128:(hl + 1) * 128],
                                      func=AF.Exp, bias=sml[:, hl:hl + 1]), scv, 1)
                sc.wait_ge(ctx.sems[mm], mmB(i, hl, 2))
                if i >= 1:
                    sc.wait_ge(ctx.sems[mm], mmB(i - 1, hl, 3))
                ctx.inc(sc.copy(out=Ptb[:, hl * 128:(hl + 1) * 128],
                                in_=PT.bitcast(BF16)[:, hl * 128:(hl + 1) * 128]), scv, 1)
                sc.wait_ge(ctx.sems[mm], mmB(i, hl, 3))
                if i >= 2:
                    sc.wait_ge(ctx.sems[mm], mmE(i - 2, 1))
                ctx.inc(sc.copy(out=osl[0:64, (i % 2) * 256 + hl * 128:(i % 2) * 256 + (hl + 1) * 128],
                                in_=PO[:, hl * 128:(hl + 1) * 128]), scv, 1)
                it = 2 * i + hl
                if it + 4 < 2 * NUT:
                    i2, hl2 = divmod(it + 4, 2)
                    sc.wait_ge(ctx.sems[mm], mmB(i, hl, 3))
                    with nc.allow_non_contiguous_dma(reason="64x128 v transpose"):
                        vld.issue(sc, vtk[:, (it % 4) * 64:(it % 4 + 1) * 64],
                                  qkvd[256 + hl2 * 64:256 + (hl2 + 1) * 64,
                                       i2 * 128:(i2 + 1) * 128], transpose=True)
            for e in range(2):
                sc.wait_ge(ctx.sems[mm], mmE(i, e))
                if i >= 2:
                    yt.wait(sc, i - 2)
                ctx.inc(sc.copy(out=ystg[:, (i % 2) * 1024 + e * 512:(i % 2) * 1024 + (e + 1) * 512],
                                in_=PY[:, e * 512:(e + 1) * 512]), scv, 1)
            assert ctx.val(scv) == scE(i, 1)

    @block.sync
    def _(sync):
        qst.wait(sync, 21)
        qst.wait(sync, 22)
        qst.wait(sync, 23)
        for uc in range(16):
            for hl in range(2):
                if uc >= 2:
                    sync.wait_ge(ctx.sems[mm], mmB(16 * (uc - 2) + 15, hl, 3))
                sl2 = uc % 2
                qld.issue(sync, qh.rearrange("p (h s c) -> p h s c", h=2, s=2)[:, hl, sl2],
                          qkvd[hl * 64:(hl + 1) * 64, uc * 2048:(uc + 1) * 2048])
                qld.issue(sync, kh.rearrange("p (h s c) -> p h s c", h=2, s=2)[:, hl, sl2],
                          qkvd[128 + hl * 64:128 + (hl + 1) * 64, uc * 2048:(uc + 1) * 2048])

    @block.gpsimd
    def _(g):
        for i in range(NUT):
            g.wait_ge(ctx.sems[scv], scE(i, 1))
            yt.issue(g, outd.rearrange("s (tb tl) e -> s tb tl e", tl=16)[:, i],
                     ystg[:, (i % 2) * 1024:(i % 2 + 1) * 1024])
        yt.wait_all(g)
        ctx.inc(g.collective_compute(
            "ReduceScatter", ALU.add, ins=[outd[:, :, :]],
            outs=[rsout[:, :]], replica_groups=RG8), cc, 1)

    def y_half_ap(i, hh):
        return ybuf[:, (i % 2) * 1024 + hh * 512:(i % 2) * 1024 + (hh + 1) * 512]

    pl = PostLN(ctx, name + "R", x_in, x_out, y_half_ap, None, yv)

    def y_target(i):
        return i + 1
    pl.y_target = y_target

    @block.gpsimd
    def _(g):
        for i in range(NT):
            ygl.wait(g, i)
            ctx.inc(g.memset(sml[0:1, 7:8], 0.0), yv, 1)
            pl.st_tile(g, i)

    @block.sync
    def _(sync):
        sync.wait_ge(ctx.sems[cc], 2)
        for i in range(NT):
            if i >= 2:
                sync.wait_ge(ctx.sems[pl.vs], pl.apply_done(i - 2))
            ygl.issue(sync, ybuf[:, (i % 2) * 1024:(i % 2 + 1) * 1024],
                      rsout[i * 128:(i + 1) * 128, :])
            pl.ld_tile(sync, i)

    @block.vector
    def _(v):
        for i in range(NT):
            pl.vec_tile(v, i)

    @block.scalar
    def _(sc):
        for i in range(NT):
            pl.sc_tile(sc, i)

    return pl.finish()


import contextlib
from concourse.bass_utils import run_bass_kernel_spmd

DH = 4096


def _cols(a, n):
    return np.ascontiguousarray(np.asarray(a).reshape(n, 128).T)


def _build(emit_fn, specs, **kw):
    import concourse.bass as bass
    nc = bass.Bass("TRN2", target_bir_lowering=False, debug=False)
    x_in = nc.dram_tensor("x_in", [T, D], F32, kind="ExternalInput").ap()
    x_out = nc.dram_tensor("x_out", [T, D], F32, kind="ExternalOutput").ap()
    xn_bounce = nc.dram_tensor("xn_bounce", [T, D], BF16, kind="Internal").ap()
    aps = {}
    for nm, shape, dt in specs:
        aps[nm] = nc.dram_tensor(nm, shape, dt, kind="ExternalInput").ap()
    with contextlib.ExitStack() as stack:
        block = stack.enter_context(nc.Block())
        ctx = Ctx(nc, block, stack)
        ctx.setup_cpool()
        fin = emit_fn(ctx, x_in, x_out, *[aps[nm] for nm, _, _ in specs],
                      xn_bounce, **kw)

        @block.sync
        def _(sync):
            sync.wait_ge(ctx.sems[fin[0]], fin[1])
    return nc, ctx


def kernel(**inputs):
    inp = inputs
    x = np.ascontiguousarray(inp["L0_attn_x"].astype(np.float32))

    launches = []

    i64 = np.arange(64, dtype=np.float64)
    maskL = np.where(i64[:, None] <= i64[None, :], 1.0, 0.0).astype(np.float32)
    maskL2 = np.concatenate([maskL, maskL], axis=1)
    lxm = np.zeros((64, 66), np.float32)
    lxm[:, 0:64] = (np.where(i64[:, None] <= i64[None, :], 1.0, 0.0)
                    - np.where(i64 <= 31, 1.0, 0.0)[:, None] * np.ones((1, 64)))
    lxm[:, 64] = (i64 <= 31).astype(np.float32)
    lxm[:, 65] = 1.0
    mixs = [inp["L0_attn_mix_r"], inp["L0_attn_mix_k"], inp["L0_attn_mix_v"],
            inp["L0_attn_mix_g"], inp["L0_attn_mix_w"]]
    mix_cols = np.concatenate([_cols(m, 8) for m in mixs], axis=1).astype(np.float32)
    specs = [("ddw_T", [D, D], F32), ("rw_T", [D, D], F32), ("kw_T", [D, D], F32),
             ("vw_T", [D, D], F32), ("gw_T", [D, D], F32), ("ww_T", [D, D], F32),
             ("ow_T", [D, D], F32), ("mix_cols", [128, 40], F32),
             ("gnw_row", [1, D], F32), ("gnb_cols", [128, 8], F32),
             ("maskL2_d", [64, 128], F32), ("lx_d", [64, 66], F32)]
    nc0, ctx0 = _build(emit_l0_attn, specs)
    feed0 = {"cpool": ctx0.cpool_np(),
             "ddw_T": np.ascontiguousarray(inp["L0_attn_ddlerp_w"].T),
             "rw_T": np.ascontiguousarray(inp["L0_attn_r_w"].T),
             "kw_T": np.ascontiguousarray(inp["L0_attn_k_w"].T),
             "vw_T": np.ascontiguousarray(inp["L0_attn_v_w"].T),
             "gw_T": np.ascontiguousarray(inp["L0_attn_g_w"].T),
             "ww_T": np.ascontiguousarray(inp["L0_attn_w_w"].T),
             "ow_T": np.ascontiguousarray(inp["L0_attn_out_w"].T),
             "mix_cols": np.ascontiguousarray(mix_cols),
             "gnw_row": np.ascontiguousarray(inp["L0_attn_gn_weight"].reshape(1, D)),
             "gnb_cols": _cols(inp["L0_attn_gn_bias"], 8),
             "maskL2_d": np.ascontiguousarray(maskL2), "lx_d": lxm}
    launches.append((nc0, [dict(feed0) for _ in range(8)]))

    specs = [("w_T", [D, D], F32), ("ln_gamma", [128, ND], F32),
             ("ln_beta", [128, ND], F32)]
    nc1, ctx1 = _build(emit_l0_ffn, specs)
    feed1 = {"cpool": ctx1.cpool_np(), "w_T": np.ascontiguousarray(inp["L0_ffn_weight"]),
             "ln_gamma": _cols(inp["L0_ffn_ln_gamma"], 8),
             "ln_beta": _cols(inp["L0_ffn_ln_beta"], 8)}
    launches.append((nc1, [dict(feed1) for _ in range(8)]))

    ipw = inp["L1_attn_in_proj_weight"]
    ipb = inp["L1_attn_in_proj_bias"]
    opw = inp["L1_attn_out_proj_weight"]
    opb = inp["L1_attn_out_proj_bias"]
    u128 = np.arange(128)
    maskneg = np.where(u128[:, None] % 16 == u128[None, :] % 16, 0.0, -1e30).astype(np.float32)
    specs = [("ipw_T", [D, 384], F32), ("ipb_cols", [128, 3], F32),
             ("opw_h", [64, 2 * D], F32), ("opb_row", [1, D], F32),
             ("maskneg_d", [128, 128], F32)]
    nc2, ctx2 = _build(emit_l1_attn, specs)
    feeds2 = []
    for c in range(8):
        rows = np.concatenate([np.arange(2 * c * 64, 2 * c * 64 + 128),
                               1024 + np.arange(2 * c * 64, 2 * c * 64 + 128),
                               2048 + np.arange(2 * c * 64, 2 * c * 64 + 128)])
        opw_c = opw.T[2 * c * 64:2 * c * 64 + 128]
        feeds2.append({"cpool": ctx2.cpool_np(),
                       "ipw_T": np.ascontiguousarray(ipw[rows].T),
                       "ipb_cols": np.ascontiguousarray(ipb[rows].reshape(3, 128).T),
                       "opw_h": np.ascontiguousarray(
                           opw_c.reshape(2, 64, D).transpose(1, 0, 2).reshape(64, 2 * D)),
                       "opb_row": np.ascontiguousarray(opb.reshape(1, D)),
                       "maskneg_d": maskneg})
    launches.append((nc2, feeds2))

    specs = [("wr_T_d", [16, D, D], F32), ("ws0_T", [D, D], F32),
             ("ws1_T", [D, D], F32), ("cen_T", [D, 16], F32),
             ("ebias_row", [1, 16], F32), ("br_d", [16, D], F32),
             ("bs_row", [1, 2 * D], F32)]
    nc3, ctx3 = _build(emit_l1_ffn, specs)
    feed3 = {"cpool": ctx3.cpool_np(),
             "wr_T_d": np.ascontiguousarray(inp["L1_ffn_Wr"].transpose(0, 2, 1)),
             "ws0_T": np.ascontiguousarray(inp["L1_ffn_Ws"][0].T),
             "ws1_T": np.ascontiguousarray(inp["L1_ffn_Ws"][1].T),
             "cen_T": np.ascontiguousarray(inp["L1_ffn_centroids"].T),
             "ebias_row": np.ascontiguousarray(inp["L1_ffn_expert_biases"].reshape(1, 16)),
             "br_d": np.ascontiguousarray(inp["L1_ffn_br"]),
             "bs_row": np.ascontiguousarray(inp["L1_ffn_bs"].reshape(1, 2 * D))}
    launches.append((nc3, [dict(feed3) for _ in range(8)]))

    dh_ = 256
    inv_freq = 1.0 / 10000.0 ** (np.arange(0, dh_, 2, dtype=np.float64) / dh_)
    tpos = np.arange(T, dtype=np.float64)
    fr = tpos[:, None] * inv_freq[None, :]
    cos_d = np.ascontiguousarray(np.cos(fr).T.astype(np.float32))
    sin_d = np.ascontiguousarray(np.sin(fr).T.astype(np.float32))
    gammas = 1.0 - 2.0 ** (-5.0 - np.arange(4, dtype=np.float64))
    i128 = np.arange(128, dtype=np.float64)
    maskT = np.zeros((128, 512), np.float32)
    crossdec = np.zeros((128, 512), np.float32)
    for h in range(4):
        diff = i128[None, :] - i128[:, None]
        maskT[:, h * 128:(h + 1) * 128] = np.where(
            diff >= 0, gammas[h] ** np.maximum(diff, 0), 0.0) / 16.0
        crossdec[:, h * 128:(h + 1) * 128] = (gammas[h] ** (i128 + 1) / 16.0)[None, :]
    kdec = np.stack([gammas[h] ** (127.0 - i128) for h in range(4)], axis=1).astype(np.float32)
    specs = [("wq_T", [D, D], F32), ("wk_T", [D, D], F32), ("wv_T", [D, 2048], F32),
             ("wg_T", [D, 2048], F32), ("wo_T", [2048, D], F32),
             ("gn_cols", [128, 4], F32), ("cos_d", [128, T], F32),
             ("sin_d", [128, T], F32), ("maskT_d", [128, 512], F32),
             ("crossdec_d", [128, 512], F32), ("kdec_d", [128, 4], F32)]
    nc4, ctx4 = _build(emit_l2_attn, specs)
    feed4 = {"cpool": ctx4.cpool_np(),
             "wq_T": np.ascontiguousarray(inp["L2_attn_Wq"].T),
             "wk_T": np.ascontiguousarray(inp["L2_attn_Wk"].T),
             "wv_T": np.ascontiguousarray(inp["L2_attn_Wv"].T),
             "wg_T": np.ascontiguousarray(inp["L2_attn_Wg"].T),
             "wo_T": np.ascontiguousarray(inp["L2_attn_Wo"].T),
             "gn_cols": _cols(inp["L2_attn_gnorm_w"], 4),
             "cos_d": cos_d, "sin_d": sin_d, "maskT_d": maskT,
             "crossdec_d": crossdec, "kdec_d": kdec}
    launches.append((nc4, [dict(feed4) for _ in range(8)]))

    specs = [("w1_d", [D, DH], F32), ("w2_d", [DH, D], F32),
             ("wr_col", [128, 8], F32), ("b1_col", [128, 32], F32),
             ("b2_row", [1, D], F32)]
    nc5, ctx5 = _build(emit_l2_ffn, specs)
    feed5 = {"cpool": ctx5.cpool_np(),
             "w1_d": np.ascontiguousarray(inp["L2_ffn_W1"]),
             "w2_d": np.ascontiguousarray(inp["L2_ffn_W2"]),
             "wr_col": _cols(inp["L2_ffn_w_router"], 8),
             "b1_col": _cols(inp["L2_ffn_b1"], 32),
             "b2_row": np.ascontiguousarray(inp["L2_ffn_b2"].reshape(1, D))}
    launches.append((nc5, [dict(feed5) for _ in range(8)]))

    specs = [("bw_T", [D, 256], F32), ("dw_T", [D, D], F32), ("cw_T", [256, D], F32),
             ("a_col", [128, 2], F32), ("bb_col", [128, 2], F32),
             ("cb_row", [1, D], F32), ("db_row", [1, D], F32)]
    nc6, ctx6 = _build(emit_l3_attn, specs)
    feed6 = {"cpool": ctx6.cpool_np(),
             "bw_T": np.ascontiguousarray(inp["L3_attn_B_w"].T),
             "dw_T": np.ascontiguousarray(inp["L3_attn_D_w"].T),
             "cw_T": np.ascontiguousarray(inp["L3_attn_C_w"].T),
             "a_col": _cols(inp["L3_attn_A"], 2),
             "bb_col": _cols(inp["L3_attn_B_b"], 2),
             "cb_row": np.ascontiguousarray(inp["L3_attn_C_b"].reshape(1, D)),
             "db_row": np.ascontiguousarray(inp["L3_attn_D_b"].reshape(1, D))}
    launches.append((nc6, [dict(feed6) for _ in range(8)]))

    specs = [("w1_T", [D, DH], F32), ("w2_T", [D, DH], F32), ("h1_T", [DH, DH], F32),
             ("h2_T", [DH, DH], F32), ("w3_T", [DH, D], F32),
             ("s1", [128, 32], F32), ("s2", [128, 32], F32), ("hs1", [128, 32], F32),
             ("hs2", [128, 32], F32), ("s3", [1, D], F32)]
    nc7, ctx7 = _build(emit_l3_ffn, specs)
    feed7 = {"cpool": ctx7.cpool_np(),
             "w1_T": np.ascontiguousarray(inp["L3_ffn_w1_w"].T),
             "w2_T": np.ascontiguousarray(inp["L3_ffn_w2_w"].T),
             "h1_T": np.ascontiguousarray(inp["L3_ffn_h1_w"].T),
             "h2_T": np.ascontiguousarray(inp["L3_ffn_h2_w"].T),
             "w3_T": np.ascontiguousarray(inp["L3_ffn_w3_w"].T),
             "s1": _cols(inp["L3_ffn_w1_s"], 32), "s2": _cols(inp["L3_ffn_w2_s"], 32),
             "hs1": _cols(inp["L3_ffn_h1_s"], 32), "hs2": _cols(inp["L3_ffn_h2_s"], 32),
             "s3": np.ascontiguousarray(inp["L3_ffn_w3_s"].reshape(1, D))}
    launches.append((nc7, [dict(feed7) for _ in range(8)]))

    for nc_i, feeds in launches:
        maps = [dict(feeds[b], x_in=np.ascontiguousarray(x[b])) for b in range(8)]
        res = run_bass_kernel_spmd(nc_i, maps, core_ids=list(range(8)))
        x = np.stack([res.results[b]["x_out"] for b in range(8)])
    return x


# version 03
# speedup vs baseline: 1.0125x; 1.0125x over previous
import numpy as np
import concourse.bass as bass
import concourse.mybir as mybir

F32 = mybir.dt.float32
BF16 = mybir.dt.bfloat16
I32 = mybir.dt.int32
AF = mybir.ActivationFunctionType
ALU = mybir.AluOpType
AX = mybir.AxisListType

T = 4096
D = 1024
NT = T // 128
ND = D // 128
RG8 = [list(range(8))]


class Ctx:

    def __init__(self, nc, block, stack):
        self.nc = nc
        self.block = block
        self.stack = stack
        self.sems = {}
        self.cnt = {}
        self.uid = 0
        self.cvals = {}
        self.cpool_sb = None
        self.cpool_dram = None

    def setup_cpool(self, max_consts=64):
        self.cpool_dram = self.nc.dram_tensor("cpool", [128, max_consts], F32,
                                              kind="ExternalInput").ap()
        self.cpool_sb = self.sb("cpool_sb", [128, max_consts], F32)
        block = self.block

        @block.sync
        def _(sync):
            self.inc(sync.dma_start(out=self.cpool_sb[:, :], in_=self.cpool_dram[:, :]),
                     "cpool", 16)

    def c(self, val):
        key = float(val)
        if key not in self.cvals:
            self.cvals[key] = len(self.cvals)
            assert len(self.cvals) <= self.cpool_sb.shape[-1]
        i = self.cvals[key]
        return self.cpool_sb[:, i:i + 1]

    def cpool_np(self, max_consts=64):
        a = np.zeros((128, max_consts), dtype=np.float32)
        for v, i in self.cvals.items():
            a[:, i] = v
        return a

    def wcp(self, eng):
        eng.wait_ge(self.sem("cpool"), 16)

    def wait(self, eng, name, target=None):
        s = self.sem(name)
        eng.wait_ge(s, self.cnt[name] if target is None else target)

    def sem(self, name):
        if name not in self.sems:
            self.sems[name] = self.stack.enter_context(self.nc.semaphore(name))
            self.cnt[name] = 0
        return self.sems[name]

    def inc(self, instr, name, n=1):
        instr.then_inc(self.sem(name), n)
        self.cnt[name] += n
        return self.cnt[name]

    def val(self, name):
        self.sem(name)
        return self.cnt[name]

    def fresh(self, prefix="s"):
        self.uid += 1
        return f"{prefix}{self.uid}"

    def sb(self, name, shape, dtype):
        return self.stack.enter_context(self.nc.sbuf_tensor(name, shape, dtype))

    def shared(self, key, shape, dtype):
        if not hasattr(self, "_shared"):
            self._shared = {}
        if key not in self._shared:
            self._shared[key] = self.sb(key, shape, dtype)
        t = self._shared[key]
        assert list(t.shape) == list(shape) and t.dtype == dtype
        return t

    def ps(self, shape, dtype=F32, name=None):
        if name is None:
            self.uid += 1
            name = "ps%d" % self.uid
        return self.stack.enter_context(self.nc.psum_tensor(name, shape, dtype))


def ln_stats_emit(ctx, v, x_ap, stats, mv, sem, width=None):
    width = width if width is not None else x_ap.shape[-1]
    nchunk = (width + 511) // 512
    s0 = ctx.val(sem)
    for c in range(nchunk):
        w = min(512, width - c * 512)
        ctx.inc(v.bn_stats(stats[:, 6 * c:6 * c + 6], x_ap[:, c * 512:c * 512 + w]), sem, 1)
    v.wait_ge(ctx.sems[sem], s0 + nchunk)
    ctx.inc(v.bn_aggr(mv[:, 0:2], stats[:, 0:6 * nchunk]), sem, 1)
    return ctx.val(sem)


def rstd_emit(sc, v, mv, eps, mubuf, rbuf):
    sc.activation(out=rbuf[:, 0:1], in_=mv[:, 1:2], func=AF.Ln, bias=float(eps))
    sc.activation(out=rbuf[:, 0:1], in_=rbuf[:, 0:1], func=AF.Exp, scale=-0.5)
    v.tensor_tensor(out=mubuf[:, 0:1], in0=mv[:, 0:1], in1=rbuf[:, 0:1], op=ALU.mult)
    v.tensor_scalar(out=mubuf[:, 0:1], in0=mubuf[:, 0:1], scalar1=-1.0, scalar2=None, op0=ALU.mult)


def np_bf16(x):
    import ml_dtypes
    return np.ascontiguousarray(x).astype(ml_dtypes.bfloat16)


class DmaStream:

    def __init__(self, ctx, name, depth=4):
        self.ctx, self.name, self.depth = ctx, name, depth
        self.n = 0
        self.bases = [ctx.val(f"{name}{s}") for s in range(depth)]

    def issue(self, eng, out, in_, transpose=False, **kw):
        s = self.n % self.depth
        if transpose:
            ins = eng.dma_start_transpose(out, in_)
        else:
            ins = eng.dma_start(out=out, in_=in_, **kw)
        self.ctx.inc(ins, f"{self.name}{s}", 16)
        self.n += 1
        return ins

    def wait(self, eng, i):
        s = i % self.depth
        eng.wait_ge(self.ctx.sem(f"{self.name}{s}"),
                    self.bases[s] + (i // self.depth + 1) * 16)

    def wait_all(self, eng):
        for s in range(self.depth):
            c = self.ctx.val(f"{self.name}{s}")
            if c > self.bases[s]:
                eng.wait_ge(self.ctx.sem(f"{self.name}{s}"), c)


"""Sub-block emitters for mk_sandwich_d4 (v1).

Per-core shapes: x [4096, 1024] f32 in DRAM. Each sub-block computes
x_out = x_in + LN(f(LN(x_in))).

Sync discipline:
- DMA via DmaStream (per-slot sems; one in-flight per slot, paced by consumer).
- Compute sems: per-tile ANALYTIC targets (base + (i+1)*K), asserted.
- Queues: sync = loads; gpsimd = stores/collectives/memsets; scalar = dma
  transposes. then_inc on real instructions; reduce-class ops get edges.
"""

RD = 2


def an(base, i, k):
    return base + (i + 1) * k



def pre_ln_to_fm(ctx, name, x_in, xn_bounce, xn_fm, inner_eps=None, xn_tok=None):
    block = ctx.block
    vs1, svs, ssc, sc2 = name + "vs1", name + "svs", name + "ssc", name + "sc2"

    xbuf = ctx.shared("sh_xb", [128, RD * D], F32)
    xnb = ctx.sb(name + "xn", [128, RD * D], BF16)
    stats = ctx.sb(name + "stt", [128, 12], F32)
    mv = ctx.sb(name + "mv", [128, 2 * NT], F32)
    acol = ctx.sb(name + "ac", [128, NT], F32)
    ccol = ctx.sb(name + "cc", [128, NT], F32)
    tmp = ctx.sb(name + "tmp", [128, NT], F32)

    ld = DmaStream(ctx, "pre_ld")
    b_vs1 = ctx.val(vs1)

    @block.vector
    def _(v):
        for i in range(NT):
            ld.wait(v, i)
            if i >= 1:
                v.wait_ge(ctx.sems[vs1], an(b_vs1, i - 1, 3))
            sl = xbuf[:, (i % RD) * D:(i % RD + 1) * D]
            ctx.inc(v.bn_stats(stats[:, 0:6], sl[:, 0:512]), vs1, 1)
            ctx.inc(v.bn_stats(stats[:, 6:12], sl[:, 512:1024]), vs1, 1)
            v.wait_ge(ctx.sems[vs1], b_vs1 + 3 * i + 2)
            ctx.inc(v.bn_aggr(mv[:, 2 * i:2 * i + 2], stats[:, 0:12]), vs1, 1)
    assert ctx.val(vs1) == b_vs1 + 3 * NT

    @block.sync
    def _(sync):
        for i in range(NT):
            if i >= RD:
                sync.wait_ge(ctx.sems[vs1], an(b_vs1, i - RD, 3))
            ld.issue(sync, xbuf[:, (i % RD) * D:(i % RD + 1) * D],
                     x_in[i * 128:(i + 1) * 128, :])

    @block.scalar
    def _(sc):
        sc.wait_ge(ctx.sems[vs1], b_vs1 + 3 * NT)
        ctx.wcp(sc)
        ctx.inc(sc.activation(out=tmp[:, :], in_=mv[:, 1:2 * NT:2], func=AF.Ln, bias=ctx.c(1e-5)), ssc, 1)
        sc.wait_ge(ctx.sems[ssc], ctx.val(ssc))
        ctx.inc(sc.activation(out=acol[:, :], in_=tmp[:, :], func=AF.Exp, scale=-0.5), ssc, 1)

    if inner_eps is not None:
        @block.vector
        def _(v):
            v.wait_ge(ctx.sems[ssc], ctx.val(ssc))
            ctx.inc(v.tensor_tensor(out=tmp[:, :], in0=acol[:, :], in1=acol[:, :], op=ALU.mult), svs, 1)
            v.wait_ge(ctx.sems[svs], ctx.val(svs))
            ctx.inc(v.tensor_tensor(out=tmp[:, :], in0=tmp[:, :], in1=mv[:, 1:2 * NT:2],
                                    op=ALU.mult), svs, 1)

        @block.scalar
        def _(sc):
            sc.wait_ge(ctx.sems[svs], ctx.val(svs))
            ctx.inc(sc.activation(out=tmp[:, :], in_=tmp[:, :], func=AF.Ln, bias=ctx.c(inner_eps)), ssc, 1)
            sc.wait_ge(ctx.sems[ssc], ctx.val(ssc))
            ctx.inc(sc.activation(out=tmp[:, :], in_=tmp[:, :], func=AF.Exp, scale=-0.5), ssc, 1)

        @block.vector
        def _(v):
            v.wait_ge(ctx.sems[ssc], ctx.val(ssc))
            ctx.inc(v.tensor_tensor(out=acol[:, :], in0=acol[:, :], in1=tmp[:, :], op=ALU.mult), svs, 1)

    @block.vector
    def _(v):
        v.wait_ge(ctx.sems[ssc], ctx.val(ssc))
        ctx.wait(v, svs)
        ctx.inc(v.tensor_tensor(out=ccol[:, :], in0=mv[:, 0:2 * NT:2], in1=acol[:, :], op=ALU.mult), svs, 1)
        v.wait_ge(ctx.sems[svs], ctx.val(svs))
        ctx.inc(v.tensor_scalar(out=ccol[:, :], in0=ccol[:, :], scalar1=-1.0, scalar2=None,
                                op0=ALU.mult), svs, 1)
    coeff_t = ctx.val(svs)

    ld2 = DmaStream(ctx, "pre_ld")
    st2 = DmaStream(ctx, "pre_st")
    b_sc2 = ctx.val(sc2)

    @block.scalar
    def _(sc):
        sc.wait_ge(ctx.sems[svs], coeff_t)
        for i in range(NT):
            ld2.wait(sc, i)
            if i >= RD:
                st2.wait(sc, i - RD)
            sl = xnb[:, (i % RD) * D:(i % RD + 1) * D]
            ctx.inc(sc.activation(out=sl, in_=xbuf[:, (i % RD) * D:(i % RD + 1) * D],
                                  func=AF.Identity, scale=acol[:, i:i + 1],
                                  bias=ccol[:, i:i + 1]), sc2, 1)
            if xn_tok is not None:
                sc.copy(out=xn_tok[:, i * D:(i + 1) * D], in_=sl)
    assert ctx.val(sc2) == b_sc2 + NT

    @block.sync
    def _(sync):
        sync.wait_ge(ctx.sems[vs1], b_vs1 + 3 * NT)
        for i in range(NT):
            if i >= RD:
                sync.wait_ge(ctx.sems[sc2], an(b_sc2, i - RD, 1))
            ld2.issue(sync, xbuf[:, (i % RD) * D:(i % RD + 1) * D],
                      x_in[i * 128:(i + 1) * 128, :])

    @block.gpsimd
    def _(g):
        for i in range(NT):
            g.wait_ge(ctx.sems[sc2], an(b_sc2, i, 1))
            st2.issue(g, xn_bounce[i * 128:(i + 1) * 128, :],
                      xnb[:, (i % RD) * D:(i % RD + 1) * D])

    fmk = name + "fmk"
    if xn_fm is None:
        @block.scalar
        def _(sc):
            st2.wait_all(sc)
            ctx.inc(sc.copy(out=stats[:, 0:1], in_=stats[:, 0:1]), fmk, 1)
        return fmk, ctx.val(fmk)
    tp = DmaStream(ctx, "pre_tp", depth=8)

    @block.scalar
    def _(sc):
        st2.wait_all(sc)
        for j in range(ND):
            tp.issue(sc, xn_fm[:, j * T:(j + 1) * T],
                     xn_bounce[:, j * 128:(j + 1) * 128], transpose=True)
        for j in range(ND):
            tp.wait(sc, j)
        ctx.inc(sc.copy(out=stats[:, 0:1], in_=stats[:, 0:1]), fmk, 1)
    return fmk, ctx.val(fmk)



def post_ln_residual(ctx, name, x_in, x_out, y_half_ap, y_target, y_sem,
                     scale_col=None, pre_bases=None):
    block = ctx.block
    vs, scs = name + "vs", name + "sc"
    b_vs, b_sc = ctx.val(vs), ctx.val(scs)
    if pre_bases is not None:
        assert (b_vs, b_sc) == pre_bases, (b_vs, b_sc, pre_bases)
    PR = 1 if scale_col is not None else 0
    KV = 9 if scale_col is not None else 7

    def var_ready(i):
        return b_vs + PR + i * KV + (4 if scale_col is not None else 3)

    def apply_done(i):
        return b_vs + PR + (i + 1) * KV

    ld = DmaStream(ctx, "pst_ld")
    st = DmaStream(ctx, "pst_st")
    xbuf = ctx.sb(name + "xb", [128, RD * D], F32)
    obuf = ctx.sb(name + "ob", [128, RD * D], F32)
    stats = ctx.sb(name + "stt", [128, 12], F32)
    mv = ctx.sb(name + "mv", [128, 2], F32)
    rcol = ctx.sb(name + "rc", [128, 4], F32)

    @block.vector
    def _(v):
        if scale_col is not None:
            ctx.inc(v.tensor_tensor(out=rcol[:, 2:3], in0=scale_col, in1=scale_col,
                                    op=ALU.mult), vs, 1)
            v.wait_ge(ctx.sems[vs], ctx.val(vs))
        for i in range(NT):
            v.wait_ge(ctx.sems[y_sem], y_target(i))
            if i >= 1:
                v.wait_ge(ctx.sems[vs], b_vs + PR + (i - 1) * KV + 3)
            if i == 0 and getattr(ctx, "dbg_y", None) is not None:
                v.tensor_copy(out=ctx.dbg_y[:, 0:512], in_=y_half_ap(0, 0))
                ctx.inc(v.tensor_copy(out=ctx.dbg_y[:, 512:1024], in_=y_half_ap(0, 1)), "dbgy", 1)
            ctx.inc(v.bn_stats(stats[:, 0:6], y_half_ap(i, 0)), vs, 1)
            ctx.inc(v.bn_stats(stats[:, 6:12], y_half_ap(i, 1)), vs, 1)
            v.wait_ge(ctx.sems[vs], ctx.val(vs))
            ctx.inc(v.bn_aggr(mv[:, 0:2], stats[:, 0:12]), vs, 1)
            if scale_col is not None:
                v.wait_ge(ctx.sems[vs], ctx.val(vs))
                ctx.inc(v.tensor_tensor(out=mv[:, 1:2], in0=mv[:, 1:2], in1=rcol[:, 2:3],
                                        op=ALU.mult), vs, 1)
            assert ctx.val(vs) == var_ready(i)
            v.wait_ge(ctx.sems[scs], an(b_sc, i, 2))
            ld.wait(v, i)
            if i >= RD:
                st.wait(v, i - RD)
            ob = obuf[:, (i % RD) * D:(i % RD + 1) * D]
            xb = xbuf[:, (i % RD) * D:(i % RD + 1) * D]
            if scale_col is not None:
                ctx.inc(v.tensor_tensor(out=rcol[:, 0:1], in0=rcol[:, 0:1], in1=scale_col,
                                        op=ALU.mult), vs, 1)
                v.wait_ge(ctx.sems[vs], ctx.val(vs))
            ctx.inc(v.tensor_tensor(out=rcol[:, 1:2], in0=mv[:, 0:1], in1=rcol[:, 0:1],
                                    op=ALU.mult), vs, 1)
            v.wait_ge(ctx.sems[vs], ctx.val(vs))
            for h in range(2):
                ctx.inc(v.scalar_tensor_tensor(out=ob[:, h * 512:(h + 1) * 512],
                                               in0=y_half_ap(i, h), scalar=rcol[:, 0:1],
                                               in1=xb[:, h * 512:(h + 1) * 512],
                                               op0=ALU.mult, op1=ALU.add), vs, 1)
            v.wait_ge(ctx.sems[vs], ctx.val(vs))
            ctx.inc(v.tensor_scalar(out=ob, in0=ob, scalar1=rcol[:, 1:2], scalar2=None,
                                    op0=ALU.subtract), vs, 1)
            assert ctx.val(vs) == apply_done(i)

    @block.scalar
    def _(sc):
        ctx.wcp(sc)
        for i in range(NT):
            sc.wait_ge(ctx.sems[vs], var_ready(i))
            ctx.inc(sc.activation(out=rcol[:, 3:4], in_=mv[:, 1:2], func=AF.Ln,
                                  bias=ctx.c(1e-5)), scs, 1)
            sc.wait_ge(ctx.sems[scs], b_sc + 2 * i + 1)
            ctx.inc(sc.activation(out=rcol[:, 0:1], in_=rcol[:, 3:4], func=AF.Exp, scale=-0.5),
                    scs, 1)

    @block.sync
    def _(sync):
        for i in range(NT):
            if i >= RD:
                sync.wait_ge(ctx.sems[vs], apply_done(i - RD))
            ld.issue(sync, xbuf[:, (i % RD) * D:(i % RD + 1) * D],
                     x_in[i * 128:(i + 1) * 128, :])

    @block.gpsimd
    def _(g):
        for i in range(NT):
            g.wait_ge(ctx.sems[vs], apply_done(i))
            st.issue(g, x_out[i * 128:(i + 1) * 128, :],
                     obuf[:, (i % RD) * D:(i % RD + 1) * D])

    fin = name + "fin"

    @block.gpsimd
    def _(g):
        st.wait_all(g)
        ctx.inc(g.memset(stats[0:1, 0:1], 0.0), fin, 1)
    return fin, ctx.val(fin)


def plan_post_ln(ctx, name, scale=False):
    vs, scs = name + "vs", name + "sc"
    bases = (ctx.val(vs), ctx.val(scs))
    PR = 1 if scale else 0
    KV = 9 if scale else 7
    b_vs = bases[0]

    def apply_done(i):
        return b_vs + PR + (i + 1) * KV
    return bases, apply_done, vs



def emit_l0_ffn(ctx, x_in, x_out, w_T, ln_gamma, ln_beta, xn_bounce, dbg=None, dbg_bf=None):
    nc, block = ctx.nc, ctx.block
    name = "l0f"
    QB = 128.0

    xn_fm = ctx.sb(name + "xnfm", [128, ND * T], BF16)
    wbuf = ctx.sb(name + "w", [128, ND * D], F32)
    wq = ctx.sb(name + "wq", [128, ND * D], BF16)
    tmpf = ctx.sb(name + "tmpf", [128, T], F32)
    gb = ctx.sb(name + "gb", [128, 2 * ND], F32)
    red = ctx.sb(name + "red", [128, 24], F32)
    smalls = ctx.sb(name + "sml", [128, 8], F32)
    one_r = ctx.sb(name + "oner", [1, 128], F32)
    one_rb = ctx.sb(name + "onerb", [1, 128], BF16)
    scal = ctx.sb(name + "scal", [1, 16], F32)
    wi = ctx.sb(name + "wi", [128, D], I32)
    ps_small = ctx.ps([128, 16], F32)
    gmax_i = nc.dram_tensor(name + "_gmax_i", [1, 1], F32, kind="Internal").ap()
    gmax_o = nc.dram_tensor(name + "_gmax_o", [1, 1], F32, kind="Internal", addr_space="Shared").ap()
    red_dram = nc.dram_tensor(name + "_red_d", [128, 1], F32, kind="Internal").ap()

    vsm, gps, ccs = name + "vsm", name + "gp", name + "cc"
    scB, scq, vsq, mmq = name + "scB", name + "scq", name + "vsq", name + "mmq"

    fm_ready = pre_ln_to_fm(ctx, name + "P", x_in, xn_bounce, xn_fm, inner_eps=1e-3)

    wld = DmaStream(ctx, "wld")

    @block.sync
    def _(sync):
        wld.issue(sync, gb[:, 0:ND], ln_gamma[:, :])
        wld.issue(sync, gb[:, ND:2 * ND], ln_beta[:, :])
        wld.issue(sync, wbuf[:, :], w_T.rearrange("(j p) o -> p j o", p=128))

    @block.gpsimd
    def _(g):
        g.memset(one_r[:, :], 1.0)
        ctx.inc(g.memset(smalls[:, 0:1], 1.0), gps, 1)

    b_scB = ctx.val(scB)

    @block.scalar
    def _(sc):
        sc.wait_ge(ctx.sems[fm_ready[0]], fm_ready[1])
        wld.wait(sc, 0)
        wld.wait(sc, 1)
        for j in range(ND):
            ctx.inc(sc.activation(out=xn_fm[:, j * T:(j + 1) * T], in_=xn_fm[:, j * T:(j + 1) * T],
                                  func=AF.Identity, scale=gb[:, j:j + 1],
                                  bias=gb[:, ND + j:ND + j + 1]), scB, 1)

    b_vsm = ctx.val(vsm)

    @block.vector
    def _(v):
        for j in range(ND):
            v.wait_ge(ctx.sems[scB], an(b_scB, j, 1))
            ctx.inc(v.reduce_max(red[:, j:j + 1], xn_fm[:, j * T:(j + 1) * T],
                                 axis=AX.X, apply_absolute_value=True), vsm, 1)
        v.wait_ge(ctx.sems[vsm], b_vsm + ND)
        ctx.inc(v.reduce_max(red[:, 8:9], red[:, 0:ND], axis=AX.X), vsm, 1)
    absmax_t = ctx.val(vsm)

    msc = DmaStream(ctx, "msc")

    @block.sync
    def _(sync):
        sync.wait_ge(ctx.sems[vsm], absmax_t)
        msc.issue(sync, red_dram[:, :], red[:, 8:9])
        msc.wait(sync, 0)
        msc.issue(sync, tmpf[0:1, 0:128], red_dram.rearrange("p one -> one p"))

    @block.vector
    def _(v):
        msc.wait(v, 1)
        ctx.inc(v.reduce_max(scal[0:1, 0:1], tmpf[0:1, 0:128], axis=AX.X), vsq, 1)

    @block.sync
    def _(sync):
        sync.wait_ge(ctx.sems[vsq], ctx.val(vsq))
        msc.issue(sync, gmax_i[:, :], scal[0:1, 0:1])

    @block.gpsimd
    def _(g):
        msc.wait(g, 2)
        ctx.inc(g.collective_compute("AllReduce", ALU.max, RG8, [gmax_i.opt()], [gmax_o.opt()]),
                ccs, 1)

    @block.sync
    def _(sync):
        sync.wait_ge(ctx.sems[ccs], ctx.val(ccs))
        msc.issue(sync, scal[0:1, 0:1], gmax_o[:, :])
    gmax_idx = 3

    b_scq = ctx.val(scq)

    @block.scalar
    def _(sc):
        wld.wait(sc, 2)
        for j in range(ND):
            ctx.inc(sc.activation(out=wq[:, j * D:(j + 1) * D], in_=wbuf[:, j * D:(j + 1) * D],
                                  func=AF.Abs, accum_out=red[:, 10 + j:11 + j]), scq, 1)

    @block.vector
    def _(v):
        v.wait_ge(ctx.sems[scq], b_scq + ND)
        ctx.inc(v.reduce_sum(red[:, 18:19], red[:, 10:18], axis=AX.X), vsq, 1)

    @block.tensor
    def _(t):
        t.wait_ge(ctx.sems[vsq], ctx.val(vsq))
        t.wait_ge(ctx.sems[gps], 1)
        ctx.inc(t.matmul(ps_small[0:1, 0:1], smalls[:, 0:1], red[:, 18:19], start=True, stop=True),
                mmq, 1)

    @block.scalar
    def _(sc):
        sc.wait_ge(ctx.sems[mmq], ctx.val(mmq))
        ctx.inc(sc.activation(out=scal[0:1, 1:2], in_=ps_small[0:1, 0:1], func=AF.Identity,
                              scale=1.0 / (1024.0 * 1024.0)), scq, 1)
    bh_t = ctx.val(scq)

    @block.vector
    def _(v):
        v.wait_ge(ctx.sems[scq], bh_t)
        msc.wait(v, gmax_idx)
        ctx.inc(v.tensor_scalar(out=scal[0:1, 6:7], in0=scal[0:1, 1:2], scalar1=1e-5,
                                scalar2=None, op0=ALU.add), vsq, 1)
        v.wait_ge(ctx.sems[vsq], ctx.val(vsq))
        ctx.inc(v.reciprocal(scal[0:1, 2:3], scal[0:1, 6:7]), vsq, 1)
        ctx.inc(v.reciprocal(scal[0:1, 3:4], scal[0:1, 0:1]), vsq, 1)
        v.wait_ge(ctx.sems[vsq], ctx.val(vsq))
        ctx.inc(v.tensor_tensor(out=scal[0:1, 4:5], in0=scal[0:1, 1:2], in1=scal[0:1, 0:1],
                                op=ALU.mult), vsq, 1)
        v.wait_ge(ctx.sems[vsq], ctx.val(vsq))
        ctx.inc(v.tensor_scalar(out=scal[0:1, 4:5], in0=scal[0:1, 4:5], scalar1=1.0 / QB,
                                scalar2=None, op0=ALU.mult), vsq, 1)

    @block.tensor
    def _(t):
        t.wait_ge(ctx.sems[vsq], ctx.val(vsq))
        ctx.inc(t.matmul(ps_small[:, 1:4], one_r[0:1, :], scal[0:1, 2:5], start=True, stop=True),
                mmq, 1)

    @block.scalar
    def _(sc):
        sc.wait_ge(ctx.sems[mmq], ctx.val(mmq))
        sc.copy(out=smalls[:, 2:3], in_=ps_small[:, 1:2])
        sc.activation(out=smalls[:, 3:4], in_=ps_small[:, 2:3], func=AF.Identity, scale=QB)
        ctx.inc(sc.copy(out=smalls[:, 4:5], in_=ps_small[:, 3:4]), scq, 1)
    bcast_t = ctx.val(scq)

    qa = ctx.sb(name + "qa", [128, D], F32)
    qs = ctx.sb(name + "qs", [128, D], F32)
    qt = ctx.sb(name + "qt", [128, D], F32)
    qf = ctx.sb(name + "qf", [128, D], F32)

    @block.vector
    def _(v):
        v.wait_ge(ctx.sems[scq], bcast_t)
        for j in range(ND):
            wj = wbuf[:, j * D:(j + 1) * D]
            ctx.inc(v.tensor_scalar(out=wj, in0=wj, scalar1=smalls[:, 2:3], scalar2=None,
                                    op0=ALU.mult), vsq, 1)
            v.wait_ge(ctx.sems[vsq], ctx.val(vsq))
            ctx.inc(v.tensor_scalar(out=qs[:, :], in0=wj, scalar1=0.0, scalar2=None,
                                    op0=ALU.is_ge), vsq, 1)
            v.wait_ge(ctx.sems[vsq], ctx.val(vsq))
            ctx.inc(v.tensor_scalar(out=qs[:, :], in0=qs[:, :], scalar1=2.0, scalar2=-1.0,
                                    op0=ALU.mult, op1=ALU.add), vsq, 1)
            v.wait_ge(ctx.sems[vsq], ctx.val(vsq))
            ctx.inc(v.tensor_tensor(out=qt[:, :], in0=wj, in1=qs[:, :], op=ALU.mult), vsq, 1)
            v.wait_ge(ctx.sems[vsq], ctx.val(vsq))
            ctx.inc(v.tensor_scalar(out=qt[:, :], in0=qt[:, :], scalar1=0.5, scalar2=None,
                                    op0=ALU.add), vsq, 1)
            v.wait_ge(ctx.sems[vsq], ctx.val(vsq))
            ctx.inc(v.tensor_copy(out=wi[:, :], in_=qt[:, :]), vsq, 1)
            v.wait_ge(ctx.sems[vsq], ctx.val(vsq))
            ctx.inc(v.tensor_copy(out=qf[:, :], in_=wi[:, :]), vsq, 1)
            v.wait_ge(ctx.sems[vsq], ctx.val(vsq))
            ctx.inc(v.tensor_tensor(out=qa[:, :], in0=qf[:, :], in1=qt[:, :], op=ALU.is_gt),
                    vsq, 1)
            v.wait_ge(ctx.sems[vsq], ctx.val(vsq))
            ctx.inc(v.tensor_tensor(out=qf[:, :], in0=qf[:, :], in1=qa[:, :], op=ALU.subtract),
                    vsq, 1)
            v.wait_ge(ctx.sems[vsq], ctx.val(vsq))
            ctx.inc(v.tensor_tensor(out=qf[:, :], in0=qf[:, :], in1=qs[:, :], op=ALU.mult),
                    vsq, 1)
            v.wait_ge(ctx.sems[vsq], ctx.val(vsq))
            ctx.inc(v.tensor_scalar(out=wq[:, j * D:(j + 1) * D], in0=qf[:, :], scalar1=1.0,
                                    scalar2=-1.0, op0=ALU.min, op1=ALU.max), vsq, 1)
    wq_t = ctx.val(vsq)

    vcl = name + "vcl"
    b_scx, b_vcl = ctx.val(scq), ctx.val(vcl)

    @block.scalar
    def _(sc):
        sc.wait_ge(ctx.sems[vsm], absmax_t)
        sc.wait_ge(ctx.sems[scq], bcast_t)
        for j in range(ND):
            if j >= 1:
                sc.wait_ge(ctx.sems[vcl], an(b_vcl, j - 1, 1))
            ctx.inc(sc.activation(out=tmpf[:, 0:T],
                                  in_=xn_fm[:, j * T:(j + 1) * T], func=AF.Identity,
                                  scale=smalls[:, 3:4]), scq, 1)

    @block.vector
    def _(v):
        for j in range(ND):
            v.wait_ge(ctx.sems[scq], an(b_scx, j, 1))
            ctx.inc(v.tensor_scalar(out=xn_fm[:, j * T:(j + 1) * T],
                                    in0=tmpf[:, 0:T],
                                    scalar1=QB - 1e-5, scalar2=-(QB - 1e-5),
                                    op0=ALU.min, op1=ALU.max), vcl, 1)
    xq_t = ctx.val(vcl)

    mmy = name + "mmy"
    ps_y = [ctx.ps([128, 512], F32) for _ in range(4)]
    bases, apply_done, post_vs = plan_post_ln(ctx, "l0fR", scale=True)
    b_mmy = ctx.val(mmy)

    @block.tensor
    def _(t):
        t.wait_ge(ctx.sems[vcl], xq_t)
        t.wait_ge(ctx.sems[vsq], wq_t)
        for i in range(NT):
            if i >= 2:
                t.wait_ge(ctx.sems[post_vs], apply_done(i - 2))
            for e in range(2):
                pb = ps_y[2 * (i % 2) + e]
                for j in range(ND):
                    ins = t.matmul(pb[:, :], xn_fm[:, j * T + i * 128:j * T + (i + 1) * 128],
                                   wq[:, j * D + e * 512:j * D + (e + 1) * 512],
                                   start=(j == 0), stop=(j == ND - 1))
                ctx.inc(ins, mmy, 1)

    def y_half(i, h):
        return ps_y[2 * (i % 2) + h][:, :]

    def y_target(i):
        return an(b_mmy, i, 2)

    if dbg is not None:
        dbs = DmaStream(ctx, "dbs")

        @block.gpsimd
        def _(g):
            g.wait_ge(ctx.sems[vcl], xq_t)
            g.wait_ge(ctx.sems[vsq], wq_t)
            dbs.issue(g, dbg[0:1, 0:5], scal[0:1, 0:5])
            dbs.issue(g, dbg[:, 16:19], smalls[:, 2:5])
            dbs.issue(g, dbg[:, 24:33], red[:, 0:9])
            dbs.issue(g, dbg_bf[:, 64:192], xn_fm[:, 0:128])
            for jj in range(ND):
                dbs.issue(g, dbg_bf[:, 320 + 8 * jj:328 + 8 * jj], xn_fm[:, jj * T:jj * T + 8])
            dbs.issue(g, dbg_bf[:, 192:320], wq[:, 0:128])
            dbs.issue(g, ctx.dbg_w1, wbuf[:, 0:128])
            dbs.issue(g, ctx.dbg_w2, wi[:, 0:128])

    return post_ln_residual(ctx, "l0fR", x_in, x_out, y_half, y_target, mmy,
                            scale_col=smalls[:, 4:5], pre_bases=bases)



def emit_l3_attn(ctx, x_in, x_out, bw_T, dw_T, cw_T, a_col, bb_col, cb_row, db_row,
                 xn_bounce):
    nc, block = ctx.nc, ctx.block
    name = "l3a"
    NST = 256

    xn_fm = ctx.sb(name + "xnfm", [128, ND * T], BF16)
    bw_sb = ctx.sb(name + "bw", [128, ND * NST], BF16)
    dw_sb = ctx.sb(name + "dw", [128, ND * D], BF16)
    cw_f32 = ctx.sb(name + "cw", [128, 2 * D], F32)
    bu_il = ctx.sb(name + "bu", [128, 2 * T], F32)
    st_il = ctx.sb(name + "st", [128, 2 * T], F32)
    cols = ctx.sb(name + "cols", [128, 8], F32)
    rows = ctx.sb(name + "rows", [1, 3 * D], F32)
    one_r = ctx.sb(name + "oner", [1, 128], F32)
    one_rb = ctx.sb(name + "onerb", [1, 128], BF16)

    ld = DmaStream(ctx, "wld")
    cld = DmaStream(ctx, "cld")
    scs, vss, mmb, gps = name + "sc", name + "vs", name + "mmb", name + "gp"

    @block.sync
    def _(sync):
        ld.issue(sync, cols[:, 0:2], a_col[:, :])
        ld.issue(sync, cols[:, 2:4], bb_col[:, :])
        ld.issue(sync, rows[0:1, 0:D], cb_row[:, :])
        ld.issue(sync, rows[0:1, D:2 * D], db_row[:, :])
    i_a, i_bb, i_cb, i_db = 0, 1, 2, 3

    @block.gpsimd
    def _(g):
        cld.issue(g, bw_sb.rearrange("p (j n) -> p j n", j=ND),
                  bw_T.rearrange("(j p) n -> p j n", p=128))
        cld.issue(g, dw_sb.rearrange("p (j o) -> p j o", j=ND),
                  dw_T.rearrange("(j p) o -> p j o", p=128))
        cld.issue(g, cw_f32.rearrange("p (j o) -> p j o", j=2),
                  cw_T.rearrange("(j p) o -> p j o", p=128))
        ctx.inc(g.memset(one_r[:, :], 1.0), gps, 1)
    i_bw, i_dw, i_cw = 0, 1, 2

    @block.scalar
    def _(sc):
        ld.wait(sc, i_a)
        ctx.inc(sc.activation(out=cols[:, 4:6], in_=cols[:, 0:2], func=AF.Exp), scs, 1)

    @block.vector
    def _(v):
        ld.wait(v, i_cb)
        ld.wait(v, i_db)
        ctx.inc(v.tensor_tensor(out=rows[0:1, 2 * D:3 * D], in0=rows[0:1, 0:D],
                                in1=rows[0:1, D:2 * D], op=ALU.add), vss, 1)
    brow_t = ctx.val(vss)

    fm_ready = pre_ln_to_fm(ctx, name + "P", x_in, xn_bounce, xn_fm)

    mmu = name + "mmu"
    ps_bu = [ctx.ps([128, 512], F32) for _ in range(4)]
    b_mmu = ctx.val(mmu)
    b_sev = ctx.val(scs)

    @block.tensor
    def _(t):
        t.wait_ge(ctx.sems[fm_ready[0]], fm_ready[1])
        cld.wait(t, i_bw)
        k = 0
        for m in range(2):
            for c in range(8):
                pb = ps_bu[k % 4]
                if k >= 4:
                    t.wait_ge(ctx.sems[scs], b_sev + (k - 4 + 1))
                for j in range(ND):
                    ins = t.matmul(pb[:, :], bw_sb[:, j * NST + m * 128:j * NST + (m + 1) * 128],
                                   xn_fm[:, j * T + c * 512:j * T + (c + 1) * 512],
                                   start=(j == 0), stop=(j == ND - 1))
                ctx.inc(ins, mmu, 1)
                k += 1

    @block.scalar
    def _(sc):
        ld.wait(sc, i_bb)
        k = 0
        for m in range(2):
            for c in range(8):
                sc.wait_ge(ctx.sems[mmu], an(b_mmu, k, 1))
                pb = ps_bu[k % 4]
                dst = bu_il[:, 2 * c * 512 + m: 2 * (c + 1) * 512: 2]
                ctx.inc(sc.activation(out=dst, in_=pb[:, :], func=AF.Identity,
                                      bias=cols[:, 2 + m:3 + m]), scs, 1)
                k += 1
    bu_done = ctx.val(scs)

    b_scan = ctx.val(scs)

    @block.scalar
    def _(sc):
        sc.wait_ge(ctx.sems[scs], bu_done)
        for m in range(2):
            ctx.inc(sc.activation(out=st_il[:, m:m + 1], in_=bu_il[:, m:m + 1],
                                  func=AF.Tanh), scs, 1)
            for tt in range(1, T):
                sc.wait_ge(ctx.sems[scs], ctx.val(scs))
                ctx.inc(sc.activation(out=st_il[:, 2 * tt + m:2 * tt + m + 1],
                                      in_=st_il[:, 2 * (tt - 1) + m:2 * (tt - 1) + m + 1],
                                      func=AF.Tanh, scale=cols[:, 4 + m:5 + m],
                                      bias=bu_il[:, 2 * tt + m:2 * tt + m + 1]), scs, 1)
    scan_done = ctx.val(scs)


    mmy = name + "mmy"
    ps_y = [ctx.ps([128, 512], F32) for _ in range(4)]
    bases, apply_done, post_vs = plan_post_ln(ctx, name + "R", scale=False)
    b_mmy = ctx.val(mmy)

    @block.tensor
    def _(t):
        t.wait_ge(ctx.sems[scs], scan_done)
        cld.wait(t, i_dw)
        cld.wait(t, i_cw)
        t.wait_ge(ctx.sems[vss], brow_t)
        t.wait_ge(ctx.sems[gps], 1)
        for i in range(NT):
            if i >= 2:
                t.wait_ge(ctx.sems[post_vs], apply_done(i - 2))
            for e in range(2):
                pb = ps_y[2 * (i % 2) + e]
                for j in range(ND):
                    t.matmul(pb[:, :], xn_fm[:, j * T + i * 128:j * T + (i + 1) * 128],
                             dw_sb[:, j * D + e * 512:j * D + (e + 1) * 512],
                             start=(j == 0), stop=False)
                for m in range(2):
                    t.matmul(pb[:, :], st_il[:, 2 * i * 128 + m:2 * (i + 1) * 128:2],
                             cw_f32[:, m * D + e * 512:m * D + (e + 1) * 512],
                             start=False, stop=False)
                ctx.inc(t.matmul(pb[:, :], one_r[0:1, :],
                                 rows[0:1, 2 * D + e * 512:2 * D + (e + 1) * 512],
                                 start=False, stop=True), mmy, 1)

    def y_half(i, h):
        return ps_y[2 * (i % 2) + h][:, :]

    def y_target(i):
        return an(b_mmy, i, 2)

    return post_ln_residual(ctx, name + "R", x_in, x_out, y_half, y_target, mmy,
                            pre_bases=bases)



class PostLN:

    def __init__(self, ctx, name, x_in, x_out, y_half_ap, y_target, y_sem,
                 scale_col=None, obuf=None):
        self.ctx, self.name = ctx, name
        self.x_in, self.x_out = x_in, x_out
        self.y_half_ap, self.y_target, self.y_sem = y_half_ap, y_target, y_sem
        self.scale_col = scale_col
        self.vs, self.scs = name + "vs", name + "sc"
        self.b_vs, self.b_sc = ctx.val(self.vs), ctx.val(self.scs)
        self.PR = 1 if scale_col is not None else 0
        self.KV = 9 if scale_col is not None else 7
        self.ld = DmaStream(ctx, name + "_ld")
        self.st = DmaStream(ctx, name + "_st")
        self.xbuf = ctx.shared("sh_xb", [128, RD * D], F32)
        self.obuf = obuf if obuf is not None else ctx.sb(name + "ob", [128, RD * D], F32)
        self.stats = ctx.sb(name + "stt", [128, 12], F32)
        self.mv = ctx.sb(name + "mv", [128, 2], F32)
        self.rcol = ctx.sb(name + "rc", [128, 4], F32)
        self.prologue_done = False

    def var_ready(self, i):
        return self.b_vs + self.PR + i * self.KV + (4 if self.scale_col is not None else 3)

    def apply_done(self, i):
        return self.b_vs + self.PR + (i + 1) * self.KV

    def vec_tile(self, v, i):
        ctx = self.ctx
        vs, scs = self.vs, self.scs
        if not self.prologue_done:
            if self.scale_col is not None:
                ctx.inc(v.tensor_tensor(out=self.rcol[:, 2:3], in0=self.scale_col,
                                        in1=self.scale_col, op=ALU.mult), vs, 1)
                v.wait_ge(ctx.sems[vs], ctx.val(vs))
            self.prologue_done = True
        v.wait_ge(ctx.sems[self.y_sem], self.y_target(i))
        if i >= 1:
            v.wait_ge(ctx.sems[vs], self.b_vs + self.PR + (i - 1) * self.KV + 3)
        ctx.inc(v.bn_stats(self.stats[:, 0:6], self.y_half_ap(i, 0)), vs, 1)
        ctx.inc(v.bn_stats(self.stats[:, 6:12], self.y_half_ap(i, 1)), vs, 1)
        v.wait_ge(ctx.sems[vs], ctx.val(vs))
        ctx.inc(v.bn_aggr(self.mv[:, 0:2], self.stats[:, 0:12]), vs, 1)
        if self.scale_col is not None:
            v.wait_ge(ctx.sems[vs], ctx.val(vs))
            ctx.inc(v.tensor_tensor(out=self.mv[:, 1:2], in0=self.mv[:, 1:2],
                                    in1=self.rcol[:, 2:3], op=ALU.mult), vs, 1)
        assert ctx.val(vs) == self.var_ready(i)
        v.wait_ge(ctx.sems[scs], an(self.b_sc, i, 2))
        self.ld.wait(v, i)
        if i >= RD:
            self.st.wait(v, i - RD)
        ob = self.obuf[:, (i % RD) * D:(i % RD + 1) * D]
        xb = self.xbuf[:, (i % RD) * D:(i % RD + 1) * D]
        if self.scale_col is not None:
            ctx.inc(v.tensor_tensor(out=self.rcol[:, 0:1], in0=self.rcol[:, 0:1],
                                    in1=self.scale_col, op=ALU.mult), vs, 1)
            v.wait_ge(ctx.sems[vs], ctx.val(vs))
        ctx.inc(v.tensor_tensor(out=self.rcol[:, 1:2], in0=self.mv[:, 0:1],
                                in1=self.rcol[:, 0:1], op=ALU.mult), vs, 1)
        v.wait_ge(ctx.sems[vs], ctx.val(vs))
        for h in range(2):
            ctx.inc(v.scalar_tensor_tensor(out=ob[:, h * 512:(h + 1) * 512],
                                           in0=self.y_half_ap(i, h), scalar=self.rcol[:, 0:1],
                                           in1=xb[:, h * 512:(h + 1) * 512],
                                           op0=ALU.mult, op1=ALU.add), vs, 1)
        v.wait_ge(ctx.sems[vs], ctx.val(vs))
        ctx.inc(v.tensor_scalar(out=ob, in0=ob, scalar1=self.rcol[:, 1:2], scalar2=None,
                                op0=ALU.subtract), vs, 1)
        assert ctx.val(vs) == self.apply_done(i)

    def sc_tile(self, sc, i):
        ctx = self.ctx
        if i == 0:
            ctx.wcp(sc)
        sc.wait_ge(ctx.sems[self.vs], self.var_ready(i))
        ctx.inc(sc.activation(out=self.rcol[:, 3:4], in_=self.mv[:, 1:2], func=AF.Ln,
                              bias=ctx.c(1e-5)), self.scs, 1)
        sc.wait_ge(ctx.sems[self.scs], self.b_sc + 2 * i + 1)
        ctx.inc(sc.activation(out=self.rcol[:, 0:1], in_=self.rcol[:, 3:4], func=AF.Exp,
                              scale=-0.5), self.scs, 1)

    def emit_sync_loads(self, first_wait=None):
        ctx, block = self.ctx, self.ctx.block
        @block.sync
        def _(sync):
            if first_wait is not None:
                sync.wait_ge(ctx.sems[first_wait[0]], first_wait[1])
            for i in range(NT):
                self.ld_tile(sync, i)

    def ld_tile(self, sync, i):
        ctx = self.ctx
        if i >= RD:
            sync.wait_ge(ctx.sems[self.vs], self.apply_done(i - RD))
        self.ld.issue(sync, self.xbuf[:, (i % RD) * D:(i % RD + 1) * D],
                      self.x_in[i * 128:(i + 1) * 128, :])

    def emit_gpsimd_stores(self):
        ctx, block = self.ctx, self.ctx.block
        @block.gpsimd
        def _(g):
            for i in range(NT):
                self.st_tile(g, i)

    def st_tile(self, g, i):
        ctx = self.ctx
        g.wait_ge(ctx.sems[self.vs], self.apply_done(i))
        self.st.issue(g, self.x_out[i * 128:(i + 1) * 128, :],
                      self.obuf[:, (i % RD) * D:(i % RD + 1) * D])

    def finish(self):
        ctx, block = self.ctx, self.ctx.block
        fin = self.name + "fin"
        @block.gpsimd
        def _(g):
            self.st.wait_all(g)
            ctx.inc(g.memset(self.stats[0:1, 0:1], 0.0), fin, 1)
        return fin, ctx.val(fin)



def emit_l3_ffn(ctx, x_in, x_out, w1_T, w2_T, h1_T, h2_T, w3_T,
                s1_col, s2_col, hs1_col, hs2_col, s3_row, xn_bounce):
    nc, block = ctx.nc, ctx.block
    name = "l3f"
    TB = 1024
    NP = T // TB

    xn_p = ctx.sb(name + "xnp", [128, ND * TB], BF16)
    P1 = ctx.sb(name + "p1", [128, 32 * TB], BF16)
    P2 = ctx.sb(name + "p2", [128, 32 * TB], BF16)
    slab = ctx.sb(name + "slab", [128, 32 * 512], BF16)
    tmp1 = slab[:, 8 * 512:16 * 512]
    scol = ctx.sb(name + "scol", [128, 4 * 32], F32)
    s3b = ctx.sb(name + "s3b", [128, D], BF16)
    s3r = ctx.sb(name + "s3r", [1, D], F32)
    one_r = ctx.sb(name + "oner", [1, 128], F32)
    one_rb = ctx.sb(name + "onerb", [1, 128], BF16)

    def ybuf_ap(i, h):
        base = (i % 8) * 2 * D + h * D
        return P2[:, base:base + D].bitcast(F32)

    ld = DmaStream(ctx, "wld8", depth=8)
    sl = DmaStream(ctx, "l3sl", depth=1)
    tpx = DmaStream(ctx, "pre_tp", depth=8)
    mm, scv, vsv, yv, gps = name + "mm", name + "sc", name + "vs", name + "yv", name + "gp"
    mmb = name + "mmb"

    bounce_done = pre_ln_to_fm(ctx, name + "P", x_in, xn_bounce, None)

    @block.sync
    def _(sync):
        ld.issue(sync, scol[:, 0:32], s1_col[:, :])
        ld.issue(sync, scol[:, 32:64], s2_col[:, :])
        ld.issue(sync, scol[:, 64:96], hs1_col[:, :])
        ld.issue(sync, scol[:, 96:128], hs2_col[:, :])
        ld.issue(sync, s3r[0:1, :], s3_row[:, :])
    i_s1, i_s2, i_h1, i_h2, i_s3 = 0, 1, 2, 3, 4

    @block.gpsimd
    def _(g):
        ctx.inc(g.memset(one_r[:, :], 1.0), gps, 1)

    ps_pool = [ctx.ps([128, 512], F32) for _ in range(8)]

    @block.tensor
    def _(t):
        ld.wait(t, i_s3)
        t.wait_ge(ctx.sems[gps], 1)
        ctx.inc(t.matmul(ps_pool[0][:, :], one_r[0:1, :], s3r[0:1, 0:512],
                         start=True, stop=True), mmb, 1)
        ctx.inc(t.matmul(ps_pool[1][:, :], one_r[0:1, :], s3r[0:1, 512:1024],
                         start=True, stop=True), mmb, 1)

    @block.scalar
    def _(sc):
        sc.wait_ge(ctx.sems[mmb], ctx.val(mmb))
        sc.copy(out=s3b[:, 0:512], in_=ps_pool[0][:, :])
        ctx.inc(sc.copy(out=s3b[:, 512:1024], in_=ps_pool[1][:, :]), scv, 1)
    s3b_t = ctx.val(scv)

    pl = PostLN(ctx, name + "R", x_in, x_out, ybuf_ap, None, yv)
    b_mm = ctx.val(mm)
    b_scv = ctx.val(scv)
    b_vsv = ctx.val(vsv)
    b_yv = ctx.val(yv)

    def y_target(i):
        p, tl = divmod(i, 8)
        return b_yv + p * 16 + 8 + (tl + 1)
    pl.y_target = y_target

    GA, GB, GC = 128, 64, 16
    GP = GA + 2 * GB + GC

    def mm_t(p, idx):
        return b_mm + p * GP + idx + 1

    def scv_t(p, idx):
        return b_scv + p * 192 + idx + 1

    def sl_i(p, idx):
        return p * 34 + idx

    slab_consumed = {}

    @block.tensor
    def _(t):
        t.wait_ge(ctx.sems[scv], s3b_t)
        for p in range(NP):
            for j in range(ND):
                tpx.wait(t, p * ND + j)
            for mb in range(8):
                for w12 in range(2):
                    if w12 == 1:
                        t.wait_ge(ctx.sems[scv], scv_t(p, mb * 8 + 7))
                    elif mb >= 1:
                        t.wait_ge(ctx.sems[vsv], b_vsv + p * 64 + (mb - 1) * 8 + 8)
                    elif p >= 1:
                        t.wait_ge(ctx.sems[yv], b_yv + p * 16)
                    sl.wait(t, sl_i(p, mb * 2 + w12))
                    for grp in range(8):
                        m_l, tc = divmod(grp, 2)
                        pb = ps_pool[grp]
                        for k in range(ND):
                            ins = t.matmul(
                                pb[:, :],
                                slab[:, k * 512 + m_l * 128:k * 512 + (m_l + 1) * 128],
                                xn_p[:, k * TB + tc * 512:k * TB + (tc + 1) * 512],
                                start=(k == 0), stop=(k == ND - 1))
                        ctx.inc(ins, mm, 1)
                    slab_consumed[sl_i(p, mb * 2 + w12)] = ctx.val(mm)
            for stg in range(2):
                src_p = P1 if stg == 0 else P2
                for mb in range(8):
                    if stg == 0 and mb == 0:
                        t.wait_ge(ctx.sems[vsv], b_vsv + p * 64 + 64)
                    elif stg == 1 and mb == 0:
                        t.wait_ge(ctx.sems[scv], scv_t(p, 64 + 63))
                    else:
                        t.wait_ge(ctx.sems[scv], scv_t(p, 64 + stg * 64 + (mb - 1) * 8 + 7))
                    li = sl_i(p, 16 + stg * 8 + mb)
                    sl.wait(t, li)
                    for grp in range(8):
                        m_l, tc = divmod(grp, 2)
                        pb = ps_pool[grp]
                        for k in range(32):
                            ins = t.matmul(
                                pb[:, :],
                                slab[:, k * 512 + m_l * 128:k * 512 + (m_l + 1) * 128],
                                src_p[:, k * TB + tc * 512:k * TB + (tc + 1) * 512],
                                start=(k == 0), stop=(k == 31))
                        ctx.inc(ins, mm, 1)
                    slab_consumed[li] = ctx.val(mm)
            for e in range(2):
                li = sl_i(p, 32 + e)
                sl.wait(t, li)
                for tl in range(8):
                    pb = ps_pool[tl]
                    if e == 0:
                        if tl == 0:
                            t.wait_ge(ctx.sems[scv], scv_t(p, 191))
                    else:
                        t.wait_ge(ctx.sems[yv], b_yv + p * 16 + tl + 1)
                    for k in range(32):
                        ins = t.matmul(
                            pb[:, :],
                            P1[:, k * TB + tl * 128:k * TB + (tl + 1) * 128],
                            slab[:, k * 512:(k + 1) * 512],
                            start=(k == 0), stop=(k == 31))
                    ctx.inc(ins, mm, 1)
                slab_consumed[li] = ctx.val(mm)
    assert ctx.val(mm) == b_mm + NP * GP

    @block.scalar
    def _(sc):
        sc.wait_ge(ctx.sems[bounce_done[0]], bounce_done[1])
        ld.wait(sc, i_s1)
        ld.wait(sc, i_s2)
        ld.wait(sc, i_h1)
        ld.wait(sc, i_h2)
        for p in range(NP):
            if p >= 1:
                sc.wait_ge(ctx.sems[mm], mm_t(p - 1, GA - 1))
            for j in range(ND):
                tpx.issue(sc, xn_p[:, j * TB:(j + 1) * TB],
                          xn_bounce[p * TB:(p + 1) * TB, j * 128:(j + 1) * 128],
                          transpose=True)
            for mb in range(8):
                for grp in range(8):
                    sc.wait_ge(ctx.sems[mm], mm_t(p, mb * 16 + grp))
                    m_l, tc = divmod(grp, 2)
                    m_glob = mb * 4 + m_l
                    ctx.inc(sc.activation(
                        out=tmp1[:, grp * 512:(grp + 1) * 512],
                        in_=ps_pool[grp][:, :], func=AF.Silu,
                        scale=scol[:, m_glob:m_glob + 1]), scv, 1)
            for stg in range(2):
                dstP = P2 if stg == 0 else P1
                off = 64 if stg == 0 else 96
                for mb in range(8):
                    for grp in range(8):
                        if stg == 0 and mb == 0 and grp == 0 and p >= 1:
                            sc.wait_ge(ctx.sems[pl.vs], pl.apply_done(p * 8 - 1))
                        sc.wait_ge(ctx.sems[mm], mm_t(p, 128 + stg * 64 + mb * 8 + grp))
                        m_l, tc = divmod(grp, 2)
                        m_glob = mb * 4 + m_l
                        ctx.inc(sc.activation(
                            out=dstP[:, m_glob * TB + tc * 512:m_glob * TB + (tc + 1) * 512],
                            in_=ps_pool[grp][:, :], func=AF.Gelu,
                            scale=scol[:, off + m_glob:off + m_glob + 1]), scv, 1)
            for tl in range(8):
                pl.sc_tile(sc, p * 8 + tl)

    @block.vector
    def _(v):
        for p in range(NP):
            for mb in range(8):
                for grp in range(8):
                    v.wait_ge(ctx.sems[mm], mm_t(p, mb * 16 + 8 + grp))
                    v.wait_ge(ctx.sems[scv], scv_t(p, mb * 8 + grp))
                    if p >= 1 and mb == 0 and grp == 0:
                        v.wait_ge(ctx.sems[mm], mm_t(p - 1, GP - 1))
                    m_l, tc = divmod(grp, 2)
                    m_glob = mb * 4 + m_l
                    ctx.inc(v.scalar_tensor_tensor(
                        out=P1[:, m_glob * TB + tc * 512:m_glob * TB + (tc + 1) * 512],
                        in0=ps_pool[grp][:, :], scalar=scol[:, 32 + m_glob:33 + m_glob],
                        in1=tmp1[:, grp * 512:(grp + 1) * 512],
                        op0=ALU.mult, op1=ALU.mult), vsv, 1)
            for e in range(2):
                for tl in range(8):
                    i_glob = p * 8 + tl
                    v.wait_ge(ctx.sems[mm], mm_t(p, 256 + e * 8 + tl))
                    if i_glob >= 8 and e == 0:
                        v.wait_ge(ctx.sems[pl.vs], pl.apply_done(i_glob - 8))
                    if p == 0 and e == 0 and tl == 0:
                        v.wait_ge(ctx.sems[scv], s3b_t)
                    ctx.inc(v.scalar_tensor_tensor(
                        out=ybuf_ap(i_glob, e),
                        in0=ps_pool[tl][:, :], scalar=1.0,
                        in1=s3b[:, e * 512:(e + 1) * 512],
                        op0=ALU.bypass, op1=ALU.mult), yv, 1)
            for tl in range(8):
                pl.vec_tile(v, p * 8 + tl)

    @block.gpsimd
    def _(g):
        n = 0
        for p in range(NP):
            defs = []
            for mb in range(8):
                for w12 in range(2):
                    w = w1_T if w12 == 0 else w2_T
                    defs.append((w, ND, mb))
            for stg in range(2):
                for mb in range(8):
                    defs.append((h1_T if stg == 0 else h2_T, 32, mb))
            for e in range(2):
                defs.append((w3_T, 32, e))
            for li, (w, nk, mb) in enumerate(defs):
                if li == 16:
                    g.wait_ge(ctx.sems[vsv], b_vsv + p * 64 + 64)
                if n >= 1:
                    g.wait_ge(ctx.sems[mm], slab_consumed[n - 1])
                sl.issue(g, slab.rearrange("q (k c) -> q k c", c=512)[:, 0:nk, :],
                         w.rearrange("(k q) o -> q k o", q=128)[:, :, mb * 512:(mb + 1) * 512])
                n += 1
            for tl in range(8):
                pl.st_tile(g, p * 8 + tl)

    pl.emit_sync_loads(first_wait=bounce_done)
    return pl.finish()



def emit_l2_ffn(ctx, x_in, x_out, w1_d, w2_d, wr_col, b1_col, b2_row, xn_bounce, stop=None):
    nc, block = ctx.nc, ctx.block
    name = "l2f"
    TB = 1024
    NP = T // TB
    NITER = 80
    CGELU = 0.044715
    SGELU = 0.7978845608028654

    xn_p = ctx.sb(name + "xnp", [128, ND * TB], BF16)
    P1 = ctx.sb(name + "p1", [128, 32 * TB], BF16)
    slab = ctx.sb(name + "slab", [128, 32 * 512], BF16)
    zbuf = ctx.sb(name + "zb", [128, 4 * 512], F32)
    tbuf = ctx.sb(name + "tb", [128, 4 * 512], F32)
    ybuf = ctx.sb(name + "yb", [128, 8 * D], F32)
    xtok = ctx.sb(name + "xtok", [128, 8 * D], BF16)
    wrc = ctx.sb(name + "wrc", [128, 8], BF16)
    b1c = ctx.sb(name + "b1c", [128, 32], F32)
    b2r = ctx.sb(name + "b2r", [1, D], F32)
    s32 = ctx.sb(name + "s32", [32, 128], F32)
    stil = ctx.sb(name + "stil", [128, 32], F32)
    ge = ctx.sb(name + "ge", [128, 32], F32)
    gate = ctx.sb(name + "gate", [128, 32], F32)
    bis = ctx.sb(name + "bis", [128, 8], F32)
    srow = ctx.sb(name + "srow", [1, 512], F32)
    ones_m = ctx.sb(name + "onesm", [128, 128], F32)
    idm = ctx.sb(name + "idm", [128, 128], F32)
    one_r = ctx.sb(name + "oner", [1, 128], F32)
    one_rb = ctx.sb(name + "onerb", [1, 128], BF16)
    sc_d = nc.dram_tensor(name + "_scores", [32, 128], F32, kind="Internal").ap()

    ld = DmaStream(ctx, "wld8", depth=8)
    sl = DmaStream(ctx, "l3sl", depth=1)
    tpx = DmaStream(ctx, "pre_tp", depth=8)
    msc = DmaStream(ctx, "msc")
    sst = DmaStream(ctx, "l2ss", depth=8)
    mm, scv, vsv, yv, gps = name + "mm", name + "sc", name + "vs", name + "yv", name + "gp"

    bounce_done = pre_ln_to_fm(ctx, name + "P", x_in, xn_bounce, None)

    @block.sync
    def _(sync):
        ld.issue(sync, b1c[:, :], b1_col[:, :])
        ld.issue(sync, b2r[0:1, :], b2_row[:, :])
    i_b1, i_b2 = 0, 1
    wrl = DmaStream(ctx, "l2wr")

    @block.gpsimd
    def _(g):
        wrl.issue(g, wrc[:, :], wr_col[:, :])
    i_wr = 0

    @block.gpsimd
    def _(g):
        ctx.inc(g.memset(ones_m[:, :], 1.0), gps, 1)
        g.memset(one_r[:, :], 1.0)
        g.wait_ge(ctx.sems[gps], ctx.val(gps))
        ctx.inc(g.affine_select(idm[:, :], ones_m[:, :], pattern=[[1, 128]],
                                compare_op=ALU.is_equal, fill=0.0, base=0,
                                channel_multiplier=-1), gps, 1)

    b_mm0 = ctx.val(mm)

    ps_s = ctx.ps([1, 512], F32)
    ps_t = ctx.ps([128, 32], F32)
    ps_c = ctx.ps([128, 1], F32)
    b_mm, b_scv, b_vsv = ctx.val(mm), ctx.val(scv), ctx.val(vsv)

    @block.tensor
    def _(t):
        wrl.wait(t, i_wr)
        for p in range(NP):
            for j in range(ND):
                tpx.wait(t, p * ND + j)
            for c in range(2):
                if p + c > 0:
                    t.wait_ge(ctx.sems[scv], b_scv + p * 2 + c)
                for j in range(ND):
                    ins = t.matmul(ps_s[0:1, :], wrc[:, j:j + 1],
                                   xn_p[:, j * TB + c * 512:j * TB + (c + 1) * 512],
                                   start=(j == 0), stop=(j == ND - 1))
                ctx.inc(ins, mm, 1)
    sc_mm_t = ctx.val(mm)

    @block.scalar
    def _(sc):
        sc.wait_ge(ctx.sems[bounce_done[0]], bounce_done[1])
        for p in range(NP):
            if p >= 1:
                sc.wait_ge(ctx.sems[mm], b_mm + 2 * p)
            for j in range(ND):
                tpx.issue(sc, xn_p[:, j * TB:(j + 1) * TB],
                          xn_bounce[p * TB:(p + 1) * TB, j * 128:(j + 1) * 128],
                          transpose=True)
            for c in range(2):
                n = p * 2 + c
                sc.wait_ge(ctx.sems[mm], an(b_mm, n, 1))
                if n >= 1:
                    sst.wait(sc, n - 1)
                ctx.inc(sc.copy(out=srow[0:1, :], in_=ps_s[0:1, :]), scv, 1)
    b_scst = ctx.val(scv)

    @block.gpsimd
    def _(g):
        for p in range(NP):
            for c in range(2):
                g.wait_ge(ctx.sems[scv], an(b_scv, p * 2 + c, 1))
                sst.issue(g, sc_d[(p * 2 + c) * 4:(p * 2 + c) * 4 + 4, :], srow[0:1, :])

    @block.sync
    def _(sync):
        for k in range(8):
            sst.wait(sync, k)
        msc.issue(sync, s32[:, :], sc_d[:, :])
    i_s32 = 0

    mmt = name + "mmt"

    @block.tensor
    def _(t):
        msc.wait(t, i_s32)
        t.wait_ge(ctx.sems[gps], 2)
        ctx.inc(t.transpose(ps_t[:, :], s32[:, :], idm[0:32, 0:32]), mmt, 1)

    @block.scalar
    def _(sc):
        sc.wait_ge(ctx.sems[mmt], ctx.val(mmt))
        ctx.inc(sc.copy(out=stil[:, :], in_=ps_t[:, :]), scv, 1)
    stil_t = ctx.val(scv)

    b2_mm = ctx.val(mm)
    b2_scv = ctx.val(scv)
    b2_vsv = ctx.val(vsv)
    KB = 12

    @block.vector
    def _(v):
        v.wait_ge(ctx.sems[scv], stil_t)
        v.memset(bis[:, 0:1], -16.0)
        ctx.inc(v.memset(bis[:, 1:2], 16.0), vsv, 1)
        for it in range(NITER):
            v.wait_ge(ctx.sems[vsv], ctx.val(vsv))
            if it >= 1:
                v.wait_ge(ctx.sems[scv], b2_scv + it)
            ctx.inc(v.tensor_tensor(out=bis[:, 2:3], in0=bis[:, 0:1], in1=bis[:, 1:2],
                                    op=ALU.add), vsv, 1)
            v.wait_ge(ctx.sems[vsv], ctx.val(vsv))
            ctx.inc(v.tensor_scalar(out=bis[:, 2:3], in0=bis[:, 2:3], scalar1=0.5,
                                    scalar2=None, op0=ALU.mult), vsv, 1)
            v.wait_ge(ctx.sems[vsv], ctx.val(vsv))
            ctx.inc(v.tensor_scalar(out=ge[:, :], in0=stil[:, :], scalar1=bis[:, 2:3],
                                    scalar2=None, op0=ALU.is_ge), vsv, 1)
            v.wait_ge(ctx.sems[vsv], ctx.val(vsv))
            ctx.inc(v.reduce_sum(bis[:, 4:5], ge[:, :], axis=AX.X), vsv, 1)
            v.wait_ge(ctx.sems[scv], b2_scv + it + 1)
            ctx.inc(v.tensor_scalar(out=bis[:, 3:4], in0=bis[:, 5:6], scalar1=2048.0,
                                    scalar2=None, op0=ALU.is_ge), vsv, 1)
            v.wait_ge(ctx.sems[vsv], ctx.val(vsv))
            ctx.inc(v.tensor_tensor(out=bis[:, 6:7], in0=bis[:, 2:3], in1=bis[:, 0:1],
                                    op=ALU.subtract), vsv, 1)
            v.wait_ge(ctx.sems[vsv], ctx.val(vsv))
            ctx.inc(v.tensor_tensor(out=bis[:, 6:7], in0=bis[:, 6:7], in1=bis[:, 3:4],
                                    op=ALU.mult), vsv, 1)
            v.wait_ge(ctx.sems[vsv], ctx.val(vsv))
            ctx.inc(v.tensor_tensor(out=bis[:, 0:1], in0=bis[:, 0:1], in1=bis[:, 6:7],
                                    op=ALU.add), vsv, 1)
            ctx.inc(v.tensor_scalar(out=bis[:, 7:8], in0=bis[:, 3:4], scalar1=-1.0,
                                    scalar2=1.0, op0=ALU.mult, op1=ALU.add), vsv, 1)
            v.wait_ge(ctx.sems[vsv], ctx.val(vsv))
            ctx.inc(v.tensor_tensor(out=bis[:, 6:7], in0=bis[:, 2:3], in1=bis[:, 1:2],
                                    op=ALU.subtract), vsv, 1)
            v.wait_ge(ctx.sems[vsv], ctx.val(vsv))
            ctx.inc(v.tensor_tensor(out=bis[:, 6:7], in0=bis[:, 6:7], in1=bis[:, 7:8],
                                    op=ALU.mult), vsv, 1)
            v.wait_ge(ctx.sems[vsv], ctx.val(vsv))
            ctx.inc(v.tensor_tensor(out=bis[:, 1:2], in0=bis[:, 1:2], in1=bis[:, 6:7],
                                    op=ALU.add), vsv, 1)
        v.wait_ge(ctx.sems[vsv], ctx.val(vsv))
        ctx.inc(v.tensor_scalar(out=ge[:, :], in0=stil[:, :], scalar1=bis[:, 0:1],
                                scalar2=None, op0=ALU.is_ge), vsv, 1)
        v.wait_ge(ctx.sems[scv], b2_scv + NITER + 1)
        v.wait_ge(ctx.sems[vsv], ctx.val(vsv))
        ctx.inc(v.tensor_tensor(out=gate[:, :], in0=gate[:, :], in1=ge[:, :],
                                op=ALU.mult), vsv, 1)
    gate_t = ctx.val(vsv)
    assert gate_t == b2_vsv + 1 + NITER * KB + 2

    @block.tensor
    def _(t):
        for it in range(NITER):
            t.wait_ge(ctx.sems[vsv], b2_vsv + 1 + it * KB + 4)
            if it >= 1:
                t.wait_ge(ctx.sems[scv], b2_scv + it)
            ctx.inc(t.matmul(ps_c[:, :], ones_m[:, :], bis[:, 4:5], start=True, stop=True),
                    mm, 1)
    b3_mm = ctx.val(mm) - NITER

    @block.scalar
    def _(sc):
        for it in range(NITER):
            sc.wait_ge(ctx.sems[mm], b3_mm + it + 1)
            ctx.inc(sc.copy(out=bis[:, 5:6], in_=ps_c[:, :]), scv, 1)
        sc.wait_ge(ctx.sems[scv], stil_t)
        ctx.inc(sc.activation(out=gate[:, :], in_=stil[:, :], func=AF.Sigmoid), scv, 1)
    assert ctx.val(scv) == b2_scv + NITER + 1

    if stop == "p2":
        dmp = DmaStream(ctx, "l2dmp")

        @block.gpsimd
        def _(g):
            g.wait_ge(ctx.sems[vsv], gate_t)
            dmp.issue(g, x_out[0:128, 0:32], gate[:, :])
            dmp.issue(g, x_out[128:256, 0:32], stil[:, :])
            dmp.issue(g, x_out[256:384, 0:8], bis[:, :])

        fin2 = name + "f2"

        @block.gpsimd
        def _(g):
            dmp.wait(g, 0)
            dmp.wait(g, 1)
            dmp.wait(g, 2)
            ctx.inc(g.memset(srow[0:1, 0:1], 0.0), fin2, 1)
        return fin2, ctx.val(fin2)

    ps_pool = [ctx.ps([128, 512], F32) for _ in range(4)]
    pl = PostLN(ctx, name + "R", x_in, x_out,
                lambda i, h: ybuf[:, (i % 8) * D + h * 512:(i % 8) * D + (h + 1) * 512],
                None, yv)
    b4_mm = ctx.val(mm)
    b4_scv = ctx.val(scv)
    b4_vsv = ctx.val(vsv)
    b_yv = ctx.val(yv)

    def y_target(i):
        p, tl = divmod(i, 8)
        return b_yv + p * 16 + 8 + (tl + 1)
    pl.y_target = y_target

    GA, GC = 64, 16
    GP = (GA + GC) if stop != "A" else GA

    def mm4_t(p, idx):
        return b4_mm + p * GP + idx + 1

    SLP = 10 if stop != "A" else 8

    def sl_i(p, idx):
        return p * SLP + idx

    slab_consumed = {}

    def scvA_t(p, mb, grp, k):
        return b4_scv + p * 192 + (mb * 8 + grp) * 3 + k + 1

    def vsvA_t(p, mb, grp, k):
        return b4_vsv + p * 256 + (mb * 8 + grp) * 4 + k + 1

    @block.tensor
    def _(t):
        ld.wait(t, i_b1)
        ld.wait(t, i_b2)
        t.wait_ge(ctx.sems[vsv], gate_t)
        for p in range(NP):
            for j in range(ND):
                tpx.wait(t, 32 + p * ND + j)
            for mb in range(8):
                sl.wait(t, sl_i(p, mb))
                for grp in range(8):
                    m_l, tc = divmod(grp, 2)
                    pb = ps_pool[grp % 4]
                    if mb * 8 + grp >= 4:
                        pi = mb * 8 + grp - 4
                        t.wait_ge(ctx.sems[scv], scvA_t(p, pi // 8, pi % 8, 0))
                    elif p >= 1:
                        if stop == "A":
                            t.wait_ge(ctx.sems[scv], scvA_t(p - 1, 7, 4 + grp, 0))
                        else:
                            t.wait_ge(ctx.sems[yv], b_yv + p * 16)
                    for k in range(ND):
                        ins = t.matmul(
                            pb[:, :],
                            slab[:, k * 512 + m_l * 128:k * 512 + (m_l + 1) * 128],
                            xn_p[:, k * TB + tc * 512:k * TB + (tc + 1) * 512],
                            start=(k == 0), stop=(k == ND - 1))
                    ctx.inc(ins, mm, 1)
                slab_consumed[sl_i(p, mb)] = ctx.val(mm)
            if stop == "A":
                continue
            for e in range(2):
                li = sl_i(p, 8 + e)
                sl.wait(t, li)
                for tl in range(8):
                    pb = ps_pool[tl % 4]
                    gi = e * 8 + tl
                    if gi >= 4:
                        t.wait_ge(ctx.sems[yv], b_yv + p * 16 + gi - 3)
                    else:
                        pi = 60 + gi
                        t.wait_ge(ctx.sems[scv], scvA_t(p, pi // 8, pi % 8, 0))
                    t.wait_ge(ctx.sems[vsv], b4_vsv + p * 256 + 256)
                    for k in range(32):
                        t.matmul(pb[:, :],
                                 P1[:, k * TB + tl * 128:k * TB + (tl + 1) * 128],
                                 slab[:, k * 512:(k + 1) * 512],
                                 start=(k == 0), stop=False)
                    ctx.inc(t.matmul(pb[:, :], one_r[0:1, :],
                                     b2r[0:1, e * 512:(e + 1) * 512],
                                     start=False, stop=True), mm, 1)
                slab_consumed[li] = ctx.val(mm)
    if stop != "A":
        assert ctx.val(mm) == b4_mm + NP * GP

    @block.scalar
    def _(sc):
        ctx.wcp(sc)
        ld.wait(sc, i_b1)
        for p in range(NP):
            if p == 0:
                sc.wait_ge(ctx.sems[mm], sc_mm_t)
            else:
                sc.wait_ge(ctx.sems[mm], mm4_t(p - 1, GA - 1))
            for j in range(ND):
                tpx.issue(sc, xn_p[:, j * TB:(j + 1) * TB],
                          xn_bounce[p * TB:(p + 1) * TB, j * 128:(j + 1) * 128],
                          transpose=True)
            for mb in range(8):
                for grp in range(8):
                    ti = mb * 8 + grp
                    sc.wait_ge(ctx.sems[mm], mm4_t(p, ti))
                    zs = zbuf[:, (ti % 4) * 512:(ti % 4 + 1) * 512]
                    ts_ = tbuf[:, (ti % 4) * 512:(ti % 4 + 1) * 512]
                    m_l, tc = divmod(grp, 2)
                    m_glob = mb * 4 + m_l
                    if ti >= 4:
                        pi = ti - 4
                        sc.wait_ge(ctx.sems[vsv], vsvA_t(p, pi // 8, pi % 8, 3))
                    ctx.inc(sc.activation(out=zs, in_=ps_pool[grp % 4][:, :], func=AF.Identity,
                                          bias=b1c[:, m_glob:m_glob + 1]), scv, 1)
                    sc.wait_ge(ctx.sems[scv], ctx.val(scv))
                    ctx.inc(sc.activation(out=ts_, in_=zs, func=AF.Square), scv, 1)
                    sc.wait_ge(ctx.sems[vsv], vsvA_t(p, mb, grp, 1))
                    ctx.inc(sc.activation(out=ts_, in_=ts_, func=AF.Tanh, scale=SGELU), scv, 1)
            if stop not in ("A", "B"):
                for tl in range(8):
                    pl.sc_tile(sc, p * 8 + tl)

    xls = DmaStream(ctx, "l2xl", depth=8)

    @block.vector
    def _(v):
        for p in range(NP):
            for mb in range(8):
                for grp in range(8):
                    ti = mb * 8 + grp
                    zs = zbuf[:, (ti % 4) * 512:(ti % 4 + 1) * 512]
                    ts_ = tbuf[:, (ti % 4) * 512:(ti % 4 + 1) * 512]
                    m_l, tc = divmod(grp, 2)
                    m_glob = mb * 4 + m_l
                    v.wait_ge(ctx.sems[scv], scvA_t(p, mb, grp, 1))
                    ctx.inc(v.tensor_scalar(out=ts_, in0=ts_, scalar1=CGELU, scalar2=1.0,
                                            op0=ALU.mult, op1=ALU.add), vsv, 1)
                    v.wait_ge(ctx.sems[vsv], ctx.val(vsv))
                    ctx.inc(v.tensor_tensor(out=ts_, in0=ts_, in1=zs, op=ALU.mult), vsv, 1)
                    v.wait_ge(ctx.sems[scv], scvA_t(p, mb, grp, 2))
                    ctx.inc(v.tensor_scalar(out=ts_, in0=ts_, scalar1=0.5, scalar2=0.5,
                                            op0=ALU.mult, op1=ALU.add), vsv, 1)
                    v.wait_ge(ctx.sems[vsv], ctx.val(vsv))
                    ctx.inc(v.tensor_tensor(
                        out=P1[:, m_glob * TB + tc * 512:m_glob * TB + (tc + 1) * 512],
                        in0=ts_, in1=zs, op=ALU.mult), vsv, 1)
            if stop == "A":
                continue
            for e in range(2):
                for tl in range(8):
                    i_glob = p * 8 + tl
                    v.wait_ge(ctx.sems[mm], mm4_t(p, GA + e * 8 + tl))
                    xls.wait(v, i_glob)
                    if i_glob >= 8 and e == 0 and stop != "B":
                        v.wait_ge(ctx.sems[pl.vs], pl.apply_done(i_glob - 8))
                    ctx.inc(v.scalar_tensor_tensor(
                        out=ybuf[:, (i_glob % 8) * D + e * 512:(i_glob % 8) * D + (e + 1) * 512],
                        in0=ps_pool[tl % 4][:, :], scalar=gate[:, (i_glob % 32):(i_glob % 32) + 1],
                        in1=xtok[:, (i_glob % 8) * D + e * 512:(i_glob % 8) * D + (e + 1) * 512],
                        op0=ALU.mult, op1=ALU.add), yv, 1)
            if stop not in ("A", "B"):
                for tl in range(8):
                    pl.vec_tile(v, p * 8 + tl)

    @block.sync
    def _(sync):
        if stop == "A":
            pass
        else:
            sync.wait_ge(ctx.sems[bounce_done[0]], bounce_done[1])
            for p in range(NP):
                for tl in range(8):
                    i = p * 8 + tl
                    if i >= 8:
                        sync.wait_ge(ctx.sems[yv], b_yv + ((i - 8) // 8) * 16 + 8 + ((i - 8) % 8) + 1)
                    xls.issue(sync, xtok[:, (i % 8) * D:(i % 8 + 1) * D],
                              xn_bounce[i * 128:(i + 1) * 128, :])
                if stop != "B":
                    for tl in range(8):
                        pl.ld_tile(sync, p * 8 + tl)

    @block.gpsimd
    def _(g):
        last_key = None
        for p in range(NP):
            for mb in range(8):
                if last_key is not None:
                    g.wait_ge(ctx.sems[mm], slab_consumed[last_key])
                sl.issue(g, slab.rearrange("q (k c) -> q k c", c=512)[:, 0:ND, :],
                         w1_d.rearrange("(k q) o -> q k o", q=128)[:, :, mb * 512:(mb + 1) * 512])
                last_key = sl_i(p, mb)
            if stop == "A":
                continue
            for e in range(2):
                g.wait_ge(ctx.sems[mm], slab_consumed[last_key])
                sl.issue(g, slab.rearrange("q (k c) -> q k c", c=512)[:, :, :],
                         w2_d.rearrange("(k q) o -> q k o", q=128)[:, :, e * 512:(e + 1) * 512])
                last_key = sl_i(p, 8 + e)
            if stop not in ("A", "B"):
                for tl in range(8):
                    pl.st_tile(g, p * 8 + tl)

    if stop == "B":
        dmpB = DmaStream(ctx, "l2dmB")
        finB = name + "fB"

        @block.gpsimd
        def _(g):
            g.wait_ge(ctx.sems[yv], b_yv + NP * 16)
            dmpB.issue(g, x_out[0:128, :], ybuf[:, 0:1024])
            dmpB.wait(g, 0)
            ctx.inc(g.memset(srow[0:1, 0:1], 0.0), finB, 1)
        return finB, ctx.val(finB)

    if stop == "A":
        dmp2 = DmaStream(ctx, "l2dm2")
        finA = name + "fA"

        @block.gpsimd
        def _(g):
            g.wait_ge(ctx.sems[vsv], b4_vsv + NP * 256)
            dmp2.issue(g, x_out[0:128, :], P1.bitcast(F32)[:, 0:1024])
            dmp2.wait(g, 0)
            ctx.inc(g.memset(srow[0:1, 0:1], 0.0), finA, 1)
        return finA, ctx.val(finA)

    return pl.finish()



def emit_l1_ffn(ctx, x_in, x_out, wr_T_d, ws0_T, ws1_T, cen_T, ebias_row, br_d, bs_row,
                xn_bounce):
    nc, block = ctx.nc, ctx.block
    name = "l1f"
    W = 512
    NW = T // W

    xn_fm = ctx.sb(name + "xnfm", [128, ND * T], BF16)
    wsum = ctx.sb(name + "wsum", [128, ND * D], BF16)
    slab = ctx.sb(name + "slab", [128, ND * D], BF16)
    gu = ctx.sb(name + "gu", [128, 2 * ND * W], BF16)
    gb16 = ctx.sb(name + "gb", [128, 16 * W], BF16)
    ybuf = ctx.sb(name + "yb", [128, 4 * D], F32)
    xtok = ctx.sb(name + "xtok", [128, 4 * D], BF16)
    cenc = ctx.sb(name + "cen", [128, ND * 16], BF16)
    sfm = ctx.sb(name + "sfm", [16, 2 * 512], F32)
    stok = ctx.sb(name + "stok", [128, 32 * 16], F32)
    g_tok = ctx.sb(name + "gtok", [128, 32 * 16], F32)
    g_fm = ctx.sb(name + "gfm", [16, T], BF16)
    brs = ctx.sb(name + "brs", [16, D], BF16)
    bsr = ctx.sb(name + "bsr", [1, 2 * D], F32)
    brow = ctx.sb(name + "brow", [1, D], F32)
    ebias = ctx.sb(name + "eb", [1, 16], F32)
    ebias_b = ctx.sb(name + "ebb", [128, 16], F32)
    one_r = ctx.sb(name + "oner", [1, 128], F32)
    one_rb = ctx.sb(name + "onerb", [1, 128], BF16)
    idm = ctx.sb(name + "idm", [128, 128], F32)
    ones_m = ctx.sb(name + "onesm", [128, 128], F32)
    red = ctx.sb(name + "red", [128, 8], F32)
    wtmp = None
    wr_bf = nc.dram_tensor(name + "_wrbf", [16 * D, D], BF16, kind="Internal").ap()
    g_dram = nc.dram_tensor(name + "_gd", [16, T], BF16, kind="Internal").ap()
    grow = ctx.sb(name + "grow", [1, 2 * 512], BF16)

    wtmp = gu[:, 0:4 * D].bitcast(F32)
    ld = DmaStream(ctx, "wld8", depth=8)
    sl = DmaStream(ctx, "l3sl", depth=1)
    pc = DmaStream(ctx, "l1pc", depth=1)
    wsl = DmaStream(ctx, "l1ws", depth=2)
    xls = DmaStream(ctx, "l2xl", depth=8)
    mm, scv, vsv, yv, gps = name + "mm", name + "sc", name + "vs", name + "yv", name + "gp"

    fm_ready = pre_ln_to_fm(ctx, name + "P", x_in, xn_bounce, xn_fm)

    @block.sync
    def _(sync):
        ld.issue(sync, bsr[0:1, :], bs_row[:, :])
        ld.issue(sync, ebias[0:1, :], ebias_row[:, :])
    i_bs, i_eb = 0, 1
    brl = DmaStream(ctx, "l1br", depth=1)

    @block.gpsimd
    def _(g):
        brl.issue(g, brs[:, :], br_d[:, :])
    i_br = 0

    cenl = DmaStream(ctx, "l1ce", depth=1)

    @block.gpsimd
    def _(g):
        g.memset(one_r[:, :], 1.0)
        g.memset(one_rb[:, :], 1.0)
        ctx.inc(g.memset(ones_m[:, :], 1.0), gps, 1)
        g.wait_ge(ctx.sems[gps], ctx.val(gps))
        ctx.inc(g.affine_select(idm[:, :], ones_m[:, :], pattern=[[1, 128]],
                                compare_op=ALU.is_equal, fill=0.0, base=0,
                                channel_multiplier=-1), gps, 1)
        cenl.issue(g, cenc.rearrange("q (j e) -> q j e", e=16),
                   cen_T.rearrange("(j q) e -> q j e", q=128))
        for e in range(16):
            if e >= 1:
                pc.wait(g, 2 * e - 1)
            pc.issue(g, slab.rearrange("q (j o) -> q j o", o=D),
                     wr_T_d[e].rearrange("(j q) o -> q j o", q=128))
            pc.wait(g, 2 * e)
            pc.issue(g, wr_bf.rearrange("(e j q) o -> e q j o", e=16, q=128)[e],
                     slab.rearrange("q (j o) -> q j o", o=D))
    precast_done = 32

    b_vsw = ctx.val(vsv)

    @block.sync
    def _(sync):
        for j in range(ND):
            if j >= 1:
                sync.wait_ge(ctx.sems[vsv], b_vsw + 2 * j)
            wsl.issue(sync, wtmp[:, 0:D], ws0_T[j * 128:(j + 1) * 128, :])
            wsl.issue(sync, wtmp[:, D:2 * D], ws1_T[j * 128:(j + 1) * 128, :])

    @block.vector
    def _(v):
        for j in range(ND):
            wsl.wait(v, 2 * j)
            wsl.wait(v, 2 * j + 1)
            ctx.inc(v.tensor_tensor(out=wtmp[:, 0:D], in0=wtmp[:, 0:D], in1=wtmp[:, D:2 * D],
                                    op=ALU.add), vsv, 1)
            v.wait_ge(ctx.sems[vsv], ctx.val(vsv))
            ctx.inc(v.tensor_scalar(out=wsum[:, j * D:(j + 1) * D], in0=wtmp[:, 0:D],
                                    scalar1=0.5, scalar2=None, op0=ALU.mult), vsv, 1)
    wsum_t = ctx.val(vsv)

    b_vsb = ctx.val(vsv)

    @block.vector
    def _(v):
        ld.wait(v, i_bs)
        ctx.inc(v.tensor_tensor(out=brow[0:1, :], in0=bsr[0:1, 0:D], in1=bsr[0:1, D:2 * D],
                                op=ALU.add), vsv, 1)
        v.wait_ge(ctx.sems[vsv], ctx.val(vsv))
        ctx.inc(v.tensor_scalar(out=brow[0:1, :], in0=brow[0:1, :], scalar1=0.5,
                                scalar2=None, op0=ALU.mult), vsv, 1)
    brow_t = ctx.val(vsv)

    ps_pool = [ctx.ps([128, 512], F32) for _ in range(8)]
    b_mm = ctx.val(mm)
    mmb = name + "mmb"
    sfmE, stokE = name + "sfE", name + "stE"
    b_sfmE, b_stokE, b_mmb = ctx.val(sfmE), ctx.val(stokE), ctx.val(mmb)

    @block.tensor
    def _(t):
        t.wait_ge(ctx.sems[fm_ready[0]], fm_ready[1])
        cenl.wait(t, 0)
        ld.wait(t, i_eb)
        t.wait_ge(ctx.sems[gps], 2)
        ctx.inc(t.matmul(ps_pool[2][:, 0:16], one_r[0:1, :], ebias[0:1, :],
                         start=True, stop=True), mmb, 1)
        for c in range(8):
            if c >= 2:
                t.wait_ge(ctx.sems[sfmE], an(b_sfmE, c - 2, 1))
            for j in range(ND):
                ins = t.matmul(ps_pool[c % 2][0:16, :], cenc[:, j * 16:(j + 1) * 16],
                               xn_fm[:, j * T + c * 512:j * T + (c + 1) * 512],
                               start=(j == 0), stop=(j == ND - 1))
            ctx.inc(ins, mm, 1)
            if c >= 1:
                cc = c - 1
                t.wait_ge(ctx.sems[sfmE], an(b_sfmE, cc, 1))
                for j in range(4):
                    i = cc * 4 + j
                    if i >= 4:
                        t.wait_ge(ctx.sems[stokE], an(b_stokE, i - 4, 1))
                    ctx.inc(t.transpose(ps_pool[3 + i % 4][0:128, 0:16],
                                        sfm[:, (cc % 2) * 512 + j * 128:(cc % 2) * 512 + (j + 1) * 128],
                                        idm[0:16, 0:16]), mmb, 1)
        cc = 7
        t.wait_ge(ctx.sems[sfmE], an(b_sfmE, cc, 1))
        for j in range(4):
            i = cc * 4 + j
            t.wait_ge(ctx.sems[stokE], an(b_stokE, i - 4, 1))
            ctx.inc(t.transpose(ps_pool[3 + i % 4][0:128, 0:16],
                                sfm[:, (cc % 2) * 512 + j * 128:(cc % 2) * 512 + (j + 1) * 128],
                                idm[0:16, 0:16]), mmb, 1)
    sc_mm = ctx.val(mm)

    @block.scalar
    def _(sc):
        sc.wait_ge(ctx.sems[mmb], b_mmb + 1)
        ctx.inc(sc.copy(out=ebias_b[:, :], in_=ps_pool[2][:, 0:16]), scv, 1)
        for c in range(8):
            sc.wait_ge(ctx.sems[mm], an(b_mm, c, 1))
            if c >= 2:
                sc.wait_ge(ctx.sems[mmb], b_mmb + 1 + 4 * (c - 1))
            ctx.inc(sc.copy(out=sfm[:, (c % 2) * 512:(c % 2 + 1) * 512],
                            in_=ps_pool[c % 2][0:16, :]), sfmE, 1)
            if c >= 1:
                for j in range(4):
                    i = (c - 1) * 4 + j
                    sc.wait_ge(ctx.sems[mmb], b_mmb + 1 + i + 1)
                    ctx.inc(sc.copy(out=stok[:, i * 16:(i + 1) * 16],
                                    in_=ps_pool[3 + i % 4][0:128, 0:16]), stokE, 1)
        for j in range(4):
            i = 28 + j
            sc.wait_ge(ctx.sems[mmb], b_mmb + 1 + i + 1)
            ctx.inc(sc.copy(out=stok[:, i * 16:(i + 1) * 16],
                            in_=ps_pool[3 + i % 4][0:128, 0:16]), stokE, 1)
    stok_t = ctx.val(stokE)

    b_vsg = ctx.val(vsv)
    b_scg = ctx.val(scv)
    KG = 11

    @block.vector
    def _(v):
        v.wait_ge(ctx.sems[stokE], stok_t)
        for i in range(NT):
            sl_ = stok[:, i * 16:(i + 1) * 16]
            gl = g_tok[:, i * 16:(i + 1) * 16]
            ctx.inc(v.tensor_tensor(out=sl_, in0=sl_, in1=ebias_b[:, :], op=ALU.add), vsv, 1)
            v.wait_ge(ctx.sems[vsv], ctx.val(vsv))
            ctx.inc(v.reduce_max(red[:, 0:1], sl_, axis=AX.X, negate=True), vsv, 1)
            v.wait_ge(ctx.sems[vsv], ctx.val(vsv))
            v.wait_ge(ctx.sems[scv], an(b_scg, i, 1))
            ctx.inc(v.reduce_sum(red[:, 1:2], gl, axis=AX.X), vsv, 1)
            v.wait_ge(ctx.sems[vsv], ctx.val(vsv))
            ctx.inc(v.reciprocal(red[:, 2:3], red[:, 1:2]), vsv, 1)
            v.wait_ge(ctx.sems[vsv], ctx.val(vsv))
            ctx.inc(v.tensor_scalar(out=gl, in0=gl, scalar1=red[:, 2:3], scalar2=None,
                                    op0=ALU.mult), vsv, 1)
            v.wait_ge(ctx.sems[vsv], ctx.val(vsv))
            ctx.inc(v.reduce_max(red[:, 3:4], gl, axis=AX.X), vsv, 1)
            v.wait_ge(ctx.sems[vsv], ctx.val(vsv))
            ctx.inc(v.tensor_scalar(out=sl_, in0=gl, scalar1=red[:, 3:4], scalar2=None,
                                    op0=ALU.is_lt), vsv, 1)
            v.wait_ge(ctx.sems[vsv], ctx.val(vsv))
            ctx.inc(v.tensor_tensor(out=sl_, in0=gl, in1=sl_, op=ALU.mult), vsv, 1)
            v.wait_ge(ctx.sems[vsv], ctx.val(vsv))
            ctx.inc(v.reduce_max(red[:, 4:5], sl_, axis=AX.X), vsv, 1)
            v.wait_ge(ctx.sems[vsv], ctx.val(vsv))
            ctx.inc(v.tensor_scalar(out=sl_, in0=gl, scalar1=red[:, 4:5], scalar2=None,
                                    op0=ALU.is_ge), vsv, 1)
            v.wait_ge(ctx.sems[vsv], ctx.val(vsv))
            ctx.inc(v.tensor_tensor(out=gl, in0=gl, in1=sl_, op=ALU.mult), vsv, 1)
    gate_t = ctx.val(vsv)
    assert gate_t == b_vsg + NT * KG

    @block.scalar
    def _(sc):
        for i in range(NT):
            sc.wait_ge(ctx.sems[vsv], b_vsg + i * KG + 2)
            ctx.inc(sc.activation(out=g_tok[:, i * 16:(i + 1) * 16],
                                  in_=stok[:, i * 16:(i + 1) * 16], func=AF.Exp,
                                  bias=red[:, 0:1]), scv, 1)

    mmg = name + "mmg"
    b_scf = ctx.val(scv)

    @block.tensor
    def _(t):
        t.wait_ge(ctx.sems[vsv], gate_t)
        for i in range(NT):
            if i >= 4:
                t.wait_ge(ctx.sems[scv], an(b_scf, i - 4, 1))
            ctx.inc(t.transpose(ps_pool[3 + i % 4][0:16, 0:128],
                                g_tok[:, i * 16:(i + 1) * 16], idm[:, :]), mmg, 1)

    @block.scalar
    def _(sc):
        for i in range(NT):
            sc.wait_ge(ctx.sems[mmg], i + 1)
            ctx.inc(sc.copy(out=g_fm[:, i * 128:(i + 1) * 128], in_=ps_pool[3 + i % 4][0:16, 0:128]),
                    scv, 1)
    gfm_t = ctx.val(scv)

    gst = DmaStream(ctx, "l1gs", depth=1)

    @block.gpsimd
    def _(g):
        g.wait_ge(ctx.sems[scv], gfm_t)
        gst.issue(g, g_dram[:, :], g_fm[:, :])

    grl = DmaStream(ctx, "l1gr", depth=2)
    mmw = name + "mmw"
    b_mmw = ctx.val(mmw)

    pl = PostLN(ctx, name + "R", x_in, x_out,
                lambda i, h: ybuf[:, (i % 4) * D + h * 512:(i % 4) * D + (h + 1) * 512],
                None, yv)
    b2_mm = ctx.val(mm)
    b2_scv = ctx.val(scv)
    b2_vsv = ctx.val(vsv)
    b_yv = ctx.val(yv)

    def y_target(i):
        w, tt = divmod(i, 4)
        return b_yv + w * 8 + 2 * tt + 2
    pl.y_target = y_target

    slab_done = {}

    def grp_mm_t(w):
        return b2_mm + (w + 1) * 17

    @block.tensor
    def _(t):
        t.wait_ge(ctx.sems[scv], gfm_t)
        t.wait_ge(ctx.sems[vsv], wsum_t)
        t.wait_ge(ctx.sems[vsv], brow_t)
        brl.wait(t, i_br)
        for w in range(NW):
            for e in range(16):
                grl.wait(t, w * 16 + e)
                if w * 16 + e >= 1:
                    t.wait_ge(ctx.sems[scv], b2_scv + w * 16 + e)
                if w >= 1:
                    t.wait_ge(ctx.sems[vsv], b2_vsv + ((w - 1) * 16 + e + 1) * ND)
                if w >= 1 and e == 0:
                    t.wait_ge(ctx.sems[yv], b_yv + w * 8)
                n = w * 16 + e
                ctx.inc(t.matmul(ps_pool[7][:, :], one_rb[0:1, :],
                                 grow[0:1, (n % 2) * 512:(n % 2 + 1) * 512],
                                 start=True, stop=True), mmw, 1)
            t.wait_ge(ctx.sems[scv], b2_scv + (w + 1) * 16)
            for e in range(16):
                sl.wait(t, w * 16 + e)
                t.wait_ge(ctx.sems[vsv], b2_vsv + (w * 16 + e + 1) * ND)
                for tt in range(4):
                    for o in range(2):
                        pb = ps_pool[tt * 2 + o]
                        if e == 0 and w >= 1:
                            t.wait_ge(ctx.sems[yv], b_yv + w * 8)
                        for k in range(ND):
                            ins = t.matmul(
                                pb[:, :],
                                gu[:, (e % 2) * ND * W + k * W + tt * 128:
                                      (e % 2) * ND * W + k * W + (tt + 1) * 128],
                                slab[:, k * D + o * 512:k * D + (o + 1) * 512],
                                start=(e == 0 and k == 0), stop=False)
                ctx.inc(ins, mm, 1)
                slab_done[w * 16 + e] = ctx.val(mm)
            for tt in range(4):
                for o in range(2):
                    pb = ps_pool[tt * 2 + o]
                    for k in range(ND):
                        t.matmul(pb[:, :],
                                 xn_fm[:, k * T + w * 512 + tt * 128:k * T + w * 512 + (tt + 1) * 128],
                                 wsum[:, k * D + o * 512:k * D + (o + 1) * 512],
                                 start=False, stop=False)
                    t.matmul(pb[:, :], g_fm[:, w * 512 + tt * 128:w * 512 + (tt + 1) * 128],
                             brs[:, o * 512:(o + 1) * 512], start=False, stop=False)
                    ins = t.matmul(pb[:, :], one_r[0:1, :], brow[0:1, o * 512:(o + 1) * 512],
                                   start=False, stop=True)
            ctx.inc(ins, mm, 1)
    grp_mm = ctx.val(mm)

    @block.scalar
    def _(sc):
        for w in range(NW):
            for e in range(16):
                sc.wait_ge(ctx.sems[mmw], an(b_mmw, w * 16 + e, 1))
                ctx.inc(sc.copy(out=gb16[:, e * W:(e + 1) * W], in_=ps_pool[7][:, :]), scv, 1)
            for tt in range(4):
                pl.sc_tile(sc, w * 4 + tt)

    @block.vector
    def _(v):
        for w in range(NW):
            for e in range(16):
                v.wait_ge(ctx.sems[scv], b2_scv + w * 16 + e + 1)
                if w * 16 + e >= 2:
                    v.wait_ge(ctx.sems[mm], slab_done[w * 16 + e - 2])
                for k in range(ND):
                    ctx.inc(v.tensor_tensor(
                        out=gu[:, (e % 2) * ND * W + k * W:(e % 2) * ND * W + (k + 1) * W],
                        in0=xn_fm[:, k * T + w * 512:k * T + (w + 1) * 512],
                        in1=gb16[:, e * W:(e + 1) * W], op=ALU.mult), vsv, 1)
            for tt in range(4):
                for o in range(2):
                    i_glob = w * 4 + tt
                    v.wait_ge(ctx.sems[mm], grp_mm_t(w))
                    xls.wait(v, i_glob)
                    if i_glob >= 4 and o == 0:
                        v.wait_ge(ctx.sems[pl.vs], pl.apply_done(i_glob - 4))
                    ctx.inc(v.scalar_tensor_tensor(
                        out=ybuf[:, (i_glob % 4) * D + o * 512:(i_glob % 4) * D + (o + 1) * 512],
                        in0=ps_pool[tt * 2 + o][:, :], scalar=1.0,
                        in1=xtok[:, (i_glob % 4) * D + o * 512:(i_glob % 4) * D + (o + 1) * 512],
                        op0=ALU.bypass, op1=ALU.add), yv, 1)
            for tt in range(4):
                pl.vec_tile(v, w * 4 + tt)

    @block.sync
    def _(sync):
        sync.wait_ge(ctx.sems[fm_ready[0]], fm_ready[1])
        gst.wait(sync, 0)
        for w in range(NW):
            for e in range(16):
                n = w * 16 + e
                if n >= 2:
                    sync.wait_ge(ctx.sems[mmw], an(b_mmw, n - 2, 1))
                grl.issue(sync, grow[0:1, (n % 2) * 512:(n % 2 + 1) * 512],
                          g_dram[e:e + 1, w * W:(w + 1) * W])
            for tt in range(4):
                i = w * 4 + tt
                if i >= 4:
                    sync.wait_ge(ctx.sems[yv], y_target(i - 4))
                xls.issue(sync, xtok[:, (i % 4) * D:(i % 4 + 1) * D],
                          xn_bounce[i * 128:(i + 1) * 128, :])
            for tt in range(4):
                pl.ld_tile(sync, w * 4 + tt)

    @block.gpsimd
    def _(g):
        pc.wait(g, precast_done - 1)
        for w in range(NW):
            for e in range(16):
                n = w * 16 + e
                if n >= 1:
                    g.wait_ge(ctx.sems[mm], slab_done[n - 1])
                sl.issue(g, slab.rearrange("q (j o) -> q j o", o=D),
                         wr_bf.rearrange("(e j q) o -> e q j o", e=16, q=128)[e])
            for tt in range(4):
                pl.st_tile(g, w * 4 + tt)

    return pl.finish()



def emit_l2_attn(ctx, x_in, x_out, wq_T, wk_T, wv_T, wg_T, wo_T, gn_cols,
                 cos_d, sin_d, maskT_d, crossdec_d, kdec_d, xn_bounce):
    nc, block = ctx.nc, ctx.block
    name = "l2a"
    H = 4
    NCH = 32
    GC = [float((1.0 - 2.0 ** (-5.0 - h)) ** 128) for h in range(H)]

    xn_fm = ctx.sb(name + "xnfm", [128, ND * T], BF16)
    wbuf = ctx.sb(name + "wbuf", [128, 16 * 1024], BF16)
    cosb = ctx.sb(name + "cos", [128, 2 * 512], F32)
    sinb = ctx.sb(name + "sin", [128, 2 * 512], F32)
    rsc = ctx.sb(name + "rsc", [128, 4 * 512], F32)
    stg = ctx.sb(name + "stg", [128, 2048], BF16)
    mask = ctx.sb(name + "mask", [128, H * 128], F32)
    cdec = ctx.sb(name + "cdec", [128, H * 128], F32)
    kdecc = ctx.sb(name + "kdec", [128, H], F32)
    gnc = ctx.sb(name + "gnc", [128, H], F32)
    S = ctx.sb(name + "S", [128, H * 1024], F32)
    S_bf = ctx.sb(name + "Sbf", [128, H * 1024], BF16)
    qf = ctx.sb(name + "qf", [128, 2 * 1024], BF16)
    qt = ctx.sb(name + "qt", [128, 1024], BF16)
    kf = ctx.sb(name + "kf", [128, 2 * 1024], BF16)
    ktok = ctx.sb(name + "ktok", [128, 2 * 1024], BF16)
    ktd = ctx.sb(name + "ktd", [128, 1024], BF16)
    sqb = ctx.sb(name + "sqb", [128, 2 * 512], BF16)
    atm = ctx.sb(name + "atm", [128, 2 * 128], BF16)
    rc2 = ctx.sb(name + "rc2", [128, H], F32)
    rcols = ctx.sb(name + "rcols", [128, H], F32)
    ybuf = ctx.sb(name + "yb", [128, 2 * 1024], F32)
    onec_bf = ctx.sb(name + "onebf", [128, 1], BF16)
    og = xn_fm[:, 0:4096]
    o32 = xn_fm[:, 4096:8192].bitcast(F32)
    vtok = xn_fm[:, 12288:16384]
    sg = xn_fm[:, 16384:20480]

    qd = nc.dram_tensor(name + "_qd", [D, T], BF16, kind="Internal").ap()
    kd = nc.dram_tensor(name + "_kd", [D, T], BF16, kind="Internal").ap()
    vd = nc.dram_tensor(name + "_vd", [2048, T], BF16, kind="Internal").ap()
    gd = nc.dram_tensor(name + "_gd", [2048, T], BF16, kind="Internal").ap()

    ld = DmaStream(ctx, "wld8", depth=8)
    wl = DmaStream(ctx, "l2awl", depth=2)
    pst = DmaStream(ctx, "l2aps", depth=4)
    rl = DmaStream(ctx, "l2arl", depth=4)
    cl = DmaStream(ctx, "l2acl", depth=8)
    ct = DmaStream(ctx, "l2act", depth=4)
    mm, scv, vsv, gpv, gps = (name + "mm", name + "sc", name + "vs",
                              name + "gp", name + "g0")

    fm_ready = pre_ln_to_fm(ctx, name + "P", x_in, xn_bounce, xn_fm)

    @block.sync
    def _(sync):
        ld.issue(sync, mask[:, :], maskT_d[:, :])
        ld.issue(sync, cdec[:, :], crossdec_d[:, :])
        ld.issue(sync, kdecc[:, :], kdec_d[:, :])
        ld.issue(sync, gnc[:, :], gn_cols[:, :])
    i_mask, i_cdec, i_kdec, i_gn = 0, 1, 2, 3

    @block.gpsimd
    def _(g):
        ctx.inc(g.memset(onec_bf[:, :], 1.0), gps, 1)
    gps_ones = ctx.val(gps)

    ps_pool = [ctx.ps([128, 512], F32) for _ in range(7)]
    b_mm = ctx.val(mm)
    b_rope = ctx.val(vsv)
    b_scv0 = ctx.val(scv)
    b_gpv = ctx.val(gpv)
    b2_mm = b_mm + 384
    b2_vsv = b_rope + 384 + 16
    b2_scv = b_scv0 + 256
    b2_gpv = b_gpv + 2

    def mm_r(n, off):
        return b2_mm + 40 * n + off

    def vs_r(n, off):
        return b2_vsv + 34 * n + off

    def sc_r(n, off):
        return b2_scv + 13 * n + off

    def gp_r(n, off):
        return b2_gpv + 4 * n + off

    @block.tensor
    def _(t):
        t.wait_ge(ctx.sems[fm_ready[0]], fm_ready[1])
        for p in range(2):
            wl.wait(t, p)
            for tc in range(8):
                for h in range(H):
                    idx = p * 32 + tc * 4 + h
                    if idx >= 1:
                        t.wait_ge(ctx.sems[vsv], b_rope + 6 * idx)
                    for half in range(2):
                        co = (2 * h + half) * 128
                        for j in range(ND):
                            ins = t.matmul(
                                ps_pool[half][:, :],
                                wbuf[:, j * 2048 + co:j * 2048 + co + 128],
                                xn_fm[:, j * T + tc * 512:j * T + (tc + 1) * 512],
                                start=(j == 0), stop=(j == ND - 1))
                        ctx.inc(ins, mm, 1)
        t.wait_ge(ctx.sems[vsv], b_rope + 384)
        for p2 in range(2):
            wl.wait(t, 2 + p2)
            for j16 in range(16):
                for tc in range(8):
                    gi = p2 * 128 + j16 * 8 + tc
                    if gi >= 2:
                        t.wait_ge(ctx.sems[scv], b_scv0 + gi - 1)
                    for j in range(ND):
                        ins = t.matmul(
                            ps_pool[gi % 2][:, :],
                            wbuf[:, j * 2048 + j16 * 128:j * 2048 + (j16 + 1) * 128],
                            xn_fm[:, j * T + tc * 512:j * T + (tc + 1) * 512],
                            start=(j == 0), stop=(j == ND - 1))
                    ctx.inc(ins, mm, 1)
    assert ctx.val(mm) == b2_mm

    @block.sync
    def _(sync):
        for p in range(2):
            for tc in range(8):
                rli = p * 8 + tc
                if rli >= 2:
                    sync.wait_ge(ctx.sems[vsv], b_rope + 24 * (rli - 1))
                rl.issue(sync, cosb[:, (rli % 2) * 512:(rli % 2 + 1) * 512],
                         cos_d[:, tc * 512:(tc + 1) * 512])
                rl.issue(sync, sinb[:, (rli % 2) * 512:(rli % 2 + 1) * 512],
                         sin_d[:, tc * 512:(tc + 1) * 512])

    @block.vector
    def _(v):
        for p in range(2):
            for tc in range(8):
                rli = p * 8 + tc
                for h in range(H):
                    idx = p * 32 + tc * 4 + h
                    v.wait_ge(ctx.sems[mm], b_mm + 2 * idx + 2)
                    if h == 0:
                        rl.wait(v, 2 * rli)
                        rl.wait(v, 2 * rli + 1)
                    if idx >= 2:
                        pst.wait(v, 2 * (idx - 2))
                        pst.wait(v, 2 * (idx - 2) + 1)
                    cs = cosb[:, (rli % 2) * 512:(rli % 2 + 1) * 512]
                    sn = sinb[:, (rli % 2) * 512:(rli % 2 + 1) * 512]
                    s0 = stg[:, (idx % 2) * 1024:(idx % 2) * 1024 + 512]
                    s1 = stg[:, (idx % 2) * 1024 + 512:(idx % 2 + 1) * 1024]
                    t1, t2, t3, t4 = (rsc[:, c * 512:(c + 1) * 512] for c in range(4))
                    ctx.inc(v.tensor_tensor(out=t1, in0=ps_pool[0][:, :], in1=cs,
                                            op=ALU.mult), vsv, 1)
                    ctx.inc(v.tensor_tensor(out=t2, in0=ps_pool[1][:, :], in1=sn,
                                            op=ALU.mult), vsv, 1)
                    v.wait_ge(ctx.sems[vsv], ctx.val(vsv))
                    ctx.inc(v.tensor_tensor(out=s0, in0=t1, in1=t2, op=ALU.subtract), vsv, 1)
                    ctx.inc(v.tensor_tensor(out=t3, in0=ps_pool[1][:, :], in1=cs,
                                            op=ALU.mult), vsv, 1)
                    ctx.inc(v.tensor_tensor(out=t4, in0=ps_pool[0][:, :], in1=sn,
                                            op=ALU.mult), vsv, 1)
                    v.wait_ge(ctx.sems[vsv], ctx.val(vsv))
                    ctx.inc(v.tensor_tensor(out=s1, in0=t3, in1=t4, op=ALU.add), vsv, 1)
    assert ctx.val(vsv) == b_rope + 384

    @block.scalar
    def _(sc):
        for s in range(4):
            pst.wait(sc, 124 + s)
        for gi in range(256):
            sc.wait_ge(ctx.sems[mm], b_mm + 128 + gi + 1)
            if gi >= 4:
                pst.wait(sc, 128 + gi - 4)
            ctx.inc(sc.copy(out=stg[:, (gi % 4) * 512:(gi % 4 + 1) * 512],
                            in_=ps_pool[gi % 2][:, :]), scv, 1)
    assert ctx.val(scv) == b2_scv

    @block.gpsimd
    def _(g):
        wl.issue(g, wbuf.rearrange("q (j o) -> q j o", o=2048)[:, :, 0:1024],
                 wq_T.rearrange("(j q) o -> q j o", q=128))
        for p in range(2):
            dst = qd if p == 0 else kd
            for tc in range(8):
                for h in range(H):
                    idx = p * 32 + tc * 4 + h
                    g.wait_ge(ctx.sems[vsv], b_rope + 6 * (idx + 1))
                    s0 = stg[:, (idx % 2) * 1024:(idx % 2) * 1024 + 512]
                    s1 = stg[:, (idx % 2) * 1024 + 512:(idx % 2 + 1) * 1024]
                    pst.issue(g, dst[(2 * h) * 128:(2 * h + 1) * 128,
                                     tc * 512:(tc + 1) * 512], s0)
                    pst.issue(g, dst[(2 * h + 1) * 128:(2 * h + 2) * 128,
                                     tc * 512:(tc + 1) * 512], s1)
            if p == 0:
                g.wait_ge(ctx.sems[mm], b_mm + 64)
                wl.issue(g, wbuf.rearrange("q (j o) -> q j o", o=2048)[:, :, 0:1024],
                         wk_T.rearrange("(j q) o -> q j o", q=128))
        g.wait_ge(ctx.sems[mm], b_mm + 128)
        wl.issue(g, wbuf.rearrange("q (j o) -> q j o", o=2048),
                 wv_T.rearrange("(j q) o -> q j o", q=128))
        for p2 in range(2):
            dst = vd if p2 == 0 else gd
            for j16 in range(16):
                for tc in range(8):
                    gi = p2 * 128 + j16 * 8 + tc
                    g.wait_ge(ctx.sems[scv], b_scv0 + gi + 1)
                    pst.issue(g, dst[j16 * 128:(j16 + 1) * 128,
                                     tc * 512:(tc + 1) * 512],
                              stg[:, (gi % 4) * 512:(gi % 4 + 1) * 512])
            if p2 == 0:
                g.wait_ge(ctx.sems[mm], b_mm + 256)
                wl.issue(g, wbuf.rearrange("q (j o) -> q j o", o=2048),
                         wg_T.rearrange("(j q) o -> q j o", q=128))
        g.wait_ge(ctx.sems[mm], b_mm + 384)
        wl.issue(g, wbuf.rearrange("q (j o) -> q j o", o=1024),
                 wo_T.rearrange("(j q) o -> q j o", q=128))
    assert pst.n == 384

    def y_half_ap(i, hh):
        base = (i % 2) * 1024 + hh * 512
        return ybuf[:, base:base + 512]

    pl = PostLN(ctx, name + "R", x_in, x_out, y_half_ap, None, vsv)

    def y_target(i):
        return vs_r(i, 34)
    pl.y_target = y_target

    @block.tensor
    def _(t):
        t.wait_ge(ctx.sems[scv], b2_scv)
        wl.wait(t, 4)
        psF = [ps_pool[0], ps_pool[1], ps_pool[4], ps_pool[5]]
        for n in range(NCH):
            sl = n % 2
            for h in range(H):
                if h >= 1:
                    t.wait_ge(ctx.sems[vsv], vs_r(n, 13 + 4 * (h - 1)))
                elif n >= 1:
                    t.wait_ge(ctx.sems[vsv], vs_r(n - 1, 25))
                if h == 0:
                    cl.wait(t, 3 * n)
                    cl.wait(t, 3 * n + 1)
                    ct.wait(t, 2 * n + 1)
                psA = ps_pool[2][:, 0:128]
                for kk in range(2):
                    j = 2 * h + kk
                    ins = t.matmul(psA,
                                   kf[:, sl * 1024 + j * 128:sl * 1024 + (j + 1) * 128],
                                   qf[:, sl * 1024 + j * 128:sl * 1024 + (j + 1) * 128],
                                   start=(kk == 0), stop=(kk == 1))
                ctx.inc(ins, mm, 1)
                t.wait_ge(ctx.sems[vsv], vs_r(n, 13 + 4 * h))
                t.wait_ge(ctx.sems[gpv], gp_r(n, h + 1))
                if h >= 1:
                    t.wait_ge(ctx.sems[scv], sc_r(n, 2 * h + 1))
                elif n >= 1:
                    t.wait_ge(ctx.sems[scv], sc_r(n - 1, 9))
                for m in range(4):
                    reg = ps_pool[3][:, m * 128:(m + 1) * 128]
                    t.matmul(reg,
                             vtok[:, sl * 2048 + h * 512 + m * 128:
                                  sl * 2048 + h * 512 + (m + 1) * 128],
                             atm[:, (h % 2) * 128:(h % 2 + 1) * 128],
                             start=True, stop=False)
                    t.matmul(reg,
                             S_bf[:, h * 1024 + m * 128:h * 1024 + (m + 1) * 128],
                             qt[:, (2 * h) * 128:(2 * h + 1) * 128],
                             start=False, stop=False)
                    ins = t.matmul(reg,
                                   S_bf[:, h * 1024 + 512 + m * 128:
                                        h * 1024 + 512 + (m + 1) * 128],
                                   qt[:, (2 * h + 1) * 128:(2 * h + 2) * 128],
                                   start=False, stop=True)
                    ctx.inc(ins, mm, 1)
                if h == 0:
                    t.wait_ge(ctx.sems[vsv], vs_r(n, 12))
                    if n >= 1:
                        t.wait_ge(ctx.sems[vsv], vs_r(n - 1, 34))
                else:
                    t.wait_ge(ctx.sems[vsv], vs_r(n, 12 + 4 * h))
                for kk in range(2):
                    ins = t.matmul(ps_pool[4 + kk][:, :],
                                   ktd[:, h * 256 + kk * 128:h * 256 + (kk + 1) * 128],
                                   vtok[:, sl * 2048 + h * 512:sl * 2048 + (h + 1) * 512],
                                   start=True, stop=True)
                    ctx.inc(ins, mm, 1)
                t.wait_ge(ctx.sems[scv], sc_r(n, 2 * h + 3))
                if h == 0:
                    t.wait_ge(ctx.sems[gps], gps_ones)
                    if n >= 1:
                        t.wait_ge(ctx.sems[scv], sc_r(n - 1, 10))
                for m in range(4):
                    ins = t.matmul(ps_pool[6][:, h:h + 1],
                                   sqb[:, (h % 2) * 512 + m * 128:
                                       (h % 2) * 512 + (m + 1) * 128],
                                   onec_bf[:, :],
                                   start=(m == 0), stop=(m == 3))
                ctx.inc(ins, mm, 1)
            t.wait_ge(ctx.sems[vsv], vs_r(n, 28))
            if n >= 1:
                t.wait_ge(ctx.sems[scv], sc_r(n - 1, 13))
            for e in range(2):
                if e == 1:
                    t.wait_ge(ctx.sems[scv], sc_r(n, 12))
                    t.wait_ge(ctx.sems[vsv], vs_r(n, 31))
                for h in range(H):
                    for m in range(4):
                        j16 = 4 * h + m
                        ins = t.matmul(psF[h][:, :],
                                       og[:, sl * 2048 + j16 * 128:
                                          sl * 2048 + (j16 + 1) * 128],
                                       wbuf[:, j16 * 1024 + e * 512:
                                            j16 * 1024 + e * 512 + 512],
                                       start=(m == 0), stop=(m == 3))
                    ctx.inc(ins, mm, 1)
            assert ctx.val(mm) == mm_r(n, 40)

    @block.vector
    def _(v):
        wl.wait(v, 4)
        ld.wait(v, i_gn)
        for j in range(16):
            ctx.inc(v.tensor_scalar(out=wbuf[:, j * 1024:(j + 1) * 1024],
                                    in0=wbuf[:, j * 1024:(j + 1) * 1024],
                                    scalar1=gnc[:, j % 4:j % 4 + 1], scalar2=None,
                                    op0=ALU.mult), vsv, 1)
        assert ctx.val(vsv) == b2_vsv
        psF = [ps_pool[0], ps_pool[1], ps_pool[4], ps_pool[5]]
        for n in range(NCH):
            sl = n % 2
            cl.wait(v, 3 * n)
            if n == 0:
                ld.wait(v, i_cdec)
            else:
                v.wait_ge(ctx.sems[mm], mm_r(n - 1, 29))
            for j in range(ND):
                h = j // 2
                ctx.inc(v.tensor_tensor(out=qt[:, j * 128:(j + 1) * 128],
                                        in0=qf[:, sl * 1024 + j * 128:
                                             sl * 1024 + (j + 1) * 128],
                                        in1=cdec[:, h * 128:(h + 1) * 128],
                                        op=ALU.mult), vsv, 1)
            ct.wait(v, 2 * n)
            if n == 0:
                ld.wait(v, i_kdec)
            else:
                v.wait_ge(ctx.sems[mm], mm_r(n - 1, 31))
            for h in range(H):
                ctx.inc(v.tensor_scalar(out=ktd[:, h * 256:(h + 1) * 256],
                                        in0=ktok[:, sl * 1024 + h * 256:
                                             sl * 1024 + (h + 1) * 256],
                                        scalar1=kdecc[:, h:h + 1], scalar2=None,
                                        op0=ALU.mult), vsv, 1)
            for h in range(H):
                v.wait_ge(ctx.sems[mm], mm_r(n, 8 * h + 1))
                if n == 0 and h == 0:
                    ld.wait(v, i_mask)
                if h >= 2:
                    v.wait_ge(ctx.sems[mm], mm_r(n, 8 * (h - 2) + 5))
                elif n >= 1:
                    v.wait_ge(ctx.sems[mm], mm_r(n - 1, 8 * (h + 2) + 5))
                ctx.inc(v.tensor_tensor(out=atm[:, (h % 2) * 128:(h % 2 + 1) * 128],
                                        in0=ps_pool[2][:, 0:128],
                                        in1=mask[:, h * 128:(h + 1) * 128],
                                        op=ALU.mult), vsv, 1)
                v.wait_ge(ctx.sems[scv], sc_r(n, 2 * h + 2))
                if n >= 2:
                    v.wait_ge(ctx.sems[mm], mm_r(n - 2, 40))
                ctx.inc(v.tensor_tensor(out=og[:, sl * 2048 + h * 512:
                                             sl * 2048 + (h + 1) * 512],
                                        in0=o32[:, h * 512:(h + 1) * 512],
                                        in1=sg[:, sl * 2048 + h * 512:
                                             sl * 2048 + (h + 1) * 512],
                                        op=ALU.mult), vsv, 1)
                v.wait_ge(ctx.sems[mm], mm_r(n, 8 * h + 7))
                v.wait_ge(ctx.sems[gpv], gp_r(n, h + 1))
                for kk in range(2):
                    ctx.inc(v.scalar_tensor_tensor(
                        out=S[:, h * 1024 + kk * 512:h * 1024 + (kk + 1) * 512],
                        in0=S[:, h * 1024 + kk * 512:h * 1024 + (kk + 1) * 512],
                        scalar=GC[h], in1=ps_pool[4 + kk][:, :],
                        op0=ALU.mult, op1=ALU.add), vsv, 1)
            for e in range(2):
                v.wait_ge(ctx.sems[scv], sc_r(n, 12 + e))
                for h in range(1, H):
                    v.wait_ge(ctx.sems[mm], mm_r(n, 32 + 4 * e + h + 1))
                    if h >= 2:
                        v.wait_ge(ctx.sems[vsv], ctx.val(vsv))
                    ctx.inc(v.scalar_tensor_tensor(
                        out=ybuf[:, sl * 1024 + e * 512:sl * 1024 + (e + 1) * 512],
                        in0=psF[h][:, :], scalar=rcols[:, h:h + 1],
                        in1=ybuf[:, sl * 1024 + e * 512:sl * 1024 + (e + 1) * 512],
                        op0=ALU.mult, op1=ALU.add), vsv, 1)
            assert ctx.val(vsv) == vs_r(n, 34)
            pl.vec_tile(v, n)

    @block.scalar
    def _(sc):
        for s in range(4):
            pst.wait(sc, 380 + s)
        for nn in range(2):
            ct.issue(sc, ktok[:, nn * 1024:(nn + 1) * 1024],
                     kd[:, nn * 128:(nn + 1) * 128], transpose=True)
            ct.issue(sc, vtok[:, nn * 2048:(nn + 1) * 2048],
                     vd[:, nn * 128:(nn + 1) * 128], transpose=True)
        for n in range(NCH):
            sl = n % 2
            cl.wait(sc, 3 * n + 2)
            if n >= 2:
                sc.wait_ge(ctx.sems[vsv], vs_r(n - 2, 26))
            ctx.inc(sc.activation(out=sg[:, sl * 2048:(sl + 1) * 2048],
                                  in_=sg[:, sl * 2048:(sl + 1) * 2048],
                                  func=AF.Silu), scv, 1)
            for h in range(H):
                sc.wait_ge(ctx.sems[mm], mm_r(n, 8 * h + 5))
                if n >= 1:
                    sc.wait_ge(ctx.sems[vsv], vs_r(n - 1, 14 + 4 * h))
                ctx.inc(sc.copy(out=o32[:, h * 512:(h + 1) * 512],
                                in_=ps_pool[3][:, :]), scv, 1)
                if h >= 2:
                    sc.wait_ge(ctx.sems[mm], mm_r(n, 8 * (h - 2) + 8))
                elif n >= 1:
                    sc.wait_ge(ctx.sems[mm], mm_r(n - 1, 8 * (h + 2) + 8))
                ctx.inc(sc.activation(out=sqb[:, (h % 2) * 512:(h % 2 + 1) * 512],
                                      in_=ps_pool[3][:, :], func=AF.Square), scv, 1)
            sc.wait_ge(ctx.sems[mm], mm_r(n, 32))
            if n >= 1:
                sc.wait_ge(ctx.sems[vsv], vs_r(n - 1, 34))
            ctx.inc(sc.activation(out=rc2[:, :], in_=ps_pool[6][:, 0:4], func=AF.Ln,
                                  scale=1.0 / 512.0, bias=ctx.c(1e-5)), scv, 1)
            sc.wait_ge(ctx.sems[scv], ctx.val(scv))
            ctx.inc(sc.activation(out=rcols[:, :], in_=rc2[:, :], func=AF.Exp,
                                  scale=-0.5), scv, 1)
            sc.wait_ge(ctx.sems[scv], ctx.val(scv))
            for e in range(2):
                sc.wait_ge(ctx.sems[mm], mm_r(n, 32 + 4 * e + 1))
                if n >= 2:
                    sc.wait_ge(ctx.sems[pl.vs], pl.apply_done(n - 2))
                ctx.inc(sc.activation(out=ybuf[:, sl * 1024 + e * 512:
                                           sl * 1024 + (e + 1) * 512],
                                      in_=ps_pool[0][:, :], func=AF.Identity,
                                      scale=rcols[:, 0:1]), scv, 1)
            assert ctx.val(scv) == sc_r(n, 13)
            if n + 2 < NCH:
                sc.wait_ge(ctx.sems[vsv], vs_r(n, 12))
                sc.wait_ge(ctx.sems[mm], mm_r(n, 31))
                s2 = (n + 2) % 2
                ct.issue(sc, ktok[:, s2 * 1024:(s2 + 1) * 1024],
                         kd[:, (n + 2) * 128:(n + 3) * 128], transpose=True)
                ct.issue(sc, vtok[:, s2 * 2048:(s2 + 1) * 2048],
                         vd[:, (n + 2) * 128:(n + 3) * 128], transpose=True)
            pl.sc_tile(sc, n)

    @block.gpsimd
    def _(g):
        ctx.inc(g.memset(S[:, :], 0.0), gpv, 1)
        ctx.inc(g.memset(S_bf[:, :], 0.0), gpv, 1)
        g.wait_ge(ctx.sems[gpv], b2_gpv)
        for n in range(NCH):
            for h in range(H):
                if n >= 1:
                    g.wait_ge(ctx.sems[vsv], vs_r(n - 1, 16 + 4 * h))
                    g.wait_ge(ctx.sems[mm], mm_r(n - 1, 8 * h + 5))
                ctx.inc(g.tensor_copy(out=S_bf[:, h * 1024:(h + 1) * 1024],
                                      in_=S[:, h * 1024:(h + 1) * 1024]), gpv, 1)
            pl.st_tile(g, n)

    @block.sync
    def _(sync):
        for s in range(4):
            pst.wait(sync, 380 + s)
        for n in range(NCH):
            if n >= 2:
                sync.wait_ge(ctx.sems[vsv], vs_r(n - 2, 26))
                sync.wait_ge(ctx.sems[mm], mm_r(n - 2, 25))
            sl = n % 2
            cl.issue(sync, qf.rearrange("p (s j c) -> p s j c", s=2, c=128)[:, sl],
                     qd.rearrange("(j p) t -> p j t", p=128)[:, :, n * 128:(n + 1) * 128])
            cl.issue(sync, kf.rearrange("p (s j c) -> p s j c", s=2, c=128)[:, sl],
                     kd.rearrange("(j p) t -> p j t", p=128)[:, :, n * 128:(n + 1) * 128])
            cl.issue(sync, sg.rearrange("p (s j c) -> p s j c", s=2, c=128)[:, sl],
                     gd.rearrange("(j p) t -> p j t", p=128)[:, :, n * 128:(n + 1) * 128])
            pl.ld_tile(sync, n)

    return pl.finish()



def emit_l0_attn(ctx, x_in, x_out, ddw_T, rw_T, kw_T, vw_T, gw_T, ww_T, ow_T,
                 mix_cols, gnw_row, gnb_cols, maskL2_d, lx_d, xn_bounce, stop=None, n2=64, dbg2=(), n3=NT):
    nc, block = ctx.nc, ctx.block
    name = "l0a"
    GN_EPS = 0.00064

    wpool = ctx.sb(name + "wp", [128, 6 * 8192], BF16)
    wof = {"dd": 0, "w": 8192, "k": 2 * 8192, "r": 3 * 8192, "v": 4 * 8192,
           "g": 5 * 8192}
    xsb_s = ctx.sb(name + "xs", [128, 4096], BF16)
    xnb_s = ctx.sb(name + "xn", [128, 4096], BF16)
    lerpb = ctx.sb(name + "lp", [128, 4096], BF16)
    dlt = ctx.sb(name + "dl", [128, 4096], BF16)
    ddl = ctx.sb(name + "ddl", [128, 4096], BF16)
    ddb = ctx.sb(name + "ddb", [128, 2 * 512], BF16)
    tnh = ctx.sb(name + "tnh", [128, 2 * 512], BF16)
    Eb = ctx.sb(name + "Eb", [128, 4096], BF16)
    kab = ctx.sb(name + "kab", [128, 2 * 512], BF16)
    wtmp = ctx.sb(name + "wtm", [128, 2 * 512], F32)
    wtok = ctx.sb(name + "wtk", [64, 1024], F32)
    sumsb = ctx.sb(name + "sms", [128, 2 * 16], F32)
    lngb = ctx.sb(name + "lng", [128, 2 * 8], F32)
    fmmb = ctx.sb(name + "fmm", [128, 8], F32)
    carry = ctx.sb(name + "cry", [128, 8], F32)
    gstg = ctx.sb(name + "gst", [128, 2 * 8], F32)
    stg = ctx.sb(name + "stg", [128, 4 * 512], BF16)
    katg = ctx.sb(name + "ktg", [64, 2 * 1024], BF16)
    mixc = ctx.sb(name + "mx", [128, 40], F32)
    omix = ctx.sb(name + "om", [128, 40], F32)
    maskL = ctx.sb(name + "mkl", [64, 128], F32)
    lx = ctx.sb(name + "lxc", [64, 66], F32)
    idm = ctx.sb(name + "idm", [128, 128], F32)
    idmb = ctx.sb(name + "idb", [128, 128], BF16)
    onesb = ctx.sb(name + "on1", [128, 128], F32)
    onesbb = ctx.sb(name + "on2", [128, 128], BF16)
    gnwr = ctx.sb(name + "gnw", [1, 1024], F32)
    gnbc = ctx.sb(name + "gnb", [128, 8], F32)
    ind2 = ctx.sb(name + "ind", [128, 2], BF16)
    Stv = ctx.sb(name + "S2v", [64, 1024], BF16)
    rbB = ctx.sb(name + "rbB", [64, 2 * 512], BF16)
    vbB = ctx.sb(name + "vbB", [64, 2 * 512], BF16)
    kat2 = ctx.sb(name + "kt2", [64, 2 * 1024], BF16)
    vtk2 = ctx.sb(name + "vt2", [64, 2 * 1024], BF16)
    mmb2 = ctx.sb(name + "Mm2", [64, 2 * 128], BF16)
    gcb = ctx.sb(name + "gcb", [128, 2 * 64], F32)

    stv = ctx.sb(name + "stv", [128, 48], F32)
    rc3 = ctx.sb(name + "rc3", [128, 16], F32)
    scols = ctx.sb(name + "sco", [128, 32], F32)

    ow_buf = wpool[:, 0:8192]
    S = wpool[:, 8192:9216].bitcast(F32)
    o32 = wpool[:, 9216:10240].bitcast(F32)
    t_i = wpool[:, 10240:11264].bitcast(F32)
    odb = wpool[:, 11264:13312]
    gtb = wpool[:, 13312:15360]
    sqb = wpool[:, 15360:16384]
    Qs = wpool[:, 16384:18432]
    srows = wpool[0:1, 18432:26624].bitcast(F32)
    e1b = wpool[:, 30720:30976].bitcast(F32)
    e2b = wpool[:, 30976:31232].bitcast(F32)
    s32 = wpool[0:32, 31232:31744].bitcast(F32)
    Stmp = wpool[:, 31744:31872].bitcast(F32)
    rb2 = wpool[:, 31872:32896]
    vb2 = wpool[:, 32896:33920]
    Ab2 = wpool[:, 33920:34944]
    ostg = wpool[:, 34944:35968]
    bev = wpool[:, 35968:36992].bitcast(F32)

    xsd = nc.dram_tensor(name + "_xs", [T, D], BF16, kind="Internal").ap()
    rd = nc.dram_tensor(name + "_rd", [D, T], BF16, kind="Internal").ap()
    vd = nc.dram_tensor(name + "_vd", [D, T], BF16, kind="Internal").ap()
    gtd = nc.dram_tensor(name + "_gd", [D, T], BF16, kind="Internal").ap()
    Ad = nc.dram_tensor(name + "_Ad", [D, T], BF16, kind="Internal").ap()
    katd = nc.dram_tensor(name + "_kA", [T, D], BF16, kind="Internal").ap()
    gCd = nc.dram_tensor(name + "_gC", [D, 64], F32, kind="Internal").ap()
    od = nc.dram_tensor(name + "_od", [D, T], BF16, kind="Internal").ap()
    statd = nc.dram_tensor(name + "_st", [2, 32, 128], F32, kind="Internal").ap()

    ld = DmaStream(ctx, "wld8", depth=8)
    wl = DmaStream(ctx, name + "wl", depth=8)
    tpx = DmaStream(ctx, name + "tp", depth=16)
    xstr = DmaStream(ctx, name + "xd", depth=2)
    pstA = DmaStream(ctx, name + "pA", depth=4)
    pstK = DmaStream(ctx, name + "pK", depth=2)
    pstG = DmaStream(ctx, name + "pG", depth=2)
    pstO = DmaStream(ctx, name + "pO", depth=2)
    pstS = DmaStream(ctx, name + "pS", depth=2)
    cl = DmaStream(ctx, name + "cl", depth=8)
    cl2 = DmaStream(ctx, name + "cl2", depth=8)
    cg = DmaStream(ctx, name + "cg", depth=2)
    ct = DmaStream(ctx, name + "ct", depth=4)
    mm, scv, vsv, gpv, gps = (name + "mm", name + "sc", name + "vs",
                              name + "gp", name + "g0")

    bd = pre_ln_to_fm(ctx, name + "P", x_in, xn_bounce, None)

    @block.sync
    def _(sync):
        ld.issue(sync, mixc[:, :], mix_cols[:, :])
        ld.issue(sync, gnwr[:, :], gnw_row[:, :])
        ld.issue(sync, gnbc[:, :], gnb_cols[:, :])
        ld.issue(sync, maskL[:, :], maskL2_d[:, :])
        ld.issue(sync, lx[:, :], lx_d[:, :])
    i_mix, i_gnw, i_gnb, i_mskl, i_lx = 0, 1, 2, 3, 4

    @block.gpsimd
    def _(g):
        ctx.inc(g.memset(onesb[:, :], 1.0), gps, 1)
        ctx.inc(g.memset(onesbb[:, :], 1.0), gps, 1)
        g.wait_ge(ctx.sems[gps], ctx.val(gps))
        ctx.inc(g.affine_select(idm[:, :], onesb[:, :], pattern=[[1, 128]],
                                compare_op=ALU.is_equal, fill=0.0, base=0,
                                channel_multiplier=-1), gps, 1)
        ctx.inc(g.affine_select(idmb[:, :], onesbb[:, :], pattern=[[1, 128]],
                                compare_op=ALU.is_equal, fill=0.0, base=0,
                                channel_multiplier=-1), gps, 1)
        ctx.inc(g.memset(ind2[:, :], 0.0), gps, 1)
        g.wait_ge(ctx.sems[gps], ctx.val(gps))
        ctx.inc(g.memset(ind2[0:64, 0:1], 1.0), gps, 1)
        ctx.inc(g.memset(ind2[64:128, 1:2], 1.0), gps, 1)
        ctx.inc(g.memset(carry[:, :], 0.0), gps, 1)
        ctx.inc(g.memset(stg[0:1, 0:1024], 0.0), gps, 1)
        g.wait_ge(ctx.sems[gps], ctx.val(gps))
        xstr.issue(g, xsd[0:1, :], stg[0:1, 0:1024])
        g.wait_ge(ctx.sems[bd[0]], bd[1])
        xstr.issue(g, xsd[1:T, :], xn_bounce[0:T - 1, :])
        for nm, src in [("dd", ddw_T), ("w", ww_T), ("k", kw_T), ("r", rw_T),
                        ("v", vw_T), ("g", gw_T)]:
            wl.issue(g, wpool.rearrange("q (j o) -> q j o", o=1024)[:, wof[nm] // 1024:wof[nm] // 1024 + 8, :],
                     src.rearrange("(j q) o -> q j o", q=128))
    i_wdd, i_ww, i_wk, i_wr, i_wv, i_wg = 0, 1, 2, 3, 4, 5

    PA = ctx.ps([128, 512], F32)
    PB = ctx.ps([128, 512], F32)
    PC = ctx.ps([128, 512], F32)
    PD = ctx.ps([128, 512], F32)
    PE = ctx.ps([128, 1024], F32)
    PF = ctx.ps([128, 512], F32)
    PG9 = ctx.ps([128, 192], F32)
    PG = PG9[:, 0:64]
    PH = PG9[0:32, 64:192]

    M0 = ctx.val(mm)
    S0 = ctx.val(scv)

    def mm1(tc, off):
        return M0 + 80 * tc + off

    def sc1(tc, off):
        return S0 + 112 * tc + off

    WPR = [9, 10, 12, 16, 20, 24, 28, 32]
    WTR = [11, 15, 19, 23, 27, 31, 35, 38]
    WCU = [13, 17, 21, 25, 29, 33, 36, 39]
    WSU = [14, 18, 22, 26, 30, 34, 37, 40]
    WCH1 = [17, 20, 23, 31, 39, 47, 55, 63]
    EVB = [25, 33, 41, 49, 57, 65, 70, 75]

    def wpr(jo):
        return WPR[jo]

    def wtr(j):
        return WTR[j]

    def wcu(j):
        return WCU[j]

    def wsu(j):
        return WSU[j]

    def kpr(jo):
        return 41 + jo if jo <= 2 else 2 * jo + 39

    def ktr(j):
        return 2 * j + 44 if j <= 5 else 49 + j


    @block.vector
    def _(v):
        ld.wait(v, i_mix)
        ctx.inc(v.tensor_scalar(out=omix[:, :], in0=mixc[:, :], scalar1=-1.0,
                                scalar2=None, op0=ALU.mult), vsv, 1)
        v.wait_ge(ctx.sems[vsv], ctx.val(vsv))
        ctx.inc(v.tensor_scalar(out=omix[:, :], in0=omix[:, :], scalar1=1.0,
                                scalar2=None, op0=ALU.add), vsv, 1)
    V0 = ctx.val(vsv)

    def vs1(tc, off):
        return V0 + 136 * tc + off

    def emit_lerp(v, tc, pidx, dst):
        for j in range(ND):
            ctx.inc(v.scalar_tensor_tensor(
                out=dst[:, j * 512:(j + 1) * 512], in0=dlt[:, j * 512:(j + 1) * 512],
                scalar=mixc[:, pidx * 8 + j:pidx * 8 + j + 1],
                in1=xsb_s[:, j * 512:(j + 1) * 512],
                op0=ALU.mult, op1=ALU.add), vsv, 1)
            v.wait_ge(ctx.sems[vsv], ctx.val(vsv))
            ctx.inc(v.scalar_tensor_tensor(
                out=dst[:, j * 512:(j + 1) * 512], in0=ddl[:, j * 512:(j + 1) * 512],
                scalar=omix[:, pidx * 8 + j:pidx * 8 + j + 1],
                in1=dst[:, j * 512:(j + 1) * 512],
                op0=ALU.mult, op1=ALU.add), vsv, 1)


    @block.tensor
    def _(t):
        for tc in range(8):
            for jo in range(ND):
                if jo == 0:
                    for j in range(ND):
                        tpx.wait(t, tc * 16 + 8 + j)
                    wl.wait(t, i_wdd)
                if jo >= 2:
                    t.wait_ge(ctx.sems[scv], sc1(tc, 2 + 2 * (jo - 2)))
                elif tc >= 1:
                    t.wait_ge(ctx.sems[scv], sc1(tc - 1, 2 + 2 * (6 + jo)))
                pb = [PA, PB][jo % 2]
                for j in range(ND):
                    ins = t.matmul(pb[:, :],
                                   wpool[:, wof["dd"] + j * 1024 + jo * 128:
                                         wof["dd"] + j * 1024 + (jo + 1) * 128],
                                   xsb_s[:, j * 512:(j + 1) * 512],
                                   start=(j == 0), stop=(j == ND - 1))
                ctx.inc(ins, mm, 1)
            def emit_wproj(jo):
                if jo == 0:
                    t.wait_ge(ctx.sems[vsv], vs1(tc, 32))
                    wl.wait(t, i_ww)
                if jo >= 2:
                    t.wait_ge(ctx.sems[scv], sc1(tc, WCH1[jo - 2]))
                elif tc >= 1:
                    t.wait_ge(ctx.sems[scv], sc1(tc - 1, 105 + 6 + jo))
                pb = [PC, PD][jo % 2]
                for j in range(ND):
                    ins = t.matmul(pb[:, :],
                                   wpool[:, wof["w"] + j * 1024 + jo * 128:
                                         wof["w"] + j * 1024 + (jo + 1) * 128],
                                   lerpb[:, j * 512:(j + 1) * 512],
                                   start=(j == 0), stop=(j == ND - 1))
                ctx.inc(ins, mm, 1)

            def emit_wtrans(jo):
                t.wait_ge(ctx.sems[scv], sc1(tc, WCH1[jo] + 2))
                if jo >= 1:
                    t.wait_ge(ctx.sems[scv], sc1(tc, EVB[jo - 1] + 1))
                elif tc >= 1:
                    t.wait_ge(ctx.sems[scv], sc1(tc - 1, 81 + 7))
                wslab = wtmp[:, (jo % 2) * 512:(jo % 2 + 1) * 512]
                for c8 in range(8):
                    ins = t.transpose(PE[0:64, c8 * 128:(c8 + 1) * 128],
                                      wslab[:, c8 * 64:(c8 + 1) * 64], idm[:, :])
                ctx.inc(ins, mm, 1)

            def emit_wcum(jo):
                t.wait_ge(ctx.sems[scv], sc1(tc, EVB[jo] + 1))
                if jo >= 1:
                    t.wait_ge(ctx.sems[scv], sc1(tc, EVB[jo - 1] + 4))
                ld.wait(t, i_lx)
                for c8 in range(8):
                    ins = t.matmul(PF[:, c8 * 64:(c8 + 1) * 64],
                                   wtok[:, c8 * 128:(c8 + 1) * 128], lx[:, 0:64],
                                   start=True, stop=True)
                ctx.inc(ins, mm, 1)

            def emit_wsums(jo):
                if jo >= 1:
                    t.wait_ge(ctx.sems[scv], sc1(tc, EVB[jo - 1] + 2))
                for c8 in range(8):
                    ins = t.matmul(PG[:, c8 * 2:(c8 + 1) * 2],
                                   wtok[:, c8 * 128:(c8 + 1) * 128], lx[:, 64:66],
                                   start=True, stop=True)
                ctx.inc(ins, mm, 1)

            wseq = [("wp", 0), ("wp", 1), ("tr", 0), ("wp", 2), ("cu", 0),
                    ("su", 0), ("tr", 1), ("wp", 3), ("cu", 1), ("su", 1),
                    ("tr", 2), ("wp", 4), ("cu", 2), ("su", 2), ("tr", 3),
                    ("wp", 5), ("cu", 3), ("su", 3), ("tr", 4), ("wp", 6),
                    ("cu", 4), ("su", 4), ("tr", 5), ("wp", 7), ("cu", 5),
                    ("su", 5), ("tr", 6), ("cu", 6), ("su", 6), ("tr", 7),
                    ("cu", 7), ("su", 7)]
            for op, jo in wseq:
                {"wp": emit_wproj, "tr": emit_wtrans, "cu": emit_wcum,
                 "su": emit_wsums}[op](jo)
                assert ctx.val(mm) == mm1(tc, {"wp": wpr, "tr": wtr, "cu": wcu,
                                               "su": wsu}[op](jo))
            def emit_kproj(jo):
                if jo == 0:
                    t.wait_ge(ctx.sems[vsv], vs1(tc, 80))
                    wl.wait(t, i_wk)
                if jo >= 2:
                    t.wait_ge(ctx.sems[vsv], vs1(tc, 81 + jo - 2))
                else:
                    t.wait_ge(ctx.sems[scv], sc1(tc, WCH1[6 + jo]))
                pb = [PC, PD][jo % 2]
                for j in range(ND):
                    ins = t.matmul(pb[:, :],
                                   wpool[:, wof["k"] + j * 1024 + jo * 128:
                                         wof["k"] + j * 1024 + (jo + 1) * 128],
                                   xnb_s[:, j * 512:(j + 1) * 512],
                                   start=(j == 0), stop=(j == ND - 1))
                ctx.inc(ins, mm, 1)

            def emit_ktrans(jo):
                t.wait_ge(ctx.sems[vsv], vs1(tc, 81 + jo))
                if jo >= 1:
                    t.wait_ge(ctx.sems[scv], sc1(tc, 81 + jo - 1))
                else:
                    t.wait_ge(ctx.sems[scv], sc1(tc, EVB[7] + 1))
                ka = kab[:, (jo % 2) * 512:(jo % 2 + 1) * 512]
                PEb = PE.bitcast(BF16)
                for c8 in range(8):
                    ins = t.transpose(PEb[0:64, c8 * 128:(c8 + 1) * 128],
                                      ka[:, c8 * 64:(c8 + 1) * 64], idmb[:, :])
                ctx.inc(ins, mm, 1)

            emit_kproj(0)
            emit_kproj(1)
            emit_kproj(2)
            emit_ktrans(0)
            for jo in range(3, ND):
                emit_kproj(jo)
                emit_ktrans(jo - 3 + 1)
            emit_ktrans(6)
            emit_ktrans(7)
            for pi, nm in enumerate(["r", "v", "g"]):
                rhs = [lerpb, xnb_s, lerpb][pi]
                for jo in range(ND):
                    if jo == 0:
                        t.wait_ge(ctx.sems[vsv], vs1(tc, 104 + 16 * pi))
                        wl.wait(t, [i_wr, i_wv, i_wg][pi])
                    if jo >= 2:
                        t.wait_ge(ctx.sems[scv], sc1(tc, 89 + 8 * pi + jo - 2))
                    elif pi == 0:
                        t.wait_ge(ctx.sems[vsv], vs1(tc, 81 + 6 + jo))
                    else:
                        t.wait_ge(ctx.sems[scv], sc1(tc, 89 + 8 * (pi - 1) + 6 + jo))
                    pb = [PC, PD][jo % 2]
                    for j in range(ND):
                        ins = t.matmul(pb[:, :],
                                       wpool[:, wof[nm] + j * 1024 + jo * 128:
                                             wof[nm] + j * 1024 + (jo + 1) * 128],
                                       rhs[:, j * 512:(j + 1) * 512],
                                       start=(j == 0), stop=(j == ND - 1))
                    ctx.inc(ins, mm, 1)
            assert ctx.val(mm) == mm1(tc, 80)

    @block.vector
    def _(v):
        for tc in range(8):
            for j in range(ND):
                tpx.wait(v, tc * 16 + j)
                tpx.wait(v, tc * 16 + 8 + j)
                ctx.inc(v.tensor_tensor(out=dlt[:, j * 512:(j + 1) * 512],
                                        in0=xnb_s[:, j * 512:(j + 1) * 512],
                                        in1=xsb_s[:, j * 512:(j + 1) * 512],
                                        op=ALU.subtract), vsv, 1)
            v.wait_ge(ctx.sems[vsv], ctx.val(vsv))
            for j in range(ND):
                v.wait_ge(ctx.sems[scv], sc1(tc, 2 + 2 * j))
                ctx.inc(v.tensor_tensor(out=ddl[:, j * 512:(j + 1) * 512],
                                        in0=ddb[:, (j % 2) * 512:(j % 2 + 1) * 512],
                                        in1=dlt[:, j * 512:(j + 1) * 512],
                                        op=ALU.mult), vsv, 1)
            assert ctx.val(vsv) == vs1(tc, 16)
            v.wait_ge(ctx.sems[vsv], ctx.val(vsv))
            if tc >= 1:
                v.wait_ge(ctx.sems[mm], mm1(tc - 1, 80))
            emit_lerp(v, tc, 4, lerpb)
            assert ctx.val(vsv) == vs1(tc, 32)
            for jo in range(ND):
                v.wait_ge(ctx.sems[scv], sc1(tc, EVB[jo] + 2))
                if jo >= 2:
                    v.wait_ge(ctx.sems[scv], sc1(tc, EVB[jo - 2] + 5))
                elif tc >= 1:
                    v.wait_ge(ctx.sems[scv], sc1(tc - 1, EVB[6 + jo] + 5))
                sl_ = sumsb[:, (jo % 2) * 16:(jo % 2 + 1) * 16]
                lg = lngb[:, (jo % 2) * 8:(jo % 2 + 1) * 8]
                ctx.inc(v.tensor_tensor(out=fmmb[:, :], in0=sl_[:, 1:16:2],
                                        in1=sl_[:, 0:16:2], op=ALU.subtract), vsv, 1)
                v.wait_ge(ctx.sems[vsv], ctx.val(vsv))
                ctx.inc(v.tensor_tensor(out=lg[:, 0:1], in0=sl_[:, 0:1],
                                        in1=carry[:, jo:jo + 1], op=ALU.add), vsv, 1)
                ctx.inc(v.tensor_tensor(out=lg[:, 1:8], in0=sl_[:, 2:16:2],
                                        in1=fmmb[:, 0:7], op=ALU.add), vsv, 1)
                v.wait_ge(ctx.sems[vsv], ctx.val(vsv))
                ctx.inc(v.tensor_copy(out=carry[:, jo:jo + 1], in_=fmmb[:, 7:8]), vsv, 1)
            assert ctx.val(vsv) == vs1(tc, 64)
            emit_lerp(v, tc, 1, xnb_s)
            for jo in range(ND):
                v.wait_ge(ctx.sems[mm], mm1(tc, kpr(jo)))
                v.wait_ge(ctx.sems[scv], sc1(tc, EVB[jo] + 4))
                if jo >= 2:
                    v.wait_ge(ctx.sems[mm], mm1(tc, ktr(jo - 2)))
                pb = [PC, PD][jo % 2]
                ctx.inc(v.tensor_tensor(out=kab[:, (jo % 2) * 512:(jo % 2 + 1) * 512],
                                        in0=pb[:, :], in1=Eb[:, jo * 512:(jo + 1) * 512],
                                        op=ALU.mult), vsv, 1)
            assert ctx.val(vsv) == vs1(tc, 88)
            v.wait_ge(ctx.sems[mm], mm1(tc, 16))
            emit_lerp(v, tc, 0, lerpb)
            v.wait_ge(ctx.sems[mm], mm1(tc, 48))
            emit_lerp(v, tc, 2, xnb_s)
            v.wait_ge(ctx.sems[mm], mm1(tc, 64))
            emit_lerp(v, tc, 3, lerpb)
            assert ctx.val(vsv) == vs1(tc, 136)

    @block.scalar
    def _(sc):
        ctx.wcp(sc)
        for tc in range(8):
            for jo in range(ND):
                sc.wait_ge(ctx.sems[mm], mm1(tc, 1 + jo))
                if jo >= 2:
                    sc.wait_ge(ctx.sems[vsv], vs1(tc, 9 + jo - 2))
                elif tc >= 1:
                    sc.wait_ge(ctx.sems[vsv], vs1(tc - 1, 9 + 6 + jo))
                pb = [PA, PB][jo % 2]
                th = tnh[:, (jo % 2) * 512:(jo % 2 + 1) * 512]
                ctx.inc(sc.activation(out=th, in_=pb[:, :], func=AF.Tanh), scv, 1)
                sc.wait_ge(ctx.sems[scv], ctx.val(scv))
                ctx.inc(sc.activation(out=ddb[:, (jo % 2) * 512:(jo % 2 + 1) * 512],
                                      in_=th, func=AF.Sigmoid), scv, 1)
            def emit_wch(jo):
                sc.wait_ge(ctx.sems[mm], mm1(tc, wpr(jo)))
                if jo >= 2:
                    sc.wait_ge(ctx.sems[mm], mm1(tc, wtr(jo - 2)))
                pb = [PC, PD][jo % 2]
                ws = wtmp[:, (jo % 2) * 512:(jo % 2 + 1) * 512]
                ctx.inc(sc.activation(out=ws, in_=pb[:, :], func=AF.Exp, scale=-1.0), scv, 1)
                sc.wait_ge(ctx.sems[scv], ctx.val(scv))
                ctx.inc(sc.activation(out=ws, in_=ws, func=AF.Ln, bias=ctx.c(1.0)), scv, 1)
                sc.wait_ge(ctx.sems[scv], ctx.val(scv))
                ctx.inc(sc.activation(out=ws, in_=ws, func=AF.Identity, scale=-1.0,
                                      bias=ctx.c(-0.5)), scv, 1)

            def emit_evb(jo):
                sc.wait_ge(ctx.sems[mm], mm1(tc, wtr(jo)))
                if jo >= 1:
                    sc.wait_ge(ctx.sems[mm], mm1(tc, wcu(jo - 1)))
                ctx.inc(sc.copy(out=wtok[:, :], in_=PE[0:64, :]), scv, 1)
                sc.wait_ge(ctx.sems[mm], mm1(tc, wsu(jo)))
                if jo >= 2:
                    sc.wait_ge(ctx.sems[vsv], vs1(tc, 36 + 4 * (jo - 2)))
                elif tc >= 1 and jo < 2:
                    sc.wait_ge(ctx.sems[vsv], vs1(tc - 1, 36 + 4 * (6 + jo)))
                ctx.inc(sc.copy(out=sumsb[:, (jo % 2) * 16:(jo % 2 + 1) * 16],
                                in_=PG[:, 0:16]), scv, 1)
                sc.wait_ge(ctx.sems[mm], mm1(tc, wcu(jo)))
                sidx = tc * 32 + jo
                if sidx >= 4:
                    pstA.wait(sc, sidx - 4)
                if sidx <= 1:
                    xstr.wait(sc, 0)
                ctx.inc(sc.activation(out=stg[:, (sidx % 4) * 512:(sidx % 4 + 1) * 512],
                                      in_=PF[:, :], func=AF.Exp), scv, 1)
                if tc >= 1:
                    sc.wait_ge(ctx.sems[vsv], vs1(tc - 1, 81 + jo))
                ctx.inc(sc.activation(out=Eb[:, jo * 512:(jo + 1) * 512],
                                      in_=PF[:, :], func=AF.Exp, scale=-1.0), scv, 1)
                sc.wait_ge(ctx.sems[vsv], vs1(tc, 36 + 4 * jo))
                gidx = tc * 8 + jo
                if gidx >= 2:
                    pstG.wait(sc, gidx - 2)
                ctx.inc(sc.activation(out=gstg[:, (gidx % 2) * 8:(gidx % 2 + 1) * 8],
                                      in_=lngb[:, (jo % 2) * 8:(jo % 2 + 1) * 8],
                                      func=AF.Exp), scv, 1)

            sseq = [("wch", 0), ("wch", 1), ("wch", 2), ("evb", 0), ("wch", 3),
                    ("evb", 1), ("wch", 4), ("evb", 2), ("wch", 5), ("evb", 3),
                    ("wch", 6), ("evb", 4), ("wch", 7), ("evb", 5), ("evb", 6),
                    ("evb", 7)]
            for op, jo in sseq:
                {"wch": emit_wch, "evb": emit_evb}[op](jo)
                assert ctx.val(scv) == sc1(tc, (WCH1[jo] + 2) if op == "wch"
                                           else (EVB[jo] + 5))
            assert ctx.val(scv) == sc1(tc, 80)
            for jo in range(ND):
                sc.wait_ge(ctx.sems[mm], mm1(tc, ktr(jo)))
                kidx = tc * 8 + jo
                if kidx >= 2:
                    pstK.wait(sc, kidx - 2)
                ctx.inc(sc.copy(out=katg[:, (kidx % 2) * 1024:(kidx % 2 + 1) * 1024],
                                in_=PE.bitcast(BF16)[0:64, 0:1024]), scv, 1)
            for pi in range(3):
                for jo in range(ND):
                    sc.wait_ge(ctx.sems[mm], mm1(tc, 57 + 8 * pi + jo))
                    sidx = tc * 32 + 8 + pi * 8 + jo
                    if sidx >= 4:
                        pstA.wait(sc, sidx - 4)
                    pb = [PC, PD][jo % 2]
                    dst = stg[:, (sidx % 4) * 512:(sidx % 4 + 1) * 512]
                    if pi == 2:
                        ctx.inc(sc.activation(out=dst, in_=pb[:, :], func=AF.Silu), scv, 1)
                    else:
                        ctx.inc(sc.copy(out=dst, in_=pb[:, :]), scv, 1)
            assert ctx.val(scv) == sc1(tc, 112)

    @block.gpsimd
    def _(g):
        for tc in range(8):
            for jo in range(ND):
                g.wait_ge(ctx.sems[scv], sc1(tc, EVB[jo] + 3))
                sidx = tc * 32 + jo
                pstA.issue(g, Ad[jo * 128:(jo + 1) * 128, tc * 512:(tc + 1) * 512],
                           stg[:, (sidx % 4) * 512:(sidx % 4 + 1) * 512])
                g.wait_ge(ctx.sems[scv], sc1(tc, EVB[jo] + 5))
                gidx = tc * 8 + jo
                pstG.issue(g, gCd.rearrange("(j p) c -> p j c", p=128)[:, jo, tc * 8:(tc + 1) * 8],
                           gstg[:, (gidx % 2) * 8:(gidx % 2 + 1) * 8])
            for jo in range(ND):
                g.wait_ge(ctx.sems[scv], sc1(tc, 81 + jo))
                kidx = tc * 8 + jo
                pstK.issue(g, katd[tc * 512:(tc + 1) * 512, jo * 128:(jo + 1) * 128]
                           .rearrange("(c s) k -> s c k", s=64),
                           katg[:, (kidx % 2) * 1024:(kidx % 2 + 1) * 1024]
                           .rearrange("s (c k) -> s c k", k=128))
            for pi, dst_d in [(0, rd), (1, vd), (2, gtd)]:
                for jo in range(ND):
                    g.wait_ge(ctx.sems[scv], sc1(tc, 89 + 8 * pi + jo))
                    sidx = tc * 32 + 8 + pi * 8 + jo
                    pstA.issue(g, dst_d[jo * 128:(jo + 1) * 128, tc * 512:(tc + 1) * 512],
                               stg[:, (sidx % 4) * 512:(sidx % 4 + 1) * 512])
        g.wait_ge(ctx.sems[mm], M0 + 640)
        wl.issue(g, wpool.rearrange("q (j o) -> q j o", o=1024)[:, 0:8, :],
                 ow_T.rearrange("(j q) o -> q j o", q=128))
    i_wow = 6

    @block.sync
    def _(sync):
        sync.wait_ge(ctx.sems[bd[0]], bd[1])
        for tc in range(8):
            if tc >= 1:
                sync.wait_ge(ctx.sems[mm], mm1(tc - 1, 72))
            for j in range(ND):
                tpx.issue(sync, xnb_s[:, j * 512:(j + 1) * 512],
                          xn_bounce[tc * 512:(tc + 1) * 512, j * 128:(j + 1) * 128],
                          transpose=True)
            if tc == 0:
                xstr.wait(sync, 0)
                xstr.wait(sync, 1)
            else:
                sync.wait_ge(ctx.sems[vsv], vs1(tc - 1, 136))
            for j in range(ND):
                tpx.issue(sync, xsb_s[:, j * 512:(j + 1) * 512],
                          xsd[tc * 512:(tc + 1) * 512, j * 128:(j + 1) * 128],
                          transpose=True)

    if stop == "p1":
        @block.gpsimd
        def _(g):
            pstA.wait_all(g)
            pstK.wait_all(g)
            pstG.wait_all(g)
            wl.wait(g, i_wow)
            ctx.inc(g.memset(stv[0:1, 0:1], 0.0), name + "fin", 1)
        return name + "fin", ctx.val(name + "fin")

    M2 = M0 + 640
    S2 = S0 + 896
    V2 = V0 + 1088

    def mm2(c, off):
        return M2 + 40 * c + off

    def vs2(c, off):
        return V2 + 35 * c + off

    def sc2(c, off):
        return S2 + c + off

    @block.tensor
    def _(t):
        t.wait_ge(ctx.sems[scv], S2)
        t.wait_ge(ctx.sems[vsv], V2)
        for c in range(n2):
            sl = c % 2
            for j in range(ND):
                if c == 0:
                    if j == 0:
                        t.wait_ge(ctx.sem(gpv), 1)
                else:
                    t.wait_ge(ctx.sems[vsv], vs2(c - 1, 10 + 2 * j))
                    if j == 0:
                        t.wait_ge(ctx.sems[scv], sc2(c - 1, 1))
                if "tr" in dbg2:
                    ins = t.transpose(PE[0:64, j * 128:(j + 1) * 128],
                                      wtmp[:, 0:64], idm[:, :])
                else:
                    ins = t.transpose(PE[0:64, j * 128:(j + 1) * 128],
                                      S[:, j * 64:(j + 1) * 64], idm[:, :])
                ctx.inc(ins, mm, 1)
            for j in range(ND):
                if j == 0:
                    cl.wait(t, 3 * c)
                    cl.wait(t, 3 * c + 1)
                    cl2.wait(t, 3 * c + 1)
                    cl2.wait(t, 3 * c + 2)
                if j >= 2:
                    t.wait_ge(ctx.sems[vsv], vs2(c, 1 + j - 2))
                elif c >= 1:
                    t.wait_ge(ctx.sems[vsv], vs2(c - 1, 1 + 6 + j))
                for h2 in range(2):
                    b = h2 * 64
                    if "M" in dbg2:
                        ins = t.matmul(PF[0:64, (j % 2) * 128 + b:(j % 2) * 128 + b + 64],
                                       kat2[0:64, 0:64], kat2[0:64, 0:64],
                                       start=True, stop=True)
                    elif "M1" in dbg2:
                        ins = t.matmul(PF[0:64, (j % 2) * 128 + b:(j % 2) * 128 + b + 64],
                                       vb2[0:64, sl * 512 + j * 64:sl * 512 + (j + 1) * 64],
                                       rb2[0:64, sl * 512 + j * 64:sl * 512 + (j + 1) * 64],
                                       start=True, stop=True)
                    elif "M2" in dbg2:
                        ins = t.matmul(PF[0:64, (j % 2) * 128:(j % 2) * 128 + 64],
                                       vb2[b:b + 64, sl * 512 + j * 64:sl * 512 + (j + 1) * 64],
                                       rb2[b:b + 64, sl * 512 + j * 64:sl * 512 + (j + 1) * 64],
                                       start=(b == 0), stop=(b == 64))
                    else:
                        vop = (vb2 if b == 0 else vbB)[0:64, sl * 512 + j * 64:sl * 512 + (j + 1) * 64]
                        rop = (rb2 if b == 0 else rbB)[0:64, sl * 512 + j * 64:sl * 512 + (j + 1) * 64]
                        ins = t.matmul(PF[0:64, (j % 2) * 128 + b:(j % 2) * 128 + b + 64],
                                       vop, rop, start=True, stop=True)
                ctx.inc(ins, mm, 1)
                t.wait_ge(ctx.sems[vsv], vs2(c, 1 + j))
                if j == 0:
                    cl.wait(t, 3 * c + 2)
                    if c >= 1:
                        t.wait_ge(ctx.sems[vsv], vs2(c - 1, 25))
                for h2 in range(2):
                    b = h2 * 64
                    if "O" in dbg2:
                        ins = t.matmul(PA[b:b + 64, j * 64:(j + 1) * 64],
                                       kat2[0:64, 0:64], kat2[0:64, 0:64],
                                       start=True, stop=True)
                    else:
                        ins = t.matmul(PA[b:b + 64, j * 64:(j + 1) * 64],
                                       kat2[0:64, sl * 1024 + j * 128 + b:sl * 1024 + j * 128 + b + 64],
                                       mmb2[0:64, (j % 2) * 128 + b:(j % 2) * 128 + b + 64],
                                       start=True, stop=True)
                ctx.inc(ins, mm, 1)
                if j == 0:
                    cl2.wait(t, 3 * c + 1)
                    t.wait_ge(ctx.sems[scv], sc2(c, 1))
                    if c >= 1:
                        t.wait_ge(ctx.sems[vsv], vs2(c - 1, 33))
                for h2 in range(2):
                    b = h2 * 64
                    rsl = (rb2[0:64, sl * 512 + j * 64:sl * 512 + (j + 1) * 64]
                           if h2 == 0 else
                           rbB[0:64, sl * 512 + j * 64:sl * 512 + (j + 1) * 64])
                    if "I" in dbg2:
                        ins = t.matmul(PB[b:b + 64, j * 64:(j + 1) * 64],
                                       kat2[0:64, 0:64], kat2[0:64, 0:64],
                                       start=True, stop=True)
                    else:
                        ins = t.matmul(PB[b:b + 64, j * 64:(j + 1) * 64],
                                       Stv[0:64, j * 128 + b:j * 128 + b + 64],
                                       rsl, start=True, stop=True)
                ctx.inc(ins, mm, 1)
                if j == 0:
                    ct.wait(t, c)
                for h2 in range(2):
                    b = h2 * 64
                    if "P" in dbg2:
                        ins = t.matmul(PC[b:b + 64, j * 64:(j + 1) * 64],
                                       kat2[0:64, 0:64], kat2[0:64, 0:64],
                                       start=True, stop=True)
                    else:
                        ins = t.matmul(PC[b:b + 64, j * 64:(j + 1) * 64],
                                       kat2[0:64, sl * 1024 + j * 128 + b:sl * 1024 + j * 128 + b + 64],
                                       vtk2[0:64, sl * 1024 + j * 128 + b:sl * 1024 + j * 128 + b + 64],
                                       start=True, stop=True)
                ctx.inc(ins, mm, 1)
            assert ctx.val(mm) == mm2(c, 40)

    @block.vector
    def _(v):
        for c in range(n2):
            sl = c % 2
            for j in range(ND):
                v.wait_ge(ctx.sems[mm], mm2(c, 9 + 4 * j))
                if c == 0 and j == 0:
                    ld.wait(v, i_mskl)
                if j >= 2:
                    v.wait_ge(ctx.sems[mm], mm2(c, 10 + 4 * (j - 2)))
                elif c >= 1:
                    v.wait_ge(ctx.sems[mm], mm2(c - 1, 10 + 4 * (6 + j)))
                if "msk" in dbg2:
                    ctx.inc(v.tensor_tensor(out=mmb2[:, (j % 2) * 128:(j % 2 + 1) * 128],
                                            in0=maskL[:, :], in1=maskL[:, :],
                                            op=ALU.mult), vsv, 1)
                else:
                    ctx.inc(v.tensor_tensor(out=mmb2[:, (j % 2) * 128:(j % 2 + 1) * 128],
                                            in0=PF[0:64, (j % 2) * 128:(j % 2 + 1) * 128],
                                            in1=maskL[:, :], op=ALU.mult), vsv, 1)
            for j in range(ND):
                v.wait_ge(ctx.sems[mm], mm2(c, 12 + 4 * j))
                v.wait_ge(ctx.sems[mm], mm2(c, 1 + j))
                if j == 0:
                    cg.wait(v, c // 8)
                v.wait_ge(ctx.sems[vsv], ctx.val(vsv))
                ctx.inc(v.tensor_scalar(
                    out=Stmp[:, :], in0=S[:, j * 64:(j + 1) * 64],
                    scalar1=gcb[:, (c // 8 % 2) * 64 + j * 8 + c % 8:(c // 8 % 2) * 64 + j * 8 + c % 8 + 1],
                    scalar2=None, op0=ALU.mult), vsv, 1)
                v.wait_ge(ctx.sems[vsv], ctx.val(vsv))
                ctx.inc(v.tensor_tensor(
                    out=S[:, j * 64:(j + 1) * 64], in0=Stmp[:, :],
                    in1=PC[:, j * 64:(j + 1) * 64], op=ALU.add), vsv, 1)
            assert ctx.val(vsv) == vs2(c, 24)
            v.wait_ge(ctx.sems[mm], mm2(c, 38))
            cl2.wait(v, 3 * c)
            if "oev" in dbg2:
                ctx.inc(v.tensor_scalar(out=o32[:, :], in0=o32[:, :], scalar1=0.0,
                                        scalar2=None, op0=ALU.mult), vsv, 1)
            else:
                ctx.inc(v.tensor_tensor(out=o32[:, :], in0=PA[:, :],
                                        in1=Ab2[:, sl * 512:(sl + 1) * 512],
                                        op=ALU.mult), vsv, 1)
            v.wait_ge(ctx.sems[mm], mm2(c, 39))
            for j in range(ND):
                gc_ = (c // 8 % 2) * 64 + j * 8 + c % 8
                if "oev" in dbg2:
                    ctx.inc(v.tensor_scalar(out=t_i[:, j * 64:(j + 1) * 64],
                                            in0=t_i[:, j * 64:(j + 1) * 64],
                                            scalar1=0.0, scalar2=None,
                                            op0=ALU.mult), vsv, 1)
                else:
                    ctx.inc(v.tensor_scalar(out=t_i[:, j * 64:(j + 1) * 64],
                                            in0=PB[:, j * 64:(j + 1) * 64],
                                            scalar1=gcb[:, gc_:gc_ + 1], scalar2=None,
                                            op0=ALU.mult), vsv, 1)
            v.wait_ge(ctx.sems[vsv], ctx.val(vsv))
            ctx.inc(v.tensor_tensor(out=t_i[:, :], in0=t_i[:, :],
                                    in1=Ab2[:, sl * 512:(sl + 1) * 512],
                                    op=ALU.mult), vsv, 1)
            v.wait_ge(ctx.sems[vsv], ctx.val(vsv))
            if c >= 2:
                pstO.wait(v, c - 2)
            ctx.inc(v.tensor_tensor(out=ostg[:, sl * 512:(sl + 1) * 512],
                                    in0=t_i[:, :], in1=o32[:, :],
                                    op=ALU.add), vsv, 1)
            assert ctx.val(vsv) == vs2(c, 35)

    @block.scalar
    def _(sc):
        for s in range(4):
            pstA.wait(sc, 252 + s)
        pstK.wait(sc, 62)
        pstK.wait(sc, 63)
        pstG.wait(sc, 62)
        pstG.wait(sc, 63)
        with nc.allow_non_contiguous_dma(reason="64-col transpose"):
            for nn in range(2):
                ct.issue(sc, vtk2[:, nn * 1024:(nn + 1) * 1024],
                         vd[:, nn * 64:(nn + 1) * 64], transpose=True)
        for c in range(n2):
            sc.wait_ge(ctx.sems[mm], mm2(c, 8))
            if c >= 1:
                sc.wait_ge(ctx.sems[mm], mm2(c - 1, 39))
            if "sev" in dbg2:
                ctx.inc(sc.copy(out=Stv[:, :], in_=katg[:, 0:1024]), scv, 1)
            else:
                ctx.inc(sc.copy(out=Stv[:, :], in_=PE[0:64, 0:1024]), scv, 1)
            if c + 2 < n2:
                sc.wait_ge(ctx.sems[mm], mm2(c, 40))
                s2_ = (c + 2) % 2
                with nc.allow_non_contiguous_dma(reason="64-col transpose"):
                    ct.issue(sc, vtk2[:, s2_ * 1024:(s2_ + 1) * 1024],
                             vd[:, (c + 2) * 64:(c + 3) * 64], transpose=True)

    @block.gpsimd
    def _(g):
        g.wait_ge(ctx.sems[mm], M0 + 640)
        ctx.inc(g.memset(S[:, :], 0.0), gpv, 1)
        pstA.wait(g, 252)
        pstA.wait(g, 253)
        pstA.wait(g, 254)
        pstA.wait(g, 255)
        for c in range(n2):
            sl = c % 2
            if c >= 2:
                g.wait_ge(ctx.sems[mm], mm2(c - 2, 40))
                g.wait_ge(ctx.sems[vsv], vs2(c - 2, 35))
            cl2.issue(g, Ab2.rearrange("p (s j f) -> p s j f", s=2, f=64)[:, sl],
                      Ad.rearrange("(j p) t -> p j t", p=128)[:, :, c * 64:(c + 1) * 64])
            cl2.issue(g, rbB.rearrange("p (s j f) -> p s j f", s=2, f=64)[:, sl],
                      rd.rearrange("(j two p) t -> p two j t", two=2, p=64)[:, 1, :, c * 64:(c + 1) * 64])
            cl2.issue(g, vbB.rearrange("p (s j f) -> p s j f", s=2, f=64)[:, sl],
                      vd.rearrange("(j two p) t -> p two j t", two=2, p=64)[:, 1, :, c * 64:(c + 1) * 64])
            g.wait_ge(ctx.sems[vsv], vs2(c, 35))
            pstO.issue(g, od.rearrange("(j p) t -> p j t", p=128)[:, :, c * 64:(c + 1) * 64],
                       ostg[:, (c % 2) * 512:(c % 2 + 1) * 512]
                       .rearrange("p (j f) -> p j f", f=64))

    @block.sync
    def _(sync):
        for s in range(4):
            pstA.wait(sync, 252 + s)
        pstK.wait(sync, 62)
        pstK.wait(sync, 63)
        pstG.wait(sync, 62)
        pstG.wait(sync, 63)
        for c in range(n2):
            if c >= 2:
                sync.wait_ge(ctx.sems[mm], mm2(c - 2, 40))
                sync.wait_ge(ctx.sems[vsv], vs2(c - 2, 35))
            sl = c % 2
            cl.issue(sync, rb2.rearrange("p (s j f) -> p s j f", s=2, f=64)[:, sl],
                     rd.rearrange("(j p) t -> p j t", p=128)[:, :, c * 64:(c + 1) * 64])
            cl.issue(sync, vb2.rearrange("p (s j f) -> p s j f", s=2, f=64)[:, sl],
                     vd.rearrange("(j p) t -> p j t", p=128)[:, :, c * 64:(c + 1) * 64])
            cl.issue(sync, kat2[:, sl * 1024:(sl + 1) * 1024], katd[c * 64:(c + 1) * 64, :])
            if c % 8 == 0:
                pn = c // 8
                if pn >= 2:
                    sync.wait_ge(ctx.sems[vsv], vs2(8 * (pn - 2) + 7, 35))
                cg.issue(sync, gcb.rearrange("p (s j f) -> p s j f", s=2, f=8)[:, pn % 2],
                         gCd.rearrange("(j p) f -> p j f", p=128)[:, :, pn * 8:(pn + 1) * 8])

    if stop == "p2":
        @block.gpsimd
        def _(g):
            pstO.wait_all(g)
            ctx.inc(g.memset(stv[0:1, 0:1], 0.0), name + "fin", 1)
        return name + "fin", ctx.val(name + "fin")

    M3 = M2 + 2560
    S3 = S2 + 64
    V3 = V2 + 2240

    def mm3(tt, off):
        return M3 + 35 * tt + off

    def vs3(tt, off):
        return V3 + 45 * tt + off

    def sc3(tt, off):
        return S3 + 3 * tt + off

    def y_half_ap(i, e):
        return [[PA, PB], [PC, PD]][e][i % 2][:, 0:512]

    pl = PostLN(ctx, name + "R", x_in, x_out, y_half_ap, None, mm,
                obuf=wpool[:, 26624:30720].bitcast(F32))

    def y_target(i):
        return mm3(i, 35)
    pl.y_target = y_target

    @block.tensor
    def _(t):
        t.wait_ge(ctx.sems[vsv], V3)
        t.wait_ge(ctx.sems[scv], S3)
        wl.wait(t, i_wow)
        for tt in range(n3):
            sl = tt % 2
            for j in range(ND):
                t.wait_ge(ctx.sems[vsv], vs3(tt, 1 + j))
                if j == 0:
                    cl.wait(t, 192 + 3 * tt)
                    t.wait_ge(ctx.sems[gps], 7)
                    if tt >= 1:
                        t.wait_ge(ctx.sems[vsv], vs3(tt - 1, 10))
                if "S3" in dbg2:
                    ins = t.matmul(PG[:, j * 2:(j + 1) * 2], idmb[:, 0:128],
                                   ind2[:, :], start=True, stop=True)
                    ctx.inc(ins, mm, 1)
                    ins = t.matmul(PG[:, 16 + j * 2:16 + (j + 1) * 2], idmb[:, 0:128],
                                   ind2[:, :], start=True, stop=True)
                    ctx.inc(ins, mm, 1)
                else:
                    ins = t.matmul(PG[:, j * 2:(j + 1) * 2],
                                   odb[:, sl * 1024 + j * 128:sl * 1024 + (j + 1) * 128],
                                   ind2[:, :], start=True, stop=True)
                    ctx.inc(ins, mm, 1)
                    ins = t.matmul(PG[:, 16 + j * 2:16 + (j + 1) * 2],
                                   sqb[:, j * 128:(j + 1) * 128],
                                   ind2[:, :], start=True, stop=True)
                    ctx.inc(ins, mm, 1)
            t.wait_ge(ctx.sems[scv], sc3(tt, 2))
            t.wait_ge(ctx.sems[vsv], vs3(tt, 13))
            if tt >= 1:
                t.wait_ge(ctx.sems[scv], sc3(tt - 1, 3))
            if "T3" in dbg2:
                ctx.inc(t.transpose(PH[:, :], idm[:, 0:32], idm[:, :]), mm, 1)
            else:
                ctx.inc(t.transpose(PH[:, :], scols[:, :], idm[:, :]), mm, 1)
            for j in range(ND):
                if j == 0:
                    cl.wait(t, 192 + 3 * tt + 2)
                    ld.wait(t, i_gnw)
                    if tt >= 1:
                        t.wait_ge(ctx.sems[vsv], vs3(tt - 1, 45))
                else:
                    t.wait_ge(ctx.sems[vsv], vs3(tt, 14 + 4 * (j - 1)))
                for h2 in range(2):
                    h = 2 * j + h2
                    if "B3" in dbg2:
                        ins = t.matmul(PF[h2 * 64:h2 * 64 + 64, 0:128],
                                       gnwr[0:1, 0:64], gnwr[0:1, 0:128],
                                       start=True, stop=True)
                    else:
                        ins = t.matmul(PF[h2 * 64:h2 * 64 + 64, 0:128],
                                       gnwr[0:1, j * 128 + h2 * 64:j * 128 + h2 * 64 + 64],
                                       srows[0:1, h * 128:(h + 1) * 128],
                                       start=True, stop=True)
                ctx.inc(ins, mm, 1)
                for h2 in range(2):
                    h = 2 * j + h2
                    if "B3" in dbg2:
                        ins = t.matmul(PF[h2 * 64:h2 * 64 + 64, 128:256],
                                       gnwr[0:1, 0:64], gnwr[0:1, 0:128],
                                       start=True, stop=True)
                    else:
                        ins = t.matmul(PF[h2 * 64:h2 * 64 + 64, 128:256],
                                       gnwr[0:1, j * 128 + h2 * 64:j * 128 + h2 * 64 + 64],
                                       srows[0:1, 2048 + h * 128:2048 + (h + 1) * 128],
                                       start=True, stop=True)
                ctx.inc(ins, mm, 1)
            t.wait_ge(ctx.sems[vsv], vs3(tt, 45))
            if tt >= 2:
                t.wait_ge(ctx.sems[pl.vs], pl.apply_done(tt - 2))
            for e in range(2):
                pY = y_half_ap(tt, e)
                for j in range(ND):
                    if "F3" in dbg2:
                        ins = t.matmul(pY, idmb[:, :],
                                       ow_buf[:, 0:512],
                                       start=(j == 0), stop=(j == ND - 1))
                    else:
                        ins = t.matmul(pY,
                                       Qs[:, sl * 1024 + j * 128:sl * 1024 + (j + 1) * 128],
                                       ow_buf[:, j * 1024 + e * 512:j * 1024 + e * 512 + 512],
                                       start=(j == 0), stop=(j == ND - 1))
                ctx.inc(ins, mm, 1)
            assert ctx.val(mm) == mm3(tt, 35)

    @block.vector
    def _(v):
        for tt in range(n3):
            sl = tt % 2
            for j in range(ND):
                if j == 0:
                    cl.wait(v, 192 + 3 * tt)
                    if tt >= 1:
                        v.wait_ge(ctx.sems[mm], mm3(tt - 1, 16))
                ctx.inc(v.tensor_tensor(out=sqb[:, j * 128:(j + 1) * 128],
                                        in0=odb[:, sl * 1024 + j * 128:sl * 1024 + (j + 1) * 128],
                                        in1=odb[:, sl * 1024 + j * 128:sl * 1024 + (j + 1) * 128],
                                        op=ALU.mult), vsv, 1)
            v.wait_ge(ctx.sems[mm], mm3(tt, 16))
            ctx.inc(v.tensor_scalar(out=stv[:, 0:16], in0=PG[:, 0:16],
                                    scalar1=1.0 / 64.0, scalar2=None, op0=ALU.mult), vsv, 1)
            ctx.inc(v.tensor_scalar(out=stv[:, 16:32], in0=PG[:, 16:32],
                                    scalar1=1.0 / 64.0, scalar2=None, op0=ALU.mult), vsv, 1)
            v.wait_ge(ctx.sems[vsv], ctx.val(vsv))
            ctx.inc(v.tensor_tensor(out=stv[:, 32:48], in0=stv[:, 0:16],
                                    in1=stv[:, 0:16], op=ALU.mult), vsv, 1)
            v.wait_ge(ctx.sems[vsv], ctx.val(vsv))
            ctx.inc(v.tensor_tensor(out=stv[:, 32:48], in0=stv[:, 16:32],
                                    in1=stv[:, 32:48], op=ALU.subtract), vsv, 1)
            v.wait_ge(ctx.sems[scv], sc3(tt, 2))
            if "MR" in dbg2:
                ctx.inc(v.tensor_tensor(out=scols[:, 16:32], in0=stv[:, 0:16],
                                        in1=scols[:, 0:16], op=ALU.mult), vsv, 1)
            else:
                ctx.inc(v.scalar_tensor_tensor(out=scols[:, 16:32], in0=stv[:, 0:16],
                                               scalar=-1.0, in1=scols[:, 0:16],
                                               op0=ALU.mult, op1=ALU.mult), vsv, 1)
            assert ctx.val(vsv) == vs3(tt, 13)
            for j in range(ND):
                v.wait_ge(ctx.sems[mm], mm3(tt, 19 + 2 * j))
                v.wait_ge(ctx.sems[vsv], ctx.val(vsv))
                ctx.inc(v.tensor_scalar(out=bev[:, (j % 2) * 256:(j % 2 + 1) * 256],
                                        in0=PF[:, 0:256], scalar1=1.0, scalar2=None,
                                        op0=ALU.mult), vsv, 1)
                v.wait_ge(ctx.sems[vsv], ctx.val(vsv))
                ctx.inc(v.tensor_tensor(out=e1b[:, :],
                                        in0=bev[:, (j % 2) * 256:(j % 2) * 256 + 128],
                                        in1=odb[:, sl * 1024 + j * 128:sl * 1024 + (j + 1) * 128],
                                        op=ALU.mult), vsv, 1)
                v.wait_ge(ctx.sems[vsv], ctx.val(vsv))
                ctx.inc(v.tensor_tensor(out=e2b[:, :],
                                        in0=bev[:, (j % 2) * 256 + 128:(j % 2 + 1) * 256],
                                        in1=e1b[:, :], op=ALU.add), vsv, 1)
                v.wait_ge(ctx.sems[vsv], ctx.val(vsv))
                if j == 0:
                    cl.wait(v, 192 + 3 * tt + 1)
                    ld.wait(v, i_gnb)
                    if tt >= 2:
                        v.wait_ge(ctx.sems[mm], mm3(tt - 2, 35))
                ctx.inc(v.scalar_tensor_tensor(
                    out=Qs[:, sl * 1024 + j * 128:sl * 1024 + (j + 1) * 128],
                    in0=e2b[:, :], scalar=gnbc[:, j:j + 1],
                    in1=gtb[:, sl * 1024 + j * 128:sl * 1024 + (j + 1) * 128],
                    op0=ALU.add, op1=ALU.mult), vsv, 1)
            assert ctx.val(vsv) == vs3(tt, 45)
            pl.vec_tile(v, tt)

    @block.scalar
    def _(sc):
        for tt in range(n3):
            sc.wait_ge(ctx.sems[vsv], vs3(tt, 12))
            ctx.inc(sc.activation(out=rc3[:, :], in_=stv[:, 32:48], func=AF.Ln,
                                  bias=ctx.c(GN_EPS)), scv, 1)
            sc.wait_ge(ctx.sems[scv], ctx.val(scv))
            ctx.inc(sc.activation(out=scols[:, 0:16], in_=rc3[:, :], func=AF.Exp,
                                  scale=-0.5), scv, 1)
            sc.wait_ge(ctx.sems[mm], mm3(tt, 17))
            if tt >= 2:
                pstS.wait(sc, tt - 2)
            if "C3" in dbg2:
                ctx.inc(sc.copy(out=s32[:, (tt % 2) * 128:(tt % 2 + 1) * 128],
                                in_=katg[0:32, 0:128]), scv, 1)
            else:
                ctx.inc(sc.copy(out=s32[:, (tt % 2) * 128:(tt % 2 + 1) * 128],
                                in_=PH[:, :]), scv, 1)
            assert ctx.val(scv) == sc3(tt, 3)
            pl.sc_tile(sc, tt)

    @block.gpsimd
    def _(g):
        for tt in range(n3):
            g.wait_ge(ctx.sems[scv], sc3(tt, 3))
            if "ST3" in dbg2:
                pstS.issue(g, statd[tt % 2], katg[0:32, 0:128])
            else:
                pstS.issue(g, statd[tt % 2], s32[:, (tt % 2) * 128:(tt % 2 + 1) * 128])
            pl.st_tile(g, tt)

    @block.sync
    def _(sync):
        pstO.wait(sync, 62)
        pstO.wait(sync, 63)
        for tt in range(n3):
            if tt >= 2:
                sync.wait_ge(ctx.sems[mm], mm3(tt - 2, 16))
                sync.wait_ge(ctx.sems[vsv], vs3(tt - 2, 45))
            sl = tt % 2
            cl.issue(sync, odb.rearrange("p (s j f) -> p s j f", s=2, f=128)[:, sl],
                     od.rearrange("(j p) t -> p j t", p=128)[:, :, tt * 128:(tt + 1) * 128])
            cl.issue(sync, gtb.rearrange("p (s j f) -> p s j f", s=2, f=128)[:, sl],
                     gtd.rearrange("(j p) t -> p j t", p=128)[:, :, tt * 128:(tt + 1) * 128])
            pstS.wait(sync, tt)
            if tt >= 1:
                sync.wait_ge(ctx.sems[mm], mm3(tt - 1, 33))
            if "sr" in dbg2:
                cl.issue(sync, srows[0:1, :],
                         gCd.rearrange("a b -> (a b)").rearrange("(one f) -> one f", one=1)[0:1, 0:4096])
            else:
                cl.issue(sync, srows[0:1, :],
                         statd.rearrange("s r c -> s (r c)")[tt % 2:tt % 2 + 1, :])
            pl.ld_tile(sync, tt)

    return pl.finish()



def emit_l1_attn(ctx, x_in, x_out, ipw_T, ipb_cols, opw_h, opb_row, maskneg_d,
                 xn_bounce):
    nc, block = ctx.nc, ctx.block
    name = "l1a"
    NU = 32768
    NUT = 256

    xnfs = ctx.sb(name + "xnf", [128, 2 * 4096], BF16)
    stage = ctx.sb(name + "stg", [128, 3 * 4096], BF16)
    wip = ctx.sb(name + "wip", [128, 8 * 384], BF16)
    opw = ctx.sb(name + "opw", [64, 2 * 1024], BF16)
    qh = ctx.sb(name + "qh", [64, 2 * 2 * 2048], BF16)
    kh = ctx.sb(name + "kh", [64, 2 * 2 * 2048], BF16)
    vtk = ctx.sb(name + "vtk", [128, 4 * 64], BF16)
    Lpb = ctx.sb(name + "Lp", [128, 2 * 128], F32)
    Ebf = ctx.sb(name + "Eb", [128, 2 * 128], BF16)
    Pbf = ctx.sb(name + "Pb", [128, 2 * 128], BF16)
    Ptb = ctx.sb(name + "Pt", [128, 2 * 128], BF16)
    osl = ctx.sb(name + "osl", [64, 2 * 2 * 128], BF16)
    ystg = ctx.sb(name + "yst", [128, 2 * 1024], BF16)
    ybuf = ctx.sb(name + "yb", [128, 2 * 1024], BF16)
    sml = ctx.sb(name + "sml", [128, 8], F32)
    bic = ctx.sb(name + "bic", [128, 3], F32)
    qbc = ctx.sb(name + "qbc", [128, 1], F32)
    opbs = ctx.sb(name + "opb", [1, 1024], F32)
    mskn = ctx.sb(name + "msk", [128, 128], F32)
    one_r = ctx.sb(name + "onr", [1, 128], F32)
    idmb2 = ctx.sb(name + "idb", [128, 128], BF16)
    onesb2 = ctx.sb(name + "on2", [128, 128], BF16)

    qkvd = nc.dram_tensor(name + "_qkv", [384, NU], BF16, kind="Internal").ap()
    outd = nc.dram_tensor(name + "_out", [8, T, D], BF16, kind="Internal").ap()
    xnag = nc.dram_tensor(name + "_xag", [8, T, D], BF16, kind="Internal",
                          addr_space="Shared").ap()
    rsout = nc.dram_tensor(name + "_rs", [T, D], BF16, kind="Internal").ap()

    ld = DmaStream(ctx, "wld8", depth=8)
    wl = DmaStream(ctx, name + "wl", depth=2)
    tpw = DmaStream(ctx, name + "tp", depth=16)
    qst = DmaStream(ctx, name + "qs", depth=3)
    qld = DmaStream(ctx, name + "ql", depth=8)
    vld = DmaStream(ctx, name + "vl", depth=4)
    yt = DmaStream(ctx, name + "ys", depth=2)
    ygl = DmaStream(ctx, name + "yg", depth=2)
    mm, scv, vsv, gps, cc, yv = (name + "mm", name + "sc", name + "vs",
                                 name + "g0", name + "cc", name + "yv")

    bd = pre_ln_to_fm(ctx, name + "P", x_in, xn_bounce, None)

    @block.sync
    def _(sync):
        ld.issue(sync, bic[:, :], ipb_cols[:, :])
        ld.issue(sync, mskn[:, :], maskneg_d[:, :])
        ld.issue(sync, opbs[:, :], opb_row[:, :])
    i_bic, i_msk, i_opb = 0, 1, 2

    @block.gpsimd
    def _(g):
        ctx.inc(g.memset(one_r[:, :], 1.0), gps, 1)
        ctx.inc(g.memset(onesb2[:, :], 1.0), gps, 1)
        g.wait_ge(ctx.sems[gps], ctx.val(gps))
        ctx.inc(g.affine_select(idmb2[:, :], onesb2[:, :], pattern=[[1, 128]],
                                compare_op=ALU.is_equal, fill=0.0, base=0,
                                channel_multiplier=-1), gps, 1)
        wl.issue(g, wip.rearrange("q (j o) -> q j o", o=384),
                 ipw_T.rearrange("(j q) o -> q j o", q=128))
        wl.issue(g, opw.rearrange("q (h o) -> q h o", o=1024),
                 opw_h.rearrange("q (h o) -> q h o", o=1024))
        g.wait_ge(ctx.sems[bd[0]], bd[1])
        ctx.inc(g.collective_compute(
            "AllGather", ALU.bypass, ins=[xn_bounce[:, :]],
            outs=[xnag[:, :, :]], replica_groups=RG8), cc, 1)
    gps_t = ctx.val(gps)

    @block.vector
    def _(v):
        ld.wait(v, i_bic)
        ctx.inc(v.tensor_scalar(out=qbc[:, :], in0=bic[:, 0:1], scalar1=0.125,
                                scalar2=None, op0=ALU.mult), vsv, 1)
        ld.wait(v, i_opb)
        ctx.inc(v.tensor_scalar(out=opbs[:, :], in0=opbs[:, :], scalar1=0.125,
                                scalar2=None, op0=ALU.mult), vsv, 1)
    V0 = ctx.val(vsv)

    PA = ctx.ps([128, 512], F32)
    PB = ctx.ps([128, 512], F32)
    PC = ctx.ps([128, 512], F32)
    PL = ctx.ps([128, 2 * 128], F32)
    PT = ctx.ps([128, 2 * 128], F32)
    PO = ctx.ps([64, 2 * 128], F32)
    PY = ctx.ps([128, 2 * 512], F32)

    M0 = ctx.val(mm)
    S0 = ctx.val(scv)
    MB = M0 + 192
    SB = S0 + 192

    def mmA(g24):
        return M0 + g24 + 1

    def scA(g24):
        return S0 + g24 + 1

    def mmB(i, hl, k):
        return MB + 8 * i + 3 * hl + k

    def mmE(i, e):
        return MB + 8 * i + 6 + e + 1

    def vsB(i, hl, k):
        return V0 + 10 * i + 5 * hl + k

    def scB(i, hl, k):
        return SB + 8 * i + 3 * hl + k

    def scE(i, e):
        return SB + 8 * i + 6 + e + 1

    @block.tensor
    def _(t):
        wl.wait(t, 0)
        for tc in range(8):
            for s in range(8):
                k_use = tc * 8 + s
                for j in range(ND):
                    tpw.wait(t, k_use * 8 + j)
                for ot in range(3):
                    g24 = tc * 24 + s * 3 + ot
                    if g24 >= 3:
                        t.wait_ge(ctx.sems[scv], scA(g24 - 3))
                    pb = [PA, PB, PC][g24 % 3]
                    for j in range(ND):
                        ins = t.matmul(pb[:, :],
                                       wip[:, j * 384 + ot * 128:j * 384 + (ot + 1) * 128],
                                       xnfs[:, (s % 2) * 4096 + j * 512:(s % 2) * 4096 + (j + 1) * 512],
                                       start=(j == 0), stop=(j == ND - 1))
                    ctx.inc(ins, mm, 1)
    assert ctx.val(mm) == MB

    @block.scalar
    def _(sc):
        ctx.wcp(sc)
        sc.wait_ge(ctx.sems[vsv], V0)
        for tc in range(8):
            for s in range(8):
                for ot in range(3):
                    g24 = tc * 24 + s * 3 + ot
                    sc.wait_ge(ctx.sems[mm], mmA(g24))
                    if s == 0 and tc >= 1:
                        qst.wait(sc, (tc - 1) * 3 + ot)
                    pb = [PA, PB, PC][g24 % 3]
                    dst = stage.rearrange("p (o ut sx tl) -> p o ut sx tl",
                                          o=3, sx=8, tl=16)[:, ot, :, s, :]
                    psv = pb.rearrange("p (a b) -> p a b", b=16)
                    if ot == 0:
                        ctx.inc(sc.activation(out=dst, in_=psv, func=AF.Identity,
                                              scale=0.125, bias=qbc[:, 0:1]), scv, 1)
                    else:
                        ctx.inc(sc.activation(out=dst, in_=psv, func=AF.Identity,
                                              bias=bic[:, ot:ot + 1]), scv, 1)
    assert ctx.val(scv) == SB

    @block.sync
    def _(sync):
        sync.wait_ge(ctx.sems[cc], 1)
        for tc in range(8):
            for s in range(8):
                k_use = tc * 8 + s
                if k_use >= 2:
                    sync.wait_ge(ctx.sems[mm], mmA((k_use - 2) * 3 + 2))
                for j in range(ND):
                    tpw.issue(sync, xnfs[:, (s % 2) * 4096 + j * 512:(s % 2) * 4096 + (j + 1) * 512],
                              xnag[s, tc * 512:(tc + 1) * 512, j * 128:(j + 1) * 128],
                              transpose=True)

    @block.gpsimd
    def _(g):
        for tc in range(8):
            for ot in range(3):
                g.wait_ge(ctx.sems[scv], scA(tc * 24 + 21 + ot))
                qst.issue(g, qkvd[ot * 128:(ot + 1) * 128, tc * 4096:(tc + 1) * 4096],
                          stage[:, ot * 4096:(ot + 1) * 4096])

    @block.tensor
    def _(t):
        wl.wait(t, 1)
        for i in range(NUT):
            uc = i // 16
            for hl in range(2):
                if i % 16 == 0:
                    qld.wait(t, uc * 4 + hl * 2)
                    qld.wait(t, uc * 4 + hl * 2 + 1)
                if i >= 1:
                    t.wait_ge(ctx.sems[vsv], vsB(i - 1, hl, 1))
                sl2 = (uc % 2)
                co = hl * 2 * 2048 + sl2 * 2048 + (i % 16) * 128
                ins = t.matmul(PL[:, (hl % 2) * 128:(hl % 2 + 1) * 128],
                               qh[0:64, co:co + 128], kh[0:64, co:co + 128],
                               start=True, stop=True)
                ctx.inc(ins, mm, 1)
                t.wait_ge(ctx.sems[vsv], vsB(i, hl, 5))
                if i >= 1:
                    t.wait_ge(ctx.sems[scv], scB(i - 1, hl, 2))
                ins = t.transpose(PT.bitcast(BF16)[:, hl * 128:(hl + 1) * 128],
                                  Pbf[:, hl * 128:(hl + 1) * 128], idmb2[:, :])
                ctx.inc(ins, mm, 1)
                t.wait_ge(ctx.sems[scv], scB(i, hl, 2))
                vld.wait(t, 2 * i + hl)
                if i >= 1:
                    t.wait_ge(ctx.sems[scv], scB(i - 1, hl, 3))
                ins = t.matmul(PO[:, hl * 128:(hl + 1) * 128],
                               vtk[:, ((2 * i + hl) % 4) * 64:((2 * i + hl) % 4 + 1) * 64],
                               Ptb[:, hl * 128:(hl + 1) * 128],
                               start=True, stop=True)
                ctx.inc(ins, mm, 1)
            t.wait_ge(ctx.sems[scv], scB(i, 1, 3))
            t.wait_ge(ctx.sems[gps], gps_t)
            for e in range(2):
                if i >= 1:
                    t.wait_ge(ctx.sems[scv], scE(i - 1, e))
                for hl in range(2):
                    t.matmul(PY[:, e * 512:(e + 1) * 512],
                             osl[0:64, (i % 2) * 256 + hl * 128:(i % 2) * 256 + (hl + 1) * 128],
                             opw[0:64, hl * 1024 + e * 512:hl * 1024 + e * 512 + 512],
                             start=(hl == 0), stop=False)
                ins = t.matmul(PY[:, e * 512:(e + 1) * 512],
                               one_r[0:1, :], opbs[0:1, e * 512:(e + 1) * 512],
                               start=False, stop=True)
                ctx.inc(ins, mm, 1)
            assert ctx.val(mm) == mmE(i, 1)

    @block.vector
    def _(v):
        for i in range(NUT):
            for hl in range(2):
                it = 2 * i + hl
                v.wait_ge(ctx.sems[mm], mmB(i, hl, 1))
                if i == 0 and hl == 0:
                    ld.wait(v, i_msk)
                if i >= 1:
                    v.wait_ge(ctx.sems[scv], scB(i - 1, hl, 1))
                lp = Lpb[:, hl * 128:(hl + 1) * 128]
                ctx.inc(v.tensor_tensor(out=lp, in0=PL[:, (hl % 2) * 128:(hl % 2 + 1) * 128],
                                        in1=mskn[:, :], op=ALU.add), vsv, 1)
                v.wait_ge(ctx.sems[vsv], ctx.val(vsv))
                ctx.inc(v.reduce_max(sml[:, hl:hl + 1], lp, axis=AX.X,
                                     negate=True), vsv, 1)
                v.wait_ge(ctx.sems[vsv], ctx.val(vsv))
                v.wait_ge(ctx.sems[scv], scB(i, hl, 1))
                ctx.inc(v.reduce_sum(sml[:, 2 + hl:3 + hl],
                                     Ebf[:, hl * 128:(hl + 1) * 128], axis=AX.X), vsv, 1)
                v.wait_ge(ctx.sems[vsv], ctx.val(vsv))
                ctx.inc(v.reciprocal(sml[:, 4 + hl:5 + hl],
                                     sml[:, 2 + hl:3 + hl]), vsv, 1)
                v.wait_ge(ctx.sems[vsv], ctx.val(vsv))
                if i >= 1:
                    v.wait_ge(ctx.sems[mm], mmB(i - 1, hl, 2))
                ctx.inc(v.tensor_scalar(out=Pbf[:, hl * 128:(hl + 1) * 128],
                                        in0=Ebf[:, hl * 128:(hl + 1) * 128],
                                        scalar1=sml[:, 4 + hl:5 + hl], scalar2=None,
                                        op0=ALU.mult), vsv, 1)
                assert ctx.val(vsv) == vsB(i, hl, 5)

    @block.scalar
    def _(sc):
        qst.wait(sc, 2)
        for it0 in range(4):
            i0, hl0 = divmod(it0, 2)
            with nc.allow_non_contiguous_dma(reason="64x128 v transpose"):
                vld.issue(sc, vtk[:, (it0 % 4) * 64:(it0 % 4 + 1) * 64],
                          qkvd[256 + hl0 * 64:256 + (hl0 + 1) * 64,
                               i0 * 128:(i0 + 1) * 128], transpose=True)
        for i in range(NUT):
            for hl in range(2):
                sc.wait_ge(ctx.sems[vsv], vsB(i, hl, 2))
                if i >= 1:
                    sc.wait_ge(ctx.sems[vsv], vsB(i - 1, hl, 5))
                ctx.inc(sc.activation(out=Ebf[:, hl * 128:(hl + 1) * 128],
                                      in_=Lpb[:, hl * 128:(hl + 1) * 128],
                                      func=AF.Exp, bias=sml[:, hl:hl + 1]), scv, 1)
                sc.wait_ge(ctx.sems[mm], mmB(i, hl, 2))
                if i >= 1:
                    sc.wait_ge(ctx.sems[mm], mmB(i - 1, hl, 3))
                ctx.inc(sc.copy(out=Ptb[:, hl * 128:(hl + 1) * 128],
                                in_=PT.bitcast(BF16)[:, hl * 128:(hl + 1) * 128]), scv, 1)
                sc.wait_ge(ctx.sems[mm], mmB(i, hl, 3))
                if i >= 2:
                    sc.wait_ge(ctx.sems[mm], mmE(i - 2, 1))
                ctx.inc(sc.copy(out=osl[0:64, (i % 2) * 256 + hl * 128:(i % 2) * 256 + (hl + 1) * 128],
                                in_=PO[:, hl * 128:(hl + 1) * 128]), scv, 1)
                it = 2 * i + hl
                if it + 4 < 2 * NUT:
                    i2, hl2 = divmod(it + 4, 2)
                    sc.wait_ge(ctx.sems[mm], mmB(i, hl, 3))
                    with nc.allow_non_contiguous_dma(reason="64x128 v transpose"):
                        vld.issue(sc, vtk[:, (it % 4) * 64:(it % 4 + 1) * 64],
                                  qkvd[256 + hl2 * 64:256 + (hl2 + 1) * 64,
                                       i2 * 128:(i2 + 1) * 128], transpose=True)
            for e in range(2):
                sc.wait_ge(ctx.sems[mm], mmE(i, e))
                if i >= 2:
                    yt.wait(sc, i - 2)
                ctx.inc(sc.copy(out=ystg[:, (i % 2) * 1024 + e * 512:(i % 2) * 1024 + (e + 1) * 512],
                                in_=PY[:, e * 512:(e + 1) * 512]), scv, 1)
            assert ctx.val(scv) == scE(i, 1)

    @block.sync
    def _(sync):
        qst.wait(sync, 21)
        qst.wait(sync, 22)
        qst.wait(sync, 23)
        for uc in range(16):
            for hl in range(2):
                if uc >= 2:
                    sync.wait_ge(ctx.sems[mm], mmB(16 * (uc - 2) + 15, hl, 3))
                sl2 = uc % 2
                qld.issue(sync, qh.rearrange("p (h s c) -> p h s c", h=2, s=2)[:, hl, sl2],
                          qkvd[hl * 64:(hl + 1) * 64, uc * 2048:(uc + 1) * 2048])
                qld.issue(sync, kh.rearrange("p (h s c) -> p h s c", h=2, s=2)[:, hl, sl2],
                          qkvd[128 + hl * 64:128 + (hl + 1) * 64, uc * 2048:(uc + 1) * 2048])

    @block.gpsimd
    def _(g):
        for i in range(NUT):
            g.wait_ge(ctx.sems[scv], scE(i, 1))
            yt.issue(g, outd.rearrange("s (tb tl) e -> s tb tl e", tl=16)[:, i],
                     ystg[:, (i % 2) * 1024:(i % 2 + 1) * 1024])
        yt.wait_all(g)
        ctx.inc(g.collective_compute(
            "ReduceScatter", ALU.add, ins=[outd[:, :, :]],
            outs=[rsout[:, :]], replica_groups=RG8), cc, 1)

    def y_half_ap(i, hh):
        return ybuf[:, (i % 2) * 1024 + hh * 512:(i % 2) * 1024 + (hh + 1) * 512]

    pl = PostLN(ctx, name + "R", x_in, x_out, y_half_ap, None, yv)

    def y_target(i):
        return i + 1
    pl.y_target = y_target

    @block.gpsimd
    def _(g):
        for i in range(NT):
            ygl.wait(g, i)
            ctx.inc(g.memset(sml[0:1, 7:8], 0.0), yv, 1)
            pl.st_tile(g, i)

    @block.sync
    def _(sync):
        sync.wait_ge(ctx.sems[cc], 2)
        for i in range(NT):
            if i >= 2:
                sync.wait_ge(ctx.sems[pl.vs], pl.apply_done(i - 2))
            ygl.issue(sync, ybuf[:, (i % 2) * 1024:(i % 2 + 1) * 1024],
                      rsout[i * 128:(i + 1) * 128, :])
            pl.ld_tile(sync, i)

    @block.vector
    def _(v):
        for i in range(NT):
            pl.vec_tile(v, i)

    @block.scalar
    def _(sc):
        for i in range(NT):
            pl.sc_tile(sc, i)

    return pl.finish()


import contextlib
from concourse.bass_utils import run_bass_kernel_spmd

DH = 4096


def _cols(a, n):
    return np.ascontiguousarray(np.asarray(a).reshape(n, 128).T)


def _build(emit_fn, specs, **kw):
    import concourse.bass as bass
    nc = bass.Bass("TRN2", target_bir_lowering=False, debug=False)
    x_in = nc.dram_tensor("x_in", [T, D], F32, kind="ExternalInput").ap()
    x_out = nc.dram_tensor("x_out", [T, D], F32, kind="ExternalOutput").ap()
    xn_bounce = nc.dram_tensor("xn_bounce", [T, D], BF16, kind="Internal").ap()
    aps = {}
    for nm, shape, dt in specs:
        aps[nm] = nc.dram_tensor(nm, shape, dt, kind="ExternalInput").ap()
    with contextlib.ExitStack() as stack:
        block = stack.enter_context(nc.Block())
        ctx = Ctx(nc, block, stack)
        ctx.setup_cpool()
        fin = emit_fn(ctx, x_in, x_out, *[aps[nm] for nm, _, _ in specs],
                      xn_bounce, **kw)

        @block.sync
        def _(sync):
            sync.wait_ge(ctx.sems[fin[0]], fin[1])
    return nc, ctx


def kernel(**inputs):
    inp = inputs
    x = np.ascontiguousarray(inp["L0_attn_x"].astype(np.float32))

    launches = []

    i64 = np.arange(64, dtype=np.float64)
    maskL = np.where(i64[:, None] <= i64[None, :], 1.0, 0.0).astype(np.float32)
    maskL2 = np.concatenate([maskL, maskL], axis=1)
    lxm = np.zeros((64, 66), np.float32)
    lxm[:, 0:64] = (np.where(i64[:, None] <= i64[None, :], 1.0, 0.0)
                    - np.where(i64 <= 31, 1.0, 0.0)[:, None] * np.ones((1, 64)))
    lxm[:, 64] = (i64 <= 31).astype(np.float32)
    lxm[:, 65] = 1.0
    mixs = [inp["L0_attn_mix_r"], inp["L0_attn_mix_k"], inp["L0_attn_mix_v"],
            inp["L0_attn_mix_g"], inp["L0_attn_mix_w"]]
    mix_cols = np.concatenate([_cols(m, 8) for m in mixs], axis=1).astype(np.float32)
    specs = [("ddw_T", [D, D], F32), ("rw_T", [D, D], F32), ("kw_T", [D, D], F32),
             ("vw_T", [D, D], F32), ("gw_T", [D, D], F32), ("ww_T", [D, D], F32),
             ("ow_T", [D, D], F32), ("mix_cols", [128, 40], F32),
             ("gnw_row", [1, D], F32), ("gnb_cols", [128, 8], F32),
             ("maskL2_d", [64, 128], F32), ("lx_d", [64, 66], F32)]
    nc0, ctx0 = _build(emit_l0_attn, specs)
    feed0 = {"cpool": ctx0.cpool_np(),
             "ddw_T": np.ascontiguousarray(inp["L0_attn_ddlerp_w"].T),
             "rw_T": np.ascontiguousarray(inp["L0_attn_r_w"].T),
             "kw_T": np.ascontiguousarray(inp["L0_attn_k_w"].T),
             "vw_T": np.ascontiguousarray(inp["L0_attn_v_w"].T),
             "gw_T": np.ascontiguousarray(inp["L0_attn_g_w"].T),
             "ww_T": np.ascontiguousarray(inp["L0_attn_w_w"].T),
             "ow_T": np.ascontiguousarray(inp["L0_attn_out_w"].T),
             "mix_cols": np.ascontiguousarray(mix_cols),
             "gnw_row": np.ascontiguousarray(inp["L0_attn_gn_weight"].reshape(1, D)),
             "gnb_cols": _cols(inp["L0_attn_gn_bias"], 8),
             "maskL2_d": np.ascontiguousarray(maskL2), "lx_d": lxm}
    launches.append((nc0, [dict(feed0) for _ in range(8)]))

    specs = [("w_T", [D, D], F32), ("ln_gamma", [128, ND], F32),
             ("ln_beta", [128, ND], F32)]
    nc1, ctx1 = _build(emit_l0_ffn, specs)
    feed1 = {"cpool": ctx1.cpool_np(), "w_T": np.ascontiguousarray(inp["L0_ffn_weight"]),
             "ln_gamma": _cols(inp["L0_ffn_ln_gamma"], 8),
             "ln_beta": _cols(inp["L0_ffn_ln_beta"], 8)}
    launches.append((nc1, [dict(feed1) for _ in range(8)]))

    ipw = inp["L1_attn_in_proj_weight"]
    ipb = inp["L1_attn_in_proj_bias"]
    opw = inp["L1_attn_out_proj_weight"]
    opb = inp["L1_attn_out_proj_bias"]
    u128 = np.arange(128)
    maskneg = np.where(u128[:, None] % 16 == u128[None, :] % 16, 0.0, -1e30).astype(np.float32)
    specs = [("ipw_T", [D, 384], F32), ("ipb_cols", [128, 3], F32),
             ("opw_h", [64, 2 * D], F32), ("opb_row", [1, D], F32),
             ("maskneg_d", [128, 128], F32)]
    nc2, ctx2 = _build(emit_l1_attn, specs)
    feeds2 = []
    for c in range(8):
        rows = np.concatenate([np.arange(2 * c * 64, 2 * c * 64 + 128),
                               1024 + np.arange(2 * c * 64, 2 * c * 64 + 128),
                               2048 + np.arange(2 * c * 64, 2 * c * 64 + 128)])
        opw_c = opw.T[2 * c * 64:2 * c * 64 + 128]
        feeds2.append({"cpool": ctx2.cpool_np(),
                       "ipw_T": np.ascontiguousarray(ipw[rows].T),
                       "ipb_cols": np.ascontiguousarray(ipb[rows].reshape(3, 128).T),
                       "opw_h": np.ascontiguousarray(
                           opw_c.reshape(2, 64, D).transpose(1, 0, 2).reshape(64, 2 * D)),
                       "opb_row": np.ascontiguousarray(opb.reshape(1, D)),
                       "maskneg_d": maskneg})
    launches.append((nc2, feeds2))

    specs = [("wr_T_d", [16, D, D], F32), ("ws0_T", [D, D], F32),
             ("ws1_T", [D, D], F32), ("cen_T", [D, 16], F32),
             ("ebias_row", [1, 16], F32), ("br_d", [16, D], F32),
             ("bs_row", [1, 2 * D], F32)]
    nc3, ctx3 = _build(emit_l1_ffn, specs)
    feed3 = {"cpool": ctx3.cpool_np(),
             "wr_T_d": np.ascontiguousarray(inp["L1_ffn_Wr"].transpose(0, 2, 1)),
             "ws0_T": np.ascontiguousarray(inp["L1_ffn_Ws"][0].T),
             "ws1_T": np.ascontiguousarray(inp["L1_ffn_Ws"][1].T),
             "cen_T": np.ascontiguousarray(inp["L1_ffn_centroids"].T),
             "ebias_row": np.ascontiguousarray(inp["L1_ffn_expert_biases"].reshape(1, 16)),
             "br_d": np.ascontiguousarray(inp["L1_ffn_br"]),
             "bs_row": np.ascontiguousarray(inp["L1_ffn_bs"].reshape(1, 2 * D))}
    launches.append((nc3, [dict(feed3) for _ in range(8)]))

    dh_ = 256
    inv_freq = 1.0 / 10000.0 ** (np.arange(0, dh_, 2, dtype=np.float64) / dh_)
    tpos = np.arange(T, dtype=np.float64)
    fr = tpos[:, None] * inv_freq[None, :]
    cos_d = np.ascontiguousarray(np.cos(fr).T.astype(np.float32))
    sin_d = np.ascontiguousarray(np.sin(fr).T.astype(np.float32))
    gammas = 1.0 - 2.0 ** (-5.0 - np.arange(4, dtype=np.float64))
    i128 = np.arange(128, dtype=np.float64)
    maskT = np.zeros((128, 512), np.float32)
    crossdec = np.zeros((128, 512), np.float32)
    for h in range(4):
        diff = i128[None, :] - i128[:, None]
        maskT[:, h * 128:(h + 1) * 128] = np.where(
            diff >= 0, gammas[h] ** np.maximum(diff, 0), 0.0) / 16.0
        crossdec[:, h * 128:(h + 1) * 128] = (gammas[h] ** (i128 + 1) / 16.0)[None, :]
    kdec = np.stack([gammas[h] ** (127.0 - i128) for h in range(4)], axis=1).astype(np.float32)
    specs = [("wq_T", [D, D], F32), ("wk_T", [D, D], F32), ("wv_T", [D, 2048], F32),
             ("wg_T", [D, 2048], F32), ("wo_T", [2048, D], F32),
             ("gn_cols", [128, 4], F32), ("cos_d", [128, T], F32),
             ("sin_d", [128, T], F32), ("maskT_d", [128, 512], F32),
             ("crossdec_d", [128, 512], F32), ("kdec_d", [128, 4], F32)]
    nc4, ctx4 = _build(emit_l2_attn, specs)
    feed4 = {"cpool": ctx4.cpool_np(),
             "wq_T": np.ascontiguousarray(inp["L2_attn_Wq"].T),
             "wk_T": np.ascontiguousarray(inp["L2_attn_Wk"].T),
             "wv_T": np.ascontiguousarray(inp["L2_attn_Wv"].T),
             "wg_T": np.ascontiguousarray(inp["L2_attn_Wg"].T),
             "wo_T": np.ascontiguousarray(inp["L2_attn_Wo"].T),
             "gn_cols": _cols(inp["L2_attn_gnorm_w"], 4),
             "cos_d": cos_d, "sin_d": sin_d, "maskT_d": maskT,
             "crossdec_d": crossdec, "kdec_d": kdec}
    launches.append((nc4, [dict(feed4) for _ in range(8)]))

    specs = [("w1_d", [D, DH], F32), ("w2_d", [DH, D], F32),
             ("wr_col", [128, 8], F32), ("b1_col", [128, 32], F32),
             ("b2_row", [1, D], F32)]
    nc5, ctx5 = _build(emit_l2_ffn, specs)
    feed5 = {"cpool": ctx5.cpool_np(),
             "w1_d": np.ascontiguousarray(inp["L2_ffn_W1"]),
             "w2_d": np.ascontiguousarray(inp["L2_ffn_W2"]),
             "wr_col": _cols(inp["L2_ffn_w_router"], 8),
             "b1_col": _cols(inp["L2_ffn_b1"], 32),
             "b2_row": np.ascontiguousarray(inp["L2_ffn_b2"].reshape(1, D))}
    launches.append((nc5, [dict(feed5) for _ in range(8)]))

    specs = [("bw_T", [D, 256], F32), ("dw_T", [D, D], F32), ("cw_T", [256, D], F32),
             ("a_col", [128, 2], F32), ("bb_col", [128, 2], F32),
             ("cb_row", [1, D], F32), ("db_row", [1, D], F32)]
    nc6, ctx6 = _build(emit_l3_attn, specs)
    feed6 = {"cpool": ctx6.cpool_np(),
             "bw_T": np.ascontiguousarray(inp["L3_attn_B_w"].T),
             "dw_T": np.ascontiguousarray(inp["L3_attn_D_w"].T),
             "cw_T": np.ascontiguousarray(inp["L3_attn_C_w"].T),
             "a_col": _cols(inp["L3_attn_A"], 2),
             "bb_col": _cols(inp["L3_attn_B_b"], 2),
             "cb_row": np.ascontiguousarray(inp["L3_attn_C_b"].reshape(1, D)),
             "db_row": np.ascontiguousarray(inp["L3_attn_D_b"].reshape(1, D))}
    launches.append((nc6, [dict(feed6) for _ in range(8)]))

    specs = [("w1_T", [D, DH], F32), ("w2_T", [D, DH], F32), ("h1_T", [DH, DH], F32),
             ("h2_T", [DH, DH], F32), ("w3_T", [DH, D], F32),
             ("s1", [128, 32], F32), ("s2", [128, 32], F32), ("hs1", [128, 32], F32),
             ("hs2", [128, 32], F32), ("s3", [1, D], F32)]
    nc7, ctx7 = _build(emit_l3_ffn, specs)
    feed7 = {"cpool": ctx7.cpool_np(),
             "w1_T": np.ascontiguousarray(inp["L3_ffn_w1_w"].T),
             "w2_T": np.ascontiguousarray(inp["L3_ffn_w2_w"].T),
             "h1_T": np.ascontiguousarray(inp["L3_ffn_h1_w"].T),
             "h2_T": np.ascontiguousarray(inp["L3_ffn_h2_w"].T),
             "w3_T": np.ascontiguousarray(inp["L3_ffn_w3_w"].T),
             "s1": _cols(inp["L3_ffn_w1_s"], 32), "s2": _cols(inp["L3_ffn_w2_s"], 32),
             "hs1": _cols(inp["L3_ffn_h1_s"], 32), "hs2": _cols(inp["L3_ffn_h2_s"], 32),
             "s3": np.ascontiguousarray(inp["L3_ffn_w3_s"].reshape(1, D))}
    launches.append((nc7, [dict(feed7) for _ in range(8)]))

    for nc_i, feeds in launches:
        maps = [dict(feeds[b], x_in=np.ascontiguousarray(x[b])) for b in range(8)]
        res = run_bass_kernel_spmd(nc_i, maps, core_ids=list(range(8)))
        x = np.stack([res.results[b]["x_out"] for b in range(8)])
    return x
